# version 23
# baseline (speedup 1.0000x reference)
import math
import numpy as np
import ml_dtypes
import concourse.bass as bass
import concourse.mybir as mybir
from concourse.bass import IndirectOffsetOnAxis as IOA
from concourse.bass_isa import ReduceOp
from concourse.library_config import attnmlp as LIB_ATTNMLP
from concourse.bass_utils import run_bass_kernel_spmd
from contextlib import ExitStack

F32 = mybir.dt.float32
BF16 = mybir.dt.bfloat16
AF = mybir.ActivationFunctionType
OP = mybir.AluOpType
AX = mybir.AxisListType

DM = 1024
NH = 16
HD = 64
RD = 32
LAT = 256
IFF = 2048
LN_EPS = 1e-5

ENGINES = ("sync", "gpsimd", "tensor", "vector", "scalar")


class Ctx:

    def __init__(self, nc, stack):
        self.nc = nc
        self.stack = stack
        self.plan = {e: [] for e in ENGINES}
        self.sems = {}
        self.cnt = {}
        self.last_wait = {}
        self.owner = {}

    def sem(self, name):
        if name not in self.sems:
            self.sems[name] = self.stack.enter_context(self.nc.semaphore(name))
            self.cnt[name] = 0
        return self.sems[name]

    def bump(self, name, n):
        self.sem(name)
        self.cnt[name] += n
        return self.cnt[name]

    def emit(self, eng, fn):
        self.plan[eng].append(fn)

    def dma(self, eng, out, in_, sem_name, accum=False):
        self.owner[sem_name] = eng
        val = self.bump(sem_name, 16)
        s = self.sems[sem_name]
        if accum:
            def fn(e, out=out, in_=in_, s=s):
                e.dma_start(out=out, in_=in_, accum_op=OP.add).then_inc(s, 16)
        else:
            def fn(e, out=out, in_=in_, s=s):
                e.dma_start(out=out, in_=in_).then_inc(s, 16)
        self.emit(eng, fn)
        return (sem_name, val)

    def idma(self, out, out_off, in_, in_off, sem_name):
        self.owner[sem_name] = "gpsimd"
        val = self.bump(sem_name, 16)
        s = self.sems[sem_name]

        def fn(e, out=out, oo=out_off, in_=in_, io=in_off, s=s):
            e.indirect_dma_start(out=out, out_offset=oo, in_=in_,
                                 in_offset=io).then_inc(s, 16)
        self.emit("gpsimd", fn)
        return (sem_name, val)

    def wait(self, eng, *toks):
        for tok in toks:
            if tok is None:
                continue
            name, val = tok
            if val <= 0:
                continue
            if self.last_wait.get((eng, name), -1) >= val:
                continue
            self.last_wait[(eng, name)] = val
            s = self.sem(name)
            self.emit(eng, lambda e, s=s, v=val: e.wait_ge(s, v))

    def op(self, eng, fn, sem_name=None):
        if sem_name is None:
            self.emit(eng, fn)
            return None
        self.owner[sem_name] = eng
        val = self.bump(sem_name, 1)
        s = self.sems[sem_name]

        def wrapped(e, fn=fn, s=s):
            fn(e).then_inc(s, 1)

        self.emit(eng, wrapped)
        return (sem_name, val)

    def tok(self, name):
        return (name, self.cnt.get(name, 0))

    barrier_exempt = ("wcast", "xz")

    def barrier(self):
        for eng in ENGINES:
            for name, owner in list(self.owner.items()):
                if owner == eng and name not in self.barrier_exempt:
                    self.wait(eng, (name, self.cnt[name]))
        vals = {}
        for eng in ENGINES:
            vals[eng] = self.bump("bar_" + eng, 1)
        for eng in ENGINES:
            s = self.sems["bar_" + eng]
            self.emit(eng, lambda e, s=s: e.nop().then_inc(s, 1))
        for eng in ENGINES:
            for other in ENGINES:
                s = self.sems["bar_" + other]
                self.emit(eng, lambda e, s=s, v=vals[other]: e.wait_ge(s, v))

    def flush(self):
        if not any(self.plan[e] for e in ENGINES):
            return
        with self.nc.Block() as block:
            for eng_name in ENGINES:
                items = self.plan[eng_name]
                if not items:
                    continue
                section = getattr(block, eng_name)

                def make(items):
                    def _(engine):
                        for fn in items:
                            fn(engine)
                    return _

                section(make(items))
        self.plan = {e: [] for e in ENGINES}


class StopBuild(Exception):
    pass


class MK:
    def __init__(self, T, dbg=(), upto=None):
        self.upto = upto
        self.phase_n = 0
        self.T = T
        self.NT = T // 128
        self.TC = T // 512
        self.NSH = T // 512
        self.NCH = 2 * (T // 512) + 7
        self.NSLOT = self.NCH * 512
        self.nc = bass.Bass("TRN2", target_bir_lowering=False)
        self.stack = ExitStack()
        self.ctx = Ctx(self.nc, self.stack)
        self.dbg = set(dbg)
        self.d = {}
        self.sb = {}
        self.open_stacks = []

    def phase(self):
        self.phase_n += 1
        if self.upto is not None and self.phase_n > self.upto:
            raise StopBuild()

    def estack(self):
        s = ExitStack()
        self.open_stacks.append(s)
        return s

    def close_stack(self, s):
        self.open_stacks.remove(s)
        s.close()

    def dram(self, name, shape, dtype=F32, kind=None):
        if name in self.dbg and kind is None:
            kind = "ExternalOutput"
        t = (self.nc.dram_tensor(name, shape, dtype, kind=kind) if kind
             else self.nc.dram_tensor(name, shape, dtype))
        self.d[name] = t.ap()
        return self.d[name]

    _uid = 0

    def _uniq(self, name):
        MK._uid += 1
        return "%s_%d" % (name, MK._uid)

    def sbuf(self, name, shape, dtype, stack=None, side=None):
        t = (stack or self.stack).enter_context(
            self.nc.sbuf_tensor(self._uniq(name), shape, dtype, side=side))
        self.sb[name] = t
        return t

    def psum(self, name, shape, dtype, stack=None):
        t = (stack or self.stack).enter_context(
            self.nc.psum_tensor(self._uniq(name), shape, dtype))
        self.sb[name] = t
        return t

    def declare(self):
        d, T = self.dram, self.T
        d("x_in", [T, DM], kind="ExternalInput")
        d("y_out", [T, DM], kind="ExternalOutput")
        d("c_ident", [128, 128], kind="ExternalInput")
        d("c_mask", [128, 128], kind="ExternalInput")
        d("c_cos", [T, 512], kind="ExternalInput")
        d("c_sinS", [T, 512], kind="ExternalInput")
        d("c_gam", [128, 6, DM], kind="ExternalInput")
        d("c_bo", [128, DM], kind="ExternalInput")
        d("c_router", [8, 128, 8], kind="ExternalInput")
        d("c_rbias", [128, 8], kind="ExternalInput")
        d("c_half2", [2, 128], kind="ExternalInput")
        d("m_kv_T", [8, 128, LAT], kind="ExternalInput")
        d("m_q_T", [8, 128, LAT], kind="ExternalInput")
        d("m_knope_T", [2, 128, 512], kind="ExternalInput")
        d("m_qnope_T", [2, 128, 512], kind="ExternalInput")
        d("m_v_T", [2, 128, DM], kind="ExternalInput")
        d("m_rq_T", [2, 128, 512], kind="ExternalInput")
        d("m_rk_T", [8, 128, 512], kind="ExternalInput")
        d("m_o_T", [8, 128, DM], kind="ExternalInput")
        d("a_q_T", [8, 128, DM], kind="ExternalInput")
        d("a_k_T", [8, 128, DM], kind="ExternalInput")
        d("a_v_T", [8, 128, DM], kind="ExternalInput")
        d("a_o_T", [8, 128, DM], kind="ExternalInput")
        d("moe_gu", [8, 16, 128, 2, 8, 128], kind="ExternalInput")
        d("moe_dn", [8, 128, 16, 2, 512], kind="ExternalInput")
        d("c_rmeta", [128, 115 + 2 * self.NCH], kind="ExternalInput")
        d("gu_bf", [8 * 16 * 128, 2048], BF16)
        d("dn_bf", [8 * 128, 16 * 2 * 512], BF16)
        d("xn_tm", [T, DM], BF16)
        d("xnd", [self.NSLOT, DM], BF16)
        d("eod", [self.NSLOT, DM], BF16)
        d("ridx", [2, 128, self.NT], mybir.dt.int32)
        d("rw", [2, 128, self.NT])
        d("u_s", [T, DM])
        d("x_cur", [T, DM])
        d("acc", [T, DM])
        d("kfm", [NH, HD, T], BF16)
        d("qfm", [NH, HD, T], BF16)
        d("vtm", [self.NT, 128, NH * 65], BF16)
        d("ofm", [8, self.TC, 128, 512], BF16)
        d("lsum", [NH, T], F32)

    def load_consts(self):
        ctx, sb, d = self.ctx, self.sb, self.d
        self.sbuf("c_ident", [128, 128], BF16)
        self.sbuf("c_mask", [128, 128], BF16)
        self.sbuf("c_gam", [128, 6, DM], F32)
        self.sbuf("c_bo", [128, DM], F32)
        self.sbuf("c_rbias", [128, 8], F32)
        self.sbuf("c_eps", [128, 1], F32)
        self.sbuf("c_tiny", [128, 1], F32)
        self.sbuf("c_mln8", [128, 1], F32)
        self.sbuf("c_one64", [1, 64], BF16)
        self.sbuf("c_ones1b", [1, 128], BF16)
        self.sbuf("c_half2", [2, 128], BF16)
        self.sbuf("c_rmeta", [128, 115 + 2 * self.NCH], F32)
        self.sbuf("c_ones128", [128, 128], BF16)
        self.sbuf("zbuf", [128, 2, DM], BF16)
        toks = [
            ctx.dma("gpsimd", sb["c_ident"][:, :], d["c_ident"][:, :], "wks"),
            ctx.dma("gpsimd", sb["c_mask"][:, :], d["c_mask"][:, :], "wks"),
            ctx.dma("gpsimd", sb["c_gam"][:, :, :], d["c_gam"][:, :, :], "wks"),
            ctx.dma("gpsimd", sb["c_bo"][:, :], d["c_bo"][:, :], "wks"),
            ctx.dma("gpsimd", sb["c_rbias"][:, :], d["c_rbias"][:, :], "wks"),
            ctx.dma("gpsimd", sb["c_half2"][:, :], d["c_half2"][:, :], "wks"),
            ctx.dma("gpsimd", sb["c_rmeta"][:, :], d["c_rmeta"][:, :], "wks"),
        ]
        g1 = ctx.op("gpsimd", lambda e: (
            e.memset(sb["c_eps"][:, :], float(LN_EPS)),
            e.memset(sb["c_tiny"][:, :], 1e-16),
            e.memset(sb["c_mln8"][:, :], -math.log(8.0)),
            e.memset(sb["c_one64"][:, :], 1.0),
            e.memset(sb["c_ones1b"][:, :], 1.0),
            e.memset(sb["c_ones128"][:, :], 1.0))[-1], "gpk")
        for eng in ENGINES:
            ctx.wait(eng, toks[-1], g1)
            ctx.wait(eng, ("wks", ctx.cnt["wks"]))

    def gamma(self, idx):
        return self.sb["c_gam"][:, idx, :]

    def rmeta(self, name, n):
        off = {"lt7": 0, "le7": 49, "thr": 98, "chunk": 98 + self.NCH,
               "ipp": 98 + 2 * self.NCH, "pid": 114 + 2 * self.NCH}[name]
        return self.sb["c_rmeta"][:, off:off + n]

    def prep_weights(self):
        ctx, d = self.ctx, self.d
        for e_ in range(8):
            ctx.dma("gpsimd",
                    bass.AP(d["gu_bf"].tensor, e_ * 2048 * 2048,
                            [[2048, 2048], [1, 2048]]),
                    bass.AP(d["moe_gu"].tensor, e_ * 2048 * 2048,
                            [[2048, 2048], [1, 2048]]), "wcast")
            ctx.dma("gpsimd", d["dn_bf"][e_ * 128:(e_ + 1) * 128, :],
                    bass.AP(d["moe_dn"].tensor, e_ * 128 * 16384,
                            [[16384, 128], [1, 16384]]), "wcast")
        zb = self.sb["zbuf"]
        gz = ctx.op("gpsimd", lambda e: e.memset(zb[:, :, :], 0.0), "gpz")
        ctx.wait("gpsimd", gz)
        for c in range(2 * self.NCH):
            ctx.dma("gpsimd",
                    bass.AP(d["xnd"].tensor, c * 256 * DM,
                            [[DM, 128], [128 * DM, 2], [1, DM]]),
                    zb[:, :, :], "xz")
        self._wprep = (ctx.tok("wcast"), ctx.tok("xz"))

    def ln_pass(self, src, dst_tm=None, fm_sb=None, gamma=None, beta=None, hook=None):
        ctx, sb, NT = self.ctx, self.sb, self.NT
        G = 4
        st = self.estack()
        xt = self.sbuf("lnxt", [128, 2 * G, DM], F32, st)
        ut = self.sbuf("lnut", [128, 2 * G, DM], F32, st)
        gt = self.sbuf("lngt", [128, 2 * G, DM], BF16, st)
        f1 = self.sbuf("lnf1", [128, 2, DM], F32, st)
        sqj = self.sbuf("lnsqj", [128, DM], BF16, st)
        sm = {n: self.sbuf("ln" + n, [128, 2 * G], F32, st) for n in
              ("s", "q", "m", "t", "v", "l", "r")}
        ptp = self.psum("lnptp", [128, 1024], BF16, st)
        want_fm = fm_sb is not None
        slot_free = [None] * (2 * G)
        hook_last = None
        assert NT % G == 0
        for g0 in range(0, NT, G):
            tts = list(range(g0, g0 + G))
            slots = [tt % (2 * G) for tt in tts]
            for tt, sl in zip(tts, slots):
                if slot_free[sl] is not None:
                    ctx.wait("sync", *slot_free[sl])
                ctx.dma("sync", xt[:, sl, :], src[tt * 128:(tt + 1) * 128, :],
                        "lnl%d" % (sl // G))
            ld = ctx.tok("lnl%d" % (slots[0] // G))
            ctx.wait("vector", ld)
            ctx.wait("scalar", ld)
            for tt, sl in zip(tts, slots):
                ctx.op("vector", lambda e, sl=sl: e.reduce_sum(
                    sm["s"][:, sl:sl + 1], xt[:, sl, :], AX.X), "lnd")
            for tt, sl in zip(tts, slots):
                ctx.wait("scalar", ctx.tok("lna"))
                ctx.op("scalar", lambda e, sl=sl: e.activation(
                    sqj[:, :], xt[:, sl, :], AF.Square,
                    accum_out=sm["q"][:, sl:sl + 1]), "lna")
            ctx.wait("vector", ctx.tok("lnd"))
            for tt, sl in zip(tts, slots):
                ctx.op("vector", lambda e, sl=sl: (
                    e.tensor_scalar(sm["m"][:, sl:sl + 1], sm["s"][:, sl:sl + 1],
                                    1.0 / DM, None, OP.mult),
                    e.tensor_tensor(sm["t"][:, sl:sl + 1], sm["s"][:, sl:sl + 1],
                                    sm["s"][:, sl:sl + 1], OP.mult))[-1], "lnd")
            ctx.wait("vector", ctx.tok("lnd"), ctx.tok("lna"))
            for tt, sl in zip(tts, slots):
                ctx.op("vector", lambda e, sl=sl: e.scalar_tensor_tensor(
                    sm["v"][:, sl:sl + 1], sm["t"][:, sl:sl + 1], -1.0 / DM,
                    sm["q"][:, sl:sl + 1], OP.mult, OP.add), "lnd")
            ctx.wait("scalar", ctx.tok("lnd"))
            for tt, sl in zip(tts, slots):
                ctx.op("scalar", lambda e, sl=sl: e.activation(
                    sm["l"][:, sl:sl + 1], sm["v"][:, sl:sl + 1], AF.Ln,
                    scale=1.0 / DM, bias=sb["c_eps"][:, 0:1]), "lna")
            ctx.wait("scalar", ctx.tok("lna"))
            for tt, sl in zip(tts, slots):
                ctx.op("scalar", lambda e, sl=sl: e.activation(
                    sm["r"][:, sl:sl + 1], sm["l"][:, sl:sl + 1], AF.Exp,
                    scale=-0.5), "lna")
            ctx.wait("vector", ctx.tok("lna"))
            applies = {}
            for tt, sl in zip(tts, slots):
                applies[tt] = ctx.op("vector", lambda e, sl=sl: e.tensor_scalar(
                    ut[:, sl, :], xt[:, sl, :], sm["m"][:, sl:sl + 1],
                    sm["r"][:, sl:sl + 1], OP.subtract, OP.mult), "lnd")
            if dst_tm is not None:
                ctx.wait("sync", applies[tts[-1]])
                for tt, sl in zip(tts, slots):
                    ctx.dma("sync", dst_tm[tt * 128:(tt + 1) * 128, :],
                            ut[:, sl, :], "lns%d" % (sl // G))
                st_tok = ctx.tok("lns%d" % (slots[0] // G))
            else:
                st_tok = None
            if want_fm or hook is not None:
                ctx.wait("vector", ctx.tok("lnd"))
                for tt, sl in zip(tts, slots):
                    b2 = tt % 2
                    fdep = None
                    if gamma is not None and beta is not None:
                        ctx.wait("vector", hook_last)
                        v5 = ctx.op("vector", lambda e, sl=sl, b2=b2: e.tensor_tensor(
                            f1[:, b2, :], ut[:, sl, :], gamma, OP.mult), "lnd")
                        ctx.wait("vector", v5)
                        v6 = ctx.op("vector", lambda e, sl=sl, b2=b2: e.tensor_tensor(
                            gt[:, sl, :], f1[:, b2, :], beta, OP.add), "lnd")
                    elif gamma is not None and hook is not None:
                        ctx.wait("vector", hook_last)
                        v5 = ctx.op("vector", lambda e, sl=sl, b2=b2: e.tensor_tensor(
                            f1[:, b2, :], ut[:, sl, :], gamma, OP.mult), "lnd")
                        ctx.wait("vector", v5)
                        v6 = ctx.op("vector", lambda e, sl=sl, b2=b2: e.tensor_copy(
                            gt[:, sl, :], f1[:, b2, :]), "lnd")
                    elif gamma is not None:
                        v6 = ctx.op("vector", lambda e, sl=sl: e.tensor_tensor(
                            gt[:, sl, :], ut[:, sl, :], gamma, OP.mult), "lnd")
                    else:
                        v6 = ctx.op("vector", lambda e, sl=sl: e.tensor_copy(
                            gt[:, sl, :], ut[:, sl, :]), "lnd")
                    if want_fm:
                        ctx.wait("tensor", v6, ctx.tok("lnt"))
                        pe_v = None
                        for fc in range(8):
                            pe_v = ctx.op("tensor", lambda e, sl=sl, fc=fc: e.transpose(
                                ptp[:, fc * 128:(fc + 1) * 128],
                                gt[:, sl, fc * 128:(fc + 1) * 128],
                                sb["c_ident"][:, :]), "lnp" if fc == 7 else None)
                        ctx.wait("vector", pe_v)
                        v7 = ctx.op("vector", lambda e, tt=tt: e.tensor_copy(
                            fm_sb[:, :, tt * 128:(tt + 1) * 128],
                            ptp[:, :].rearrange("p (f t) -> p f t", f=8)), "lnt")
                    else:
                        v7 = None
                    hk = None
                    hks = ()
                    if hook is not None:
                        hk = hook(tt, f1[:, b2, :], gt[:, sl, :], v6, v7)
                        if isinstance(hk, tuple):
                            hks = hk[1:]
                            hk = hk[0]
                        hook_last = hk
                    slot_free[sl] = [t for t in (applies[tt], v7, hk, st_tok,
                                                 ctx.tok("lna")) + hks
                                     if t is not None]
            else:
                for tt, sl in zip(tts, slots):
                    slot_free[sl] = [t for t in (applies[tt], st_tok,
                                                 ctx.tok("lna")) if t is not None]
        ctx.wait("sync", ctx.tok("lns0"), ctx.tok("lns1"))
        out_toks = [ctx.tok("lnd"), ctx.tok("lna"), ctx.tok("lnt")]
        self.close_stack(st)
        return out_toks

    def tile_norms(self, tile, dep, mstack, tt, sq, s16):
        ctx = self.ctx
        ctx.wait("scalar", dep, ctx.tok("nsd"))
        a = ctx.op("scalar", lambda e: e.activation(sq[:, :], tile, AF.Square), "nsa")
        ctx.wait("vector", a)
        v = ctx.op("vector", lambda e: e.tensor_reduce(
            s16[:, :], sq[:, :].rearrange("p (h d) -> p h d", h=NH), AX.X, OP.add), "nsd")
        ctx.wait("vector", v)
        v2 = ctx.op("vector", lambda e, tt=tt: e.reduce_max(
            mstack[:, tt:tt + 1], s16[:, :], AX.X), "nsd")
        return v2, a

    def bound(self, mq, mk_, scale, deps, ptp, pbnd):
        ctx, sb = self.ctx, self.sb
        ctx.wait("vector", *deps)
        t1 = self.sb["bnd_t1"]
        t1b = self.sb["bnd_t1b"]
        t2 = self.sb["bnd_t2"]
        cb = self.sb["cbias"]
        v1 = ctx.op("vector", lambda e: (
            e.reduce_max(t1[:, 0:1], mq[:, :], AX.X),
            e.reduce_max(t1[:, 1:2], mk_[:, :], AX.X))[-1], "nsd")
        ctx.wait("vector", v1)
        v1b = ctx.op("vector", lambda e: e.tensor_copy(t1b[:, 0:2], t1[:, 0:2]), "nsd")
        ctx.wait("tensor", v1b, ctx.tok("nsd"), ctx.tok("nsa"), ctx.tok("prt"))
        p1 = ctx.op("tensor", lambda e: e.transpose(
            ptp[:, 0:128][0:2, :], t1b[:, 0:2], sb["c_ident"][:, :]), "bnp")
        ctx.wait("vector", p1)
        v2 = ctx.op("vector", lambda e: e.reduce_max(
            t2[:, 0:1][0:2, :], ptp[:, 0:128][0:2, :], AX.X), "nsd")
        ctx.wait("vector", v2)
        v2b = ctx.op("vector", lambda e: e.tensor_copy(t1b[:, 2:3][0:2, :],
                                                       t2[:, 0:1][0:2, :]), "nsd")
        ctx.wait("tensor", v2b)
        p2 = ctx.op("tensor", lambda e: e.transpose(
            ptp[:, 128:130][0:1, :], t1b[:, 2:3][0:2, :], sb["c_ident"][0:2, 0:2]), "bnp")
        ctx.wait("vector", p2)
        v3 = ctx.op("vector", lambda e: e.tensor_copy(t2[:, 1:3][0:1, :],
                                                      ptp[:, 128:130][0:1, :]), "nsd")
        ctx.wait("scalar", v3)
        a1 = ctx.op("scalar", lambda e: e.activation(t2[:, 3:5][0:1, :],
                                                    t2[:, 1:3][0:1, :], AF.Ln), "nsa")
        ctx.wait("vector", a1)
        v4 = ctx.op("vector", lambda e: e.tensor_tensor(
            t2[:, 5:6][0:1, :], t2[:, 3:4][0:1, :], t2[:, 4:5][0:1, :], OP.add), "nsd")
        ctx.wait("scalar", v4)
        a2 = ctx.op("scalar", lambda e: e.activation(t2[:, 6:7][0:1, :],
                                                    t2[:, 5:6][0:1, :], AF.Exp,
                                                    scale=0.5), "nsa")
        ctx.wait("vector", a2)
        v5 = ctx.op("vector", lambda e: e.tensor_scalar(
            t1b[:, 3:4][0:1, :], t2[:, 6:7][0:1, :], -1.02 * scale, None, OP.mult), "nsd")
        ctx.wait("tensor", v5)
        p3 = ctx.op("tensor", lambda e: e.matmul(
            pbnd[:, 0:1], sb["c_ones1b"][0:1, :], t1b[:, 3:4][0:1, :],
            start=True, stop=True), "bnp")
        ctx.wait("vector", p3)
        v6 = ctx.op("vector", lambda e: e.tensor_copy(cb[:, 0:1], pbnd[:, 0:1]), "nsd")
        return v6

    def mla_prep(self, st):
        ctx, sb, d, NT, TC = self.ctx, self.sb, self.d, self.NT, self.TC
        vfm = sb["vfm"]
        wkv = self.sbuf("wkv", [128, 8, LAT], BF16, st)
        wq = self.sbuf("wq", [128, 8, LAT], BF16, st)
        wkn = self.sbuf("wkn", [128, 2, 512], BF16, st)
        wqn = self.sbuf("wqn", [128, 2, 512], BF16, st)
        wv = self.sbuf("wv", [128, 2, DM], BF16, st)
        wrq = self.sbuf("wrq", [128, 2, 512], BF16, st)
        wrk = self.sbuf("wrk", [128, 8, 512], BF16, st)
        wt = []
        wt.append(ctx.dma("gpsimd", wkv[:, :, :],
                          d["m_kv_T"][:, :, :].rearrange("kc p l -> p kc l"), "wks"))
        wt.append(ctx.dma("gpsimd", wq[:, :, :],
                          d["m_q_T"][:, :, :].rearrange("kc p l -> p kc l"), "wks"))
        wt.append(ctx.dma("gpsimd", wkn[:, :, :],
                          d["m_knope_T"][:, :, :].rearrange("kc p l -> p kc l"), "wks"))
        wt.append(ctx.dma("gpsimd", wqn[:, :, :],
                          d["m_qnope_T"][:, :, :].rearrange("kc p l -> p kc l"), "wks"))
        wt.append(ctx.dma("gpsimd", wv[:, :, :],
                          d["m_v_T"][:, :, :].rearrange("kc p l -> p kc l"), "wks"))
        wt.append(ctx.dma("gpsimd", wrq[:, :, :],
                          d["m_rq_T"][:, :, :].rearrange("kc p l -> p kc l"), "wks"))
        wt.append(ctx.dma("gpsimd", wrk[:, :, :],
                          d["m_rk_T"][:, :, :].rearrange("kc p l -> p kc l"), "wks"))
        wtok = ctx.tok("wks")
        ctx.wait("tensor", wtok)

        latk = self.sbuf("latk", [128, 2, self.T], BF16, st)
        latq = self.sbuf("latq", [128, 2, self.T], BF16, st)
        lat_st = self.estack()
        pslat = self.psum("pslat", [128, 4, 512], F32, lat_st)
        ctx.wait("vector", ctx.tok("lnt"))
        ctx.wait("tensor", ctx.tok("lnt"))
        for (lat_sb, wsrc) in ((latk, wkv), (latq, wq)):
            for lc in range(2):
                for tcg in range(0, TC, 4):
                    ng = min(4, TC - tcg)
                    ctx.wait("tensor", ctx.tok("lpe"))
                    pv = None
                    for kc in range(8):
                        for j in range(ng):
                            tc = tcg + j
                            pv = ctx.op("tensor",
                                        lambda e, kc=kc, lc=lc, tc=tc, j=j, wsrc=wsrc:
                                        e.matmul(pslat[:, j, :],
                                                 wsrc[:, kc, lc * 128:(lc + 1) * 128],
                                                 vfm[:, kc, tc * 512:(tc + 1) * 512],
                                                 start=(kc == 0), stop=(kc == 7)),
                                        "lmm" if (kc == 7 and j == ng - 1) else None)
                    ctx.wait("vector", pv)
                    ev = None
                    for j in range(ng):
                        tc = tcg + j
                        ev = ctx.op("vector", lambda e, lat_sb=lat_sb, lc=lc, tc=tc, j=j:
                                    e.tensor_copy(lat_sb[:, lc, tc * 512:(tc + 1) * 512],
                                                  pslat[:, j, :]), "lpe")
        ctx.wait("tensor", ctx.tok("lpe"))
        if "dbg_latk" in self.dbg:
            dl = self.dram("dbg_latk", [128, 2, self.T], BF16, kind="ExternalOutput")
            ctx.wait("sync", ctx.tok("lpe"))
            ctx.dma("sync", dl[:, :, :], latk[:, :, :], "dbgs")
            dl2 = self.dram("dbg_latq", [128, 2, self.T], BF16, kind="ExternalOutput")
            ctx.dma("sync", dl2[:, :, :], latq[:, :, :], "dbgs")
            ctx.wait("sync", ctx.tok("dbgs"))
        ctx.barrier()
        self.close_stack(lat_st)

        pkr = self.psum("pkr", [128, 512], F32, st)
        pkn = self.psum("pkn", [128, 512], F32, st)
        pv_ = self.psum("pv", [128, 2, 512], F32, st)
        pqr = self.psum("pqr", [128, 512], F32, st)
        pqn = self.psum("pqn", [128, 512], F32, st)
        ptp = self.psum("ptp2", [128, 1024], BF16, st)
        cs = self.sbuf("cs", [128, 2, 512], F32, st)
        sn = self.sbuf("sn", [128, 2, 512], F32, st)
        rf1 = self.sbuf("rf1", [128, 512], F32, st)
        rf2 = self.sbuf("rf2", [128, 512], F32, st)
        ktm = self.sbuf("ktm", [128, 2, DM], BF16, st)
        qtm = self.sbuf("qtm", [128, 2, DM], BF16, st)
        vstg = self.sbuf("vstg", [128, 2, NH * 65], BF16, st)
        kstg = self.sbuf("kstg", [128, 8, 512], BF16, st)
        qstg = self.sbuf("qstg", [128, 8, 512], BF16, st)
        sqn = self.sbuf("sqn", [128, DM], BF16, st)
        s16 = self.sbuf("s16", [128, NH], F32, st)
        mq = self.sbuf("mq", [128, NT], F32, st)
        mk = self.sbuf("mk", [128, NT], F32, st)
        self.sbuf("bnd_t1", [128, 8], F32, st)
        self.sbuf("bnd_t1b", [128, 4], BF16, st)
        self.sbuf("bnd_t2", [128, 8], F32, st)

        ktm_free = [None, None]
        qtm_free = [None, None]
        vstg_free = [None, None]
        cs_free = [None, None]
        kq_stage_free = None
        for tt in range(NT):
            b = tt % 2
            r0, r1 = tt * 128, (tt + 1) * 128
            if cs_free[b] is not None:
                ctx.wait("sync", cs_free[b])
            lc1 = ctx.dma("sync", cs[:, b, :], d["c_cos"][r0:r1, :], "csl%d" % b)
            lc2 = ctx.dma("sync", sn[:, b, :], d["c_sinS"][r0:r1, :], "csl%d" % b)
            ctx.wait("tensor", ctx.tok("prd"), ctx.tok("pra"))
            pv_done = []
            for kc in range(8):
                pv_done.append(ctx.op("tensor", lambda e, kc=kc, r0=r0: e.matmul(
                    pkr[:, :], vfm[:, kc, r0:r0 + 128], wrk[:, kc, :],
                    start=(kc == 0), stop=(kc == 7)), "pmm" if kc == 7 else None))
            for lc in range(2):
                ctx.op("tensor", lambda e, lc=lc, r0=r0: e.matmul(
                    pkn[:, :], latk[:, lc, r0:r0 + 128], wkn[:, lc, :],
                    start=(lc == 0), stop=(lc == 1)), None)
                for fh in range(2):
                    ctx.op("tensor", lambda e, lc=lc, fh=fh, r0=r0: e.matmul(
                        pv_[:, fh, :], latk[:, lc, r0:r0 + 128],
                        wv[:, lc, fh * 512:(fh + 1) * 512],
                        start=(lc == 0), stop=(lc == 1)), None)
            for lc in range(2):
                ctx.op("tensor", lambda e, lc=lc, r0=r0: e.matmul(
                    pqr[:, :], latq[:, lc, r0:r0 + 128], wrq[:, lc, :],
                    start=(lc == 0), stop=(lc == 1)), None)
                pv_done.append(ctx.op("tensor", lambda e, lc=lc, r0=r0: e.matmul(
                    pqn[:, :], latq[:, lc, r0:r0 + 128], wqn[:, lc, :],
                    start=(lc == 0), stop=(lc == 1)), "pmm" if lc == 1 else None))
            mm_tok = ctx.tok("pmm")
            ctx.wait("vector", mm_tok, lc2)
            if ktm_free[b] is not None:
                ctx.wait("vector", *ktm_free[b])
            if qtm_free[b] is not None:
                ctx.wait("vector", *qtm_free[b])
            if "dbg_ktm" in self.dbg:
                ctx.wait("vector", ctx.tok("dbgs"))
            if vstg_free[b] is not None:
                ctx.wait("scalar", vstg_free[b])
                ctx.wait("gpsimd", vstg_free[b])
            kt, qt, vs = ktm[:, b], qtm[:, b], vstg[:, b]

            v1 = ctx.op("vector", lambda e, b=b: (
                e.tensor_tensor(rf1[:, :], pkr[:, :], cs[:, b, :], OP.mult),
                e.tensor_tensor(rf2[:, :], bass.AP(
                    pkr[:, :].tensor, pkr[:, :].offset + 16,
                    [list(p) for p in pkr[:, :].ap[:1]] + [[32, 16], [-16, 2], [1, 16]]),
                    sn[:, b, :], OP.mult))[-1], "prd")
            ctx.wait("vector", v1)
            v2 = ctx.op("vector", lambda e, kt=kt: e.tensor_tensor(
                bass.AP(kt.tensor, kt.offset,
                        [list(p) for p in kt.ap[:1]] + [[64, 16], [1, 32]]),
                rf1[:, :], rf2[:, :], OP.add), "prd")
            v3 = ctx.op("vector", lambda e, kt=kt: e.tensor_copy(
                bass.AP(kt.tensor, kt.offset + 32,
                        [list(p) for p in kt.ap[:1]] + [[64, 16], [1, 32]]),
                pkn[:, :]), "prd")
            ctx.wait("vector", v3)
            v4 = ctx.op("vector", lambda e, b=b: (
                e.tensor_tensor(rf1[:, :], pqr[:, :], cs[:, b, :], OP.mult),
                e.tensor_tensor(rf2[:, :], bass.AP(
                    pqr[:, :].tensor, pqr[:, :].offset + 16,
                    [list(p) for p in pqr[:, :].ap[:1]] + [[32, 16], [-16, 2], [1, 16]]),
                    sn[:, b, :], OP.mult))[-1], "prd")
            ctx.wait("vector", v4)
            v5 = ctx.op("vector", lambda e, qt=qt: e.tensor_tensor(
                bass.AP(qt.tensor, qt.offset,
                        [list(p) for p in qt.ap[:1]] + [[64, 16], [1, 32]]),
                rf1[:, :], rf2[:, :], OP.add), "prd")
            v6 = ctx.op("vector", lambda e, qt=qt: e.tensor_copy(
                bass.AP(qt.tensor, qt.offset + 32,
                        [list(p) for p in qt.ap[:1]] + [[64, 16], [1, 32]]),
                pqn[:, :]), "prd")
            cs_free[b] = v4
            ctx.wait("scalar", mm_tok)
            a1 = ctx.op("scalar", lambda e, vs=vs: e.activation(
                bass.AP(vs.tensor, vs.offset,
                        [list(p) for p in vs.ap[:1]] + [[65, 8], [1, 64]]),
                pv_[:, 0, :], AF.Copy), "pra")
            a2 = ctx.op("scalar", lambda e, vs=vs: e.activation(
                bass.AP(vs.tensor, vs.offset + 8 * 65,
                        [list(p) for p in vs.ap[:1]] + [[65, 8], [1, 64]]),
                pv_[:, 1, :], AF.Copy), "pra")
            g1 = ctx.op("gpsimd", lambda e, vs=vs: e.memset(
                bass.AP(vs.tensor, vs.offset + 64,
                        [list(p) for p in vs.ap[:1]] + [[65, 16], [1, 1]]), 1.0), "gpk")
            ctx.wait("sync", a1, a2, g1)
            vstg_free[b] = ctx.dma("sync", d["vtm"][tt, :, :], vs[:, :], "vst%d" % b)
            dbg_tok = None
            if "dbg_ktm" in self.dbg:
                if tt == 0:
                    self.dram("dbg_ktm", [self.T, DM], BF16, kind="ExternalOutput")
                    self.dram("dbg_qtm", [self.T, DM], BF16, kind="ExternalOutput")
                ctx.wait("sync", v3, v6)
                ctx.dma("sync", self.d["dbg_ktm"][tt * 128:(tt + 1) * 128, :], kt[:, :], "dbgs")
                dbg_tok = ctx.dma("sync", self.d["dbg_qtm"][tt * 128:(tt + 1) * 128, :], qt[:, :], "dbgs")
            nk, nka = self.tile_norms(kt[:, :], v3, mk, tt, sqn, s16)
            nq, nqa = self.tile_norms(qt[:, :], v6, mq, tt, sqn, s16)
            ctx.wait("tensor", v3, v6, ctx.tok("prt"))
            pe = None
            for fc in range(8):
                pe = ctx.op("tensor", lambda e, kt=kt, fc=fc: e.transpose(
                    ptp[:, fc * 128:(fc + 1) * 128], kt[:, fc * 128:(fc + 1) * 128],
                    sb["c_ident"][:, :]), "pp2" if fc == 7 else None)
            ctx.wait("vector", pe, nk, nq)
            c4 = tt % 4
            if c4 == 0 and kq_stage_free is not None:
                ctx.wait("vector", kq_stage_free)
            v8 = ctx.op("vector", lambda e, c4=c4: e.tensor_copy(
                kstg[:, :, c4 * 128:(c4 + 1) * 128],
                ptp[:, :].rearrange("p (f t) -> p f t", f=8)), "prt")
            ctx.wait("tensor", v8)
            pe2 = None
            for fc in range(8):
                pe2 = ctx.op("tensor", lambda e, qt=qt, fc=fc: e.transpose(
                    ptp[:, fc * 128:(fc + 1) * 128], qt[:, fc * 128:(fc + 1) * 128],
                    sb["c_ident"][:, :]), "pp2" if fc == 7 else None)
            ctx.wait("vector", pe2)
            v9 = ctx.op("vector", lambda e, c4=c4: e.tensor_copy(
                qstg[:, :, c4 * 128:(c4 + 1) * 128],
                ptp[:, :].rearrange("p (f t) -> p f t", f=8)), "prt")
            ktm_free[b] = [v8, nka]
            qtm_free[b] = [v9, nqa]
            if c4 == 3:
                grp = tt // 4
                ctx.wait("sync", v9)
                last = None
                for fc in range(8):
                    for half in range(2):
                        h = 2 * fc + half
                        last = ctx.dma("sync", d["kfm"][h, :, grp * 512:(grp + 1) * 512],
                                       kstg[:, fc, :][half * 64:(half + 1) * 64, :], "kqs")
                        last = ctx.dma("sync", d["qfm"][h, :, grp * 512:(grp + 1) * 512],
                                       qstg[:, fc, :][half * 64:(half + 1) * 64, :], "kqs")
                kq_stage_free = last
        pbnd = self.psum("pbnd", [128, 1], F32, st)
        bt = self.bound(mq, mk, 1.0 / 8.0, [ctx.tok("nsd")], ptp, pbnd)
        ctx.wait("sync", ctx.tok("kqs"), ctx.tok("vst0"), ctx.tok("vst1"))
        return bt

    def s2_prep(self, st):
        ctx, sb, d, NT = self.ctx, self.sb, self.d, self.NT
        vfm = sb["vfm"]
        waq = self.sbuf("waq", [128, 8, DM], BF16, st)
        wak = self.sbuf("wak", [128, 8, DM], BF16, st)
        wav = self.sbuf("wav", [128, 8, DM], BF16, st)
        wts = []
        for (w, nm) in ((waq, "a_q_T"), (wak, "a_k_T"), (wav, "a_v_T")):
            wts.append(ctx.dma("gpsimd", w[:, :, :],
                               d[nm][:, :, :].rearrange("kc p f -> p kc f"), "wks"))
        ctx.wait("tensor", ctx.tok("wks"))
        ctx.wait("vector", ctx.tok("lnt"))
        ctx.wait("tensor", ctx.tok("lnt"))

        pq = self.psum("s2pq", [128, 2, 512], F32, st)
        pk = self.psum("s2pk", [128, 2, 512], F32, st)
        pv_ = self.psum("s2pv", [128, 2, 512], F32, st)
        ptp = self.psum("s2ptp", [128, 1024], BF16, st)
        ktm = self.sbuf("s2ktm", [128, 2, DM], BF16, st)
        qtm = self.sbuf("s2qtm", [128, 2, DM], BF16, st)
        vstg = self.sbuf("s2vstg", [128, 2, NH * 65], BF16, st)
        kstg = self.sbuf("s2kstg", [128, 8, 512], BF16, st)
        qstg = self.sbuf("s2qstg", [128, 8, 512], BF16, st)
        f1t = self.sbuf("s2f1", [128, 2, DM], F32, st)
        sqnt = self.sbuf("s2sqn", [128, 2, DM], BF16, st)
        s16t = self.sbuf("s2s16", [128, 2, NH], F32, st)
        rrt = self.sbuf("s2rr", [128, 2, 2, NH], F32, st)
        mq = self.sbuf("s2mq", [128, NT], F32, st)
        mk = self.sbuf("s2mk", [128, NT], F32, st)
        self.sbuf("bnd_t1", [128, 8], F32, st)
        self.sbuf("bnd_t1b", [128, 4], BF16, st)
        self.sbuf("bnd_t2", [128, 8], F32, st)

        qkb = self.sbuf("s2qkb", [128, 2, 2, 2, 512], F32, st)
        ktm_free = [None, None]
        qtm_free = [None, None]
        vstg_free = [None, None]
        kq_state = [None]
        ps_free = None
        tile_meta = {}

        def emit_tp(t):
            bb = t % 2
            ktx, qtx = ktm[:, bb], qtm[:, bb]
            chain_done, nk, nq, nka, nqa = tile_meta.pop(t)
            ctx.wait("tensor", chain_done, ctx.tok("prt"))
            for fc in range(8):
                ctx.op("tensor", lambda e, ktx=ktx, fc=fc: e.transpose(
                    ptp[:, fc * 128:(fc + 1) * 128], ktx[:, fc * 128:(fc + 1) * 128],
                    sb["c_ident"][:, :]), "pp2" if fc == 7 else None)
            ctx.wait("vector", ctx.tok("pp2"), nk, nq)
            c4 = t % 4
            if c4 == 0 and kq_state[0] is not None:
                ctx.wait("vector", kq_state[0])
            v8 = ctx.op("vector", lambda e, c4=c4: e.tensor_copy(
                kstg[:, :, c4 * 128:(c4 + 1) * 128],
                ptp[:, :].rearrange("p (f t) -> p f t", f=8)), "prt")
            ctx.wait("tensor", v8)
            for fc in range(8):
                ctx.op("tensor", lambda e, qtx=qtx, fc=fc: e.transpose(
                    ptp[:, fc * 128:(fc + 1) * 128], qtx[:, fc * 128:(fc + 1) * 128],
                    sb["c_ident"][:, :]), "pp2" if fc == 7 else None)
            ctx.wait("vector", ctx.tok("pp2"))
            v9 = ctx.op("vector", lambda e, c4=c4: e.tensor_copy(
                qstg[:, :, c4 * 128:(c4 + 1) * 128],
                ptp[:, :].rearrange("p (f t) -> p f t", f=8)), "prt")
            ktm_free[bb] = [v8, nka]
            qtm_free[bb] = [v9, nqa]
            if c4 == 3:
                grp = t // 4
                ctx.wait("sync", v9)
                last = None
                for fc in range(8):
                    for half in range(2):
                        h = 2 * fc + half
                        last = ctx.dma("sync", d["kfm"][h, :, grp * 512:(grp + 1) * 512],
                                       kstg[:, fc, :][half * 64:(half + 1) * 64, :], "kqs")
                        last = ctx.dma("sync", d["qfm"][h, :, grp * 512:(grp + 1) * 512],
                                       qstg[:, fc, :][half * 64:(half + 1) * 64, :], "kqs")
                kq_state[0] = last

        for tt in range(NT):
            b = tt % 2
            r0 = tt * 128
            if ps_free is not None:
                ctx.wait("tensor", *ps_free)
            mm = None
            for kc in range(8):
                for (ps, w) in ((pq, waq), (pk, wak), (pv_, wav)):
                    for fh in range(2):
                        mm = ctx.op("tensor", lambda e, kc=kc, r0=r0, ps=ps, w=w, fh=fh:
                                    e.matmul(ps[:, fh, :], vfm[:, kc, r0:r0 + 128],
                                             w[:, kc, fh * 512:(fh + 1) * 512],
                                             start=(kc == 0), stop=(kc == 7)),
                                    "pmm" if (kc == 7 and ps is pv_ and fh == 1) else None)
            mm_tok = ctx.tok("pmm")
            kt, qt, vs = ktm[:, b], qtm[:, b], vstg[:, b]
            ctx.wait("vector", mm_tok, ctx.tok("prd"))
            evq = ctx.op("vector", lambda e, b=b: e.tensor_copy(
                qkb[:, b, 0, :, :], pq[:, :, :]), "prd")
            ctx.wait("vector", evq)
            evk = ctx.op("vector", lambda e, b=b: e.tensor_copy(
                qkb[:, b, 1, :, :], pk[:, :, :]), "prd")
            ctx.wait("scalar", mm_tok)
            if vstg_free[b] is not None:
                ctx.wait("scalar", vstg_free[b])
                ctx.wait("gpsimd", vstg_free[b])
            a4 = ctx.op("scalar", lambda e, vs=vs: e.activation(
                bass.AP(vs.tensor, vs.offset,
                        [list(p) for p in vs.ap[:1]] + [[65, 8], [1, 64]]),
                pv_[:, 0, :], AF.Copy), "pra")
            a5 = ctx.op("scalar", lambda e, vs=vs: e.activation(
                bass.AP(vs.tensor, vs.offset + 8 * 65,
                        [list(p) for p in vs.ap[:1]] + [[65, 8], [1, 64]]),
                pv_[:, 1, :], AF.Copy), "pra")
            g1 = ctx.op("gpsimd", lambda e, vs=vs: e.memset(
                bass.AP(vs.tensor, vs.offset + 64,
                        [list(p) for p in vs.ap[:1]] + [[65, 16], [1, 1]]), 1.0), "gpk")
            ctx.wait("sync", a4, a5, g1)
            vstg_free[b] = ctx.dma("sync", d["vtm"][tt, :, :], vs[:, :], "vst%d" % b)
            for (which, dst, gidx, extra) in ((0, qt, 4, True),
                                              (1, kt, 5, False)):
                ctx.wait("scalar", evk)
                a1 = ctx.op("scalar", lambda e, b=b, which=which: e.activation(
                    sqnt[:, b, 0:512], qkb[:, b, which, 0, :], AF.Square), "pra")
                a1b = ctx.op("scalar", lambda e, b=b, which=which: e.activation(
                    sqnt[:, b, 512:1024], qkb[:, b, which, 1, :], AF.Square), "pra")
                ctx.wait("vector", a1b)
                v1 = ctx.op("vector", lambda e, b=b: e.tensor_reduce(
                    s16t[:, b, :], sqnt[:, b, :].rearrange("p (h x) -> p h x", h=NH),
                    AX.X, OP.add), "prd")
                ctx.wait("vector", v1)
                v2 = ctx.op("vector", lambda e, b=b: e.tensor_scalar(
                    s16t[:, b, :], s16t[:, b, :], 1.0 / HD, 1e-16, OP.mult, OP.max), "prd")
                ctx.wait("scalar", v2)
                a2 = ctx.op("scalar", lambda e, b=b: e.activation(
                    f1t[:, b, 0:NH], s16t[:, b, :], AF.Ln), "pra")
                ctx.wait("scalar", a2)
                bias = sb["c_mln8"][:, 0:1] if extra else 0.0
                a3 = ctx.op("scalar", lambda e, b=b, which=which, bias=bias: e.activation(
                    rrt[:, b, which, :], f1t[:, b, 0:NH], AF.Exp, scale=-0.5, bias=bias), "pra")
                ctx.wait("vector", a3)
                if (qtm_free[b] if which == 0 else ktm_free[b]) is not None:
                    ctx.wait("vector", *(qtm_free[b] if which == 0 else ktm_free[b]))
                vz = None
                for fh in range(2):
                    src = qkb[:, b, which, fh, :]
                    rrs = rrt[:, b, which, fh * 8:(fh + 1) * 8]
                    rb = bass.AP(rrs.tensor, rrs.offset,
                                 [rrs.ap[0], [1, 8], [0, HD]])
                    f1s = f1t[:, b, fh * 512:(fh + 1) * 512]
                    sb3 = bass.AP(src.tensor, src.offset,
                                  [src.ap[0], [HD, 8], [1, HD]])
                    db3 = bass.AP(f1s.tensor, f1s.offset,
                                  [f1s.ap[0], [HD, 8], [1, HD]])
                    vz = ctx.op("vector", lambda e, db3=db3, sb3=sb3, rb=rb:
                                e.tensor_tensor(db3, sb3, rb, OP.mult),
                                "prd" if fh == 1 else None)
                ctx.wait("vector", vz)
                vg = ctx.op("vector", lambda e, b=b, dst=dst, gidx=gidx: e.tensor_tensor(
                    dst[:, :], f1t[:, b, :], self.gamma(gidx), OP.mult), "prd")
            nk, nka = self.tile_norms(kt[:, :], ctx.tok("prd"), mk, tt,
                                      sqnt[:, b], s16t[:, b])
            nq, nqa = self.tile_norms(qt[:, :], ctx.tok("prd"), mq, tt,
                                      sqnt[:, b], s16t[:, b])
            tile_meta[tt] = (ctx.tok("prd"), nk, nq, nka, nqa)
            ps_free = [evq, evk, a4, a5]
            if tt > 0:
                emit_tp(tt - 1)
        emit_tp(NT - 1)
        pbnd = self.psum("pbnd", [128, 1], F32, st)
        bt = self.bound(mq, mk, 1.0, [ctx.tok("nsd"), ctx.tok("prt")], ptp, pbnd)
        ctx.wait("sync", ctx.tok("kqs"), ctx.tok("vst0"), ctx.tok("vst1"))
        return bt

    def attention(self, causal, scale, cbias_dep, st):
        ctx, sb, d, NT, TC = self.ctx, self.sb, self.d, self.NT, self.TC
        cb = sb["cbias"]
        ctx.wait("scalar", cbias_dep)
        G = 2
        NG = 3
        kh = self.sbuf("at_kh", [128, 2, self.T], BF16, st)
        qh = self.sbuf("at_qh", [128, 2, self.T], BF16, st)
        vh = self.sbuf("at_vh", [128, NT, NH * 65], BF16, st)
        pt = self.sbuf("at_pt", [128, NG, G * 512], BF16, st)
        rl = self.sbuf("at_rl", [1, 2, 512], F32, st)
        rlb = self.sbuf("at_rlb", [1, 2, 512], BF16, st)
        rbc = self.sbuf("at_rbc", [64, 2, 512], F32, st)
        og = self.sbuf("at_og", [64, 2, 512], BF16, st)
        pss = self.psum("at_ps", [128, NG, G * 512], F32, st)
        po = self.psum("at_po", [128, 2, 512], F32, st)
        lvv = ctx.dma("gpsimd", vh[:, :, :], d["vtm"][:, :, :].rearrange(
            "kt p c -> p kt c"), "akv")
        kv_free = [None, None]
        po_free = [None, None]
        og_free = [None, None]
        exp_free = [None] * NG
        av_free = [None] * NG

        kz = ctx.op("scalar", lambda e: (
            e.memzero(kh[:, :, :]), e.memzero(qh[:, :, :]))[-1], "akz")

        def issue_kq(h):
            hb = h % 2
            if kv_free[hb] is not None:
                ctx.wait("sync", *kv_free[hb])
            ctx.wait("sync", kz)
            ctx.dma("sync", kh[0:64, hb, :], d["kfm"][h, :, :], "akl%d" % hb)
            return ctx.dma("sync", qh[0:64, hb, :], d["qfm"][h, :, :], "akl%d" % hb)

        kq_toks = {0: issue_kq(0)}
        for h in range(NH):
            hb = h % 2
            if h + 1 < NH:
                kq_toks[h + 1] = issue_kq(h + 1)
            lq = kq_toks[h]
            ctx.wait("tensor", lq, lvv)
            for qc in range(TC):
                ob = qc % 2
                kt_hi = min(NT, (qc + 1) * 4) if causal else NT
                if po_free[ob] is not None:
                    ctx.wait("tensor", po_free[ob])
                groups = [(g0, min(G, kt_hi - g0)) for g0 in range(0, kt_hi, G)]

                def emit_S(gi):
                    ktg, gn = groups[gi]
                    g = gi % NG
                    if exp_free[g] is not None:
                        ctx.wait("tensor", exp_free[g])
                    sv = None
                    for i in range(gn):
                        kt = ktg + i
                        sv = ctx.op("tensor", lambda e, kt=kt, i=i, qc=qc, hb=hb, g=g:
                                    e.matmul(pss[:, g, i * 512:(i + 1) * 512],
                                             kh[:, hb, kt * 128:(kt + 1) * 128],
                                             qh[:, hb, qc * 512:(qc + 1) * 512],
                                             start=True, stop=True),
                                    "asm" if i == gn - 1 else None)
                    return sv

                def emit_exp(gi, sv):
                    ktg, gn = groups[gi]
                    g = gi % NG
                    ctx.wait("scalar", sv)
                    if av_free[g] is not None:
                        ctx.wait("scalar", av_free[g])
                    ex = ctx.op("scalar", lambda e, gn=gn, g=g: e.activation(
                        pt[:, g, 0:gn * 512], pss[:, g, 0:gn * 512], AF.Exp,
                        bias=cb[:, 0:1], scale=scale), "aex")
                    exp_free[g] = ex
                    mdep = ex
                    if causal:
                        for i in range(gn):
                            kt = ktg + i
                            j = kt - qc * 4
                            if 0 <= j <= 3:
                                ctx.wait("vector", ex)
                                if j > 0:
                                    mdep = ctx.op("vector", lambda e, i=i, j=j, g=g:
                                                  e.memset(pt[:, g, i * 512:i * 512 + j * 128], 0.0),
                                                  "amk")
                                mdep = ctx.op("vector", lambda e, i=i, j=j, g=g:
                                              e.tensor_tensor(
                                                  pt[:, g, i * 512 + j * 128:i * 512 + (j + 1) * 128],
                                                  pt[:, g, i * 512 + j * 128:i * 512 + (j + 1) * 128],
                                                  sb["c_mask"][:, :], OP.mult), "amk")
                    return mdep

                def emit_AV(gi, ex, mdep):
                    ktg, gn = groups[gi]
                    g = gi % NG
                    items = list(range(gn))
                    if causal:
                        items.sort(key=lambda i: (ktg + i != 0,
                                                  0 <= (ktg + i) - qc * 4 <= 3))
                    av = None
                    ndone = 0
                    for i in items:
                        kt = ktg + i
                        diag = causal and 0 <= kt - qc * 4 <= 3
                        ctx.wait("tensor", mdep if (diag or kt == 0) else ex)
                        ndone += 1
                        av = ctx.op("tensor", lambda e, kt=kt, i=i, ob=ob, h=h, g=g,
                                    first=(kt == 0), last=(kt == kt_hi - 1):
                                    e.matmul(po[0:65, ob, :],
                                             vh[:, kt, h * 65:(h + 1) * 65],
                                             pt[:, g, i * 512:(i + 1) * 512],
                                             start=first, stop=last),
                                    "aav" if ndone == gn else None)
                    av_free[g] = av
                    return av

                pend = []
                for gi in range(len(groups)):
                    sv = emit_S(gi)
                    md = emit_exp(gi, sv)
                    ex_tok = exp_free[gi % NG]
                    if len(pend) >= 2:
                        e0 = pend.pop(0)
                        emit_AV(*e0)
                    pend.append((gi, ex_tok, md))
                for e0 in pend:
                    emit_AV(*e0)
                ctx.wait("vector", av_free[(len(groups) - 1) % NG], ctx.tok("aav"))
                if og_free[ob] is not None:
                    ctx.wait("vector", og_free[ob])
                v1 = ctx.op("vector", lambda e, ob=ob: e.tensor_copy(
                    og[:, ob, :], po[0:64, ob, :]), "aed")
                ctx.wait("vector", v1)
                v2 = ctx.op("vector", lambda e, ob=ob: e.tensor_copy(
                    rl[:, ob, :], po[64:65, ob, :]), "aed")
                po_free[ob] = v2
                ctx.wait("sync", v2)
                ctx.dma("sync", d["lsum"][h, qc * 512:(qc + 1) * 512],
                        rl[0:1, ob, :], "aos%d" % ob)
                og_free[ob] = ctx.dma("sync", d["ofm"][h // 2, qc,
                                                       (h % 2) * 64:(h % 2) * 64 + 64, :],
                                      og[:, ob, :], "aos%d" % ob)
            kv_free[hb] = [ctx.tok("aav"), ctx.tok("asm")]
        ctx.wait("sync", ctx.tok("aos0"), ctx.tok("aos1"))

    def oproj(self, w_name, src, dst, add_bo, st):
        ctx, sb, d, NT = self.ctx, self.sb, self.d, self.NT
        wo = self.sbuf("opwo", [128, 8, DM], BF16, st)
        ctx.dma("gpsimd", wo[:, :, :],
                d[w_name][:, :, :].rearrange("dc p f -> p dc f"), "wks")
        ctx.wait("tensor", ctx.tok("wks"))
        osb = self.sbuf("oposb", [128, 2, 8, 512], BF16, st)
        osn = self.sbuf("oposn", [128, 2, 8, 128], BF16, st)
        res = self.sbuf("opres", [128, 2, DM], F32, st)
        out = self.sbuf("opout", [128, 2, DM], F32, st)
        lsb = self.sbuf("oplsb", [16, self.T], F32, st)
        lrb = self.sbuf("oplrb", [2, 8, self.T], BF16, st)
        pp = self.psum("oppp", [128, 2, 512], F32, st)
        prb = self.psum("oprb", [128, 2, 1024], F32, st)
        lt = ctx.dma("sync", lsb[:, :], d["lsum"][:, :], "opll")
        ctx.wait("vector", lt)
        vr = ctx.op("vector", lambda e: e.reciprocal(lsb[:, :], lsb[:, :]), "opd")
        ctx.wait("sync", vr)
        st1 = ctx.dma("sync", d["lsum"][:, :], lsb[:, :], "opll")
        ctx.wait("gpsimd", st1)
        lt2 = ctx.dma("gpsimd", lrb[:, :, :],
                      bass.AP(d["lsum"].tensor, 0,
                              [[self.T, 2], [2 * self.T, 8], [1, self.T]]), "opll2")
        ctx.wait("tensor", lt2)
        osb_free = [None, None]
        res_free = [None, None]
        out_free = [None, None]
        osn_free = [None, None]
        for tt in range(NT):
            b = tt % 2
            gb = (tt // 4) % 2
            if tt % 4 == 0:
                if osb_free[gb] is not None:
                    ctx.wait("sync", osb_free[gb])
                ot = None
                for fc in range(8):
                    ot = ctx.dma("sync", osb[:, gb, fc, :], d["ofm"][fc, tt // 4, :, :],
                                 "opl%d" % gb)
                self.op_ot = ot
            if res_free[b] is not None:
                ctx.wait("sync", *res_free[b])
            rt = ctx.dma("sync", res[:, b, :], src[tt * 128:(tt + 1) * 128, :],
                         "opr%d" % b)
            ctx.wait("tensor", ctx.tok("opn"))
            bc = None
            for fc in range(8):
                bc = ctx.op("tensor", lambda e, fc=fc, b=b, tt=tt: e.matmul(
                    prb[:, b, fc * 128:(fc + 1) * 128], sb["c_half2"][:, :],
                    lrb[:, fc, tt * 128:(tt + 1) * 128],
                    start=True, stop=True), "opb" if fc == 7 else None)
            ctx.wait("vector", bc, self.op_ot)
            if osn_free[b] is not None:
                ctx.wait("vector", osn_free[b])
            vn = None
            for fc in range(8):
                vn = ctx.op("vector", lambda e, fc=fc, b=b, gb=gb, tt=tt: e.tensor_tensor(
                    osn[:, b, fc, :], osb[:, gb, fc, (tt % 4) * 128:(tt % 4 + 1) * 128],
                    prb[:, b, fc * 128:(fc + 1) * 128], OP.mult),
                    "opn" if fc == 7 else None)
            ctx.wait("tensor", vn, ctx.tok("opd"))
            mm = None
            for dc in range(8):
                for fh in range(2):
                    mm = ctx.op("tensor", lambda e, dc=dc, fh=fh, b=b:
                                e.matmul(pp[:, fh, :],
                                         osn[:, b, dc, :],
                                         wo[:, dc, fh * 512:(fh + 1) * 512],
                                         start=(dc == 0), stop=(dc == 7)),
                                "pmm" if (dc == 7 and fh == 1) else None)
            osn_free[b] = mm
            ctx.wait("vector", mm, rt)
            if out_free[b] is not None:
                ctx.wait("vector", out_free[b])
            vv = None
            for fh in range(2):
                vv = ctx.op("vector", lambda e, b=b, fh=fh: e.scalar_tensor_tensor(
                    out[:, b, fh * 512:(fh + 1) * 512], pp[:, fh, :], 1.0,
                    res[:, b, fh * 512:(fh + 1) * 512], OP.mult, OP.add),
                    "opd" if fh == 1 else None)
            res_free[b] = [vv]
            if add_bo:
                ctx.wait("vector", vv)
                vv = ctx.op("vector", lambda e, b=b: e.tensor_tensor(
                    out[:, b, :], out[:, b, :], sb["c_bo"][:, :], OP.add), "opd")
            ctx.wait("sync", vv)
            out_free[b] = ctx.dma("sync", dst[tt * 128:(tt + 1) * 128, :],
                                  out[:, b, :], "opo%d" % b)
            if tt % 4 == 3:
                osb_free[gb] = ctx.tok("pmm")
        ctx.wait("sync", ctx.tok("opo0"), ctx.tok("opo1"))

    def moe_router_setup(self, st):
        ctx, sb, d, NT = self.ctx, self.sb, self.d, self.NT
        w8 = self.sbuf("moe_w", [128, NT, 8], F32, st)
        rt_st = self.estack()
        pr = self.psum("moe_pr", [128, 2, 8], F32, rt_st)
        ptpr = self.psum("moe_ptpr", [128, 2, 1024], BF16, rt_st)
        rf = self.sbuf("moe_rf", [128, 4, 8], F32, rt_st)
        self._zb = None
        rwf = self.sbuf("moe_rwf", [128, 8, 8], F32, rt_st)
        rwhi = self.sbuf("moe_rwhi", [128, 8, 8], BF16, rt_st)
        rwlo = self.sbuf("moe_rwlo", [128, 8, 8], BF16, rt_st)
        lot = self.sbuf("moe_lot", [128, 2, DM], BF16, rt_st)
        lofm = self.sbuf("moe_lofm", [128, 2, 8, 128], BF16, rt_st)
        hfm = self.sbuf("moe_hfm", [128, 2, 8, 128], BF16, rt_st)
        lw = ctx.dma("sync", rwf[:, :, :],
                     d["c_router"][:, :, :].rearrange("kc p e -> p kc e"), "rtl")
        ctx.wait("vector", lw)
        vh = ctx.op("vector", lambda e: e.tensor_copy(rwhi[:, :, :], rwf[:, :, :]), "rtd")
        ctx.wait("vector", vh)
        vl = ctx.op("vector", lambda e: e.tensor_tensor(
            rwlo[:, :, :], rwf[:, :, :], rwhi[:, :, :], OP.subtract), "rtd")
        g1 = ctx.op("gpsimd", lambda e: e.memset(w8[:, :, 7:8], 1.0), "gpk")
        zb = self.sbuf("moe_zb", [128, NT, 8], F32, rt_st)
        pp = self.sbuf("moe_pp", [128, NT, 8], F32, rt_st)
        ms = self.sbuf("moe_ms", [128, NT, 8], F32, rt_st)
        sc = self.sbuf("moe_sc", [128, 4, NT], F32, rt_st)
        self._rt = dict(w8=w8, rt_st=rt_st, pr=pr, ptpr=ptpr, rf=rf, rwhi=rwhi,
                        rwlo=rwlo, lot=lot, lofm=lofm, hfm=hfm, wdep=(vl, g1),
                        zb=zb, pp=pp, ms=ms, sc=sc)
        return self.moe_router_tile

    def moe_router_tile(self, tt, f1, g, v6, v7):
        ctx, sb = self.ctx, self.sb
        R = self._rt
        b = tt % 2
        pr, ptpr, rf, w8 = R["pr"], R["ptpr"], R["rf"], R["w8"]
        lot, lofm, hfm = R["lot"][:, b], R["lofm"][:, b], R["hfm"][:, b]
        ctx.wait("sync", v6)
        xst = ctx.dma("sync", self.d["xn_tm"][tt * 128:(tt + 1) * 128, :],
                      g[:, :], "xts")
        ctx.wait("vector", v6, ctx.tok("rmm"))
        vlo = ctx.op("vector", lambda e: e.tensor_tensor(
            lot[:, :], f1[:, :], g[:, :], OP.subtract), "rtd")
        ctx.wait("tensor", vlo, v6, ctx.tok("rtd"))
        pe = None
        for fc in range(8):
            ctx.op("tensor", lambda e, fc=fc: e.transpose(
                ptpr[:, 0, fc * 128:(fc + 1) * 128],
                g[:, fc * 128:(fc + 1) * 128], sb["c_ident"][:, :]), None)
            pe = ctx.op("tensor", lambda e, fc=fc: e.transpose(
                ptpr[:, 1, fc * 128:(fc + 1) * 128], lot[:, fc * 128:(fc + 1) * 128],
                sb["c_ident"][:, :]), "rtp" if fc == 7 else None)
        ctx.wait("vector", pe)
        vev = ctx.op("vector", lambda e: e.tensor_copy(
            hfm[:, :, :], ptpr[:, 0, :].rearrange("p (f t) -> p f t", f=8)), "rtd")
        vev = ctx.op("vector", lambda e: e.tensor_copy(
            lofm[:, :, :], ptpr[:, 1, :].rearrange("p (f t) -> p f t", f=8)), "rtd")
        ctx.wait("tensor", vev, *R["wdep"])
        mm = None
        n = 0
        for kc in range(8):
            hi = hfm[:, kc, :]
            lo = lofm[:, kc, :]
            for (sa, wb_) in ((hi, R["rwhi"][:, kc, :]), (hi, R["rwlo"][:, kc, :]),
                              (lo, R["rwhi"][:, kc, :]), (lo, R["rwlo"][:, kc, :])):
                n += 1
                mm = ctx.op("tensor", lambda e, sa=sa, wb_=wb_, b=b, first=(n == 1),
                            last=(n == 32): e.matmul(pr[:, b, :], sa, wb_,
                                                     start=first, stop=last),
                            "rmm" if n == 32 else None)
        ctx.wait("vector", mm)
        vz = ctx.op("vector", lambda e, b=b, tt=tt: e.scalar_tensor_tensor(
            R["zb"][:, tt, :], pr[:, b, :], 1.0, sb["c_rbias"][:, :],
            OP.mult, OP.add), "rtd")
        return (vz, xst)

    def moe_topk_batch(self):
        ctx, sb, NT = self.ctx, self.sb, self.NT
        R = self._rt
        zb, w8 = R["zb"], R["w8"]
        p = R["pp"]
        ms = R["ms"]
        m1, m2, den, tmp = (R["sc"][:, i, :] for i in range(4))
        ctx.wait("scalar", ctx.tok("rtd"))
        a1 = ctx.op("scalar", lambda e: e.activation(
            p[:, :, :], zb[:, :, :], AF.Exp, scale=-1.0), "rta")
        ctx.wait("vector", a1)
        v1 = ctx.op("vector", lambda e: e.tensor_scalar(
            p[:, :, :], p[:, :, :], 1.0, None, OP.add), "rtd")
        ctx.wait("vector", v1)
        v2 = ctx.op("vector", lambda e: e.reciprocal(p[:, :, :], p[:, :, :]), "rtd")
        ctx.wait("vector", v2)
        v3 = ctx.op("vector", lambda e: e.tensor_reduce(
            m1, p[:, :, 0:7], AX.X, OP.max), "rtd")
        ctx.wait("vector", v3)
        for j in range(7):
            ctx.op("vector", lambda e, j=j: e.tensor_tensor(
                ms[:, :, j], p[:, :, j], m1, OP.is_lt), "rtd")
        ctx.wait("vector", ctx.tok("rtd"))
        for j in range(7):
            ctx.op("vector", lambda e, j=j: e.tensor_tensor(
                ms[:, :, j], p[:, :, j], ms[:, :, j], OP.mult), "rtd")
        ctx.wait("vector", ctx.tok("rtd"))
        v4 = ctx.op("vector", lambda e: e.tensor_reduce(
            m2, ms[:, :, 0:7], AX.X, OP.max), "rtd")
        ctx.wait("vector", v4)
        v5 = ctx.op("vector", lambda e: e.tensor_tensor(den, m1, m2, OP.add), "rtd")
        ctx.wait("vector", v5)
        v6 = ctx.op("vector", lambda e: e.reciprocal(den, den), "rtd")
        ctx.wait("vector", v6)
        for j in range(7):
            ctx.op("vector", lambda e, j=j: e.tensor_tensor(
                ms[:, :, j], p[:, :, j], m2, OP.is_ge), "rtd")
        ctx.wait("vector", ctx.tok("rtd"))
        for j in range(7):
            ctx.op("vector", lambda e, j=j: e.tensor_tensor(
                tmp, p[:, :, j], den, OP.mult), "rtd")
            ctx.wait("vector", ctx.tok("rtd"))
            ctx.op("vector", lambda e, j=j: e.tensor_tensor(
                w8[:, :, j], tmp, ms[:, :, j], OP.mult), "rtd")
            ctx.wait("vector", ctx.tok("rtd"))
        return ctx.tok("rtd")

    def moe_routing_meta(self, st):
        ctx, sb, NT, NCH = self.ctx, self.sb, self.NT, self.NCH
        I32 = mybir.dt.int32
        R = self._rt
        rt_st = R["rt_st"]
        ms, w8 = R["ms"], R["w8"]

        def A(base, dims, off=0):
            return bass.AP(base.tensor, base.offset + off, dims)
        msb, w8b = ms[:, :, :], w8[:, :, :]
        scn_a = self.sbuf("rm_scna", [128, NT, 7], F32, rt_st)
        scn_b = self.sbuf("rm_scnb", [128, NT, 7], F32, rt_st)
        cnt8 = self.sbuf("rm_cnt", [128, 8], F32, rt_st)
        cntb8 = self.sbuf("rm_cntb", [128, 8], BF16, rt_st)
        pre8 = self.sbuf("rm_pre", [128, 8], F32, rt_st)
        tot8 = self.sbuf("rm_tot", [128, 8], F32, rt_st)
        ns8 = self.sbuf("rm_ns", [128, 8], F32, rt_st)
        segi8 = self.sbuf("rm_segi", [128, 8], F32, rt_st)
        sege8 = self.sbuf("rm_sege", [128, 8], F32, rt_st)
        base8 = self.sbuf("rm_base", [128, 8], F32, rt_st)
        t37 = self.sbuf("rm_t37", [128, 7, NCH], F32, rt_st)
        t77 = self.sbuf("rm_t77", [128, NT, 7, 7], F32, rt_st)
        fst = self.sbuf("rm_fst", [128, NT, 7], F32, rt_st)
        Sdt = self.sbuf("rm_sd", [128, NT, 7], F32, rt_st)
        sAf = self.sbuf("rm_saf", [128, NT], F32, rt_st)
        sBf = self.sbuf("rm_sbf", [128, NT], F32, rt_st)
        wAf = self.sbuf("rm_waf", [128, NT], F32, rt_st)
        wBf = self.sbuf("rm_wbf", [128, NT], F32, rt_st)
        teid = self.sbuf("rm_teid", [128, NCH, 7], F32, rt_st)
        eidv = self.sbuf("rm_eid", [128, NCH], F32, rt_st)
        tgu = self.sbuf("rm_tgu", [128, NCH, 16], F32, rt_st)
        pmt = self.psum("rm_pmt", [128, 2, 8], F32, rt_st)
        sAi = self.sbuf("rm_sai", [128, NT], I32, st, side="right")
        sBi = self.sbuf("rm_sbi", [128, NT], I32, st, side="right")
        idx_gu = self.sbuf("rm_igu", [128, NCH, 16], I32, st, side="right")
        idx_dn = self.sbuf("rm_idn", [128, NCH], I32, st, side="right")
        self._sp = dict(sAi=sAi, sBi=sBi, idx_gu=idx_gu, idx_dn=idx_dn)

        def v(fn):
            t = ctx.op("vector", fn, "rtd")
            ctx.wait("vector", t)
            return t
        pm = msb.ap[0]
        sel3 = A(msb, [pm, [8, NT], [1, 7]])
        selT = A(msb, [pm, [1, 7], [8, NT]])
        sel4 = A(msb, [pm, [8, NT], [0, 7], [1, 7]])
        w83 = A(w8b, [w8b.ap[0], [8, NT], [1, 7]])
        lt7_b, le7_b = self.rmeta("lt7", 49), self.rmeta("le7", 49)
        thr_b, chk_b = self.rmeta("thr", NCH), self.rmeta("chunk", NCH)
        ipp_b, pid_b = self.rmeta("ipp", 16), self.rmeta("pid", 1)

        v(lambda e: e.tensor_reduce(cnt8[:, 0:7], selT, AX.X, OP.add))
        v(lambda e: e.tensor_copy(cntb8[:, 0:7], cnt8[:, 0:7]))
        v(lambda e: e.tensor_copy(scn_a[:, :, :], sel3))
        src, dst = scn_a, scn_b
        k = 1
        while k < NT:
            v(lambda e, s=src, d2=dst, k=k: e.tensor_copy(
                d2[:, 0:k, :], s[:, 0:k, :]))
            v(lambda e, s=src, d2=dst, k=k: e.tensor_tensor(
                d2[:, k:NT, :], s[:, k:NT, :], s[:, 0:NT - k, :], OP.add))
            src, dst = dst, src
            k *= 2
        scn_e, tmps = src, dst
        v(lambda e, s=scn_e: e.tensor_tensor(
            s[:, :, :], s[:, :, :], sel3, OP.subtract))
        ctx.wait("tensor", ctx.tok("rtd"), ctx.tok("gpk"))
        ctx.op("tensor", lambda e: e.matmul(
            pmt[:, 0, 0:7], sb["c_mask"][:, :], cntb8[:, 0:7],
            start=True, stop=True), None)
        mm2 = ctx.op("tensor", lambda e: e.matmul(
            pmt[:, 1, 0:7], sb["c_ones128"][:, :], cntb8[:, 0:7],
            start=True, stop=True), "rmm")
        ctx.wait("vector", mm2)
        v(lambda e: e.tensor_tensor(
            pre8[:, 0:7], pmt[:, 0, 0:7], cnt8[:, 0:7], OP.subtract))
        v(lambda e: e.tensor_copy(tot8[:, 0:7], pmt[:, 1, 0:7]))
        v(lambda e: e.tensor_tensor(
            t37[:, :, :], A(tot8[:, :], [tot8[:, :].ap[0], [1, 7], [0, NCH]]),
            A(thr_b, [thr_b.ap[0], [0, 7], [1, NCH]]), OP.is_gt))
        v(lambda e: e.tensor_reduce(ns8[:, 0:7], t37[:, :, :], AX.X, OP.add))
        v(lambda e: e.tensor_tensor(
            t37[:, :, 0:7], A(ns8[:, :], [ns8[:, :].ap[0], [0, 7], [1, 7]]),
            A(le7_b, [le7_b.ap[0], [7, 7], [1, 7]]), OP.mult))
        v(lambda e: e.tensor_reduce(segi8[:, 0:7], t37[:, :, 0:7], AX.X, OP.add))
        v(lambda e: e.tensor_tensor(
            sege8[:, 0:7], segi8[:, 0:7], ns8[:, 0:7], OP.subtract))
        v(lambda e: e.scalar_tensor_tensor(
            base8[:, 0:7], sege8[:, 0:7], 512.0, pre8[:, 0:7], OP.mult, OP.add))
        v(lambda e: e.tensor_tensor(
            Sdt[:, :, :], scn_e[:, :, :],
            A(base8[:, :], [base8[:, :].ap[0], [0, NT], [1, 7]]), OP.add))
        v(lambda e: e.tensor_tensor(
            t77[:, :, :, :], sel4,
            A(lt7_b, [lt7_b.ap[0], [0, NT], [7, 7], [1, 7]]), OP.mult))
        v(lambda e: e.tensor_reduce(fst[:, :, :], t77[:, :, :, :], AX.X, OP.add))
        v(lambda e: e.tensor_scalar(
            fst[:, :, :], fst[:, :, :], 0.0, None, OP.is_equal))
        v(lambda e: e.tensor_tensor(fst[:, :, :], fst[:, :, :], sel3, OP.mult))
        v(lambda e: e.tensor_tensor(
            tmps[:, :, :], Sdt[:, :, :], fst[:, :, :], OP.mult))
        v(lambda e: e.tensor_reduce(sAf[:, :], tmps[:, :, :], AX.X, OP.add))
        v(lambda e: e.tensor_tensor(tmps[:, :, :], Sdt[:, :, :], sel3, OP.mult))
        v(lambda e: e.tensor_reduce(sBf[:, :], tmps[:, :, :], AX.X, OP.add))
        v(lambda e: e.tensor_tensor(sBf[:, :], sBf[:, :], sAf[:, :], OP.subtract))
        v(lambda e: e.tensor_tensor(tmps[:, :, :], w83, fst[:, :, :], OP.mult))
        v(lambda e: e.tensor_reduce(wAf[:, :], tmps[:, :, :], AX.X, OP.add))
        v(lambda e: e.tensor_reduce(wBf[:, :], w83, AX.X, OP.add))
        v(lambda e: e.tensor_tensor(wBf[:, :], wBf[:, :], wAf[:, :], OP.subtract))
        v(lambda e: e.tensor_scalar(
            sAf[:, :], sAf[:, :], float(self.NSLOT - 1), None, OP.min))
        v(lambda e: e.tensor_scalar(
            sBf[:, :], sBf[:, :], float(self.NSLOT - 1), None, OP.min))
        v(lambda e: e.tensor_copy(sAi[:, :], sAf[:, :]))
        v(lambda e: e.tensor_copy(sBi[:, :], sBf[:, :]))
        v(lambda e: e.tensor_tensor(
            teid[:, :, :], A(chk_b, [chk_b.ap[0], [1, NCH], [0, 7]]),
            A(segi8[:, :], [segi8[:, :].ap[0], [0, NCH], [1, 7]]), OP.is_ge))
        v(lambda e: e.tensor_reduce(eidv[:, :], teid[:, :, :], AX.X, OP.add))
        v(lambda e: e.scalar_tensor_tensor(
            tgu[:, :, :], A(eidv[:, :], [eidv[:, :].ap[0], [1, NCH], [0, 16]]),
            2048.0, A(ipp_b, [ipp_b.ap[0], [0, NCH], [1, 16]]), OP.mult, OP.add))
        v(lambda e: e.tensor_copy(idx_gu[:, :, :], tgu[:, :, :]))
        v(lambda e: e.scalar_tensor_tensor(
            eidv[:, :], eidv[:, :], 128.0,
            A(pid_b, [pid_b.ap[0], [0, NCH]]), OP.mult, OP.add))
        v(lambda e: e.tensor_copy(idx_dn[:, :], eidv[:, :]))
        ctx.wait("sync", ctx.tok("rtd"))
        ctx.dma("sync", self.d["ridx"][0, :, :], sAi[:, :], "rms")
        ctx.dma("sync", self.d["ridx"][1, :, :], sBi[:, :], "rms")
        ctx.dma("sync", self.d["rw"][0, :, :], wAf[:, :], "rms")
        ctx.dma("sync", self.d["rw"][1, :, :], wBf[:, :], "rms")
        self._sp["meta_tok"] = ctx.tok("rtd")
        self._sp["store_tok"] = ctx.tok("rms")

    def moe_experts(self, st):
        ctx, sb, d, NT, T = self.ctx, self.sb, self.d, self.NT, self.T
        NSH, NCH = self.NSH, self.NCH
        NK = NSH + NCH
        w8 = self._rt["w8"]
        rt_st = self._rt["rt_st"]
        self.moe_topk_batch()
        self.moe_routing_meta(st)
        if "dbg_w8" in self.dbg:
            dw = self.dram("dbg_w8", [128, NT, 8], F32, kind="ExternalOutput")
            ctx.wait("sync", ctx.tok("rtd"), ctx.tok("gpk"))
            ctx.dma("sync", dw[:, :, :], w8[:, :, :], "dbgs")
            ctx.wait("sync", ctx.tok("dbgs"))
        ctx.barrier()
        self.close_stack(rt_st)
        SP = self._sp
        sAi, sBi = SP["sAi"], SP["sBi"]
        idx_gu, idx_dn = SP["idx_gu"], SP["idx_dn"]
        gu = self.sbuf("moe_gu", [128, 4, 2, 8, 128], BF16, st)
        dnb = self.sbuf("moe_dn", [128, 2, 16, 2, 512], BF16, st)
        hsb = self.sbuf("moe_h", [128, 16, 512], BF16, st)
        stg = self.sbuf("moe_stg", [128, 4, DM], F32, st)
        eo_sb = self.sbuf("moe_eo", [128, 2, 4, DM], BF16, st)
        xtm = self.sbuf("moe_xtm", [128, 2, 4, DM], BF16, st)
        xfm = self.sbuf("moe_xfm", [128, 2, 8, 512], BF16, st)
        pay = self.sbuf("moe_pay", [128, 2, DM], BF16, st)
        sg = self.sbuf("moe_sg", [128, 512], BF16, st)
        ptx = self.psum("moe_ptx", [128, 2, 1024], BF16, st)
        pg = self.psum("moe_pg", [128, 512], F32, st)
        pu = self.psum("moe_pu", [128, 512], F32, st)
        pg2 = self.psum("moe_pg2", [128, 512], F32, st)
        pu2 = self.psum("moe_pu2", [128, 512], F32, st)
        pd = self.psum("moe_pd", [128, 2, 512], F32, st)
        guview = d["gu_bf"][:, :]
        dnview = d["dn_bf"][:, :]
        wc, xz = self._wprep
        ctx.wait("gpsimd", wc)

        def flat2(apx, n):
            return bass.AP(apx.tensor, apx.offset, [apx.ap[0], [1, n]])

        S = dict(gu_free=[None] * 4, h_evac=[None, None], pd_free=[None, None],
                 ptx_free=[None, None], tp_done=[None, None],
                 ev_done=[None, None], gu_last=[None, None], stg_free=None,
                 eo_free=[None, None], pay_free=[None] * 4, dn_tok={},
                 down_done={}, xtl_tok={}, mac_tok={})

        def dn_slot(k):
            if k < NSH:
                return 0
            return 1 if (k - NSH) % 2 == 0 else 0

        def dn_load(k):
            s = dn_slot(k)
            prev = None
            if k == NSH + 1:
                prev = NSH - 1
            elif k >= NSH + 2:
                prev = k - 2
            if prev is not None:
                ctx.wait("gpsimd", S["down_done"][prev])
            if k < NSH:
                t = ctx.dma("gpsimd", flat2(dnb[:, s, :, :, :], 16384),
                            dnview[7 * 128:8 * 128, :], "mdw%d" % s)
            else:
                ctx.wait("gpsimd", SP["meta_tok"])
                rc = k - NSH
                t = ctx.idma(flat2(dnb[:, s, :, :, :], 16384), None, dnview,
                             IOA(idx_dn[:, rc:rc + 1], 0), "mdw%d" % s)
            S["dn_tok"][k] = t

        def issue_gu(k, ip):
            rb = ip % 4
            if S["gu_free"][rb] is not None:
                ctx.wait("gpsimd", S["gu_free"][rb])
            if k < NSH:
                r0 = (7 * 16 + ip) * 128
                return ctx.dma("gpsimd", flat2(gu[:, rb, :, :, :], 2048),
                               guview[r0:r0 + 128, :], "mgw%d" % rb)
            ctx.wait("gpsimd", SP["meta_tok"])
            rc = k - NSH
            return ctx.idma(flat2(gu[:, rb, :, :, :], 2048), None, guview,
                            IOA(idx_gu[:, rc, ip:ip + 1], 0), "mgw%d" % rb)

        def xtl(k):
            kb = k % 2
            ctx.wait("sync", S["tp_done"][kb])
            if k == NSH:
                ctx.wait("sync", ctx.tok("dsp0"), ctx.tok("dsp1"))
            src = (d["xn_tm"] if k < NSH else d["xnd"])
            off = (k if k < NSH else k - NSH) * 512 * DM
            S["xtl_tok"][k] = ctx.dma(
                "sync", xtm[:, kb, :, :],
                bass.AP(src.tensor, off, [[DM, 128], [128 * DM, 4], [1, DM]]),
                "xtl%d" % kb)

        def tp_group(k):
            kb = k % 2
            last = None
            for j in range(4):
                tb = j % 2
                ctx.wait("tensor", S["xtl_tok"][k], S["ptx_free"][tb])
                pe = None
                for fc in range(8):
                    pe = ctx.op("tensor", lambda e, kb=kb, j=j, tb=tb, fc=fc:
                                e.transpose(
                                    ptx[:, tb, fc * 128:(fc + 1) * 128],
                                    xtm[:, kb, j, fc * 128:(fc + 1) * 128],
                                    sb["c_ident"][:, :]),
                                "mtp" if fc == 7 else None)
                ctx.wait("vector", pe, S["gu_last"][kb])
                base = xfm[:, :, :, :]
                dst = bass.AP(base.tensor, base.offset + kb * 8 * 512 + j * 128,
                              [base.ap[0], [512, 8], [1, 128]])
                last = ctx.op("vector", lambda e, tb=tb, dst=dst: e.tensor_copy(
                    dst, ptx[:, tb, :].rearrange("p (f t) -> p f t", f=8)), "mte")
                S["ptx_free"][tb] = last
            S["tp_done"][kb] = ctx.tok("mtp")
            S["ev_done"][kb] = last

        def gu_mm(k):
            kb = k % 2
            gu_toks = {}
            for ip in range(3):
                gu_toks[ip] = issue_gu(k, ip)
            mm = None
            for ip in range(16):
                rb = ip % 4
                if ip + 3 < 16:
                    gu_toks[ip + 3] = issue_gu(k, ip + 3)
                pga, pua = (pg, pu) if ip % 2 == 0 else (pg2, pu2)
                ctx.wait("tensor", gu_toks[ip], S["ev_done"][kb])
                if S["h_evac"][ip % 2] is not None:
                    ctx.wait("tensor", *S["h_evac"][ip % 2])
                for kc in range(8):
                    mm = ctx.op("tensor", lambda e, kc=kc, rb=rb, kb=kb, pga=pga:
                                e.matmul(pga[:, :], gu[:, rb, 0, kc, :],
                                         xfm[:, kb, kc, :],
                                         start=(kc == 0), stop=(kc == 7)), None)
                    mm = ctx.op("tensor", lambda e, kc=kc, rb=rb, kb=kb, pua=pua:
                                e.matmul(pua[:, :], gu[:, rb, 1, kc, :],
                                         xfm[:, kb, kc, :],
                                         start=(kc == 0), stop=(kc == 7)),
                                "gmm" if kc == 7 else None)
                S["gu_free"][rb] = mm
                ctx.wait("scalar", mm, ctx.tok("mhd"))
                aa = ctx.op("scalar", lambda e, pga=pga: e.activation(
                    sg[:, :], pga[:, :], AF.Silu), "mha")
                ctx.wait("vector", aa, ctx.tok("dmm"))
                hv = ctx.op("vector", lambda e, ip=ip, pua=pua: e.tensor_tensor(
                    hsb[:, ip, :], sg[:, :], pua[:, :], OP.mult), "mhd")
                S["h_evac"][ip % 2] = [aa, hv]
            S["gu_last"][kb] = mm

        def dispatch(k):
            for tj in range(4):
                tt = 4 * k + tj
                pb = tt % 2
                if S["pay_free"][pb] is not None:
                    ctx.wait("sync", *S["pay_free"][pb])
                pl = ctx.dma("sync", pay[:, pb, :],
                             d["xn_tm"][tt * 128:(tt + 1) * 128, :], "pyl%d" % pb)
                ctx.wait("gpsimd", pl, SP["meta_tok"], xz)
                t1 = ctx.idma(d["xnd"][:, :], IOA(sAi[:, tt:tt + 1], 0),
                              pay[:, pb, :], None, "dsp%d" % pb)
                t2 = ctx.idma(d["xnd"][:, :], IOA(sBi[:, tt:tt + 1], 0),
                              pay[:, pb, :], None, "dsp%d" % pb)
                S["pay_free"][pb] = [t2]

        def down(k):
            kb = k % 2
            s = dn_slot(k)
            shared = k < NSH
            dtok = S["dn_tok"][k if (k >= NSH or k == 0) else 0]
            ctx.wait("tensor", ctx.tok("mhd"), dtok)
            mm = None
            for tt4 in range(4):
                for fh in range(2):
                    db = (tt4 * 2 + fh) % 2
                    if S["pd_free"][db] is not None:
                        ctx.wait("tensor", S["pd_free"][db])
                    for i in range(16):
                        mm = ctx.op("tensor",
                                    lambda e, i=i, fh=fh, db=db, s=s, tt4=tt4:
                                    e.matmul(
                                        pd[:, db, :],
                                        hsb[:, i, tt4 * 128:(tt4 + 1) * 128],
                                        dnb[:, s, i, fh, :],
                                        start=(i == 0), stop=(i == 15)),
                                    "dmm" if i == 15 else None)
                    ctx.wait("vector", mm)
                    if shared:
                        if fh == 0 and S["mac_tok"].get(tt4) is not None:
                            ctx.wait("vector", S["mac_tok"][tt4])
                        ev = ctx.op("vector", lambda e, tt4=tt4, fh=fh, db=db:
                                    e.tensor_copy(
                                        stg[:, tt4, fh * 512:(fh + 1) * 512],
                                        pd[:, db, :]), "mdd")
                    else:
                        if tt4 == 0 and fh == 0 and S["eo_free"][kb] is not None:
                            ctx.wait("vector", *S["eo_free"][kb])
                        ev = ctx.op("vector",
                                    lambda e, kb=kb, tt4=tt4, fh=fh, db=db:
                                    e.tensor_copy(
                                        eo_sb[:, kb, tt4, fh * 512:(fh + 1) * 512],
                                        pd[:, db, :]), "mdd")
                    S["pd_free"][db] = ev
                if shared:
                    ctx.wait("gpsimd", ev)
                    if k == 0 and tt4 == 0:
                        ctx.wait("gpsimd", self._uacc)
                    S["mac_tok"][tt4] = ctx.dma(
                        "gpsimd",
                        bass.AP(d["acc"].tensor, (k * 512 + tt4 * 128) * DM,
                                [[DM, 128], [1, DM]]),
                        stg[:, tt4, :], "mac%d" % tt4, accum=True)
                elif tt4 == 3:
                    rc = k - NSH
                    ctx.wait("sync", ev)
                    so = ctx.dma(
                        "sync",
                        bass.AP(d["eod"].tensor, rc * 512 * DM,
                                [[DM, 128], [128 * DM, 4], [1, DM]]),
                        eo_sb[:, kb, :, :], "meo%d" % kb)
                    S["eo_free"][kb] = [so]
            S["down_done"][k] = ctx.tok("dmm")

        dn_load(0)
        xtl(0)
        for k in range(NK):
            if k == 0:
                tp_group(0)
            gu_mm(k)
            if k < NSH:
                dispatch(k)
            if k + 1 < NK:
                if k + 1 >= NSH:
                    dn_load(k + 1)
                xtl(k + 1)
                tp_group(k + 1)
            down(k)
        ctx.wait("sync", ctx.tok("mac0"), ctx.tok("mac1"), ctx.tok("mac2"),
                 ctx.tok("mac3"), ctx.tok("meo0"), ctx.tok("meo1"))

    def combine(self, a, bsrc, csrc, dst, st):
        ctx, NT = self.ctx, self.NT
        ta = self.sbuf("cb_a", [128, 2, DM], F32, st)
        tb = self.sbuf("cb_b", [128, 2, DM], F32, st)
        tc_ = self.sbuf("cb_c", [128, 2, DM], F32, st)
        to = self.sbuf("cb_o", [128, 2, DM], F32, st)
        free = [None, None]
        for tt in range(NT):
            b = tt % 2
            r0, r1 = tt * 128, (tt + 1) * 128
            if free[b] is not None:
                ctx.wait("sync", *free[b])
            l1 = ctx.dma("sync", ta[:, b, :], a[r0:r1, :], "cbl%d" % b)
            l2 = ctx.dma("sync", tb[:, b, :], bsrc[r0:r1, :], "cbl%d" % b)
            l3 = ctx.dma("sync", tc_[:, b, :], csrc[r0:r1, :], "cbl%d" % b)
            ctx.wait("vector", l3)
            v1 = ctx.op("vector", lambda e, b=b: e.tensor_tensor(
                to[:, b, :], tb[:, b, :], tc_[:, b, :], OP.add), "cbd")
            ctx.wait("vector", v1)
            v2 = ctx.op("vector", lambda e, b=b: e.scalar_tensor_tensor(
                to[:, b, :], to[:, b, :], 0.5, ta[:, b, :], OP.mult, OP.add), "cbd")
            ctx.wait("sync", v2)
            so = ctx.dma("sync", dst[r0:r1, :], to[:, b, :], "cbs%d" % b)
            free[b] = [so, v2]
        ctx.wait("sync", ctx.tok("cbs0"), ctx.tok("cbs1"))

    def combine_moe(self, a, dst, st):
        ctx, d, NT = self.ctx, self.d, self.NT
        I32 = mybir.dt.int32
        ta = self.sbuf("cm_a", [128, 2, DM], F32, st)
        tc_ = self.sbuf("cm_c", [128, 2, DM], F32, st)
        to = self.sbuf("cm_o", [128, 2, DM], F32, st)
        gsb = self.sbuf("cm_g", [128, 2, 2, DM], BF16, st)
        zsb = self.sbuf("cm_z", [128, 2, 2, DM], F32, st)
        six = self.sbuf("cm_six", [128, 2, NT], I32, st)
        wx = self.sbuf("cm_wx", [128, 2, NT], F32, st)
        l0 = ctx.dma("sync", six[:, :, :],
                     d["ridx"][:, :, :].rearrange("j p t -> p j t"), "cbr")
        l1 = ctx.dma("sync", wx[:, :, :],
                     d["rw"][:, :, :].rearrange("j p t -> p j t"), "cbr2")
        ctx.wait("gpsimd", l0)
        ctx.wait("scalar", l1)
        load_toks = {}
        gath_toks = {}
        v4s = {}
        zfree = [None, None]
        gfree = [None, None]

        def loads(tt):
            b = tt % 2
            if tt >= 2:
                ctx.wait("sync", ctx.tok("cbs%d" % b), v4s[tt - 2])
            ctx.dma("sync", ta[:, b, :], a[tt * 128:(tt + 1) * 128, :],
                    "cbl%d" % b)
            load_toks[tt] = ctx.dma(
                "sync", tc_[:, b, :], d["acc"][tt * 128:(tt + 1) * 128, :],
                "cbl%d" % b)

        def gathers(tt):
            b = tt % 2
            if gfree[b] is not None:
                ctx.wait("gpsimd", *gfree[b])
            g1 = ctx.idma(gsb[:, b, 0, :], None, d["eod"][:, :],
                          IOA(six[:, 0, tt:tt + 1], 0), "cbg%d" % (2 * b))
            g2 = ctx.idma(gsb[:, b, 1, :], None, d["eod"][:, :],
                          IOA(six[:, 1, tt:tt + 1], 0), "cbg%d" % (2 * b + 1))
            gath_toks[tt] = (g1, g2)

        loads(0)
        loads(1)
        gathers(0)
        for tt in range(NT):
            b = tt % 2
            g1, g2 = gath_toks[tt]
            ctx.wait("scalar", g1, g2, ctx.tok("cba"), zfree[b])
            z1 = ctx.op("scalar", lambda e, b=b, tt=tt: e.activation(
                zsb[:, b, 0, :], gsb[:, b, 0, :], AF.Copy,
                scale=wx[:, 0, tt:tt + 1]), "cba")
            z2 = ctx.op("scalar", lambda e, b=b, tt=tt: e.activation(
                zsb[:, b, 1, :], gsb[:, b, 1, :], AF.Copy,
                scale=wx[:, 1, tt:tt + 1]), "cba")
            gfree[b] = [z2]
            if tt + 1 < NT:
                gathers(tt + 1)
            ctx.wait("vector", load_toks[tt], z2)
            v1 = ctx.op("vector", lambda e, b=b: e.tensor_tensor(
                to[:, b, :], tc_[:, b, :], zsb[:, b, 0, :], OP.add), "cbd")
            ctx.wait("vector", v1)
            v3 = ctx.op("vector", lambda e, b=b: e.tensor_tensor(
                to[:, b, :], to[:, b, :], zsb[:, b, 1, :], OP.add), "cbd")
            ctx.wait("vector", v3)
            v4 = ctx.op("vector", lambda e, b=b: e.scalar_tensor_tensor(
                to[:, b, :], to[:, b, :], 0.5, ta[:, b, :], OP.mult, OP.add), "cbd")
            zfree[b] = v3
            v4s[tt] = v4
            ctx.wait("gpsimd", v4)
            ctx.dma("gpsimd", dst[tt * 128:(tt + 1) * 128, :], to[:, b, :],
                    "cbs%d" % b)
            if tt + 2 < NT:
                loads(tt + 2)
        ctx.wait("sync", ctx.tok("cbs0"), ctx.tok("cbs1"))

    def ffn_block(self, xsrc, dst):
        ctx, d = self.ctx, self.d
        st = self.estack()
        self.sbuf("vfm", [128, 8, self.T], BF16, st)
        self.ln_pass(xsrc, dst_tm=d["u_s"])
        ctx.barrier()
        ctx.flush()
        self.phase()
        self.ln_pass(d["u_s"], fm_sb=self.sb["vfm"], gamma=self.gamma(0))
        ctx.barrier()
        ctx.flush()
        self.phase()
        st2 = self.estack()
        self.sbuf("cbias", [128, 1], F32, st2, side="right")
        st3 = self.estack()
        bt = self.mla_prep(st3)
        ctx.barrier()
        ctx.flush()
        self.phase()
        self.close_stack(st3)
        self.close_stack(st)
        if not hasattr(self, "_wprep"):
            self.prep_weights()
        st4 = self.estack()
        self.attention(True, 1.0 / 8.0, bt, st4)
        ctx.barrier()
        ctx.flush()
        self.phase()
        self.close_stack(st4)
        self.close_stack(st2)
        st5 = self.estack()
        self.oproj("m_o_T", d["u_s"], d["u_s"], False, st5)
        ctx.barrier()
        ctx.flush()
        self.phase()
        self.close_stack(st5)
        NTc = self.NT
        self._uacc = ctx.dma(
            "scalar",
            bass.AP(d["acc"].tensor, 0, [[DM * 128, NTc], [1, DM * 128]]),
            bass.AP(d["u_s"].tensor, 0, [[DM * 128, NTc], [1, DM * 128]]),
            "uacc")
        st7 = self.estack()
        hook = self.moe_router_setup(st7)
        self.ln_pass(d["u_s"], gamma=self.gamma(1), hook=hook)
        ctx.barrier()
        ctx.flush()
        self.phase()
        self.moe_experts(st7)
        ctx.barrier()
        ctx.flush()
        self.phase()
        self.close_stack(st7)
        st8 = self.estack()
        self.combine_moe(xsrc, dst, st8)
        ctx.barrier()
        ctx.flush()
        self.phase()
        self.close_stack(st8)

    def attn_block(self, xsrc, dst):
        ctx, d = self.ctx, self.d
        st = self.estack()
        self.sbuf("vfm", [128, 8, self.T], BF16, st)
        self.ln_pass(xsrc, fm_sb=self.sb["vfm"], gamma=self.gamma(2), beta=self.gamma(3))
        ctx.barrier()
        ctx.flush()
        self.phase()
        st2 = self.estack()
        self.sbuf("cbias", [128, 1], F32, st2, side="right")
        st3 = self.estack()
        bt = self.s2_prep(st3)
        ctx.barrier()
        ctx.flush()
        self.phase()
        self.close_stack(st3)
        self.close_stack(st)
        st4 = self.estack()
        self.attention(False, 1.0, bt, st4)
        ctx.barrier()
        ctx.flush()
        self.phase()
        self.close_stack(st4)
        self.close_stack(st2)
        st5 = self.estack()
        self.oproj("a_o_T", xsrc, dst, True, st5)
        ctx.barrier()
        ctx.flush()
        self.phase()
        self.close_stack(st5)

    def build(self):
        d = self.d
        self.declare()
        self.load_consts()
        try:
            self.ffn_block(d["x_in"], d["x_cur"])
            self.attn_block(d["x_cur"], d["x_cur"])
            self.ffn_block(d["x_cur"], d["y_out"])
        except StopBuild:
            self.ctx.flush()
            for s in reversed(self.open_stacks):
                s.close()
            self.open_stacks = []
        self.ctx.flush()
        self.stack.close()
        return self.nc


def _prep_inputs(inp, T):
    f32 = np.float32
    C = np.ascontiguousarray
    out = {}
    out["c_ident"] = np.eye(128, dtype=f32)
    k_idx, q_idx = np.meshgrid(np.arange(128), np.arange(128), indexing="ij")
    out["c_mask"] = (k_idx <= q_idx).astype(f32)
    inv = 1.0 / (10000.0 ** (np.arange(0, RD, 2, dtype=np.float64) / RD))
    tpos = np.arange(T, dtype=np.float64)
    fr = np.outer(tpos, inv)
    emb = np.concatenate([fr, fr], axis=1)
    cos = np.cos(emb).astype(f32)
    sin = np.sin(emb).astype(f32)
    sinS = np.concatenate([-sin[:, :16], sin[:, 16:]], axis=1).astype(f32)
    out["c_cos"] = C(np.tile(cos, (1, NH)))
    out["c_sinS"] = C(np.tile(sinS, (1, NH)))
    gam = np.stack([
        np.asarray(inp["L0_ffn_ln1_w"], f32),
        np.asarray(inp["L0_ffn_ln2_w"], f32),
        np.asarray(inp["L0_attn_ln_gamma"], f32),
        np.asarray(inp["L0_attn_ln_beta"], f32),
        np.tile(np.asarray(inp["L0_attn_q_gamma"], f32), NH),
        np.tile(np.asarray(inp["L0_attn_k_gamma"], f32), NH),
    ])
    out["c_gam"] = C(np.broadcast_to(gam[None], (128, 6, DM)))
    out["c_bo"] = C(np.broadcast_to(np.asarray(inp["L0_attn_b_o"], f32)[None], (128, DM)))
    rw = np.zeros((DM, 8), f32)
    rw[:, :7] = np.asarray(inp["L0_ffn_router_w"], f32).T
    out["c_router"] = C(rw.reshape(8, 128, 8))
    rb = np.zeros((8,), f32)
    rb[:7] = np.asarray(inp["L0_ffn_routing_bias"], f32)
    out["c_rbias"] = C(np.broadcast_to(rb[None], (128, 8)))
    NCH = 2 * (T // 512) + 7
    ei, ej = np.meshgrid(np.arange(7), np.arange(7), indexing="ij")
    rm = np.concatenate([
        np.broadcast_to((ej < ei).astype(f32).reshape(1, 49), (128, 49)),
        np.broadcast_to((ej <= ei).astype(f32).reshape(1, 49), (128, 49)),
        np.broadcast_to(512.0 * np.arange(NCH, dtype=f32)[None], (128, NCH)),
        np.broadcast_to(np.arange(NCH, dtype=f32)[None], (128, NCH)),
        128.0 * np.arange(16, dtype=f32)[None]
        + np.arange(128, dtype=f32)[:, None],
        np.arange(128, dtype=f32)[:, None],
    ], axis=1)
    out["c_rmeta"] = C(rm.astype(f32))
    h2 = np.zeros((2, 128), f32)
    h2[0, 0:64] = 1.0
    h2[1, 64:128] = 1.0
    out["c_half2"] = h2

    def blk_kc(w):
        K, N = w.shape
        return C(w.reshape(K // 128, 128, N))

    kv_d = np.asarray(inp["L0_ffn_kv_d"], f32)
    q_d = np.asarray(inp["L0_ffn_q_d"], f32)
    k_u = np.asarray(inp["L0_ffn_k_u"], f32)
    q_u = np.asarray(inp["L0_ffn_q_u"], f32)
    v_u = np.asarray(inp["L0_ffn_v_u"], f32)
    rq = np.asarray(inp["L0_ffn_rope_q_w"], f32)
    rk = np.asarray(inp["L0_ffn_rope_k_w"], f32)
    ow = np.asarray(inp["L0_ffn_o_w"], f32)
    out["m_kv_T"] = blk_kc(kv_d.T)
    out["m_q_T"] = blk_kc(q_d.T)
    nope_rows = np.concatenate([np.arange(h * 64 + 32, h * 64 + 64) for h in range(NH)])
    out["m_knope_T"] = blk_kc(k_u[nope_rows].T)
    out["m_qnope_T"] = blk_kc(q_u[nope_rows].T)
    out["m_v_T"] = blk_kc(v_u.T)
    out["m_rq_T"] = blk_kc(rq.T)
    out["m_rk_T"] = blk_kc(rk.T)
    out["m_o_T"] = blk_kc(ow.T)
    out["a_q_T"] = blk_kc(np.asarray(inp["L0_attn_w_q"], f32).T)
    out["a_k_T"] = blk_kc(np.asarray(inp["L0_attn_w_k"], f32).T)
    out["a_v_T"] = blk_kc(np.asarray(inp["L0_attn_w_v"], f32).T)
    out["a_o_T"] = blk_kc(np.asarray(inp["L0_attn_w_o"], f32).T)

    sgw = np.asarray(inp["L0_ffn_shared_gate"], f32)[0]
    suw = np.asarray(inp["L0_ffn_shared_up"], f32)[0]
    sdw = np.asarray(inp["L0_ffn_shared_down"], f32)[0]
    rgw = np.asarray(inp["L0_ffn_routed_gate"], f32)
    ruw = np.asarray(inp["L0_ffn_routed_up"], f32)
    rdw = np.asarray(inp["L0_ffn_routed_down"], f32)
    gu = np.zeros((8, 16, 128, 2, 8, 128), f32)
    dn = np.zeros((8, 128, 16, 2, 512), f32)
    for e in range(8):
        gw = rgw[e] if e < 7 else sgw
        uw = ruw[e] if e < 7 else suw
        dw = rdw[e] if e < 7 else sdw
        gT = gw.T.reshape(8, 128, 16, 128)
        uT = uw.T.reshape(8, 128, 16, 128)
        gu[e, :, :, 0] = gT.transpose(2, 1, 0, 3)
        gu[e, :, :, 1] = uT.transpose(2, 1, 0, 3)
        dT = dw.T.reshape(16, 128, 2, 512)
        dn[e] = dT.transpose(1, 0, 2, 3)
    out["moe_gu"] = C(gu)
    out["moe_dn"] = C(dn)
    return out


def kernel(**inputs):
    x = np.asarray(inputs["L0_attn_x"], np.float32)
    B, T, _ = x.shape
    mk = MK(T)
    nc = mk.build()
    weights = _prep_inputs(inputs, T)
    in_maps = []
    core_ids = [0, 2, 4, 6][:B]
    for b in range(B):
        m = dict(weights)
        m["x_in"] = np.ascontiguousarray(x[b])
        in_maps.append(m)
    res = run_bass_kernel_spmd(nc, in_maps, core_ids=core_ids)
    out = np.stack([res.results[b]["y_out"] for b in range(B)])
    return out


# version 25
# speedup vs baseline: 1.0262x; 1.0262x over previous
import math
import numpy as np
import ml_dtypes
import concourse.bass as bass
import concourse.mybir as mybir
from concourse.bass import IndirectOffsetOnAxis as IOA
from concourse.bass_isa import ReduceOp
from concourse.library_config import attnmlp as LIB_ATTNMLP
from concourse.bass_utils import run_bass_kernel_spmd
from contextlib import ExitStack

F32 = mybir.dt.float32
BF16 = mybir.dt.bfloat16
AF = mybir.ActivationFunctionType
OP = mybir.AluOpType
AX = mybir.AxisListType

DM = 1024
NH = 16
HD = 64
RD = 32
LAT = 256
IFF = 2048
LN_EPS = 1e-5

ENGINES = ("sync", "gpsimd", "tensor", "vector", "scalar")


class Ctx:

    def __init__(self, nc, stack):
        self.nc = nc
        self.stack = stack
        self.plan = {e: [] for e in ENGINES}
        self.sems = {}
        self.cnt = {}
        self.last_wait = {}
        self.owner = {}

    def sem(self, name):
        if name not in self.sems:
            self.sems[name] = self.stack.enter_context(self.nc.semaphore(name))
            self.cnt[name] = 0
        return self.sems[name]

    def bump(self, name, n):
        self.sem(name)
        self.cnt[name] += n
        return self.cnt[name]

    def emit(self, eng, fn):
        self.plan[eng].append(fn)

    def dma(self, eng, out, in_, sem_name, accum=False):
        self.owner[sem_name] = eng
        val = self.bump(sem_name, 16)
        s = self.sems[sem_name]
        if accum:
            def fn(e, out=out, in_=in_, s=s):
                e.dma_start(out=out, in_=in_, accum_op=OP.add).then_inc(s, 16)
        else:
            def fn(e, out=out, in_=in_, s=s):
                e.dma_start(out=out, in_=in_).then_inc(s, 16)
        self.emit(eng, fn)
        return (sem_name, val)

    def idma(self, out, out_off, in_, in_off, sem_name):
        self.owner[sem_name] = "gpsimd"
        val = self.bump(sem_name, 16)
        s = self.sems[sem_name]

        def fn(e, out=out, oo=out_off, in_=in_, io=in_off, s=s):
            e.indirect_dma_start(out=out, out_offset=oo, in_=in_,
                                 in_offset=io).then_inc(s, 16)
        self.emit("gpsimd", fn)
        return (sem_name, val)

    def wait(self, eng, *toks):
        for tok in toks:
            if tok is None:
                continue
            name, val = tok
            if val <= 0:
                continue
            if self.last_wait.get((eng, name), -1) >= val:
                continue
            self.last_wait[(eng, name)] = val
            s = self.sem(name)
            self.emit(eng, lambda e, s=s, v=val: e.wait_ge(s, v))

    def op(self, eng, fn, sem_name=None):
        if sem_name is None:
            self.emit(eng, fn)
            return None
        self.owner[sem_name] = eng
        val = self.bump(sem_name, 1)
        s = self.sems[sem_name]

        def wrapped(e, fn=fn, s=s):
            fn(e).then_inc(s, 1)

        self.emit(eng, wrapped)
        return (sem_name, val)

    def tok(self, name):
        return (name, self.cnt.get(name, 0))

    barrier_exempt = ("wcast", "xz")

    def barrier(self):
        for eng in ENGINES:
            for name, owner in list(self.owner.items()):
                if owner == eng and name not in self.barrier_exempt:
                    self.wait(eng, (name, self.cnt[name]))
        vals = {}
        for eng in ENGINES:
            vals[eng] = self.bump("bar_" + eng, 1)
        for eng in ENGINES:
            s = self.sems["bar_" + eng]
            self.emit(eng, lambda e, s=s: e.nop().then_inc(s, 1))
        for eng in ENGINES:
            for other in ENGINES:
                s = self.sems["bar_" + other]
                self.emit(eng, lambda e, s=s, v=vals[other]: e.wait_ge(s, v))

    def flush(self):
        if not any(self.plan[e] for e in ENGINES):
            return
        with self.nc.Block() as block:
            for eng_name in ENGINES:
                items = self.plan[eng_name]
                if not items:
                    continue
                section = getattr(block, eng_name)

                def make(items):
                    def _(engine):
                        for fn in items:
                            fn(engine)
                    return _

                section(make(items))
        self.plan = {e: [] for e in ENGINES}


class StopBuild(Exception):
    pass


class MK:
    def __init__(self, T, dbg=(), upto=None):
        self.upto = upto
        self.phase_n = 0
        self.T = T
        self.NT = T // 128
        self.TC = T // 512
        self.NSH = T // 512
        self.NCH = 2 * (T // 512) + 7
        self.NSLOT = self.NCH * 512
        self.nc = bass.Bass("TRN2", target_bir_lowering=False)
        self.stack = ExitStack()
        self.ctx = Ctx(self.nc, self.stack)
        self.dbg = set(dbg)
        self.d = {}
        self.sb = {}
        self.open_stacks = []

    def phase(self):
        self.phase_n += 1
        if self.upto is not None and self.phase_n > self.upto:
            raise StopBuild()

    def estack(self):
        s = ExitStack()
        self.open_stacks.append(s)
        return s

    def close_stack(self, s):
        self.open_stacks.remove(s)
        s.close()

    def dram(self, name, shape, dtype=F32, kind=None):
        if name in self.dbg and kind is None:
            kind = "ExternalOutput"
        t = (self.nc.dram_tensor(name, shape, dtype, kind=kind) if kind
             else self.nc.dram_tensor(name, shape, dtype))
        self.d[name] = t.ap()
        return self.d[name]

    _uid = 0

    def _uniq(self, name):
        MK._uid += 1
        return "%s_%d" % (name, MK._uid)

    def sbuf(self, name, shape, dtype, stack=None, side=None):
        t = (stack or self.stack).enter_context(
            self.nc.sbuf_tensor(self._uniq(name), shape, dtype, side=side))
        self.sb[name] = t
        return t

    def psum(self, name, shape, dtype, stack=None):
        t = (stack or self.stack).enter_context(
            self.nc.psum_tensor(self._uniq(name), shape, dtype))
        self.sb[name] = t
        return t

    def declare(self):
        d, T = self.dram, self.T
        d("x_in", [T, DM], kind="ExternalInput")
        d("y_out", [T, DM], kind="ExternalOutput")
        d("c_ident", [128, 128], kind="ExternalInput")
        d("c_mask", [128, 128], kind="ExternalInput")
        d("c_cos", [T, 512], kind="ExternalInput")
        d("c_sinS", [T, 512], kind="ExternalInput")
        d("c_gam", [128, 6, DM], kind="ExternalInput")
        d("c_bo", [128, DM], kind="ExternalInput")
        d("c_router", [8, 128, 8], kind="ExternalInput")
        d("c_rbias", [128, 8], kind="ExternalInput")
        d("c_half2", [2, 128], kind="ExternalInput")
        d("m_kv_T", [8, 128, LAT], kind="ExternalInput")
        d("m_q_T", [8, 128, LAT], kind="ExternalInput")
        d("m_knope_T", [2, 128, 512], kind="ExternalInput")
        d("m_qnope_T", [2, 128, 512], kind="ExternalInput")
        d("m_v_T", [2, 128, DM], kind="ExternalInput")
        d("m_rq_T", [2, 128, 512], kind="ExternalInput")
        d("m_rk_T", [8, 128, 512], kind="ExternalInput")
        d("m_o_T", [8, 128, DM], kind="ExternalInput")
        d("a_q_T", [8, 128, DM], kind="ExternalInput")
        d("a_k_T", [8, 128, DM], kind="ExternalInput")
        d("a_v_T", [8, 128, DM], kind="ExternalInput")
        d("a_o_T", [8, 128, DM], kind="ExternalInput")
        d("moe_gu", [8, 16, 128, 2, 8, 128], kind="ExternalInput")
        d("moe_dn", [8, 128, 16, 2, 512], kind="ExternalInput")
        d("c_rmeta", [128, 115 + 2 * self.NCH], kind="ExternalInput")
        d("gu_bf", [8 * 16 * 128, 2048], BF16)
        d("dn_bf", [8 * 128, 16 * 2 * 512], BF16)
        d("xn_tm", [T, DM], BF16)
        d("xnd", [self.NSLOT, DM], BF16)
        d("eod", [self.NSLOT, DM], BF16)
        d("ridx", [2, 128, self.NT], mybir.dt.int32)
        d("rw", [2, 128, self.NT])
        d("u_s", [T, DM])
        d("x_cur", [T, DM])
        d("acc", [T, DM])
        d("kfm", [NH, HD, T], BF16)
        d("qfm", [NH, HD, T], BF16)
        d("vtm", [self.NT, 128, NH * 65], BF16)
        d("ofm", [8, self.TC, 128, 512], BF16)
        d("lsum", [NH, T], F32)

    def load_consts(self):
        ctx, sb, d = self.ctx, self.sb, self.d
        self.sbuf("c_ident", [128, 128], BF16)
        self.sbuf("c_mask", [128, 128], BF16)
        self.sbuf("c_gam", [128, 6, DM], F32)
        self.sbuf("c_bo", [128, DM], F32)
        self.sbuf("c_rbias", [128, 8], F32)
        self.sbuf("c_eps", [128, 1], F32)
        self.sbuf("c_tiny", [128, 1], F32)
        self.sbuf("c_mln8", [128, 1], F32)
        self.sbuf("c_one64", [1, 64], BF16)
        self.sbuf("c_ones1b", [1, 128], BF16)
        self.sbuf("c_half2", [2, 128], BF16)
        self.sbuf("c_rmeta", [128, 115 + 2 * self.NCH], F32)
        self.sbuf("c_ones128", [128, 128], BF16)
        self.sbuf("zbuf", [128, 2, DM], BF16)
        toks = [
            ctx.dma("gpsimd", sb["c_ident"][:, :], d["c_ident"][:, :], "wks"),
            ctx.dma("gpsimd", sb["c_mask"][:, :], d["c_mask"][:, :], "wks"),
            ctx.dma("gpsimd", sb["c_gam"][:, :, :], d["c_gam"][:, :, :], "wks"),
            ctx.dma("gpsimd", sb["c_bo"][:, :], d["c_bo"][:, :], "wks"),
            ctx.dma("gpsimd", sb["c_rbias"][:, :], d["c_rbias"][:, :], "wks"),
            ctx.dma("gpsimd", sb["c_half2"][:, :], d["c_half2"][:, :], "wks"),
            ctx.dma("gpsimd", sb["c_rmeta"][:, :], d["c_rmeta"][:, :], "wks"),
        ]
        g1 = ctx.op("gpsimd", lambda e: (
            e.memset(sb["c_eps"][:, :], float(LN_EPS)),
            e.memset(sb["c_tiny"][:, :], 1e-16),
            e.memset(sb["c_mln8"][:, :], -math.log(8.0)),
            e.memset(sb["c_one64"][:, :], 1.0),
            e.memset(sb["c_ones1b"][:, :], 1.0),
            e.memset(sb["c_ones128"][:, :], 1.0))[-1], "gpk")
        for eng in ENGINES:
            ctx.wait(eng, toks[-1], g1)
            ctx.wait(eng, ("wks", ctx.cnt["wks"]))

    def gamma(self, idx):
        return self.sb["c_gam"][:, idx, :]

    def rmeta(self, name, n):
        off = {"lt7": 0, "le7": 49, "thr": 98, "chunk": 98 + self.NCH,
               "ipp": 98 + 2 * self.NCH, "pid": 114 + 2 * self.NCH}[name]
        return self.sb["c_rmeta"][:, off:off + n]

    def prep_weights(self):
        ctx, d = self.ctx, self.d
        for e_ in range(8):
            ctx.dma("gpsimd",
                    bass.AP(d["gu_bf"].tensor, e_ * 2048 * 2048,
                            [[2048, 2048], [1, 2048]]),
                    bass.AP(d["moe_gu"].tensor, e_ * 2048 * 2048,
                            [[2048, 2048], [1, 2048]]), "wcast")
            ctx.dma("gpsimd", d["dn_bf"][e_ * 128:(e_ + 1) * 128, :],
                    bass.AP(d["moe_dn"].tensor, e_ * 128 * 16384,
                            [[16384, 128], [1, 16384]]), "wcast")
        zb = self.sb["zbuf"]
        gz = ctx.op("gpsimd", lambda e: e.memset(zb[:, :, :], 0.0), "gpz")
        ctx.wait("gpsimd", gz)
        for c in range(2 * self.NCH):
            ctx.dma("gpsimd",
                    bass.AP(d["xnd"].tensor, c * 256 * DM,
                            [[DM, 128], [128 * DM, 2], [1, DM]]),
                    zb[:, :, :], "xz")
        self._wprep = (ctx.tok("wcast"), ctx.tok("xz"))

    def ln_pass(self, src, dst_tm=None, fm_sb=None, gamma=None, beta=None, hook=None):
        ctx, sb, NT = self.ctx, self.sb, self.NT
        G = 4
        st = self.estack()
        xt = self.sbuf("lnxt", [128, 2 * G, DM], F32, st)
        ut = self.sbuf("lnut", [128, 2 * G, DM], F32, st)
        gt = self.sbuf("lngt", [128, 2 * G, DM], BF16, st)
        f1 = self.sbuf("lnf1", [128, 2, DM], F32, st)
        sqj = self.sbuf("lnsqj", [128, DM], BF16, st)
        sm = {n: self.sbuf("ln" + n, [128, 2 * G], F32, st) for n in
              ("s", "q", "m", "t", "v", "l", "r")}
        ptp = self.psum("lnptp", [128, 1024], BF16, st)
        want_fm = fm_sb is not None
        slot_free = [None] * (2 * G)
        hook_last = None
        assert NT % G == 0
        for g0 in range(0, NT, G):
            tts = list(range(g0, g0 + G))
            slots = [tt % (2 * G) for tt in tts]
            for tt, sl in zip(tts, slots):
                if slot_free[sl] is not None:
                    ctx.wait("sync", *slot_free[sl])
                ctx.dma("sync", xt[:, sl, :], src[tt * 128:(tt + 1) * 128, :],
                        "lnl%d" % (sl // G))
            ld = ctx.tok("lnl%d" % (slots[0] // G))
            ctx.wait("vector", ld)
            ctx.wait("scalar", ld)
            for tt, sl in zip(tts, slots):
                ctx.op("vector", lambda e, sl=sl: e.reduce_sum(
                    sm["s"][:, sl:sl + 1], xt[:, sl, :], AX.X), "lnd")
            for tt, sl in zip(tts, slots):
                ctx.wait("scalar", ctx.tok("lna"))
                ctx.op("scalar", lambda e, sl=sl: e.activation(
                    sqj[:, :], xt[:, sl, :], AF.Square,
                    accum_out=sm["q"][:, sl:sl + 1]), "lna")
            ctx.wait("vector", ctx.tok("lnd"))
            for tt, sl in zip(tts, slots):
                ctx.op("vector", lambda e, sl=sl: (
                    e.tensor_scalar(sm["m"][:, sl:sl + 1], sm["s"][:, sl:sl + 1],
                                    1.0 / DM, None, OP.mult),
                    e.tensor_tensor(sm["t"][:, sl:sl + 1], sm["s"][:, sl:sl + 1],
                                    sm["s"][:, sl:sl + 1], OP.mult))[-1], "lnd")
            ctx.wait("vector", ctx.tok("lnd"), ctx.tok("lna"))
            for tt, sl in zip(tts, slots):
                ctx.op("vector", lambda e, sl=sl: e.scalar_tensor_tensor(
                    sm["v"][:, sl:sl + 1], sm["t"][:, sl:sl + 1], -1.0 / DM,
                    sm["q"][:, sl:sl + 1], OP.mult, OP.add), "lnd")
            ctx.wait("scalar", ctx.tok("lnd"))
            for tt, sl in zip(tts, slots):
                ctx.op("scalar", lambda e, sl=sl: e.activation(
                    sm["l"][:, sl:sl + 1], sm["v"][:, sl:sl + 1], AF.Ln,
                    scale=1.0 / DM, bias=sb["c_eps"][:, 0:1]), "lna")
            ctx.wait("scalar", ctx.tok("lna"))
            for tt, sl in zip(tts, slots):
                ctx.op("scalar", lambda e, sl=sl: e.activation(
                    sm["r"][:, sl:sl + 1], sm["l"][:, sl:sl + 1], AF.Exp,
                    scale=-0.5), "lna")
            ctx.wait("vector", ctx.tok("lna"))
            applies = {}
            for tt, sl in zip(tts, slots):
                applies[tt] = ctx.op("vector", lambda e, sl=sl: e.tensor_scalar(
                    ut[:, sl, :], xt[:, sl, :], sm["m"][:, sl:sl + 1],
                    sm["r"][:, sl:sl + 1], OP.subtract, OP.mult), "lnd")
            if dst_tm is not None:
                ctx.wait("sync", applies[tts[-1]])
                for tt, sl in zip(tts, slots):
                    ctx.dma("sync", dst_tm[tt * 128:(tt + 1) * 128, :],
                            ut[:, sl, :], "lns%d" % (sl // G))
                st_tok = ctx.tok("lns%d" % (slots[0] // G))
            else:
                st_tok = None
            if want_fm or hook is not None:
                ctx.wait("vector", ctx.tok("lnd"))
                for tt, sl in zip(tts, slots):
                    b2 = tt % 2
                    fdep = None
                    if gamma is not None and beta is not None:
                        ctx.wait("vector", hook_last)
                        v5 = ctx.op("vector", lambda e, sl=sl, b2=b2: e.tensor_tensor(
                            f1[:, b2, :], ut[:, sl, :], gamma, OP.mult), "lnd")
                        ctx.wait("vector", v5)
                        v6 = ctx.op("vector", lambda e, sl=sl, b2=b2: e.tensor_tensor(
                            gt[:, sl, :], f1[:, b2, :], beta, OP.add), "lnd")
                    elif gamma is not None and hook is not None:
                        ctx.wait("vector", hook_last)
                        v5 = ctx.op("vector", lambda e, sl=sl, b2=b2: e.tensor_tensor(
                            f1[:, b2, :], ut[:, sl, :], gamma, OP.mult), "lnd")
                        ctx.wait("vector", v5)
                        v6 = ctx.op("vector", lambda e, sl=sl, b2=b2: e.tensor_copy(
                            gt[:, sl, :], f1[:, b2, :]), "lnd")
                    elif gamma is not None:
                        v6 = ctx.op("vector", lambda e, sl=sl: e.tensor_tensor(
                            gt[:, sl, :], ut[:, sl, :], gamma, OP.mult), "lnd")
                    else:
                        v6 = ctx.op("vector", lambda e, sl=sl: e.tensor_copy(
                            gt[:, sl, :], ut[:, sl, :]), "lnd")
                    if want_fm:
                        ctx.wait("tensor", v6, ctx.tok("lnt"))
                        pe_v = None
                        for fc in range(8):
                            pe_v = ctx.op("tensor", lambda e, sl=sl, fc=fc: e.transpose(
                                ptp[:, fc * 128:(fc + 1) * 128],
                                gt[:, sl, fc * 128:(fc + 1) * 128],
                                sb["c_ident"][:, :]), "lnp" if fc == 7 else None)
                        ctx.wait("vector", pe_v)
                        v7 = ctx.op("vector", lambda e, tt=tt: e.tensor_copy(
                            fm_sb[:, :, tt * 128:(tt + 1) * 128],
                            ptp[:, :].rearrange("p (f t) -> p f t", f=8)), "lnt")
                    else:
                        v7 = None
                    hk = None
                    hks = ()
                    if hook is not None:
                        hk = hook(tt, f1[:, b2, :], gt[:, sl, :], v6, v7)
                        if isinstance(hk, tuple):
                            hks = hk[1:]
                            hk = hk[0]
                        hook_last = hk
                    slot_free[sl] = [t for t in (applies[tt], v7, hk, st_tok,
                                                 ctx.tok("lna")) + hks
                                     if t is not None]
            else:
                for tt, sl in zip(tts, slots):
                    slot_free[sl] = [t for t in (applies[tt], st_tok,
                                                 ctx.tok("lna")) if t is not None]
        ctx.wait("sync", ctx.tok("lns0"), ctx.tok("lns1"))
        out_toks = [ctx.tok("lnd"), ctx.tok("lna"), ctx.tok("lnt")]
        self.close_stack(st)
        return out_toks

    def tile_norms(self, tile, dep, mstack, tt, sq, s16):
        ctx = self.ctx
        ctx.wait("scalar", dep, ctx.tok("nsd"))
        a = ctx.op("scalar", lambda e: e.activation(sq[:, :], tile, AF.Square), "nsa")
        ctx.wait("vector", a)
        v = ctx.op("vector", lambda e: e.tensor_reduce(
            s16[:, :], sq[:, :].rearrange("p (h d) -> p h d", h=NH), AX.X, OP.add), "nsd")
        ctx.wait("vector", v)
        v2 = ctx.op("vector", lambda e, tt=tt: e.reduce_max(
            mstack[:, tt:tt + 1], s16[:, :], AX.X), "nsd")
        return v2, a

    def bound(self, mq, mk_, scale, deps, ptp, pbnd):
        ctx, sb = self.ctx, self.sb
        ctx.wait("vector", *deps)
        t1 = self.sb["bnd_t1"]
        t1b = self.sb["bnd_t1b"]
        t2 = self.sb["bnd_t2"]
        cb = self.sb["cbias"]
        v1 = ctx.op("vector", lambda e: (
            e.reduce_max(t1[:, 0:1], mq[:, :], AX.X),
            e.reduce_max(t1[:, 1:2], mk_[:, :], AX.X))[-1], "nsd")
        ctx.wait("vector", v1)
        v1b = ctx.op("vector", lambda e: e.tensor_copy(t1b[:, 0:2], t1[:, 0:2]), "nsd")
        ctx.wait("tensor", v1b, ctx.tok("nsd"), ctx.tok("nsa"), ctx.tok("prt"))
        p1 = ctx.op("tensor", lambda e: e.transpose(
            ptp[:, 0:128][0:2, :], t1b[:, 0:2], sb["c_ident"][:, :]), "bnp")
        ctx.wait("vector", p1)
        v2 = ctx.op("vector", lambda e: e.reduce_max(
            t2[:, 0:1][0:2, :], ptp[:, 0:128][0:2, :], AX.X), "nsd")
        ctx.wait("vector", v2)
        v2b = ctx.op("vector", lambda e: e.tensor_copy(t1b[:, 2:3][0:2, :],
                                                       t2[:, 0:1][0:2, :]), "nsd")
        ctx.wait("tensor", v2b)
        p2 = ctx.op("tensor", lambda e: e.transpose(
            ptp[:, 128:130][0:1, :], t1b[:, 2:3][0:2, :], sb["c_ident"][0:2, 0:2]), "bnp")
        ctx.wait("vector", p2)
        v3 = ctx.op("vector", lambda e: e.tensor_copy(t2[:, 1:3][0:1, :],
                                                      ptp[:, 128:130][0:1, :]), "nsd")
        ctx.wait("scalar", v3)
        a1 = ctx.op("scalar", lambda e: e.activation(t2[:, 3:5][0:1, :],
                                                    t2[:, 1:3][0:1, :], AF.Ln), "nsa")
        ctx.wait("vector", a1)
        v4 = ctx.op("vector", lambda e: e.tensor_tensor(
            t2[:, 5:6][0:1, :], t2[:, 3:4][0:1, :], t2[:, 4:5][0:1, :], OP.add), "nsd")
        ctx.wait("scalar", v4)
        a2 = ctx.op("scalar", lambda e: e.activation(t2[:, 6:7][0:1, :],
                                                    t2[:, 5:6][0:1, :], AF.Exp,
                                                    scale=0.5), "nsa")
        ctx.wait("vector", a2)
        v5 = ctx.op("vector", lambda e: e.tensor_scalar(
            t1b[:, 3:4][0:1, :], t2[:, 6:7][0:1, :], -1.02 * scale, None, OP.mult), "nsd")
        ctx.wait("tensor", v5)
        p3 = ctx.op("tensor", lambda e: e.matmul(
            pbnd[:, 0:1], sb["c_ones1b"][0:1, :], t1b[:, 3:4][0:1, :],
            start=True, stop=True), "bnp")
        ctx.wait("vector", p3)
        v6 = ctx.op("vector", lambda e: e.tensor_copy(cb[:, 0:1], pbnd[:, 0:1]), "nsd")
        return v6

    def mla_prep(self, st):
        ctx, sb, d, NT, TC = self.ctx, self.sb, self.d, self.NT, self.TC
        vfm = sb["vfm"]
        wkv = self.sbuf("wkv", [128, 8, LAT], BF16, st)
        wq = self.sbuf("wq", [128, 8, LAT], BF16, st)
        wkn = self.sbuf("wkn", [128, 2, 512], BF16, st)
        wqn = self.sbuf("wqn", [128, 2, 512], BF16, st)
        wv = self.sbuf("wv", [128, 2, DM], BF16, st)
        wrq = self.sbuf("wrq", [128, 2, 512], BF16, st)
        wrk = self.sbuf("wrk", [128, 8, 512], BF16, st)
        wt = []
        wt.append(ctx.dma("gpsimd", wkv[:, :, :],
                          d["m_kv_T"][:, :, :].rearrange("kc p l -> p kc l"), "wks"))
        wt.append(ctx.dma("gpsimd", wq[:, :, :],
                          d["m_q_T"][:, :, :].rearrange("kc p l -> p kc l"), "wks"))
        wt.append(ctx.dma("gpsimd", wkn[:, :, :],
                          d["m_knope_T"][:, :, :].rearrange("kc p l -> p kc l"), "wks"))
        wt.append(ctx.dma("gpsimd", wqn[:, :, :],
                          d["m_qnope_T"][:, :, :].rearrange("kc p l -> p kc l"), "wks"))
        wt.append(ctx.dma("gpsimd", wv[:, :, :],
                          d["m_v_T"][:, :, :].rearrange("kc p l -> p kc l"), "wks"))
        wt.append(ctx.dma("gpsimd", wrq[:, :, :],
                          d["m_rq_T"][:, :, :].rearrange("kc p l -> p kc l"), "wks"))
        wt.append(ctx.dma("gpsimd", wrk[:, :, :],
                          d["m_rk_T"][:, :, :].rearrange("kc p l -> p kc l"), "wks"))
        wtok = ctx.tok("wks")
        ctx.wait("tensor", wtok)

        latk = self.sbuf("latk", [128, 2, self.T], BF16, st)
        latq = self.sbuf("latq", [128, 2, self.T], BF16, st)
        lat_st = self.estack()
        pslat = self.psum("pslat", [128, 4, 512], F32, lat_st)
        ctx.wait("vector", ctx.tok("lnt"))
        ctx.wait("tensor", ctx.tok("lnt"))
        for (lat_sb, wsrc) in ((latk, wkv), (latq, wq)):
            for lc in range(2):
                for tcg in range(0, TC, 4):
                    ng = min(4, TC - tcg)
                    ctx.wait("tensor", ctx.tok("lpe"))
                    pv = None
                    for kc in range(8):
                        for j in range(ng):
                            tc = tcg + j
                            pv = ctx.op("tensor",
                                        lambda e, kc=kc, lc=lc, tc=tc, j=j, wsrc=wsrc:
                                        e.matmul(pslat[:, j, :],
                                                 wsrc[:, kc, lc * 128:(lc + 1) * 128],
                                                 vfm[:, kc, tc * 512:(tc + 1) * 512],
                                                 start=(kc == 0), stop=(kc == 7)),
                                        "lmm" if (kc == 7 and j == ng - 1) else None)
                    ctx.wait("vector", pv)
                    ev = None
                    for j in range(ng):
                        tc = tcg + j
                        ev = ctx.op("vector", lambda e, lat_sb=lat_sb, lc=lc, tc=tc, j=j:
                                    e.tensor_copy(lat_sb[:, lc, tc * 512:(tc + 1) * 512],
                                                  pslat[:, j, :]), "lpe")
        ctx.wait("tensor", ctx.tok("lpe"))
        if "dbg_latk" in self.dbg:
            dl = self.dram("dbg_latk", [128, 2, self.T], BF16, kind="ExternalOutput")
            ctx.wait("sync", ctx.tok("lpe"))
            ctx.dma("sync", dl[:, :, :], latk[:, :, :], "dbgs")
            dl2 = self.dram("dbg_latq", [128, 2, self.T], BF16, kind="ExternalOutput")
            ctx.dma("sync", dl2[:, :, :], latq[:, :, :], "dbgs")
            ctx.wait("sync", ctx.tok("dbgs"))
        ctx.barrier()
        self.close_stack(lat_st)

        pkr = self.psum("pkr", [128, 512], F32, st)
        pkn = self.psum("pkn", [128, 512], F32, st)
        pv_ = self.psum("pv", [128, 2, 512], F32, st)
        pqr = self.psum("pqr", [128, 512], F32, st)
        pqn = self.psum("pqn", [128, 512], F32, st)
        ptp = self.psum("ptp2", [128, 1024], BF16, st)
        cs = self.sbuf("cs", [128, 2, 512], F32, st)
        sn = self.sbuf("sn", [128, 2, 512], F32, st)
        rf1 = self.sbuf("rf1", [128, 512], F32, st)
        rf2 = self.sbuf("rf2", [128, 512], F32, st)
        ktm = self.sbuf("ktm", [128, 2, DM], BF16, st)
        qtm = self.sbuf("qtm", [128, 2, DM], BF16, st)
        vstg = self.sbuf("vstg", [128, 2, NH * 65], BF16, st)
        kstg = self.sbuf("kstg", [128, 8, 512], BF16, st)
        qstg = self.sbuf("qstg", [128, 8, 512], BF16, st)
        sqn = self.sbuf("sqn", [128, DM], BF16, st)
        s16 = self.sbuf("s16", [128, NH], F32, st)
        mq = self.sbuf("mq", [128, NT], F32, st)
        mk = self.sbuf("mk", [128, NT], F32, st)
        self.sbuf("bnd_t1", [128, 8], F32, st)
        self.sbuf("bnd_t1b", [128, 4], BF16, st)
        self.sbuf("bnd_t2", [128, 8], F32, st)

        ktm_free = [None, None]
        qtm_free = [None, None]
        vstg_free = [None, None]
        cs_free = [None, None]
        kq_stage_free = None
        for tt in range(NT):
            b = tt % 2
            r0, r1 = tt * 128, (tt + 1) * 128
            if cs_free[b] is not None:
                ctx.wait("sync", cs_free[b])
            lc1 = ctx.dma("sync", cs[:, b, :], d["c_cos"][r0:r1, :], "csl%d" % b)
            lc2 = ctx.dma("sync", sn[:, b, :], d["c_sinS"][r0:r1, :], "csl%d" % b)
            ctx.wait("tensor", ctx.tok("prd"), ctx.tok("pra"))
            pv_done = []
            for kc in range(8):
                pv_done.append(ctx.op("tensor", lambda e, kc=kc, r0=r0: e.matmul(
                    pkr[:, :], vfm[:, kc, r0:r0 + 128], wrk[:, kc, :],
                    start=(kc == 0), stop=(kc == 7)), "pmm" if kc == 7 else None))
            for lc in range(2):
                ctx.op("tensor", lambda e, lc=lc, r0=r0: e.matmul(
                    pkn[:, :], latk[:, lc, r0:r0 + 128], wkn[:, lc, :],
                    start=(lc == 0), stop=(lc == 1)), None)
                for fh in range(2):
                    ctx.op("tensor", lambda e, lc=lc, fh=fh, r0=r0: e.matmul(
                        pv_[:, fh, :], latk[:, lc, r0:r0 + 128],
                        wv[:, lc, fh * 512:(fh + 1) * 512],
                        start=(lc == 0), stop=(lc == 1)), None)
            for lc in range(2):
                ctx.op("tensor", lambda e, lc=lc, r0=r0: e.matmul(
                    pqr[:, :], latq[:, lc, r0:r0 + 128], wrq[:, lc, :],
                    start=(lc == 0), stop=(lc == 1)), None)
                pv_done.append(ctx.op("tensor", lambda e, lc=lc, r0=r0: e.matmul(
                    pqn[:, :], latq[:, lc, r0:r0 + 128], wqn[:, lc, :],
                    start=(lc == 0), stop=(lc == 1)), "pmm" if lc == 1 else None))
            mm_tok = ctx.tok("pmm")
            ctx.wait("vector", mm_tok, lc2)
            if ktm_free[b] is not None:
                ctx.wait("vector", *ktm_free[b])
            if qtm_free[b] is not None:
                ctx.wait("vector", *qtm_free[b])
            if "dbg_ktm" in self.dbg:
                ctx.wait("vector", ctx.tok("dbgs"))
            if vstg_free[b] is not None:
                ctx.wait("scalar", vstg_free[b])
                ctx.wait("gpsimd", vstg_free[b])
            kt, qt, vs = ktm[:, b], qtm[:, b], vstg[:, b]

            v1 = ctx.op("vector", lambda e, b=b: (
                e.tensor_tensor(rf1[:, :], pkr[:, :], cs[:, b, :], OP.mult),
                e.tensor_tensor(rf2[:, :], bass.AP(
                    pkr[:, :].tensor, pkr[:, :].offset + 16,
                    [list(p) for p in pkr[:, :].ap[:1]] + [[32, 16], [-16, 2], [1, 16]]),
                    sn[:, b, :], OP.mult))[-1], "prd")
            ctx.wait("vector", v1)
            v2 = ctx.op("vector", lambda e, kt=kt: e.tensor_tensor(
                bass.AP(kt.tensor, kt.offset,
                        [list(p) for p in kt.ap[:1]] + [[64, 16], [1, 32]]),
                rf1[:, :], rf2[:, :], OP.add), "prd")
            v3 = ctx.op("vector", lambda e, kt=kt: e.tensor_copy(
                bass.AP(kt.tensor, kt.offset + 32,
                        [list(p) for p in kt.ap[:1]] + [[64, 16], [1, 32]]),
                pkn[:, :]), "prd")
            ctx.wait("vector", v3)
            v4 = ctx.op("vector", lambda e, b=b: (
                e.tensor_tensor(rf1[:, :], pqr[:, :], cs[:, b, :], OP.mult),
                e.tensor_tensor(rf2[:, :], bass.AP(
                    pqr[:, :].tensor, pqr[:, :].offset + 16,
                    [list(p) for p in pqr[:, :].ap[:1]] + [[32, 16], [-16, 2], [1, 16]]),
                    sn[:, b, :], OP.mult))[-1], "prd")
            ctx.wait("vector", v4)
            v5 = ctx.op("vector", lambda e, qt=qt: e.tensor_tensor(
                bass.AP(qt.tensor, qt.offset,
                        [list(p) for p in qt.ap[:1]] + [[64, 16], [1, 32]]),
                rf1[:, :], rf2[:, :], OP.add), "prd")
            v6 = ctx.op("vector", lambda e, qt=qt: e.tensor_copy(
                bass.AP(qt.tensor, qt.offset + 32,
                        [list(p) for p in qt.ap[:1]] + [[64, 16], [1, 32]]),
                pqn[:, :]), "prd")
            cs_free[b] = v4
            ctx.wait("scalar", mm_tok)
            a1 = ctx.op("scalar", lambda e, vs=vs: e.activation(
                bass.AP(vs.tensor, vs.offset,
                        [list(p) for p in vs.ap[:1]] + [[65, 8], [1, 64]]),
                pv_[:, 0, :], AF.Copy), "pra")
            a2 = ctx.op("scalar", lambda e, vs=vs: e.activation(
                bass.AP(vs.tensor, vs.offset + 8 * 65,
                        [list(p) for p in vs.ap[:1]] + [[65, 8], [1, 64]]),
                pv_[:, 1, :], AF.Copy), "pra")
            g1 = ctx.op("gpsimd", lambda e, vs=vs: e.memset(
                bass.AP(vs.tensor, vs.offset + 64,
                        [list(p) for p in vs.ap[:1]] + [[65, 16], [1, 1]]), 1.0), "gpk")
            ctx.wait("sync", a1, a2, g1)
            vstg_free[b] = ctx.dma("sync", d["vtm"][tt, :, :], vs[:, :], "vst%d" % b)
            dbg_tok = None
            if "dbg_ktm" in self.dbg:
                if tt == 0:
                    self.dram("dbg_ktm", [self.T, DM], BF16, kind="ExternalOutput")
                    self.dram("dbg_qtm", [self.T, DM], BF16, kind="ExternalOutput")
                ctx.wait("sync", v3, v6)
                ctx.dma("sync", self.d["dbg_ktm"][tt * 128:(tt + 1) * 128, :], kt[:, :], "dbgs")
                dbg_tok = ctx.dma("sync", self.d["dbg_qtm"][tt * 128:(tt + 1) * 128, :], qt[:, :], "dbgs")
            nk, nka = self.tile_norms(kt[:, :], v3, mk, tt, sqn, s16)
            nq, nqa = self.tile_norms(qt[:, :], v6, mq, tt, sqn, s16)
            ctx.wait("tensor", v3, v6, ctx.tok("prt"))
            pe = None
            for fc in range(8):
                pe = ctx.op("tensor", lambda e, kt=kt, fc=fc: e.transpose(
                    ptp[:, fc * 128:(fc + 1) * 128], kt[:, fc * 128:(fc + 1) * 128],
                    sb["c_ident"][:, :]), "pp2" if fc == 7 else None)
            ctx.wait("vector", pe, nk, nq)
            c4 = tt % 4
            if c4 == 0 and kq_stage_free is not None:
                ctx.wait("vector", kq_stage_free)
            v8 = ctx.op("vector", lambda e, c4=c4: e.tensor_copy(
                kstg[:, :, c4 * 128:(c4 + 1) * 128],
                ptp[:, :].rearrange("p (f t) -> p f t", f=8)), "prt")
            ctx.wait("tensor", v8)
            pe2 = None
            for fc in range(8):
                pe2 = ctx.op("tensor", lambda e, qt=qt, fc=fc: e.transpose(
                    ptp[:, fc * 128:(fc + 1) * 128], qt[:, fc * 128:(fc + 1) * 128],
                    sb["c_ident"][:, :]), "pp2" if fc == 7 else None)
            ctx.wait("vector", pe2)
            v9 = ctx.op("vector", lambda e, c4=c4: e.tensor_copy(
                qstg[:, :, c4 * 128:(c4 + 1) * 128],
                ptp[:, :].rearrange("p (f t) -> p f t", f=8)), "prt")
            ktm_free[b] = [v8, nka]
            qtm_free[b] = [v9, nqa]
            if c4 == 3:
                grp = tt // 4
                ctx.wait("sync", v9)
                last = None
                for fc in range(8):
                    for half in range(2):
                        h = 2 * fc + half
                        last = ctx.dma("sync", d["kfm"][h, :, grp * 512:(grp + 1) * 512],
                                       kstg[:, fc, :][half * 64:(half + 1) * 64, :], "kqs")
                        last = ctx.dma("sync", d["qfm"][h, :, grp * 512:(grp + 1) * 512],
                                       qstg[:, fc, :][half * 64:(half + 1) * 64, :], "kqs")
                kq_stage_free = last
        pbnd = self.psum("pbnd", [128, 1], F32, st)
        bt = self.bound(mq, mk, 1.0 / 8.0, [ctx.tok("nsd")], ptp, pbnd)
        ctx.wait("sync", ctx.tok("kqs"), ctx.tok("vst0"), ctx.tok("vst1"))
        return bt

    def s2_prep(self, st):
        ctx, sb, d, NT = self.ctx, self.sb, self.d, self.NT
        vfm = sb["vfm"]
        waq = self.sbuf("waq", [128, 8, DM], BF16, st)
        wak = self.sbuf("wak", [128, 8, DM], BF16, st)
        wav = self.sbuf("wav", [128, 8, DM], BF16, st)
        wts = []
        for (w, nm) in ((waq, "a_q_T"), (wak, "a_k_T"), (wav, "a_v_T")):
            wts.append(ctx.dma("gpsimd", w[:, :, :],
                               d[nm][:, :, :].rearrange("kc p f -> p kc f"), "wks"))
        ctx.wait("tensor", ctx.tok("wks"))
        ctx.wait("vector", ctx.tok("lnt"))
        ctx.wait("tensor", ctx.tok("lnt"))

        pq = self.psum("s2pq", [128, 2, 512], F32, st)
        pk = self.psum("s2pk", [128, 2, 512], F32, st)
        pv_ = self.psum("s2pv", [128, 2, 512], F32, st)
        ptp = self.psum("s2ptp", [128, 1024], BF16, st)
        ktm = self.sbuf("s2ktm", [128, 2, DM], BF16, st)
        qtm = self.sbuf("s2qtm", [128, 2, DM], BF16, st)
        vstg = self.sbuf("s2vstg", [128, 2, NH * 65], BF16, st)
        kstg = self.sbuf("s2kstg", [128, 8, 512], BF16, st)
        qstg = self.sbuf("s2qstg", [128, 8, 512], BF16, st)
        f1t = self.sbuf("s2f1", [128, 2, DM], F32, st)
        sqnt = self.sbuf("s2sqn", [128, 2, DM], BF16, st)
        s16t = self.sbuf("s2s16", [128, 2, NH], F32, st)
        rrt = self.sbuf("s2rr", [128, 2, 2, NH], F32, st)
        mq = self.sbuf("s2mq", [128, NT], F32, st)
        mk = self.sbuf("s2mk", [128, NT], F32, st)
        self.sbuf("bnd_t1", [128, 8], F32, st)
        self.sbuf("bnd_t1b", [128, 4], BF16, st)
        self.sbuf("bnd_t2", [128, 8], F32, st)

        qkb = self.sbuf("s2qkb", [128, 2, 2, 2, 512], F32, st)
        ktm_free = [None, None]
        qtm_free = [None, None]
        vstg_free = [None, None]
        kq_state = [None]
        ps_free = None
        tile_meta = {}

        def emit_tp(t):
            bb = t % 2
            ktx, qtx = ktm[:, bb], qtm[:, bb]
            chain_done, nk, nq, nka, nqa = tile_meta.pop(t)
            ctx.wait("tensor", chain_done, ctx.tok("prt"))
            for fc in range(8):
                ctx.op("tensor", lambda e, ktx=ktx, fc=fc: e.transpose(
                    ptp[:, fc * 128:(fc + 1) * 128], ktx[:, fc * 128:(fc + 1) * 128],
                    sb["c_ident"][:, :]), "pp2" if fc == 7 else None)
            ctx.wait("vector", ctx.tok("pp2"), nk, nq)
            c4 = t % 4
            if c4 == 0 and kq_state[0] is not None:
                ctx.wait("vector", kq_state[0])
            v8 = ctx.op("vector", lambda e, c4=c4: e.tensor_copy(
                kstg[:, :, c4 * 128:(c4 + 1) * 128],
                ptp[:, :].rearrange("p (f t) -> p f t", f=8)), "prt")
            ctx.wait("tensor", v8)
            for fc in range(8):
                ctx.op("tensor", lambda e, qtx=qtx, fc=fc: e.transpose(
                    ptp[:, fc * 128:(fc + 1) * 128], qtx[:, fc * 128:(fc + 1) * 128],
                    sb["c_ident"][:, :]), "pp2" if fc == 7 else None)
            ctx.wait("vector", ctx.tok("pp2"))
            v9 = ctx.op("vector", lambda e, c4=c4: e.tensor_copy(
                qstg[:, :, c4 * 128:(c4 + 1) * 128],
                ptp[:, :].rearrange("p (f t) -> p f t", f=8)), "prt")
            ktm_free[bb] = [v8, nka]
            qtm_free[bb] = [v9, nqa]
            if c4 == 3:
                grp = t // 4
                ctx.wait("sync", v9)
                last = None
                for fc in range(8):
                    for half in range(2):
                        h = 2 * fc + half
                        last = ctx.dma("sync", d["kfm"][h, :, grp * 512:(grp + 1) * 512],
                                       kstg[:, fc, :][half * 64:(half + 1) * 64, :], "kqs")
                        last = ctx.dma("sync", d["qfm"][h, :, grp * 512:(grp + 1) * 512],
                                       qstg[:, fc, :][half * 64:(half + 1) * 64, :], "kqs")
                kq_state[0] = last

        for tt in range(NT):
            b = tt % 2
            r0 = tt * 128
            if ps_free is not None:
                ctx.wait("tensor", *ps_free)
            mm = None
            for kc in range(8):
                for (ps, w) in ((pq, waq), (pk, wak), (pv_, wav)):
                    for fh in range(2):
                        mm = ctx.op("tensor", lambda e, kc=kc, r0=r0, ps=ps, w=w, fh=fh:
                                    e.matmul(ps[:, fh, :], vfm[:, kc, r0:r0 + 128],
                                             w[:, kc, fh * 512:(fh + 1) * 512],
                                             start=(kc == 0), stop=(kc == 7)),
                                    "pmm" if (kc == 7 and ps is pv_ and fh == 1) else None)
            mm_tok = ctx.tok("pmm")
            kt, qt, vs = ktm[:, b], qtm[:, b], vstg[:, b]
            ctx.wait("vector", mm_tok, ctx.tok("prd"))
            evq = ctx.op("vector", lambda e, b=b: e.tensor_copy(
                qkb[:, b, 0, :, :], pq[:, :, :]), "prd")
            ctx.wait("vector", evq)
            evk = ctx.op("vector", lambda e, b=b: e.tensor_copy(
                qkb[:, b, 1, :, :], pk[:, :, :]), "prd")
            ctx.wait("scalar", mm_tok)
            if vstg_free[b] is not None:
                ctx.wait("scalar", vstg_free[b])
                ctx.wait("gpsimd", vstg_free[b])
            a4 = ctx.op("scalar", lambda e, vs=vs: e.activation(
                bass.AP(vs.tensor, vs.offset,
                        [list(p) for p in vs.ap[:1]] + [[65, 8], [1, 64]]),
                pv_[:, 0, :], AF.Copy), "pra")
            a5 = ctx.op("scalar", lambda e, vs=vs: e.activation(
                bass.AP(vs.tensor, vs.offset + 8 * 65,
                        [list(p) for p in vs.ap[:1]] + [[65, 8], [1, 64]]),
                pv_[:, 1, :], AF.Copy), "pra")
            g1 = ctx.op("gpsimd", lambda e, vs=vs: e.memset(
                bass.AP(vs.tensor, vs.offset + 64,
                        [list(p) for p in vs.ap[:1]] + [[65, 16], [1, 1]]), 1.0), "gpk")
            ctx.wait("sync", a4, a5, g1)
            vstg_free[b] = ctx.dma("sync", d["vtm"][tt, :, :], vs[:, :], "vst%d" % b)
            for (which, dst, gidx, extra) in ((0, qt, 4, True),
                                              (1, kt, 5, False)):
                ctx.wait("scalar", evk)
                a1 = ctx.op("scalar", lambda e, b=b, which=which: e.activation(
                    sqnt[:, b, 0:512], qkb[:, b, which, 0, :], AF.Square), "pra")
                a1b = ctx.op("scalar", lambda e, b=b, which=which: e.activation(
                    sqnt[:, b, 512:1024], qkb[:, b, which, 1, :], AF.Square), "pra")
                ctx.wait("vector", a1b)
                v1 = ctx.op("vector", lambda e, b=b: e.tensor_reduce(
                    s16t[:, b, :], sqnt[:, b, :].rearrange("p (h x) -> p h x", h=NH),
                    AX.X, OP.add), "prd")
                ctx.wait("vector", v1)
                v2 = ctx.op("vector", lambda e, b=b: e.tensor_scalar(
                    s16t[:, b, :], s16t[:, b, :], 1.0 / HD, 1e-16, OP.mult, OP.max), "prd")
                ctx.wait("scalar", v2)
                a2 = ctx.op("scalar", lambda e, b=b: e.activation(
                    f1t[:, b, 0:NH], s16t[:, b, :], AF.Ln), "pra")
                ctx.wait("scalar", a2)
                bias = sb["c_mln8"][:, 0:1] if extra else 0.0
                a3 = ctx.op("scalar", lambda e, b=b, which=which, bias=bias: e.activation(
                    rrt[:, b, which, :], f1t[:, b, 0:NH], AF.Exp, scale=-0.5, bias=bias), "pra")
                ctx.wait("vector", a3)
                if (qtm_free[b] if which == 0 else ktm_free[b]) is not None:
                    ctx.wait("vector", *(qtm_free[b] if which == 0 else ktm_free[b]))
                vz = None
                for fh in range(2):
                    src = qkb[:, b, which, fh, :]
                    rrs = rrt[:, b, which, fh * 8:(fh + 1) * 8]
                    rb = bass.AP(rrs.tensor, rrs.offset,
                                 [rrs.ap[0], [1, 8], [0, HD]])
                    f1s = f1t[:, b, fh * 512:(fh + 1) * 512]
                    sb3 = bass.AP(src.tensor, src.offset,
                                  [src.ap[0], [HD, 8], [1, HD]])
                    db3 = bass.AP(f1s.tensor, f1s.offset,
                                  [f1s.ap[0], [HD, 8], [1, HD]])
                    vz = ctx.op("vector", lambda e, db3=db3, sb3=sb3, rb=rb:
                                e.tensor_tensor(db3, sb3, rb, OP.mult),
                                "prd" if fh == 1 else None)
                ctx.wait("vector", vz)
                vg = ctx.op("vector", lambda e, b=b, dst=dst, gidx=gidx: e.tensor_tensor(
                    dst[:, :], f1t[:, b, :], self.gamma(gidx), OP.mult), "prd")
            nk, nka = self.tile_norms(kt[:, :], ctx.tok("prd"), mk, tt,
                                      sqnt[:, b], s16t[:, b])
            nq, nqa = self.tile_norms(qt[:, :], ctx.tok("prd"), mq, tt,
                                      sqnt[:, b], s16t[:, b])
            tile_meta[tt] = (ctx.tok("prd"), nk, nq, nka, nqa)
            ps_free = [evq, evk, a4, a5]
            if tt > 0:
                emit_tp(tt - 1)
        emit_tp(NT - 1)
        pbnd = self.psum("pbnd", [128, 1], F32, st)
        bt = self.bound(mq, mk, 1.0, [ctx.tok("nsd"), ctx.tok("prt")], ptp, pbnd)
        ctx.wait("sync", ctx.tok("kqs"), ctx.tok("vst0"), ctx.tok("vst1"))
        return bt

    def attention(self, causal, scale, cbias_dep, st):
        ctx, sb, d, NT, TC = self.ctx, self.sb, self.d, self.NT, self.TC
        cb = sb["cbias"]
        ctx.wait("scalar", cbias_dep)
        G = 2
        NG = 3
        kh = self.sbuf("at_kh", [128, 2, self.T], BF16, st)
        qh = self.sbuf("at_qh", [128, 2, self.T], BF16, st)
        vh = self.sbuf("at_vh", [128, NT, NH * 65], BF16, st)
        pt = self.sbuf("at_pt", [128, NG, G * 512], BF16, st)
        rl = self.sbuf("at_rl", [1, 2, 512], F32, st)
        rlb = self.sbuf("at_rlb", [1, 2, 512], BF16, st)
        rbc = self.sbuf("at_rbc", [64, 2, 512], F32, st)
        og = self.sbuf("at_og", [64, 2, 512], BF16, st)
        pss = self.psum("at_ps", [128, NG, G * 512], F32, st)
        po = self.psum("at_po", [128, 2, 512], F32, st)
        lvv = ctx.dma("sync", vh[:, :, :], d["vtm"][:, :, :].rearrange(
            "kt p c -> p kt c"), "akv")
        kv_free = [None, None]
        po_free = [None, None]
        og_free = [None, None]
        exp_free = [None] * NG
        av_free = [None] * NG

        kz = ctx.op("scalar", lambda e: (
            e.memzero(kh[:, :, :]), e.memzero(qh[:, :, :]))[-1], "akz")

        def issue_kq(h):
            hb = h % 2
            if kv_free[hb] is not None:
                ctx.wait("sync", *kv_free[hb])
            ctx.wait("sync", kz)
            ctx.dma("sync", kh[0:64, hb, :], d["kfm"][h, :, :], "akl%d" % hb)
            return ctx.dma("sync", qh[0:64, hb, :], d["qfm"][h, :, :], "akl%d" % hb)

        kq_toks = {0: issue_kq(0)}
        for h in range(NH):
            hb = h % 2
            if h + 1 < NH:
                kq_toks[h + 1] = issue_kq(h + 1)
            lq = kq_toks[h]
            ctx.wait("tensor", lq, lvv)
            for qc in range(TC):
                ob = qc % 2
                kt_hi = min(NT, (qc + 1) * 4) if causal else NT
                if po_free[ob] is not None:
                    ctx.wait("tensor", po_free[ob])
                groups = [(g0, min(G, kt_hi - g0)) for g0 in range(0, kt_hi, G)]

                def emit_S(gi):
                    ktg, gn = groups[gi]
                    g = gi % NG
                    if exp_free[g] is not None:
                        ctx.wait("tensor", exp_free[g])
                    sv = None
                    for i in range(gn):
                        kt = ktg + i
                        sv = ctx.op("tensor", lambda e, kt=kt, i=i, qc=qc, hb=hb, g=g:
                                    e.matmul(pss[:, g, i * 512:(i + 1) * 512],
                                             kh[:, hb, kt * 128:(kt + 1) * 128],
                                             qh[:, hb, qc * 512:(qc + 1) * 512],
                                             start=True, stop=True),
                                    "asm" if i == gn - 1 else None)
                    return sv

                def emit_exp(gi, sv):
                    ktg, gn = groups[gi]
                    g = gi % NG
                    ctx.wait("scalar", sv)
                    if av_free[g] is not None:
                        ctx.wait("scalar", av_free[g])
                    ex = ctx.op("scalar", lambda e, gn=gn, g=g: e.activation(
                        pt[:, g, 0:gn * 512], pss[:, g, 0:gn * 512], AF.Exp,
                        bias=cb[:, 0:1], scale=scale), "aex")
                    exp_free[g] = ex
                    mdep = ex
                    if causal:
                        for i in range(gn):
                            kt = ktg + i
                            j = kt - qc * 4
                            if 0 <= j <= 3:
                                ctx.wait("vector", ex)
                                if j > 0:
                                    mdep = ctx.op("vector", lambda e, i=i, j=j, g=g:
                                                  e.memset(pt[:, g, i * 512:i * 512 + j * 128], 0.0),
                                                  "amk")
                                mdep = ctx.op("vector", lambda e, i=i, j=j, g=g:
                                              e.tensor_tensor(
                                                  pt[:, g, i * 512 + j * 128:i * 512 + (j + 1) * 128],
                                                  pt[:, g, i * 512 + j * 128:i * 512 + (j + 1) * 128],
                                                  sb["c_mask"][:, :], OP.mult), "amk")
                    return mdep

                def emit_AV(gi, ex, mdep):
                    ktg, gn = groups[gi]
                    g = gi % NG
                    items = list(range(gn))
                    if causal:
                        items.sort(key=lambda i: (ktg + i != 0,
                                                  0 <= (ktg + i) - qc * 4 <= 3))
                    av = None
                    ndone = 0
                    for i in items:
                        kt = ktg + i
                        diag = causal and 0 <= kt - qc * 4 <= 3
                        ctx.wait("tensor", mdep if (diag or kt == 0) else ex)
                        ndone += 1
                        av = ctx.op("tensor", lambda e, kt=kt, i=i, ob=ob, h=h, g=g,
                                    first=(kt == 0), last=(kt == kt_hi - 1):
                                    e.matmul(po[0:65, ob, :],
                                             vh[:, kt, h * 65:(h + 1) * 65],
                                             pt[:, g, i * 512:(i + 1) * 512],
                                             start=first, stop=last),
                                    "aav" if ndone == gn else None)
                    av_free[g] = av
                    return av

                pend = []
                for gi in range(len(groups)):
                    sv = emit_S(gi)
                    md = emit_exp(gi, sv)
                    ex_tok = exp_free[gi % NG]
                    if len(pend) >= 2:
                        e0 = pend.pop(0)
                        emit_AV(*e0)
                    pend.append((gi, ex_tok, md))
                for e0 in pend:
                    emit_AV(*e0)
                ctx.wait("vector", av_free[(len(groups) - 1) % NG], ctx.tok("aav"))
                if og_free[ob] is not None:
                    ctx.wait("vector", og_free[ob])
                v1 = ctx.op("vector", lambda e, ob=ob: e.tensor_copy(
                    og[:, ob, :], po[0:64, ob, :]), "aed")
                ctx.wait("vector", v1)
                v2 = ctx.op("vector", lambda e, ob=ob: e.tensor_copy(
                    rl[:, ob, :], po[64:65, ob, :]), "aed")
                po_free[ob] = v2
                ctx.wait("sync", v2)
                ctx.dma("sync", d["lsum"][h, qc * 512:(qc + 1) * 512],
                        rl[0:1, ob, :], "aos%d" % ob)
                og_free[ob] = ctx.dma("sync", d["ofm"][h // 2, qc,
                                                       (h % 2) * 64:(h % 2) * 64 + 64, :],
                                      og[:, ob, :], "aos%d" % ob)
            kv_free[hb] = [ctx.tok("aav"), ctx.tok("asm")]
        ctx.wait("sync", ctx.tok("aos0"), ctx.tok("aos1"))

    def oproj(self, w_name, src, dst, add_bo, st):
        ctx, sb, d, NT = self.ctx, self.sb, self.d, self.NT
        wo = self.sbuf("opwo", [128, 8, DM], BF16, st)
        ctx.dma("gpsimd", wo[:, :, :],
                d[w_name][:, :, :].rearrange("dc p f -> p dc f"), "wks")
        ctx.wait("tensor", ctx.tok("wks"))
        osb = self.sbuf("oposb", [128, 2, 8, 512], BF16, st)
        osn = self.sbuf("oposn", [128, 2, 8, 128], BF16, st)
        res = self.sbuf("opres", [128, 4, DM], F32, st)
        out = self.sbuf("opout", [128, 4, DM], F32, st)
        lsb = self.sbuf("oplsb", [16, self.T], F32, st)
        lrb = self.sbuf("oplrb", [2, 8, self.T], BF16, st)
        pp = self.psum("oppp", [128, 2, 2, 512], F32, st)
        prb = self.psum("oprb", [128, 2, 1024], F32, st)
        lt = ctx.dma("sync", lsb[:, :], d["lsum"][:, :], "opll")
        ctx.wait("vector", lt)
        vr = ctx.op("vector", lambda e: e.reciprocal(lsb[:, :], lsb[:, :]), "opd")
        ctx.wait("sync", vr)
        st1 = ctx.dma("sync", d["lsum"][:, :], lsb[:, :], "opll")
        ctx.wait("gpsimd", st1)
        lt2 = ctx.dma("gpsimd", lrb[:, :, :],
                      bass.AP(d["lsum"].tensor, 0,
                              [[self.T, 2], [2 * self.T, 8], [1, self.T]]), "opll2")
        ctx.wait("tensor", lt2)
        ot_toks = {}
        rt_toks = {}
        vn_toks = {}
        mm_toks = {}
        vv_toks = {}
        so_toks = {}

        def emit_bc(t):
            b = t % 2
            if t >= 2:
                ctx.wait("tensor", vn_toks[t - 2])
            for fc in range(8):
                ctx.op("tensor", lambda e, fc=fc, b=b, t=t: e.matmul(
                    prb[:, b, fc * 128:(fc + 1) * 128], sb["c_half2"][:, :],
                    lrb[:, fc, t * 128:(t + 1) * 128],
                    start=True, stop=True), "opb" if fc == 7 else None)
            return ctx.tok("opb")

        bc_toks = {0: emit_bc(0)}
        for tt in range(NT):
            b = tt % 2
            b4 = tt % 4
            gb = (tt // 4) % 2
            if tt % 4 == 0:
                if tt >= 8:
                    ctx.wait("sync", vn_toks[tt - 5])
                ot = None
                for fc in range(8):
                    ot = ctx.dma("sync", osb[:, gb, fc, :], d["ofm"][fc, tt // 4, :, :],
                                 "opl%d" % gb)
                ot_toks[tt // 4] = ot
            if tt >= 4:
                ctx.wait("scalar", vv_toks[tt - 4])
            rt_toks[tt] = ctx.dma("scalar", res[:, b4, :],
                                  src[tt * 128:(tt + 1) * 128, :], "opr")
            ctx.wait("vector", bc_toks[tt], ot_toks[tt // 4])
            if tt >= 2:
                ctx.wait("vector", mm_toks[tt - 2])
            vn = None
            for fc in range(8):
                vn = ctx.op("vector", lambda e, fc=fc, b=b, gb=gb, tt=tt: e.tensor_tensor(
                    osn[:, b, fc, :], osb[:, gb, fc, (tt % 4) * 128:(tt % 4 + 1) * 128],
                    prb[:, b, fc * 128:(fc + 1) * 128], OP.mult),
                    "opn" if fc == 7 else None)
            vn_toks[tt] = vn
            if tt + 1 < NT:
                bc_toks[tt + 1] = emit_bc(tt + 1)
            ctx.wait("tensor", vn)
            if tt >= 2:
                ctx.wait("tensor", vv_toks[tt - 2])
            mm = None
            for dc in range(8):
                for fh in range(2):
                    mm = ctx.op("tensor", lambda e, dc=dc, fh=fh, b=b:
                                e.matmul(pp[:, b, fh, :],
                                         osn[:, b, dc, :],
                                         wo[:, dc, fh * 512:(fh + 1) * 512],
                                         start=(dc == 0), stop=(dc == 7)),
                                "pmm" if (dc == 7 and fh == 1) else None)
            mm_toks[tt] = mm
            ctx.wait("vector", mm, rt_toks[tt])
            if tt >= 4:
                ctx.wait("vector", so_toks[tt - 4])
            vv = None
            for fh in range(2):
                vv = ctx.op("vector", lambda e, b=b, b4=b4, fh=fh: e.scalar_tensor_tensor(
                    out[:, b4, fh * 512:(fh + 1) * 512], pp[:, b, fh, :], 1.0,
                    res[:, b4, fh * 512:(fh + 1) * 512], OP.mult, OP.add),
                    "opd" if fh == 1 else None)
            if add_bo:
                ctx.wait("vector", vv)
                vv = ctx.op("vector", lambda e, b4=b4: e.tensor_tensor(
                    out[:, b4, :], out[:, b4, :], sb["c_bo"][:, :], OP.add), "opd")
            vv_toks[tt] = vv
            ctx.wait("sync", vv)
            so_toks[tt] = ctx.dma("sync", dst[tt * 128:(tt + 1) * 128, :],
                                  out[:, b4, :], "opo%d" % b)
        ctx.wait("sync", ctx.tok("opo0"), ctx.tok("opo1"))

    def moe_router_setup(self, st):
        ctx, sb, d, NT = self.ctx, self.sb, self.d, self.NT
        w8 = self.sbuf("moe_w", [128, NT, 8], F32, st)
        rt_st = self.estack()
        pr = self.psum("moe_pr", [128, 2, 8], F32, rt_st)
        ptpr = self.psum("moe_ptpr", [128, 2, 1024], BF16, rt_st)
        rf = self.sbuf("moe_rf", [128, 4, 8], F32, rt_st)
        self._zb = None
        rwf = self.sbuf("moe_rwf", [128, 8, 8], F32, rt_st)
        rwhi = self.sbuf("moe_rwhi", [128, 8, 8], BF16, rt_st)
        rwlo = self.sbuf("moe_rwlo", [128, 8, 8], BF16, rt_st)
        lot = self.sbuf("moe_lot", [128, 2, DM], BF16, rt_st)
        lofm = self.sbuf("moe_lofm", [128, 2, 8, 128], BF16, rt_st)
        hfm = self.sbuf("moe_hfm", [128, 2, 8, 128], BF16, rt_st)
        lw = ctx.dma("sync", rwf[:, :, :],
                     d["c_router"][:, :, :].rearrange("kc p e -> p kc e"), "rtl")
        ctx.wait("vector", lw)
        vh = ctx.op("vector", lambda e: e.tensor_copy(rwhi[:, :, :], rwf[:, :, :]), "rtd")
        ctx.wait("vector", vh)
        vl = ctx.op("vector", lambda e: e.tensor_tensor(
            rwlo[:, :, :], rwf[:, :, :], rwhi[:, :, :], OP.subtract), "rtd")
        g1 = ctx.op("gpsimd", lambda e: e.memset(w8[:, :, 7:8], 1.0), "gpk")
        zb = self.sbuf("moe_zb", [128, NT, 8], F32, rt_st)
        pp = self.sbuf("moe_pp", [128, NT, 8], F32, rt_st)
        ms = self.sbuf("moe_ms", [128, NT, 8], F32, rt_st)
        sc = self.sbuf("moe_sc", [128, 4, NT], F32, rt_st)
        self._rt = dict(w8=w8, rt_st=rt_st, pr=pr, ptpr=ptpr, rf=rf, rwhi=rwhi,
                        rwlo=rwlo, lot=lot, lofm=lofm, hfm=hfm, wdep=(vl, g1),
                        zb=zb, pp=pp, ms=ms, sc=sc)
        return self.moe_router_tile

    def moe_router_tile(self, tt, f1, g, v6, v7):
        ctx, sb = self.ctx, self.sb
        R = self._rt
        b = tt % 2
        pr, ptpr, rf, w8 = R["pr"], R["ptpr"], R["rf"], R["w8"]
        lot, lofm, hfm = R["lot"][:, b], R["lofm"][:, b], R["hfm"][:, b]
        ctx.wait("sync", v6)
        xst = ctx.dma("sync", self.d["xn_tm"][tt * 128:(tt + 1) * 128, :],
                      g[:, :], "xts")
        ctx.wait("vector", v6, ctx.tok("rmm"))
        vlo = ctx.op("vector", lambda e: e.tensor_tensor(
            lot[:, :], f1[:, :], g[:, :], OP.subtract), "rtd")
        ctx.wait("tensor", vlo, v6, ctx.tok("rtd"))
        pe = None
        for fc in range(8):
            ctx.op("tensor", lambda e, fc=fc: e.transpose(
                ptpr[:, 0, fc * 128:(fc + 1) * 128],
                g[:, fc * 128:(fc + 1) * 128], sb["c_ident"][:, :]), None)
            pe = ctx.op("tensor", lambda e, fc=fc: e.transpose(
                ptpr[:, 1, fc * 128:(fc + 1) * 128], lot[:, fc * 128:(fc + 1) * 128],
                sb["c_ident"][:, :]), "rtp" if fc == 7 else None)
        ctx.wait("vector", pe)
        vev = ctx.op("vector", lambda e: e.tensor_copy(
            hfm[:, :, :], ptpr[:, 0, :].rearrange("p (f t) -> p f t", f=8)), "rtd")
        vev = ctx.op("vector", lambda e: e.tensor_copy(
            lofm[:, :, :], ptpr[:, 1, :].rearrange("p (f t) -> p f t", f=8)), "rtd")
        ctx.wait("tensor", vev, *R["wdep"])
        mm = None
        n = 0
        for kc in range(8):
            hi = hfm[:, kc, :]
            lo = lofm[:, kc, :]
            for (sa, wb_) in ((hi, R["rwhi"][:, kc, :]), (hi, R["rwlo"][:, kc, :]),
                              (lo, R["rwhi"][:, kc, :]), (lo, R["rwlo"][:, kc, :])):
                n += 1
                mm = ctx.op("tensor", lambda e, sa=sa, wb_=wb_, b=b, first=(n == 1),
                            last=(n == 32): e.matmul(pr[:, b, :], sa, wb_,
                                                     start=first, stop=last),
                            "rmm" if n == 32 else None)
        ctx.wait("vector", mm)
        vz = ctx.op("vector", lambda e, b=b, tt=tt: e.scalar_tensor_tensor(
            R["zb"][:, tt, :], pr[:, b, :], 1.0, sb["c_rbias"][:, :],
            OP.mult, OP.add), "rtd")
        return (vz, xst)

    def moe_topk_batch(self):
        ctx, sb, NT = self.ctx, self.sb, self.NT
        R = self._rt
        zb, w8 = R["zb"], R["w8"]
        p = R["pp"]
        ms = R["ms"]
        m1, m2, den, tmp = (R["sc"][:, i, :] for i in range(4))
        ctx.wait("scalar", ctx.tok("rtd"))
        a1 = ctx.op("scalar", lambda e: e.activation(
            p[:, :, :], zb[:, :, :], AF.Exp, scale=-1.0), "rta")
        ctx.wait("vector", a1)
        v1 = ctx.op("vector", lambda e: e.tensor_scalar(
            p[:, :, :], p[:, :, :], 1.0, None, OP.add), "rtd")
        ctx.wait("vector", v1)
        v2 = ctx.op("vector", lambda e: e.reciprocal(p[:, :, :], p[:, :, :]), "rtd")
        ctx.wait("vector", v2)
        v3 = ctx.op("vector", lambda e: e.tensor_reduce(
            m1, p[:, :, 0:7], AX.X, OP.max), "rtd")
        ctx.wait("vector", v3)
        for j in range(7):
            ctx.op("vector", lambda e, j=j: e.tensor_tensor(
                ms[:, :, j], p[:, :, j], m1, OP.is_lt), "rtd")
        ctx.wait("vector", ctx.tok("rtd"))
        for j in range(7):
            ctx.op("vector", lambda e, j=j: e.tensor_tensor(
                ms[:, :, j], p[:, :, j], ms[:, :, j], OP.mult), "rtd")
        ctx.wait("vector", ctx.tok("rtd"))
        v4 = ctx.op("vector", lambda e: e.tensor_reduce(
            m2, ms[:, :, 0:7], AX.X, OP.max), "rtd")
        ctx.wait("vector", v4)
        v5 = ctx.op("vector", lambda e: e.tensor_tensor(den, m1, m2, OP.add), "rtd")
        ctx.wait("vector", v5)
        v6 = ctx.op("vector", lambda e: e.reciprocal(den, den), "rtd")
        ctx.wait("vector", v6)
        for j in range(7):
            ctx.op("vector", lambda e, j=j: e.tensor_tensor(
                ms[:, :, j], p[:, :, j], m2, OP.is_ge), "rtd")
        ctx.wait("vector", ctx.tok("rtd"))
        for j in range(7):
            ctx.op("vector", lambda e, j=j: e.tensor_tensor(
                tmp, p[:, :, j], den, OP.mult), "rtd")
            ctx.wait("vector", ctx.tok("rtd"))
            ctx.op("vector", lambda e, j=j: e.tensor_tensor(
                w8[:, :, j], tmp, ms[:, :, j], OP.mult), "rtd")
            ctx.wait("vector", ctx.tok("rtd"))
        return ctx.tok("rtd")

    def moe_routing_meta(self, st):
        ctx, sb, NT, NCH = self.ctx, self.sb, self.NT, self.NCH
        I32 = mybir.dt.int32
        R = self._rt
        rt_st = R["rt_st"]
        ms, w8 = R["ms"], R["w8"]

        def A(base, dims, off=0):
            return bass.AP(base.tensor, base.offset + off, dims)
        msb, w8b = ms[:, :, :], w8[:, :, :]
        scn_a = self.sbuf("rm_scna", [128, NT, 7], F32, rt_st)
        scn_b = self.sbuf("rm_scnb", [128, NT, 7], F32, rt_st)
        cnt8 = self.sbuf("rm_cnt", [128, 8], F32, rt_st)
        cntb8 = self.sbuf("rm_cntb", [128, 8], BF16, rt_st)
        pre8 = self.sbuf("rm_pre", [128, 8], F32, rt_st)
        tot8 = self.sbuf("rm_tot", [128, 8], F32, rt_st)
        ns8 = self.sbuf("rm_ns", [128, 8], F32, rt_st)
        segi8 = self.sbuf("rm_segi", [128, 8], F32, rt_st)
        sege8 = self.sbuf("rm_sege", [128, 8], F32, rt_st)
        base8 = self.sbuf("rm_base", [128, 8], F32, rt_st)
        t37 = self.sbuf("rm_t37", [128, 7, NCH], F32, rt_st)
        t77 = self.sbuf("rm_t77", [128, NT, 7, 7], F32, rt_st)
        fst = self.sbuf("rm_fst", [128, NT, 7], F32, rt_st)
        Sdt = self.sbuf("rm_sd", [128, NT, 7], F32, rt_st)
        sAf = self.sbuf("rm_saf", [128, NT], F32, rt_st)
        sBf = self.sbuf("rm_sbf", [128, NT], F32, rt_st)
        wAf = self.sbuf("rm_waf", [128, NT], F32, rt_st)
        wBf = self.sbuf("rm_wbf", [128, NT], F32, rt_st)
        teid = self.sbuf("rm_teid", [128, NCH, 7], F32, rt_st)
        eidv = self.sbuf("rm_eid", [128, NCH], F32, rt_st)
        tgu = self.sbuf("rm_tgu", [128, NCH, 16], F32, rt_st)
        pmt = self.psum("rm_pmt", [128, 2, 8], F32, rt_st)
        sAi = self.sbuf("rm_sai", [128, NT], I32, st, side="right")
        sBi = self.sbuf("rm_sbi", [128, NT], I32, st, side="right")
        idx_gu = self.sbuf("rm_igu", [128, NCH, 16], I32, st, side="right")
        idx_dn = self.sbuf("rm_idn", [128, NCH], I32, st, side="right")
        self._sp = dict(sAi=sAi, sBi=sBi, idx_gu=idx_gu, idx_dn=idx_dn)

        def v(fn):
            t = ctx.op("vector", fn, "rtd")
            ctx.wait("vector", t)
            return t
        pm = msb.ap[0]
        sel3 = A(msb, [pm, [8, NT], [1, 7]])
        selT = A(msb, [pm, [1, 7], [8, NT]])
        sel4 = A(msb, [pm, [8, NT], [0, 7], [1, 7]])
        w83 = A(w8b, [w8b.ap[0], [8, NT], [1, 7]])
        lt7_b, le7_b = self.rmeta("lt7", 49), self.rmeta("le7", 49)
        thr_b, chk_b = self.rmeta("thr", NCH), self.rmeta("chunk", NCH)
        ipp_b, pid_b = self.rmeta("ipp", 16), self.rmeta("pid", 1)

        v(lambda e: e.tensor_reduce(cnt8[:, 0:7], selT, AX.X, OP.add))
        v(lambda e: e.tensor_copy(cntb8[:, 0:7], cnt8[:, 0:7]))
        v(lambda e: e.tensor_copy(scn_a[:, :, :], sel3))
        src, dst = scn_a, scn_b
        k = 1
        while k < NT:
            v(lambda e, s=src, d2=dst, k=k: e.tensor_copy(
                d2[:, 0:k, :], s[:, 0:k, :]))
            v(lambda e, s=src, d2=dst, k=k: e.tensor_tensor(
                d2[:, k:NT, :], s[:, k:NT, :], s[:, 0:NT - k, :], OP.add))
            src, dst = dst, src
            k *= 2
        scn_e, tmps = src, dst
        v(lambda e, s=scn_e: e.tensor_tensor(
            s[:, :, :], s[:, :, :], sel3, OP.subtract))
        ctx.wait("tensor", ctx.tok("rtd"), ctx.tok("gpk"))
        ctx.op("tensor", lambda e: e.matmul(
            pmt[:, 0, 0:7], sb["c_mask"][:, :], cntb8[:, 0:7],
            start=True, stop=True), None)
        mm2 = ctx.op("tensor", lambda e: e.matmul(
            pmt[:, 1, 0:7], sb["c_ones128"][:, :], cntb8[:, 0:7],
            start=True, stop=True), "rmm")
        ctx.wait("vector", mm2)
        v(lambda e: e.tensor_tensor(
            pre8[:, 0:7], pmt[:, 0, 0:7], cnt8[:, 0:7], OP.subtract))
        v(lambda e: e.tensor_copy(tot8[:, 0:7], pmt[:, 1, 0:7]))
        v(lambda e: e.tensor_tensor(
            t37[:, :, :], A(tot8[:, :], [tot8[:, :].ap[0], [1, 7], [0, NCH]]),
            A(thr_b, [thr_b.ap[0], [0, 7], [1, NCH]]), OP.is_gt))
        v(lambda e: e.tensor_reduce(ns8[:, 0:7], t37[:, :, :], AX.X, OP.add))
        v(lambda e: e.tensor_tensor(
            t37[:, :, 0:7], A(ns8[:, :], [ns8[:, :].ap[0], [0, 7], [1, 7]]),
            A(le7_b, [le7_b.ap[0], [7, 7], [1, 7]]), OP.mult))
        v(lambda e: e.tensor_reduce(segi8[:, 0:7], t37[:, :, 0:7], AX.X, OP.add))
        v(lambda e: e.tensor_tensor(
            sege8[:, 0:7], segi8[:, 0:7], ns8[:, 0:7], OP.subtract))
        v(lambda e: e.scalar_tensor_tensor(
            base8[:, 0:7], sege8[:, 0:7], 512.0, pre8[:, 0:7], OP.mult, OP.add))
        v(lambda e: e.tensor_tensor(
            Sdt[:, :, :], scn_e[:, :, :],
            A(base8[:, :], [base8[:, :].ap[0], [0, NT], [1, 7]]), OP.add))
        v(lambda e: e.tensor_tensor(
            t77[:, :, :, :], sel4,
            A(lt7_b, [lt7_b.ap[0], [0, NT], [7, 7], [1, 7]]), OP.mult))
        v(lambda e: e.tensor_reduce(fst[:, :, :], t77[:, :, :, :], AX.X, OP.add))
        v(lambda e: e.tensor_scalar(
            fst[:, :, :], fst[:, :, :], 0.0, None, OP.is_equal))
        v(lambda e: e.tensor_tensor(fst[:, :, :], fst[:, :, :], sel3, OP.mult))
        v(lambda e: e.tensor_tensor(
            tmps[:, :, :], Sdt[:, :, :], fst[:, :, :], OP.mult))
        v(lambda e: e.tensor_reduce(sAf[:, :], tmps[:, :, :], AX.X, OP.add))
        v(lambda e: e.tensor_tensor(tmps[:, :, :], Sdt[:, :, :], sel3, OP.mult))
        v(lambda e: e.tensor_reduce(sBf[:, :], tmps[:, :, :], AX.X, OP.add))
        v(lambda e: e.tensor_tensor(sBf[:, :], sBf[:, :], sAf[:, :], OP.subtract))
        v(lambda e: e.tensor_tensor(tmps[:, :, :], w83, fst[:, :, :], OP.mult))
        v(lambda e: e.tensor_reduce(wAf[:, :], tmps[:, :, :], AX.X, OP.add))
        v(lambda e: e.tensor_reduce(wBf[:, :], w83, AX.X, OP.add))
        v(lambda e: e.tensor_tensor(wBf[:, :], wBf[:, :], wAf[:, :], OP.subtract))
        v(lambda e: e.tensor_scalar(
            sAf[:, :], sAf[:, :], float(self.NSLOT - 1), None, OP.min))
        v(lambda e: e.tensor_scalar(
            sBf[:, :], sBf[:, :], float(self.NSLOT - 1), None, OP.min))
        v(lambda e: e.tensor_copy(sAi[:, :], sAf[:, :]))
        v(lambda e: e.tensor_copy(sBi[:, :], sBf[:, :]))
        v(lambda e: e.tensor_tensor(
            teid[:, :, :], A(chk_b, [chk_b.ap[0], [1, NCH], [0, 7]]),
            A(segi8[:, :], [segi8[:, :].ap[0], [0, NCH], [1, 7]]), OP.is_ge))
        v(lambda e: e.tensor_reduce(eidv[:, :], teid[:, :, :], AX.X, OP.add))
        v(lambda e: e.scalar_tensor_tensor(
            tgu[:, :, :], A(eidv[:, :], [eidv[:, :].ap[0], [1, NCH], [0, 16]]),
            2048.0, A(ipp_b, [ipp_b.ap[0], [0, NCH], [1, 16]]), OP.mult, OP.add))
        v(lambda e: e.tensor_copy(idx_gu[:, :, :], tgu[:, :, :]))
        v(lambda e: e.scalar_tensor_tensor(
            eidv[:, :], eidv[:, :], 128.0,
            A(pid_b, [pid_b.ap[0], [0, NCH]]), OP.mult, OP.add))
        v(lambda e: e.tensor_copy(idx_dn[:, :], eidv[:, :]))
        ctx.wait("sync", ctx.tok("rtd"))
        ctx.dma("sync", self.d["ridx"][0, :, :], sAi[:, :], "rms")
        ctx.dma("sync", self.d["ridx"][1, :, :], sBi[:, :], "rms")
        ctx.dma("sync", self.d["rw"][0, :, :], wAf[:, :], "rms")
        ctx.dma("sync", self.d["rw"][1, :, :], wBf[:, :], "rms")
        self._sp["meta_tok"] = ctx.tok("rtd")
        self._sp["store_tok"] = ctx.tok("rms")

    def moe_experts(self, st):
        ctx, sb, d, NT, T = self.ctx, self.sb, self.d, self.NT, self.T
        NSH, NCH = self.NSH, self.NCH
        NK = NSH + NCH
        w8 = self._rt["w8"]
        rt_st = self._rt["rt_st"]
        self.moe_topk_batch()
        self.moe_routing_meta(st)
        if "dbg_w8" in self.dbg:
            dw = self.dram("dbg_w8", [128, NT, 8], F32, kind="ExternalOutput")
            ctx.wait("sync", ctx.tok("rtd"), ctx.tok("gpk"))
            ctx.dma("sync", dw[:, :, :], w8[:, :, :], "dbgs")
            ctx.wait("sync", ctx.tok("dbgs"))
        ctx.barrier()
        self.close_stack(rt_st)
        SP = self._sp
        sAi, sBi = SP["sAi"], SP["sBi"]
        idx_gu, idx_dn = SP["idx_gu"], SP["idx_dn"]
        gu = self.sbuf("moe_gu", [128, 4, 2, 8, 128], BF16, st)
        dnb = self.sbuf("moe_dn", [128, 2, 16, 2, 512], BF16, st)
        hsb = self.sbuf("moe_h", [128, 16, 512], BF16, st)
        stg = self.sbuf("moe_stg", [128, 4, DM], F32, st)
        eo_sb = self.sbuf("moe_eo", [128, 2, 4, DM], BF16, st)
        xtm = self.sbuf("moe_xtm", [128, 2, 4, DM], BF16, st)
        xfm = self.sbuf("moe_xfm", [128, 2, 8, 512], BF16, st)
        pay = self.sbuf("moe_pay", [128, 2, DM], BF16, st)
        sg = self.sbuf("moe_sg", [128, 512], BF16, st)
        ptx = self.psum("moe_ptx", [128, 2, 1024], BF16, st)
        pg = self.psum("moe_pg", [128, 512], F32, st)
        pu = self.psum("moe_pu", [128, 512], F32, st)
        pg2 = self.psum("moe_pg2", [128, 512], F32, st)
        pu2 = self.psum("moe_pu2", [128, 512], F32, st)
        pd = self.psum("moe_pd", [128, 2, 512], F32, st)
        guview = d["gu_bf"][:, :]
        dnview = d["dn_bf"][:, :]
        wc, xz = self._wprep
        ctx.wait("gpsimd", wc)

        def flat2(apx, n):
            return bass.AP(apx.tensor, apx.offset, [apx.ap[0], [1, n]])

        S = dict(gu_free=[None] * 4, h_evac=[None, None], pd_free=[None, None],
                 ptx_free=[None, None], tp_done=[None, None],
                 ev_done=[None, None], gu_last=[None, None], stg_free=None,
                 eo_free=[None, None], pay_free=[None] * 4, dn_tok={},
                 down_done={}, xtl_tok={}, mac_tok={})

        def dn_slot(k):
            if k < NSH:
                return 0
            return 1 if (k - NSH) % 2 == 0 else 0

        def dn_load(k):
            s = dn_slot(k)
            prev = None
            if k == NSH + 1:
                prev = NSH - 1
            elif k >= NSH + 2:
                prev = k - 2
            if prev is not None:
                ctx.wait("gpsimd", S["down_done"][prev])
            if k < NSH:
                t = ctx.dma("gpsimd", flat2(dnb[:, s, :, :, :], 16384),
                            dnview[7 * 128:8 * 128, :], "mdw%d" % s)
            else:
                ctx.wait("gpsimd", SP["meta_tok"])
                rc = k - NSH
                t = ctx.idma(flat2(dnb[:, s, :, :, :], 16384), None, dnview,
                             IOA(idx_dn[:, rc:rc + 1], 0), "mdw%d" % s)
            S["dn_tok"][k] = t

        def issue_gu(k, ip):
            rb = ip % 4
            if S["gu_free"][rb] is not None:
                ctx.wait("gpsimd", S["gu_free"][rb])
            if k < NSH:
                r0 = (7 * 16 + ip) * 128
                return ctx.dma("gpsimd", flat2(gu[:, rb, :, :, :], 2048),
                               guview[r0:r0 + 128, :], "mgw%d" % rb)
            ctx.wait("gpsimd", SP["meta_tok"])
            rc = k - NSH
            return ctx.idma(flat2(gu[:, rb, :, :, :], 2048), None, guview,
                            IOA(idx_gu[:, rc, ip:ip + 1], 0), "mgw%d" % rb)

        def xtl(k):
            kb = k % 2
            ctx.wait("sync", S["tp_done"][kb])
            if k == NSH:
                ctx.wait("sync", ctx.tok("dsp0"), ctx.tok("dsp1"))
            src = (d["xn_tm"] if k < NSH else d["xnd"])
            off = (k if k < NSH else k - NSH) * 512 * DM
            S["xtl_tok"][k] = ctx.dma(
                "sync", xtm[:, kb, :, :],
                bass.AP(src.tensor, off, [[DM, 128], [128 * DM, 4], [1, DM]]),
                "xtl%d" % kb)

        def tp_group(k):
            kb = k % 2
            last = None
            for j in range(4):
                tb = j % 2
                ctx.wait("tensor", S["xtl_tok"][k], S["ptx_free"][tb])
                pe = None
                for fc in range(8):
                    pe = ctx.op("tensor", lambda e, kb=kb, j=j, tb=tb, fc=fc:
                                e.transpose(
                                    ptx[:, tb, fc * 128:(fc + 1) * 128],
                                    xtm[:, kb, j, fc * 128:(fc + 1) * 128],
                                    sb["c_ident"][:, :]),
                                "mtp" if fc == 7 else None)
                ctx.wait("vector", pe, S["gu_last"][kb])
                base = xfm[:, :, :, :]
                dst = bass.AP(base.tensor, base.offset + kb * 8 * 512 + j * 128,
                              [base.ap[0], [512, 8], [1, 128]])
                last = ctx.op("vector", lambda e, tb=tb, dst=dst: e.tensor_copy(
                    dst, ptx[:, tb, :].rearrange("p (f t) -> p f t", f=8)), "mte")
                S["ptx_free"][tb] = last
            S["tp_done"][kb] = ctx.tok("mtp")
            S["ev_done"][kb] = last

        def gu_mm(k):
            kb = k % 2
            gu_toks = {}
            for ip in range(3):
                gu_toks[ip] = issue_gu(k, ip)
            mm = None
            for ip in range(16):
                rb = ip % 4
                if ip + 3 < 16:
                    gu_toks[ip + 3] = issue_gu(k, ip + 3)
                pga, pua = (pg, pu) if ip % 2 == 0 else (pg2, pu2)
                ctx.wait("tensor", gu_toks[ip], S["ev_done"][kb])
                if S["h_evac"][ip % 2] is not None:
                    ctx.wait("tensor", *S["h_evac"][ip % 2])
                for kc in range(8):
                    mm = ctx.op("tensor", lambda e, kc=kc, rb=rb, kb=kb, pga=pga:
                                e.matmul(pga[:, :], gu[:, rb, 0, kc, :],
                                         xfm[:, kb, kc, :],
                                         start=(kc == 0), stop=(kc == 7)), None)
                    mm = ctx.op("tensor", lambda e, kc=kc, rb=rb, kb=kb, pua=pua:
                                e.matmul(pua[:, :], gu[:, rb, 1, kc, :],
                                         xfm[:, kb, kc, :],
                                         start=(kc == 0), stop=(kc == 7)),
                                "gmm" if kc == 7 else None)
                S["gu_free"][rb] = mm
                ctx.wait("scalar", mm, ctx.tok("mhd"))
                aa = ctx.op("scalar", lambda e, pga=pga: e.activation(
                    sg[:, :], pga[:, :], AF.Silu), "mha")
                ctx.wait("vector", aa, ctx.tok("dmm"))
                hv = ctx.op("vector", lambda e, ip=ip, pua=pua: e.tensor_tensor(
                    hsb[:, ip, :], sg[:, :], pua[:, :], OP.mult), "mhd")
                S["h_evac"][ip % 2] = [aa, hv]
            S["gu_last"][kb] = mm

        def dispatch(k):
            for tj in range(4):
                tt = 4 * k + tj
                pb = tt % 2
                if S["pay_free"][pb] is not None:
                    ctx.wait("sync", *S["pay_free"][pb])
                pl = ctx.dma("sync", pay[:, pb, :],
                             d["xn_tm"][tt * 128:(tt + 1) * 128, :], "pyl%d" % pb)
                ctx.wait("gpsimd", pl, SP["meta_tok"], xz)
                t1 = ctx.idma(d["xnd"][:, :], IOA(sAi[:, tt:tt + 1], 0),
                              pay[:, pb, :], None, "dsp%d" % pb)
                t2 = ctx.idma(d["xnd"][:, :], IOA(sBi[:, tt:tt + 1], 0),
                              pay[:, pb, :], None, "dsp%d" % pb)
                S["pay_free"][pb] = [t2]

        def down(k):
            kb = k % 2
            s = dn_slot(k)
            shared = k < NSH
            dtok = S["dn_tok"][k if (k >= NSH or k == 0) else 0]
            ctx.wait("tensor", ctx.tok("mhd"), dtok)
            mm = None
            for tt4 in range(4):
                for fh in range(2):
                    db = (tt4 * 2 + fh) % 2
                    if S["pd_free"][db] is not None:
                        ctx.wait("tensor", S["pd_free"][db])
                    for i in range(16):
                        mm = ctx.op("tensor",
                                    lambda e, i=i, fh=fh, db=db, s=s, tt4=tt4:
                                    e.matmul(
                                        pd[:, db, :],
                                        hsb[:, i, tt4 * 128:(tt4 + 1) * 128],
                                        dnb[:, s, i, fh, :],
                                        start=(i == 0), stop=(i == 15)),
                                    "dmm" if i == 15 else None)
                    ctx.wait("vector", mm)
                    if shared:
                        if fh == 0 and S["mac_tok"].get(tt4) is not None:
                            ctx.wait("vector", S["mac_tok"][tt4])
                        ev = ctx.op("vector", lambda e, tt4=tt4, fh=fh, db=db:
                                    e.tensor_copy(
                                        stg[:, tt4, fh * 512:(fh + 1) * 512],
                                        pd[:, db, :]), "mdd")
                    else:
                        if tt4 == 0 and fh == 0 and S["eo_free"][kb] is not None:
                            ctx.wait("vector", *S["eo_free"][kb])
                        ev = ctx.op("vector",
                                    lambda e, kb=kb, tt4=tt4, fh=fh, db=db:
                                    e.tensor_copy(
                                        eo_sb[:, kb, tt4, fh * 512:(fh + 1) * 512],
                                        pd[:, db, :]), "mdd")
                    S["pd_free"][db] = ev
                if shared:
                    ctx.wait("gpsimd", ev)
                    if k == 0 and tt4 == 0:
                        ctx.wait("gpsimd", self._uacc)
                    S["mac_tok"][tt4] = ctx.dma(
                        "gpsimd",
                        bass.AP(d["acc"].tensor, (k * 512 + tt4 * 128) * DM,
                                [[DM, 128], [1, DM]]),
                        stg[:, tt4, :], "mac", accum=True)
                elif tt4 == 3:
                    rc = k - NSH
                    ctx.wait("sync", ev)
                    so = ctx.dma(
                        "sync",
                        bass.AP(d["eod"].tensor, rc * 512 * DM,
                                [[DM, 128], [128 * DM, 4], [1, DM]]),
                        eo_sb[:, kb, :, :], "meo%d" % kb)
                    S["eo_free"][kb] = [so]
            S["down_done"][k] = ctx.tok("dmm")

        dn_load(0)
        xtl(0)
        for k in range(NK):
            if k == 0:
                tp_group(0)
            gu_mm(k)
            if k < NSH:
                dispatch(k)
            if k + 1 < NK:
                if k + 1 >= NSH:
                    dn_load(k + 1)
                xtl(k + 1)
                tp_group(k + 1)
            down(k)
        ctx.wait("sync", ctx.tok("mac"), ctx.tok("meo0"), ctx.tok("meo1"))

    def combine(self, a, bsrc, csrc, dst, st):
        ctx, NT = self.ctx, self.NT
        ta = self.sbuf("cb_a", [128, 2, DM], F32, st)
        tb = self.sbuf("cb_b", [128, 2, DM], F32, st)
        tc_ = self.sbuf("cb_c", [128, 2, DM], F32, st)
        to = self.sbuf("cb_o", [128, 2, DM], F32, st)
        free = [None, None]
        for tt in range(NT):
            b = tt % 2
            r0, r1 = tt * 128, (tt + 1) * 128
            if free[b] is not None:
                ctx.wait("sync", *free[b])
            l1 = ctx.dma("sync", ta[:, b, :], a[r0:r1, :], "cbl%d" % b)
            l2 = ctx.dma("sync", tb[:, b, :], bsrc[r0:r1, :], "cbl%d" % b)
            l3 = ctx.dma("sync", tc_[:, b, :], csrc[r0:r1, :], "cbl%d" % b)
            ctx.wait("vector", l3)
            v1 = ctx.op("vector", lambda e, b=b: e.tensor_tensor(
                to[:, b, :], tb[:, b, :], tc_[:, b, :], OP.add), "cbd")
            ctx.wait("vector", v1)
            v2 = ctx.op("vector", lambda e, b=b: e.scalar_tensor_tensor(
                to[:, b, :], to[:, b, :], 0.5, ta[:, b, :], OP.mult, OP.add), "cbd")
            ctx.wait("sync", v2)
            so = ctx.dma("sync", dst[r0:r1, :], to[:, b, :], "cbs%d" % b)
            free[b] = [so, v2]
        ctx.wait("sync", ctx.tok("cbs0"), ctx.tok("cbs1"))

    def combine_moe(self, a, dst, st):
        ctx, d, NT = self.ctx, self.d, self.NT
        I32 = mybir.dt.int32
        ta = self.sbuf("cm_a", [128, 2, DM], F32, st)
        tc_ = self.sbuf("cm_c", [128, 2, DM], F32, st)
        to = self.sbuf("cm_o", [128, 2, DM], F32, st)
        gsb = self.sbuf("cm_g", [128, 2, 2, DM], BF16, st)
        zsb = self.sbuf("cm_z", [128, 2, 2, DM], F32, st)
        six = self.sbuf("cm_six", [128, 2, NT], I32, st)
        wx = self.sbuf("cm_wx", [128, 2, NT], F32, st)
        l0 = ctx.dma("sync", six[:, :, :],
                     d["ridx"][:, :, :].rearrange("j p t -> p j t"), "cbr")
        l1 = ctx.dma("sync", wx[:, :, :],
                     d["rw"][:, :, :].rearrange("j p t -> p j t"), "cbr2")
        ctx.wait("gpsimd", l0)
        ctx.wait("scalar", l1)
        load_toks = {}
        gath_toks = {}
        v4s = {}
        zfree = [None, None]
        gfree = [None, None]

        def loads(tt):
            b = tt % 2
            if tt >= 2:
                ctx.wait("sync", ctx.tok("cbs%d" % b), v4s[tt - 2])
            ctx.dma("sync", ta[:, b, :], a[tt * 128:(tt + 1) * 128, :],
                    "cbl%d" % b)
            load_toks[tt] = ctx.dma(
                "sync", tc_[:, b, :], d["acc"][tt * 128:(tt + 1) * 128, :],
                "cbl%d" % b)

        def gathers(tt):
            b = tt % 2
            if gfree[b] is not None:
                ctx.wait("gpsimd", *gfree[b])
            g1 = ctx.idma(gsb[:, b, 0, :], None, d["eod"][:, :],
                          IOA(six[:, 0, tt:tt + 1], 0), "cbg%d" % (2 * b))
            g2 = ctx.idma(gsb[:, b, 1, :], None, d["eod"][:, :],
                          IOA(six[:, 1, tt:tt + 1], 0), "cbg%d" % (2 * b + 1))
            gath_toks[tt] = (g1, g2)

        loads(0)
        loads(1)
        gathers(0)
        for tt in range(NT):
            b = tt % 2
            g1, g2 = gath_toks[tt]
            ctx.wait("scalar", g1, g2, ctx.tok("cba"), zfree[b])
            z1 = ctx.op("scalar", lambda e, b=b, tt=tt: e.activation(
                zsb[:, b, 0, :], gsb[:, b, 0, :], AF.Copy,
                scale=wx[:, 0, tt:tt + 1]), "cba")
            z2 = ctx.op("scalar", lambda e, b=b, tt=tt: e.activation(
                zsb[:, b, 1, :], gsb[:, b, 1, :], AF.Copy,
                scale=wx[:, 1, tt:tt + 1]), "cba")
            gfree[b] = [z2]
            if tt + 1 < NT:
                gathers(tt + 1)
            ctx.wait("vector", load_toks[tt], z2)
            v1 = ctx.op("vector", lambda e, b=b: e.tensor_tensor(
                to[:, b, :], tc_[:, b, :], zsb[:, b, 0, :], OP.add), "cbd")
            ctx.wait("vector", v1)
            v3 = ctx.op("vector", lambda e, b=b: e.tensor_tensor(
                to[:, b, :], to[:, b, :], zsb[:, b, 1, :], OP.add), "cbd")
            ctx.wait("vector", v3)
            v4 = ctx.op("vector", lambda e, b=b: e.scalar_tensor_tensor(
                to[:, b, :], to[:, b, :], 0.5, ta[:, b, :], OP.mult, OP.add), "cbd")
            zfree[b] = v3
            v4s[tt] = v4
            ctx.wait("gpsimd", v4)
            ctx.dma("gpsimd", dst[tt * 128:(tt + 1) * 128, :], to[:, b, :],
                    "cbs%d" % b)
            if tt + 2 < NT:
                loads(tt + 2)
        ctx.wait("sync", ctx.tok("cbs0"), ctx.tok("cbs1"))

    def ffn_block(self, xsrc, dst):
        ctx, d = self.ctx, self.d
        st = self.estack()
        self.sbuf("vfm", [128, 8, self.T], BF16, st)
        self.ln_pass(xsrc, dst_tm=d["u_s"])
        ctx.barrier()
        ctx.flush()
        self.phase()
        if not hasattr(self, "_wprep"):
            self.prep_weights()
        self.ln_pass(d["u_s"], fm_sb=self.sb["vfm"], gamma=self.gamma(0))
        ctx.barrier()
        ctx.flush()
        self.phase()
        st2 = self.estack()
        self.sbuf("cbias", [128, 1], F32, st2, side="right")
        st3 = self.estack()
        bt = self.mla_prep(st3)
        ctx.barrier()
        ctx.flush()
        self.phase()
        self.close_stack(st3)
        self.close_stack(st)
        st4 = self.estack()
        self.attention(True, 1.0 / 8.0, bt, st4)
        ctx.barrier()
        ctx.flush()
        self.phase()
        self.close_stack(st4)
        self.close_stack(st2)
        st5 = self.estack()
        self.oproj("m_o_T", d["u_s"], d["u_s"], False, st5)
        ctx.barrier()
        ctx.flush()
        self.phase()
        self.close_stack(st5)
        NTc = self.NT
        self._uacc = ctx.dma(
            "scalar",
            bass.AP(d["acc"].tensor, 0, [[DM * 128, NTc], [1, DM * 128]]),
            bass.AP(d["u_s"].tensor, 0, [[DM * 128, NTc], [1, DM * 128]]),
            "uacc")
        st7 = self.estack()
        hook = self.moe_router_setup(st7)
        self.ln_pass(d["u_s"], gamma=self.gamma(1), hook=hook)
        ctx.barrier()
        ctx.flush()
        self.phase()
        self.moe_experts(st7)
        ctx.barrier()
        ctx.flush()
        self.phase()
        self.close_stack(st7)
        st8 = self.estack()
        self.combine_moe(xsrc, dst, st8)
        ctx.barrier()
        ctx.flush()
        self.phase()
        self.close_stack(st8)

    def attn_block(self, xsrc, dst):
        ctx, d = self.ctx, self.d
        st = self.estack()
        self.sbuf("vfm", [128, 8, self.T], BF16, st)
        self.ln_pass(xsrc, fm_sb=self.sb["vfm"], gamma=self.gamma(2), beta=self.gamma(3))
        ctx.barrier()
        ctx.flush()
        self.phase()
        st2 = self.estack()
        self.sbuf("cbias", [128, 1], F32, st2, side="right")
        st3 = self.estack()
        bt = self.s2_prep(st3)
        ctx.barrier()
        ctx.flush()
        self.phase()
        self.close_stack(st3)
        self.close_stack(st)
        st4 = self.estack()
        self.attention(False, 1.0, bt, st4)
        ctx.barrier()
        ctx.flush()
        self.phase()
        self.close_stack(st4)
        self.close_stack(st2)
        st5 = self.estack()
        self.oproj("a_o_T", xsrc, dst, True, st5)
        ctx.barrier()
        ctx.flush()
        self.phase()
        self.close_stack(st5)

    def build(self):
        d = self.d
        self.declare()
        self.load_consts()
        try:
            self.ffn_block(d["x_in"], d["x_cur"])
            self.attn_block(d["x_cur"], d["x_cur"])
            self.ffn_block(d["x_cur"], d["y_out"])
        except StopBuild:
            self.ctx.flush()
            for s in reversed(self.open_stacks):
                s.close()
            self.open_stacks = []
        self.ctx.flush()
        self.stack.close()
        return self.nc


def _prep_inputs(inp, T):
    f32 = np.float32
    C = np.ascontiguousarray
    out = {}
    out["c_ident"] = np.eye(128, dtype=f32)
    k_idx, q_idx = np.meshgrid(np.arange(128), np.arange(128), indexing="ij")
    out["c_mask"] = (k_idx <= q_idx).astype(f32)
    inv = 1.0 / (10000.0 ** (np.arange(0, RD, 2, dtype=np.float64) / RD))
    tpos = np.arange(T, dtype=np.float64)
    fr = np.outer(tpos, inv)
    emb = np.concatenate([fr, fr], axis=1)
    cos = np.cos(emb).astype(f32)
    sin = np.sin(emb).astype(f32)
    sinS = np.concatenate([-sin[:, :16], sin[:, 16:]], axis=1).astype(f32)
    out["c_cos"] = C(np.tile(cos, (1, NH)))
    out["c_sinS"] = C(np.tile(sinS, (1, NH)))
    gam = np.stack([
        np.asarray(inp["L0_ffn_ln1_w"], f32),
        np.asarray(inp["L0_ffn_ln2_w"], f32),
        np.asarray(inp["L0_attn_ln_gamma"], f32),
        np.asarray(inp["L0_attn_ln_beta"], f32),
        np.tile(np.asarray(inp["L0_attn_q_gamma"], f32), NH),
        np.tile(np.asarray(inp["L0_attn_k_gamma"], f32), NH),
    ])
    out["c_gam"] = C(np.broadcast_to(gam[None], (128, 6, DM)))
    out["c_bo"] = C(np.broadcast_to(np.asarray(inp["L0_attn_b_o"], f32)[None], (128, DM)))
    rw = np.zeros((DM, 8), f32)
    rw[:, :7] = np.asarray(inp["L0_ffn_router_w"], f32).T
    out["c_router"] = C(rw.reshape(8, 128, 8))
    rb = np.zeros((8,), f32)
    rb[:7] = np.asarray(inp["L0_ffn_routing_bias"], f32)
    out["c_rbias"] = C(np.broadcast_to(rb[None], (128, 8)))
    NCH = 2 * (T // 512) + 7
    ei, ej = np.meshgrid(np.arange(7), np.arange(7), indexing="ij")
    rm = np.concatenate([
        np.broadcast_to((ej < ei).astype(f32).reshape(1, 49), (128, 49)),
        np.broadcast_to((ej <= ei).astype(f32).reshape(1, 49), (128, 49)),
        np.broadcast_to(512.0 * np.arange(NCH, dtype=f32)[None], (128, NCH)),
        np.broadcast_to(np.arange(NCH, dtype=f32)[None], (128, NCH)),
        128.0 * np.arange(16, dtype=f32)[None]
        + np.arange(128, dtype=f32)[:, None],
        np.arange(128, dtype=f32)[:, None],
    ], axis=1)
    out["c_rmeta"] = C(rm.astype(f32))
    h2 = np.zeros((2, 128), f32)
    h2[0, 0:64] = 1.0
    h2[1, 64:128] = 1.0
    out["c_half2"] = h2

    def blk_kc(w):
        K, N = w.shape
        return C(w.reshape(K // 128, 128, N))

    kv_d = np.asarray(inp["L0_ffn_kv_d"], f32)
    q_d = np.asarray(inp["L0_ffn_q_d"], f32)
    k_u = np.asarray(inp["L0_ffn_k_u"], f32)
    q_u = np.asarray(inp["L0_ffn_q_u"], f32)
    v_u = np.asarray(inp["L0_ffn_v_u"], f32)
    rq = np.asarray(inp["L0_ffn_rope_q_w"], f32)
    rk = np.asarray(inp["L0_ffn_rope_k_w"], f32)
    ow = np.asarray(inp["L0_ffn_o_w"], f32)
    out["m_kv_T"] = blk_kc(kv_d.T)
    out["m_q_T"] = blk_kc(q_d.T)
    nope_rows = np.concatenate([np.arange(h * 64 + 32, h * 64 + 64) for h in range(NH)])
    out["m_knope_T"] = blk_kc(k_u[nope_rows].T)
    out["m_qnope_T"] = blk_kc(q_u[nope_rows].T)
    out["m_v_T"] = blk_kc(v_u.T)
    out["m_rq_T"] = blk_kc(rq.T)
    out["m_rk_T"] = blk_kc(rk.T)
    out["m_o_T"] = blk_kc(ow.T)
    out["a_q_T"] = blk_kc(np.asarray(inp["L0_attn_w_q"], f32).T)
    out["a_k_T"] = blk_kc(np.asarray(inp["L0_attn_w_k"], f32).T)
    out["a_v_T"] = blk_kc(np.asarray(inp["L0_attn_w_v"], f32).T)
    out["a_o_T"] = blk_kc(np.asarray(inp["L0_attn_w_o"], f32).T)

    sgw = np.asarray(inp["L0_ffn_shared_gate"], f32)[0]
    suw = np.asarray(inp["L0_ffn_shared_up"], f32)[0]
    sdw = np.asarray(inp["L0_ffn_shared_down"], f32)[0]
    rgw = np.asarray(inp["L0_ffn_routed_gate"], f32)
    ruw = np.asarray(inp["L0_ffn_routed_up"], f32)
    rdw = np.asarray(inp["L0_ffn_routed_down"], f32)
    gu = np.zeros((8, 16, 128, 2, 8, 128), f32)
    dn = np.zeros((8, 128, 16, 2, 512), f32)
    for e in range(8):
        gw = rgw[e] if e < 7 else sgw
        uw = ruw[e] if e < 7 else suw
        dw = rdw[e] if e < 7 else sdw
        gT = gw.T.reshape(8, 128, 16, 128)
        uT = uw.T.reshape(8, 128, 16, 128)
        gu[e, :, :, 0] = gT.transpose(2, 1, 0, 3)
        gu[e, :, :, 1] = uT.transpose(2, 1, 0, 3)
        dT = dw.T.reshape(16, 128, 2, 512)
        dn[e] = dT.transpose(1, 0, 2, 3)
    out["moe_gu"] = C(gu)
    out["moe_dn"] = C(dn)
    return out


def kernel(**inputs):
    x = np.asarray(inputs["L0_attn_x"], np.float32)
    B, T, _ = x.shape
    mk = MK(T)
    nc = mk.build()
    weights = _prep_inputs(inputs, T)
    in_maps = []
    core_ids = [0, 2, 4, 6][:B]
    for b in range(B):
        m = dict(weights)
        m["x_in"] = np.ascontiguousarray(x[b])
        in_maps.append(m)
    res = run_bass_kernel_spmd(nc, in_maps, core_ids=core_ids)
    out = np.stack([res.results[b]["y_out"] for b in range(B)])
    return out


# version 26
# speedup vs baseline: 1.0565x; 1.0295x over previous
import math
import numpy as np
import ml_dtypes
import concourse.bass as bass
import concourse.mybir as mybir
from concourse.bass import IndirectOffsetOnAxis as IOA
from concourse.bass_isa import ReduceOp
from concourse.library_config import attnmlp as LIB_ATTNMLP
from concourse.bass_utils import run_bass_kernel_spmd
from contextlib import ExitStack

F32 = mybir.dt.float32
BF16 = mybir.dt.bfloat16
AF = mybir.ActivationFunctionType
OP = mybir.AluOpType
AX = mybir.AxisListType

DM = 1024
NH = 16
HD = 64
RD = 32
LAT = 256
IFF = 2048
LN_EPS = 1e-5

ENGINES = ("sync", "gpsimd", "tensor", "vector", "scalar")


class Ctx:

    def __init__(self, nc, stack):
        self.nc = nc
        self.stack = stack
        self.plan = {e: [] for e in ENGINES}
        self.sems = {}
        self.cnt = {}
        self.last_wait = {}
        self.owner = {}

    def sem(self, name):
        if name not in self.sems:
            self.sems[name] = self.stack.enter_context(self.nc.semaphore(name))
            self.cnt[name] = 0
        return self.sems[name]

    def bump(self, name, n):
        self.sem(name)
        self.cnt[name] += n
        return self.cnt[name]

    def emit(self, eng, fn):
        self.plan[eng].append(fn)

    def dma(self, eng, out, in_, sem_name, accum=False):
        self.owner[sem_name] = eng
        val = self.bump(sem_name, 16)
        s = self.sems[sem_name]
        if accum:
            def fn(e, out=out, in_=in_, s=s):
                e.dma_start(out=out, in_=in_, accum_op=OP.add).then_inc(s, 16)
        else:
            def fn(e, out=out, in_=in_, s=s):
                e.dma_start(out=out, in_=in_).then_inc(s, 16)
        self.emit(eng, fn)
        return (sem_name, val)

    def idma(self, out, out_off, in_, in_off, sem_name):
        self.owner[sem_name] = "gpsimd"
        val = self.bump(sem_name, 16)
        s = self.sems[sem_name]

        def fn(e, out=out, oo=out_off, in_=in_, io=in_off, s=s):
            e.indirect_dma_start(out=out, out_offset=oo, in_=in_,
                                 in_offset=io).then_inc(s, 16)
        self.emit("gpsimd", fn)
        return (sem_name, val)

    def wait(self, eng, *toks):
        for tok in toks:
            if tok is None:
                continue
            name, val = tok
            if val <= 0:
                continue
            if self.last_wait.get((eng, name), -1) >= val:
                continue
            self.last_wait[(eng, name)] = val
            s = self.sem(name)
            self.emit(eng, lambda e, s=s, v=val: e.wait_ge(s, v))

    def op(self, eng, fn, sem_name=None):
        if sem_name is None:
            self.emit(eng, fn)
            return None
        self.owner[sem_name] = eng
        val = self.bump(sem_name, 1)
        s = self.sems[sem_name]

        def wrapped(e, fn=fn, s=s):
            fn(e).then_inc(s, 1)

        self.emit(eng, wrapped)
        return (sem_name, val)

    def tok(self, name):
        return (name, self.cnt.get(name, 0))

    barrier_exempt = ("wcast", "xz")

    def barrier(self):
        for eng in ENGINES:
            for name, owner in list(self.owner.items()):
                if owner == eng and name not in self.barrier_exempt:
                    self.wait(eng, (name, self.cnt[name]))
        vals = {}
        for eng in ENGINES:
            vals[eng] = self.bump("bar_" + eng, 1)
        for eng in ENGINES:
            s = self.sems["bar_" + eng]
            self.emit(eng, lambda e, s=s: e.nop().then_inc(s, 1))
        for eng in ENGINES:
            for other in ENGINES:
                s = self.sems["bar_" + other]
                self.emit(eng, lambda e, s=s, v=vals[other]: e.wait_ge(s, v))

    def flush(self):
        if not any(self.plan[e] for e in ENGINES):
            return
        with self.nc.Block() as block:
            for eng_name in ENGINES:
                items = self.plan[eng_name]
                if not items:
                    continue
                section = getattr(block, eng_name)

                def make(items):
                    def _(engine):
                        for fn in items:
                            fn(engine)
                    return _

                section(make(items))
        self.plan = {e: [] for e in ENGINES}


class StopBuild(Exception):
    pass


class MK:
    def __init__(self, T, dbg=(), upto=None):
        self.upto = upto
        self.phase_n = 0
        self.T = T
        self.NT = T // 128
        self.TC = T // 512
        self.NSH = T // 512
        self.NCH = 2 * (T // 512) + 7
        self.NSLOT = self.NCH * 512
        self.nc = bass.Bass("TRN2", target_bir_lowering=False)
        self.stack = ExitStack()
        self.ctx = Ctx(self.nc, self.stack)
        self.dbg = set(dbg)
        self.d = {}
        self.sb = {}
        self.open_stacks = []

    def phase(self):
        self.phase_n += 1
        if self.upto is not None and self.phase_n > self.upto:
            raise StopBuild()

    def estack(self):
        s = ExitStack()
        self.open_stacks.append(s)
        return s

    def close_stack(self, s):
        self.open_stacks.remove(s)
        s.close()

    def dram(self, name, shape, dtype=F32, kind=None):
        if name in self.dbg and kind is None:
            kind = "ExternalOutput"
        t = (self.nc.dram_tensor(name, shape, dtype, kind=kind) if kind
             else self.nc.dram_tensor(name, shape, dtype))
        self.d[name] = t.ap()
        return self.d[name]

    _uid = 0

    def _uniq(self, name):
        MK._uid += 1
        return "%s_%d" % (name, MK._uid)

    def sbuf(self, name, shape, dtype, stack=None, side=None):
        t = (stack or self.stack).enter_context(
            self.nc.sbuf_tensor(self._uniq(name), shape, dtype, side=side))
        self.sb[name] = t
        return t

    def psum(self, name, shape, dtype, stack=None):
        t = (stack or self.stack).enter_context(
            self.nc.psum_tensor(self._uniq(name), shape, dtype))
        self.sb[name] = t
        return t

    def declare(self):
        d, T = self.dram, self.T
        d("x_in", [T, DM], kind="ExternalInput")
        d("y_out", [T, DM], kind="ExternalOutput")
        d("c_ident", [128, 128], kind="ExternalInput")
        d("c_mask", [128, 128], kind="ExternalInput")
        d("c_cos", [T, 512], kind="ExternalInput")
        d("c_sinS", [T, 512], kind="ExternalInput")
        d("c_gam", [128, 6, DM], kind="ExternalInput")
        d("c_bo", [128, DM], kind="ExternalInput")
        d("c_router", [8, 128, 8], kind="ExternalInput")
        d("c_rbias", [128, 8], kind="ExternalInput")
        d("c_half2", [2, 128], kind="ExternalInput")
        d("m_kv_T", [8, 128, LAT], kind="ExternalInput")
        d("m_q_T", [8, 128, LAT], kind="ExternalInput")
        d("m_knope_T", [2, 128, 512], kind="ExternalInput")
        d("m_qnope_T", [2, 128, 512], kind="ExternalInput")
        d("m_v_T", [2, 128, DM], kind="ExternalInput")
        d("m_rq_T", [2, 128, 512], kind="ExternalInput")
        d("m_rk_T", [8, 128, 512], kind="ExternalInput")
        d("m_o_T", [8, 128, DM], kind="ExternalInput")
        d("a_q_T", [8, 128, DM], kind="ExternalInput")
        d("a_k_T", [8, 128, DM], kind="ExternalInput")
        d("a_v_T", [8, 128, DM], kind="ExternalInput")
        d("a_o_T", [8, 128, DM], kind="ExternalInput")
        d("moe_gu", [8, 16, 128, 2, 8, 128], kind="ExternalInput")
        d("moe_dn", [8, 128, 16, 2, 512], kind="ExternalInput")
        d("c_rmeta", [128, 115 + 2 * self.NCH], kind="ExternalInput")
        d("gu_bf", [8 * 16 * 128, 2048], BF16)
        d("dn_bf", [8 * 128, 16 * 2 * 512], BF16)
        d("xn_tm", [T, DM], BF16)
        d("xnd", [self.NSLOT, DM], BF16)
        d("eod", [self.NSLOT, DM], BF16)
        d("ridx", [2, 128, self.NT], mybir.dt.int32)
        d("rw", [2, 128, self.NT])
        d("u_s", [T, DM])
        d("x_cur", [T, DM])
        d("acc", [T, DM])
        d("kfm", [NH, HD, T], BF16)
        d("qfm", [NH, HD, T], BF16)
        d("vtm", [self.NT, 128, NH * 65], BF16)
        d("ofm", [8, self.TC, 128, 512], BF16)
        d("lsum", [NH, T], F32)

    def load_consts(self):
        ctx, sb, d = self.ctx, self.sb, self.d
        self.sbuf("c_ident", [128, 128], BF16)
        self.sbuf("c_mask", [128, 128], BF16)
        self.sbuf("c_gam", [128, 6, DM], F32)
        self.sbuf("c_bo", [128, DM], F32)
        self.sbuf("c_rbias", [128, 8], F32)
        self.sbuf("c_eps", [128, 1], F32)
        self.sbuf("c_tiny", [128, 1], F32)
        self.sbuf("c_mln8", [128, 1], F32)
        self.sbuf("c_one64", [1, 64], BF16)
        self.sbuf("c_ones1b", [1, 128], BF16)
        self.sbuf("c_half2", [2, 128], BF16)
        self.sbuf("c_rmeta", [128, 115 + 2 * self.NCH], F32)
        self.sbuf("c_ones128", [128, 128], BF16)
        self.sbuf("zbuf", [128, 2, DM], BF16)
        toks = [
            ctx.dma("gpsimd", sb["c_ident"][:, :], d["c_ident"][:, :], "wks"),
            ctx.dma("gpsimd", sb["c_mask"][:, :], d["c_mask"][:, :], "wks"),
            ctx.dma("gpsimd", sb["c_gam"][:, :, :], d["c_gam"][:, :, :], "wks"),
            ctx.dma("gpsimd", sb["c_bo"][:, :], d["c_bo"][:, :], "wks"),
            ctx.dma("gpsimd", sb["c_rbias"][:, :], d["c_rbias"][:, :], "wks"),
            ctx.dma("gpsimd", sb["c_half2"][:, :], d["c_half2"][:, :], "wks"),
            ctx.dma("gpsimd", sb["c_rmeta"][:, :], d["c_rmeta"][:, :], "wks"),
        ]
        g1 = ctx.op("gpsimd", lambda e: (
            e.memset(sb["c_eps"][:, :], float(LN_EPS)),
            e.memset(sb["c_tiny"][:, :], 1e-16),
            e.memset(sb["c_mln8"][:, :], -math.log(8.0)),
            e.memset(sb["c_one64"][:, :], 1.0),
            e.memset(sb["c_ones1b"][:, :], 1.0),
            e.memset(sb["c_ones128"][:, :], 1.0))[-1], "gpk")
        for eng in ENGINES:
            ctx.wait(eng, toks[-1], g1)
            ctx.wait(eng, ("wks", ctx.cnt["wks"]))

    def gamma(self, idx):
        return self.sb["c_gam"][:, idx, :]

    def rmeta(self, name, n):
        off = {"lt7": 0, "le7": 49, "thr": 98, "chunk": 98 + self.NCH,
               "ipp": 98 + 2 * self.NCH, "pid": 114 + 2 * self.NCH}[name]
        return self.sb["c_rmeta"][:, off:off + n]

    def prep_weights(self):
        ctx, d = self.ctx, self.d
        for e_ in range(8):
            ctx.dma("gpsimd",
                    bass.AP(d["gu_bf"].tensor, e_ * 2048 * 2048,
                            [[2048, 2048], [1, 2048]]),
                    bass.AP(d["moe_gu"].tensor, e_ * 2048 * 2048,
                            [[2048, 2048], [1, 2048]]), "wcast")
            ctx.dma("gpsimd", d["dn_bf"][e_ * 128:(e_ + 1) * 128, :],
                    bass.AP(d["moe_dn"].tensor, e_ * 128 * 16384,
                            [[16384, 128], [1, 16384]]), "wcast")
        zb = self.sb["zbuf"]
        gz = ctx.op("gpsimd", lambda e: e.memset(zb[:, :, :], 0.0), "gpz")
        ctx.wait("gpsimd", gz)
        for c in range(2 * self.NCH):
            ctx.dma("gpsimd",
                    bass.AP(d["xnd"].tensor, c * 256 * DM,
                            [[DM, 128], [128 * DM, 2], [1, DM]]),
                    zb[:, :, :], "xz")
        self._wprep = (ctx.tok("wcast"), ctx.tok("xz"))

    def ln_pass(self, src, dst_tm=None, fm_sb=None, gamma=None, beta=None, hook=None):
        ctx, sb, NT = self.ctx, self.sb, self.NT
        G = 4
        st = self.estack()
        xt = self.sbuf("lnxt", [128, 2 * G, DM], F32, st)
        ut = self.sbuf("lnut", [128, 2 * G, DM], F32, st)
        gt = self.sbuf("lngt", [128, 2 * G, DM], BF16, st)
        f1 = self.sbuf("lnf1", [128, 2, DM], F32, st)
        sqj = self.sbuf("lnsqj", [128, DM], BF16, st)
        sm = {n: self.sbuf("ln" + n, [128, 2 * G], F32, st) for n in
              ("s", "q", "m", "t", "v", "l", "r")}
        ptp = self.psum("lnptp", [128, 1024], BF16, st)
        want_fm = fm_sb is not None
        slot_free = [None] * (2 * G)
        hook_last = None
        assert NT % G == 0
        for g0 in range(0, NT, G):
            tts = list(range(g0, g0 + G))
            slots = [tt % (2 * G) for tt in tts]
            for tt, sl in zip(tts, slots):
                if slot_free[sl] is not None:
                    ctx.wait("sync", *slot_free[sl])
                ctx.dma("sync", xt[:, sl, :], src[tt * 128:(tt + 1) * 128, :],
                        "lnl%d" % (sl // G))
            ld = ctx.tok("lnl%d" % (slots[0] // G))
            ctx.wait("vector", ld)
            ctx.wait("scalar", ld)
            for tt, sl in zip(tts, slots):
                ctx.op("vector", lambda e, sl=sl: e.reduce_sum(
                    sm["s"][:, sl:sl + 1], xt[:, sl, :], AX.X), "lnd")
            for tt, sl in zip(tts, slots):
                ctx.wait("scalar", ctx.tok("lna"))
                ctx.op("scalar", lambda e, sl=sl: e.activation(
                    sqj[:, :], xt[:, sl, :], AF.Square,
                    accum_out=sm["q"][:, sl:sl + 1]), "lna")
            ctx.wait("vector", ctx.tok("lnd"))
            for tt, sl in zip(tts, slots):
                ctx.op("vector", lambda e, sl=sl: (
                    e.tensor_scalar(sm["m"][:, sl:sl + 1], sm["s"][:, sl:sl + 1],
                                    1.0 / DM, None, OP.mult),
                    e.tensor_tensor(sm["t"][:, sl:sl + 1], sm["s"][:, sl:sl + 1],
                                    sm["s"][:, sl:sl + 1], OP.mult))[-1], "lnd")
            ctx.wait("vector", ctx.tok("lnd"), ctx.tok("lna"))
            for tt, sl in zip(tts, slots):
                ctx.op("vector", lambda e, sl=sl: e.scalar_tensor_tensor(
                    sm["v"][:, sl:sl + 1], sm["t"][:, sl:sl + 1], -1.0 / DM,
                    sm["q"][:, sl:sl + 1], OP.mult, OP.add), "lnd")
            ctx.wait("scalar", ctx.tok("lnd"))
            for tt, sl in zip(tts, slots):
                ctx.op("scalar", lambda e, sl=sl: e.activation(
                    sm["l"][:, sl:sl + 1], sm["v"][:, sl:sl + 1], AF.Ln,
                    scale=1.0 / DM, bias=sb["c_eps"][:, 0:1]), "lna")
            ctx.wait("scalar", ctx.tok("lna"))
            for tt, sl in zip(tts, slots):
                ctx.op("scalar", lambda e, sl=sl: e.activation(
                    sm["r"][:, sl:sl + 1], sm["l"][:, sl:sl + 1], AF.Exp,
                    scale=-0.5), "lna")
            ctx.wait("vector", ctx.tok("lna"))
            applies = {}
            for tt, sl in zip(tts, slots):
                applies[tt] = ctx.op("vector", lambda e, sl=sl: e.tensor_scalar(
                    ut[:, sl, :], xt[:, sl, :], sm["m"][:, sl:sl + 1],
                    sm["r"][:, sl:sl + 1], OP.subtract, OP.mult), "lnd")
            if dst_tm is not None:
                ctx.wait("sync", applies[tts[-1]])
                for tt, sl in zip(tts, slots):
                    ctx.dma("sync", dst_tm[tt * 128:(tt + 1) * 128, :],
                            ut[:, sl, :], "lns%d" % (sl // G))
                st_tok = ctx.tok("lns%d" % (slots[0] // G))
            else:
                st_tok = None
            if want_fm or hook is not None:
                ctx.wait("vector", ctx.tok("lnd"))
                for tt, sl in zip(tts, slots):
                    b2 = tt % 2
                    fdep = None
                    if gamma is not None and beta is not None:
                        ctx.wait("vector", hook_last)
                        v5 = ctx.op("vector", lambda e, sl=sl, b2=b2: e.tensor_tensor(
                            f1[:, b2, :], ut[:, sl, :], gamma, OP.mult), "lnd")
                        ctx.wait("vector", v5)
                        v6 = ctx.op("vector", lambda e, sl=sl, b2=b2: e.tensor_tensor(
                            gt[:, sl, :], f1[:, b2, :], beta, OP.add), "lnd")
                    elif gamma is not None and hook is not None:
                        ctx.wait("vector", hook_last)
                        v5 = ctx.op("vector", lambda e, sl=sl, b2=b2: e.tensor_tensor(
                            f1[:, b2, :], ut[:, sl, :], gamma, OP.mult), "lnd")
                        ctx.wait("vector", v5)
                        v6 = ctx.op("vector", lambda e, sl=sl, b2=b2: e.tensor_copy(
                            gt[:, sl, :], f1[:, b2, :]), "lnd")
                    elif gamma is not None:
                        v6 = ctx.op("vector", lambda e, sl=sl: e.tensor_tensor(
                            gt[:, sl, :], ut[:, sl, :], gamma, OP.mult), "lnd")
                    else:
                        v6 = ctx.op("vector", lambda e, sl=sl: e.tensor_copy(
                            gt[:, sl, :], ut[:, sl, :]), "lnd")
                    if want_fm:
                        ctx.wait("tensor", v6, ctx.tok("lnt"))
                        pe_v = None
                        for fc in range(8):
                            pe_v = ctx.op("tensor", lambda e, sl=sl, fc=fc: e.transpose(
                                ptp[:, fc * 128:(fc + 1) * 128],
                                gt[:, sl, fc * 128:(fc + 1) * 128],
                                sb["c_ident"][:, :]), "lnp" if fc == 7 else None)
                        ctx.wait("vector", pe_v)
                        v7 = ctx.op("vector", lambda e, tt=tt: e.tensor_copy(
                            fm_sb[:, :, tt * 128:(tt + 1) * 128],
                            ptp[:, :].rearrange("p (f t) -> p f t", f=8)), "lnt")
                    else:
                        v7 = None
                    hk = None
                    hks = ()
                    if hook is not None:
                        hk = hook(tt, f1[:, b2, :], gt[:, sl, :], v6, v7)
                        if isinstance(hk, tuple):
                            hks = hk[1:]
                            hk = hk[0]
                        hook_last = hk
                    slot_free[sl] = [t for t in (applies[tt], v7, hk, st_tok,
                                                 ctx.tok("lna")) + hks
                                     if t is not None]
            else:
                for tt, sl in zip(tts, slots):
                    slot_free[sl] = [t for t in (applies[tt], st_tok,
                                                 ctx.tok("lna")) if t is not None]
        ctx.wait("sync", ctx.tok("lns0"), ctx.tok("lns1"))
        out_toks = [ctx.tok("lnd"), ctx.tok("lna"), ctx.tok("lnt")]
        self.close_stack(st)
        return out_toks

    def tile_norms(self, tile, dep, mstack, tt, sq, s16):
        ctx = self.ctx
        ctx.wait("scalar", dep, ctx.tok("nsd"))
        a = ctx.op("scalar", lambda e: e.activation(sq[:, :], tile, AF.Square), "nsa")
        ctx.wait("vector", a)
        v = ctx.op("vector", lambda e: e.tensor_reduce(
            s16[:, :], sq[:, :].rearrange("p (h d) -> p h d", h=NH), AX.X, OP.add), "nsd")
        ctx.wait("vector", v)
        v2 = ctx.op("vector", lambda e, tt=tt: e.reduce_max(
            mstack[:, tt:tt + 1], s16[:, :], AX.X), "nsd")
        return v2, a

    def bound(self, mq, mk_, scale, deps, ptp, pbnd):
        ctx, sb = self.ctx, self.sb
        ctx.wait("vector", *deps)
        t1 = self.sb["bnd_t1"]
        t1b = self.sb["bnd_t1b"]
        t2 = self.sb["bnd_t2"]
        cb = self.sb["cbias"]
        v1 = ctx.op("vector", lambda e: (
            e.reduce_max(t1[:, 0:1], mq[:, :], AX.X),
            e.reduce_max(t1[:, 1:2], mk_[:, :], AX.X))[-1], "nsd")
        ctx.wait("vector", v1)
        v1b = ctx.op("vector", lambda e: e.tensor_copy(t1b[:, 0:2], t1[:, 0:2]), "nsd")
        ctx.wait("tensor", v1b, ctx.tok("nsd"), ctx.tok("nsa"), ctx.tok("prt"))
        p1 = ctx.op("tensor", lambda e: e.transpose(
            ptp[:, 0:128][0:2, :], t1b[:, 0:2], sb["c_ident"][:, :]), "bnp")
        ctx.wait("vector", p1)
        v2 = ctx.op("vector", lambda e: e.reduce_max(
            t2[:, 0:1][0:2, :], ptp[:, 0:128][0:2, :], AX.X), "nsd")
        ctx.wait("vector", v2)
        v2b = ctx.op("vector", lambda e: e.tensor_copy(t1b[:, 2:3][0:2, :],
                                                       t2[:, 0:1][0:2, :]), "nsd")
        ctx.wait("tensor", v2b)
        p2 = ctx.op("tensor", lambda e: e.transpose(
            ptp[:, 128:130][0:1, :], t1b[:, 2:3][0:2, :], sb["c_ident"][0:2, 0:2]), "bnp")
        ctx.wait("vector", p2)
        v3 = ctx.op("vector", lambda e: e.tensor_copy(t2[:, 1:3][0:1, :],
                                                      ptp[:, 128:130][0:1, :]), "nsd")
        ctx.wait("scalar", v3)
        a1 = ctx.op("scalar", lambda e: e.activation(t2[:, 3:5][0:1, :],
                                                    t2[:, 1:3][0:1, :], AF.Ln), "nsa")
        ctx.wait("vector", a1)
        v4 = ctx.op("vector", lambda e: e.tensor_tensor(
            t2[:, 5:6][0:1, :], t2[:, 3:4][0:1, :], t2[:, 4:5][0:1, :], OP.add), "nsd")
        ctx.wait("scalar", v4)
        a2 = ctx.op("scalar", lambda e: e.activation(t2[:, 6:7][0:1, :],
                                                    t2[:, 5:6][0:1, :], AF.Exp,
                                                    scale=0.5), "nsa")
        ctx.wait("vector", a2)
        v5 = ctx.op("vector", lambda e: e.tensor_scalar(
            t1b[:, 3:4][0:1, :], t2[:, 6:7][0:1, :], -1.02 * scale, None, OP.mult), "nsd")
        ctx.wait("tensor", v5)
        p3 = ctx.op("tensor", lambda e: e.matmul(
            pbnd[:, 0:1], sb["c_ones1b"][0:1, :], t1b[:, 3:4][0:1, :],
            start=True, stop=True), "bnp")
        ctx.wait("vector", p3)
        v6 = ctx.op("vector", lambda e: e.tensor_copy(cb[:, 0:1], pbnd[:, 0:1]), "nsd")
        return v6

    def mla_prep(self, st):
        ctx, sb, d, NT, TC = self.ctx, self.sb, self.d, self.NT, self.TC
        vfm = sb["vfm"]
        wkv = self.sbuf("wkv", [128, 8, LAT], BF16, st)
        wq = self.sbuf("wq", [128, 8, LAT], BF16, st)
        wkn = self.sbuf("wkn", [128, 2, 512], BF16, st)
        wqn = self.sbuf("wqn", [128, 2, 512], BF16, st)
        wv = self.sbuf("wv", [128, 2, DM], BF16, st)
        wrq = self.sbuf("wrq", [128, 2, 512], BF16, st)
        wrk = self.sbuf("wrk", [128, 8, 512], BF16, st)
        wt = []
        wt.append(ctx.dma("gpsimd", wkv[:, :, :],
                          d["m_kv_T"][:, :, :].rearrange("kc p l -> p kc l"), "wks"))
        wt.append(ctx.dma("gpsimd", wq[:, :, :],
                          d["m_q_T"][:, :, :].rearrange("kc p l -> p kc l"), "wks"))
        wt.append(ctx.dma("gpsimd", wkn[:, :, :],
                          d["m_knope_T"][:, :, :].rearrange("kc p l -> p kc l"), "wks"))
        wt.append(ctx.dma("gpsimd", wqn[:, :, :],
                          d["m_qnope_T"][:, :, :].rearrange("kc p l -> p kc l"), "wks"))
        wt.append(ctx.dma("gpsimd", wv[:, :, :],
                          d["m_v_T"][:, :, :].rearrange("kc p l -> p kc l"), "wks"))
        wt.append(ctx.dma("gpsimd", wrq[:, :, :],
                          d["m_rq_T"][:, :, :].rearrange("kc p l -> p kc l"), "wks"))
        wt.append(ctx.dma("gpsimd", wrk[:, :, :],
                          d["m_rk_T"][:, :, :].rearrange("kc p l -> p kc l"), "wks"))
        wtok = ctx.tok("wks")
        ctx.wait("tensor", wtok)

        latk = self.sbuf("latk", [128, 2, self.T], BF16, st)
        latq = self.sbuf("latq", [128, 2, self.T], BF16, st)
        lat_st = self.estack()
        pslat = self.psum("pslat", [128, 4, 512], F32, lat_st)
        ctx.wait("vector", ctx.tok("lnt"))
        ctx.wait("tensor", ctx.tok("lnt"))
        for (lat_sb, wsrc) in ((latk, wkv), (latq, wq)):
            for lc in range(2):
                for tcg in range(0, TC, 4):
                    ng = min(4, TC - tcg)
                    ctx.wait("tensor", ctx.tok("lpe"))
                    pv = None
                    for kc in range(8):
                        for j in range(ng):
                            tc = tcg + j
                            pv = ctx.op("tensor",
                                        lambda e, kc=kc, lc=lc, tc=tc, j=j, wsrc=wsrc:
                                        e.matmul(pslat[:, j, :],
                                                 wsrc[:, kc, lc * 128:(lc + 1) * 128],
                                                 vfm[:, kc, tc * 512:(tc + 1) * 512],
                                                 start=(kc == 0), stop=(kc == 7)),
                                        "lmm" if (kc == 7 and j == ng - 1) else None)
                    ctx.wait("vector", pv)
                    ev = None
                    for j in range(ng):
                        tc = tcg + j
                        ev = ctx.op("vector", lambda e, lat_sb=lat_sb, lc=lc, tc=tc, j=j:
                                    e.tensor_copy(lat_sb[:, lc, tc * 512:(tc + 1) * 512],
                                                  pslat[:, j, :]), "lpe")
        ctx.wait("tensor", ctx.tok("lpe"))
        if "dbg_latk" in self.dbg:
            dl = self.dram("dbg_latk", [128, 2, self.T], BF16, kind="ExternalOutput")
            ctx.wait("sync", ctx.tok("lpe"))
            ctx.dma("sync", dl[:, :, :], latk[:, :, :], "dbgs")
            dl2 = self.dram("dbg_latq", [128, 2, self.T], BF16, kind="ExternalOutput")
            ctx.dma("sync", dl2[:, :, :], latq[:, :, :], "dbgs")
            ctx.wait("sync", ctx.tok("dbgs"))
        ctx.barrier()
        self.close_stack(lat_st)

        pkr = self.psum("pkr", [128, 512], F32, st)
        pkn = self.psum("pkn", [128, 512], F32, st)
        pv_ = self.psum("pv", [128, 2, 512], F32, st)
        pqr = self.psum("pqr", [128, 512], F32, st)
        pqn = self.psum("pqn", [128, 512], F32, st)
        ptp = self.psum("ptp2", [128, 1024], BF16, st)
        cs = self.sbuf("cs", [128, 2, 512], F32, st)
        sn = self.sbuf("sn", [128, 2, 512], F32, st)
        rf1 = self.sbuf("rf1", [128, 512], F32, st)
        rf2 = self.sbuf("rf2", [128, 512], F32, st)
        ktm = self.sbuf("ktm", [128, 2, DM], BF16, st)
        qtm = self.sbuf("qtm", [128, 2, DM], BF16, st)
        vstg = self.sbuf("vstg", [128, 2, NH * 65], BF16, st)
        kstg = self.sbuf("kstg", [128, 8, 512], BF16, st)
        qstg = self.sbuf("qstg", [128, 8, 512], BF16, st)
        sqn = self.sbuf("sqn", [128, DM], BF16, st)
        s16 = self.sbuf("s16", [128, NH], F32, st)
        mq = self.sbuf("mq", [128, NT], F32, st)
        mk = self.sbuf("mk", [128, NT], F32, st)
        self.sbuf("bnd_t1", [128, 8], F32, st)
        self.sbuf("bnd_t1b", [128, 4], BF16, st)
        self.sbuf("bnd_t2", [128, 8], F32, st)

        ktm_free = [None, None]
        qtm_free = [None, None]
        vstg_free = [None, None]
        cs_free = [None, None]
        kq_stage_free = None
        for tt in range(NT):
            b = tt % 2
            r0, r1 = tt * 128, (tt + 1) * 128
            if cs_free[b] is not None:
                ctx.wait("sync", cs_free[b])
            lc1 = ctx.dma("sync", cs[:, b, :], d["c_cos"][r0:r1, :], "csl%d" % b)
            lc2 = ctx.dma("sync", sn[:, b, :], d["c_sinS"][r0:r1, :], "csl%d" % b)
            ctx.wait("tensor", ctx.tok("prd"), ctx.tok("pra"))
            pv_done = []
            for kc in range(8):
                pv_done.append(ctx.op("tensor", lambda e, kc=kc, r0=r0: e.matmul(
                    pkr[:, :], vfm[:, kc, r0:r0 + 128], wrk[:, kc, :],
                    start=(kc == 0), stop=(kc == 7)), "pmm" if kc == 7 else None))
            for lc in range(2):
                ctx.op("tensor", lambda e, lc=lc, r0=r0: e.matmul(
                    pkn[:, :], latk[:, lc, r0:r0 + 128], wkn[:, lc, :],
                    start=(lc == 0), stop=(lc == 1)), None)
                for fh in range(2):
                    ctx.op("tensor", lambda e, lc=lc, fh=fh, r0=r0: e.matmul(
                        pv_[:, fh, :], latk[:, lc, r0:r0 + 128],
                        wv[:, lc, fh * 512:(fh + 1) * 512],
                        start=(lc == 0), stop=(lc == 1)), None)
            for lc in range(2):
                ctx.op("tensor", lambda e, lc=lc, r0=r0: e.matmul(
                    pqr[:, :], latq[:, lc, r0:r0 + 128], wrq[:, lc, :],
                    start=(lc == 0), stop=(lc == 1)), None)
                pv_done.append(ctx.op("tensor", lambda e, lc=lc, r0=r0: e.matmul(
                    pqn[:, :], latq[:, lc, r0:r0 + 128], wqn[:, lc, :],
                    start=(lc == 0), stop=(lc == 1)), "pmm" if lc == 1 else None))
            mm_tok = ctx.tok("pmm")
            ctx.wait("vector", mm_tok, lc2)
            if ktm_free[b] is not None:
                ctx.wait("vector", *ktm_free[b])
            if qtm_free[b] is not None:
                ctx.wait("vector", *qtm_free[b])
            if "dbg_ktm" in self.dbg:
                ctx.wait("vector", ctx.tok("dbgs"))
            if vstg_free[b] is not None:
                ctx.wait("scalar", vstg_free[b])
                ctx.wait("gpsimd", vstg_free[b])
            kt, qt, vs = ktm[:, b], qtm[:, b], vstg[:, b]

            v1 = ctx.op("vector", lambda e, b=b: (
                e.tensor_tensor(rf1[:, :], pkr[:, :], cs[:, b, :], OP.mult),
                e.tensor_tensor(rf2[:, :], bass.AP(
                    pkr[:, :].tensor, pkr[:, :].offset + 16,
                    [list(p) for p in pkr[:, :].ap[:1]] + [[32, 16], [-16, 2], [1, 16]]),
                    sn[:, b, :], OP.mult))[-1], "prd")
            ctx.wait("vector", v1)
            v2 = ctx.op("vector", lambda e, kt=kt: e.tensor_tensor(
                bass.AP(kt.tensor, kt.offset,
                        [list(p) for p in kt.ap[:1]] + [[64, 16], [1, 32]]),
                rf1[:, :], rf2[:, :], OP.add), "prd")
            v3 = ctx.op("vector", lambda e, kt=kt: e.tensor_copy(
                bass.AP(kt.tensor, kt.offset + 32,
                        [list(p) for p in kt.ap[:1]] + [[64, 16], [1, 32]]),
                pkn[:, :]), "prd")
            ctx.wait("vector", v3)
            v4 = ctx.op("vector", lambda e, b=b: (
                e.tensor_tensor(rf1[:, :], pqr[:, :], cs[:, b, :], OP.mult),
                e.tensor_tensor(rf2[:, :], bass.AP(
                    pqr[:, :].tensor, pqr[:, :].offset + 16,
                    [list(p) for p in pqr[:, :].ap[:1]] + [[32, 16], [-16, 2], [1, 16]]),
                    sn[:, b, :], OP.mult))[-1], "prd")
            ctx.wait("vector", v4)
            v5 = ctx.op("vector", lambda e, qt=qt: e.tensor_tensor(
                bass.AP(qt.tensor, qt.offset,
                        [list(p) for p in qt.ap[:1]] + [[64, 16], [1, 32]]),
                rf1[:, :], rf2[:, :], OP.add), "prd")
            v6 = ctx.op("vector", lambda e, qt=qt: e.tensor_copy(
                bass.AP(qt.tensor, qt.offset + 32,
                        [list(p) for p in qt.ap[:1]] + [[64, 16], [1, 32]]),
                pqn[:, :]), "prd")
            cs_free[b] = v4
            ctx.wait("scalar", mm_tok)
            a1 = ctx.op("scalar", lambda e, vs=vs: e.activation(
                bass.AP(vs.tensor, vs.offset,
                        [list(p) for p in vs.ap[:1]] + [[65, 8], [1, 64]]),
                pv_[:, 0, :], AF.Copy), "pra")
            a2 = ctx.op("scalar", lambda e, vs=vs: e.activation(
                bass.AP(vs.tensor, vs.offset + 8 * 65,
                        [list(p) for p in vs.ap[:1]] + [[65, 8], [1, 64]]),
                pv_[:, 1, :], AF.Copy), "pra")
            g1 = ctx.op("gpsimd", lambda e, vs=vs: e.memset(
                bass.AP(vs.tensor, vs.offset + 64,
                        [list(p) for p in vs.ap[:1]] + [[65, 16], [1, 1]]), 1.0), "gpk")
            ctx.wait("sync", a1, a2, g1)
            vstg_free[b] = ctx.dma("sync", d["vtm"][tt, :, :], vs[:, :], "vst%d" % b)
            dbg_tok = None
            if "dbg_ktm" in self.dbg:
                if tt == 0:
                    self.dram("dbg_ktm", [self.T, DM], BF16, kind="ExternalOutput")
                    self.dram("dbg_qtm", [self.T, DM], BF16, kind="ExternalOutput")
                ctx.wait("sync", v3, v6)
                ctx.dma("sync", self.d["dbg_ktm"][tt * 128:(tt + 1) * 128, :], kt[:, :], "dbgs")
                dbg_tok = ctx.dma("sync", self.d["dbg_qtm"][tt * 128:(tt + 1) * 128, :], qt[:, :], "dbgs")
            nk, nka = self.tile_norms(kt[:, :], v3, mk, tt, sqn, s16)
            nq, nqa = self.tile_norms(qt[:, :], v6, mq, tt, sqn, s16)
            ctx.wait("tensor", v3, v6, ctx.tok("prt"))
            pe = None
            for fc in range(8):
                pe = ctx.op("tensor", lambda e, kt=kt, fc=fc: e.transpose(
                    ptp[:, fc * 128:(fc + 1) * 128], kt[:, fc * 128:(fc + 1) * 128],
                    sb["c_ident"][:, :]), "pp2" if fc == 7 else None)
            ctx.wait("vector", pe, nk, nq)
            c4 = tt % 4
            if c4 == 0 and kq_stage_free is not None:
                ctx.wait("vector", kq_stage_free)
            v8 = ctx.op("vector", lambda e, c4=c4: e.tensor_copy(
                kstg[:, :, c4 * 128:(c4 + 1) * 128],
                ptp[:, :].rearrange("p (f t) -> p f t", f=8)), "prt")
            ctx.wait("tensor", v8)
            pe2 = None
            for fc in range(8):
                pe2 = ctx.op("tensor", lambda e, qt=qt, fc=fc: e.transpose(
                    ptp[:, fc * 128:(fc + 1) * 128], qt[:, fc * 128:(fc + 1) * 128],
                    sb["c_ident"][:, :]), "pp2" if fc == 7 else None)
            ctx.wait("vector", pe2)
            v9 = ctx.op("vector", lambda e, c4=c4: e.tensor_copy(
                qstg[:, :, c4 * 128:(c4 + 1) * 128],
                ptp[:, :].rearrange("p (f t) -> p f t", f=8)), "prt")
            ktm_free[b] = [v8, nka]
            qtm_free[b] = [v9, nqa]
            if c4 == 3:
                grp = tt // 4
                ctx.wait("sync", v9)
                last = None
                for fc in range(8):
                    for half in range(2):
                        h = 2 * fc + half
                        last = ctx.dma("sync", d["kfm"][h, :, grp * 512:(grp + 1) * 512],
                                       kstg[:, fc, :][half * 64:(half + 1) * 64, :], "kqs")
                        last = ctx.dma("sync", d["qfm"][h, :, grp * 512:(grp + 1) * 512],
                                       qstg[:, fc, :][half * 64:(half + 1) * 64, :], "kqs")
                kq_stage_free = last
        pbnd = self.psum("pbnd", [128, 1], F32, st)
        bt = self.bound(mq, mk, 1.0 / 8.0, [ctx.tok("nsd")], ptp, pbnd)
        ctx.wait("sync", ctx.tok("kqs"), ctx.tok("vst0"), ctx.tok("vst1"))
        return bt

    def s2_prep(self, st):
        ctx, sb, d, NT = self.ctx, self.sb, self.d, self.NT
        vfm = sb["vfm"]
        waq = self.sbuf("waq", [128, 8, DM], BF16, st)
        wak = self.sbuf("wak", [128, 8, DM], BF16, st)
        wav = self.sbuf("wav", [128, 8, DM], BF16, st)
        wts = []
        for (w, nm) in ((waq, "a_q_T"), (wak, "a_k_T"), (wav, "a_v_T")):
            wts.append(ctx.dma("gpsimd", w[:, :, :],
                               d[nm][:, :, :].rearrange("kc p f -> p kc f"), "wks"))
        ctx.wait("tensor", ctx.tok("wks"))
        ctx.wait("vector", ctx.tok("lnt"))
        ctx.wait("tensor", ctx.tok("lnt"))

        pq = self.psum("s2pq", [128, 2, 512], F32, st)
        pk = self.psum("s2pk", [128, 2, 512], F32, st)
        pv_ = self.psum("s2pv", [128, 2, 512], F32, st)
        ptp = self.psum("s2ptp", [128, 1024], BF16, st)
        ktm = self.sbuf("s2ktm", [128, 2, DM], BF16, st)
        qtm = self.sbuf("s2qtm", [128, 2, DM], BF16, st)
        vstg = self.sbuf("s2vstg", [128, 2, NH * 65], BF16, st)
        kstg = self.sbuf("s2kstg", [128, 8, 512], BF16, st)
        qstg = self.sbuf("s2qstg", [128, 8, 512], BF16, st)
        f1t = self.sbuf("s2f1", [128, 2, DM], F32, st)
        sqnt = self.sbuf("s2sqn", [128, 2, DM], BF16, st)
        s16t = self.sbuf("s2s16", [128, 2, NH], F32, st)
        rrt = self.sbuf("s2rr", [128, 2, 2, NH], F32, st)
        mq = self.sbuf("s2mq", [128, NT], F32, st)
        mk = self.sbuf("s2mk", [128, NT], F32, st)
        self.sbuf("bnd_t1", [128, 8], F32, st)
        self.sbuf("bnd_t1b", [128, 4], BF16, st)
        self.sbuf("bnd_t2", [128, 8], F32, st)

        qkb = self.sbuf("s2qkb", [128, 2, 2, 2, 512], F32, st)
        ktm_free = [None, None]
        qtm_free = [None, None]
        vstg_free = [None, None]
        kq_state = [None]
        ps_free = None
        tile_meta = {}

        def emit_tp(t):
            bb = t % 2
            ktx, qtx = ktm[:, bb], qtm[:, bb]
            chain_done, nk, nq, nka, nqa = tile_meta.pop(t)
            ctx.wait("tensor", chain_done, ctx.tok("prt"))
            for fc in range(8):
                ctx.op("tensor", lambda e, ktx=ktx, fc=fc: e.transpose(
                    ptp[:, fc * 128:(fc + 1) * 128], ktx[:, fc * 128:(fc + 1) * 128],
                    sb["c_ident"][:, :]), "pp2" if fc == 7 else None)
            ctx.wait("vector", ctx.tok("pp2"), nk, nq)
            c4 = t % 4
            if c4 == 0 and kq_state[0] is not None:
                ctx.wait("vector", kq_state[0])
            v8 = ctx.op("vector", lambda e, c4=c4: e.tensor_copy(
                kstg[:, :, c4 * 128:(c4 + 1) * 128],
                ptp[:, :].rearrange("p (f t) -> p f t", f=8)), "prt")
            ctx.wait("tensor", v8)
            for fc in range(8):
                ctx.op("tensor", lambda e, qtx=qtx, fc=fc: e.transpose(
                    ptp[:, fc * 128:(fc + 1) * 128], qtx[:, fc * 128:(fc + 1) * 128],
                    sb["c_ident"][:, :]), "pp2" if fc == 7 else None)
            ctx.wait("vector", ctx.tok("pp2"))
            v9 = ctx.op("vector", lambda e, c4=c4: e.tensor_copy(
                qstg[:, :, c4 * 128:(c4 + 1) * 128],
                ptp[:, :].rearrange("p (f t) -> p f t", f=8)), "prt")
            ktm_free[bb] = [v8, nka]
            qtm_free[bb] = [v9, nqa]
            if c4 == 3:
                grp = t // 4
                ctx.wait("sync", v9)
                last = None
                for fc in range(8):
                    for half in range(2):
                        h = 2 * fc + half
                        last = ctx.dma("sync", d["kfm"][h, :, grp * 512:(grp + 1) * 512],
                                       kstg[:, fc, :][half * 64:(half + 1) * 64, :], "kqs")
                        last = ctx.dma("sync", d["qfm"][h, :, grp * 512:(grp + 1) * 512],
                                       qstg[:, fc, :][half * 64:(half + 1) * 64, :], "kqs")
                kq_state[0] = last

        for tt in range(NT):
            b = tt % 2
            r0 = tt * 128
            if ps_free is not None:
                ctx.wait("tensor", *ps_free)
            mm = None
            for kc in range(8):
                for (ps, w) in ((pq, waq), (pk, wak), (pv_, wav)):
                    for fh in range(2):
                        mm = ctx.op("tensor", lambda e, kc=kc, r0=r0, ps=ps, w=w, fh=fh:
                                    e.matmul(ps[:, fh, :], vfm[:, kc, r0:r0 + 128],
                                             w[:, kc, fh * 512:(fh + 1) * 512],
                                             start=(kc == 0), stop=(kc == 7)),
                                    "pmm" if (kc == 7 and ps is pv_ and fh == 1) else None)
            mm_tok = ctx.tok("pmm")
            kt, qt, vs = ktm[:, b], qtm[:, b], vstg[:, b]
            ctx.wait("vector", mm_tok, ctx.tok("prd"))
            evq = ctx.op("vector", lambda e, b=b: e.tensor_copy(
                qkb[:, b, 0, :, :], pq[:, :, :]), "prd")
            ctx.wait("vector", evq)
            evk = ctx.op("vector", lambda e, b=b: e.tensor_copy(
                qkb[:, b, 1, :, :], pk[:, :, :]), "prd")
            ctx.wait("scalar", mm_tok)
            if vstg_free[b] is not None:
                ctx.wait("scalar", vstg_free[b])
                ctx.wait("gpsimd", vstg_free[b])
            a4 = ctx.op("scalar", lambda e, vs=vs: e.activation(
                bass.AP(vs.tensor, vs.offset,
                        [list(p) for p in vs.ap[:1]] + [[65, 8], [1, 64]]),
                pv_[:, 0, :], AF.Copy), "pra")
            a5 = ctx.op("scalar", lambda e, vs=vs: e.activation(
                bass.AP(vs.tensor, vs.offset + 8 * 65,
                        [list(p) for p in vs.ap[:1]] + [[65, 8], [1, 64]]),
                pv_[:, 1, :], AF.Copy), "pra")
            g1 = ctx.op("gpsimd", lambda e, vs=vs: e.memset(
                bass.AP(vs.tensor, vs.offset + 64,
                        [list(p) for p in vs.ap[:1]] + [[65, 16], [1, 1]]), 1.0), "gpk")
            ctx.wait("sync", a4, a5, g1)
            vstg_free[b] = ctx.dma("sync", d["vtm"][tt, :, :], vs[:, :], "vst%d" % b)
            for (which, dst, gidx, extra) in ((0, qt, 4, True),
                                              (1, kt, 5, False)):
                ctx.wait("scalar", evk)
                a1 = ctx.op("scalar", lambda e, b=b, which=which: e.activation(
                    sqnt[:, b, 0:512], qkb[:, b, which, 0, :], AF.Square), "pra")
                a1b = ctx.op("scalar", lambda e, b=b, which=which: e.activation(
                    sqnt[:, b, 512:1024], qkb[:, b, which, 1, :], AF.Square), "pra")
                ctx.wait("vector", a1b)
                v1 = ctx.op("vector", lambda e, b=b: e.tensor_reduce(
                    s16t[:, b, :], sqnt[:, b, :].rearrange("p (h x) -> p h x", h=NH),
                    AX.X, OP.add), "prd")
                ctx.wait("vector", v1)
                v2 = ctx.op("vector", lambda e, b=b: e.tensor_scalar(
                    s16t[:, b, :], s16t[:, b, :], 1.0 / HD, 1e-16, OP.mult, OP.max), "prd")
                ctx.wait("scalar", v2)
                a2 = ctx.op("scalar", lambda e, b=b: e.activation(
                    f1t[:, b, 0:NH], s16t[:, b, :], AF.Ln), "pra")
                ctx.wait("scalar", a2)
                bias = sb["c_mln8"][:, 0:1] if extra else 0.0
                a3 = ctx.op("scalar", lambda e, b=b, which=which, bias=bias: e.activation(
                    rrt[:, b, which, :], f1t[:, b, 0:NH], AF.Exp, scale=-0.5, bias=bias), "pra")
                ctx.wait("vector", a3)
                if (qtm_free[b] if which == 0 else ktm_free[b]) is not None:
                    ctx.wait("vector", *(qtm_free[b] if which == 0 else ktm_free[b]))
                vz = None
                for fh in range(2):
                    src = qkb[:, b, which, fh, :]
                    rrs = rrt[:, b, which, fh * 8:(fh + 1) * 8]
                    rb = bass.AP(rrs.tensor, rrs.offset,
                                 [rrs.ap[0], [1, 8], [0, HD]])
                    f1s = f1t[:, b, fh * 512:(fh + 1) * 512]
                    sb3 = bass.AP(src.tensor, src.offset,
                                  [src.ap[0], [HD, 8], [1, HD]])
                    db3 = bass.AP(f1s.tensor, f1s.offset,
                                  [f1s.ap[0], [HD, 8], [1, HD]])
                    vz = ctx.op("vector", lambda e, db3=db3, sb3=sb3, rb=rb:
                                e.tensor_tensor(db3, sb3, rb, OP.mult),
                                "prd" if fh == 1 else None)
                ctx.wait("vector", vz)
                vg = ctx.op("vector", lambda e, b=b, dst=dst, gidx=gidx: e.tensor_tensor(
                    dst[:, :], f1t[:, b, :], self.gamma(gidx), OP.mult), "prd")
            nk, nka = self.tile_norms(kt[:, :], ctx.tok("prd"), mk, tt,
                                      sqnt[:, b], s16t[:, b])
            nq, nqa = self.tile_norms(qt[:, :], ctx.tok("prd"), mq, tt,
                                      sqnt[:, b], s16t[:, b])
            tile_meta[tt] = (ctx.tok("prd"), nk, nq, nka, nqa)
            ps_free = [evq, evk, a4, a5]
            if tt > 0:
                emit_tp(tt - 1)
        emit_tp(NT - 1)
        pbnd = self.psum("pbnd", [128, 1], F32, st)
        bt = self.bound(mq, mk, 1.0, [ctx.tok("nsd"), ctx.tok("prt")], ptp, pbnd)
        ctx.wait("sync", ctx.tok("kqs"), ctx.tok("vst0"), ctx.tok("vst1"))
        return bt

    def attention(self, causal, scale, cbias_dep, st):
        ctx, sb, d, NT, TC = self.ctx, self.sb, self.d, self.NT, self.TC
        cb = sb["cbias"]
        ctx.wait("scalar", cbias_dep)
        G = 2
        NG = 3
        kh = self.sbuf("at_kh", [128, 2, self.T], BF16, st)
        qh = self.sbuf("at_qh", [128, 2, self.T], BF16, st)
        vh = self.sbuf("at_vh", [128, NT, NH * 65], BF16, st)
        pt = self.sbuf("at_pt", [128, NG, G * 512], BF16, st)
        rl = self.sbuf("at_rl", [1, 2, 512], F32, st)
        rlb = self.sbuf("at_rlb", [1, 2, 512], BF16, st)
        rbc = self.sbuf("at_rbc", [64, 2, 512], F32, st)
        og = self.sbuf("at_og", [64, 2, 512], BF16, st)
        pss = self.psum("at_ps", [128, NG, G * 512], F32, st)
        po = self.psum("at_po", [128, 2, 512], F32, st)
        lvv = ctx.dma("sync", vh[:, :, :], d["vtm"][:, :, :].rearrange(
            "kt p c -> p kt c"), "akv")
        kv_free = [None, None]
        po_free = [None, None]
        og_free = [None, None]
        exp_free = [None] * NG
        av_free = [None] * NG

        kz = ctx.op("scalar", lambda e: (
            e.memzero(kh[:, :, :]), e.memzero(qh[:, :, :]))[-1], "akz")

        def issue_kq(h):
            hb = h % 2
            if kv_free[hb] is not None:
                ctx.wait("sync", *kv_free[hb])
            ctx.wait("sync", kz)
            ctx.dma("sync", kh[0:64, hb, :], d["kfm"][h, :, :], "akl%d" % hb)
            return ctx.dma("sync", qh[0:64, hb, :], d["qfm"][h, :, :], "akl%d" % hb)

        kq_toks = {0: issue_kq(0)}
        for h in range(NH):
            hb = h % 2
            if h + 1 < NH:
                kq_toks[h + 1] = issue_kq(h + 1)
            lq = kq_toks[h]
            ctx.wait("tensor", lq, lvv)
            for qc in range(TC):
                ob = qc % 2
                kt_hi = min(NT, (qc + 1) * 4) if causal else NT
                if po_free[ob] is not None:
                    ctx.wait("tensor", po_free[ob])
                groups = [(g0, min(G, kt_hi - g0)) for g0 in range(0, kt_hi, G)]

                def emit_S(gi):
                    ktg, gn = groups[gi]
                    g = gi % NG
                    if exp_free[g] is not None:
                        ctx.wait("tensor", exp_free[g])
                    sv = None
                    for i in range(gn):
                        kt = ktg + i
                        sv = ctx.op("tensor", lambda e, kt=kt, i=i, qc=qc, hb=hb, g=g:
                                    e.matmul(pss[:, g, i * 512:(i + 1) * 512],
                                             kh[:, hb, kt * 128:(kt + 1) * 128],
                                             qh[:, hb, qc * 512:(qc + 1) * 512],
                                             start=True, stop=True),
                                    "asm" if i == gn - 1 else None)
                    return sv

                def emit_exp(gi, sv):
                    ktg, gn = groups[gi]
                    g = gi % NG
                    ctx.wait("scalar", sv)
                    if av_free[g] is not None:
                        ctx.wait("scalar", av_free[g])
                    ex = ctx.op("scalar", lambda e, gn=gn, g=g: e.activation(
                        pt[:, g, 0:gn * 512], pss[:, g, 0:gn * 512], AF.Exp,
                        bias=cb[:, 0:1], scale=scale), "aex")
                    exp_free[g] = ex
                    mdep = ex
                    if causal:
                        for i in range(gn):
                            kt = ktg + i
                            j = kt - qc * 4
                            if 0 <= j <= 3:
                                ctx.wait("vector", ex)
                                if j > 0:
                                    mdep = ctx.op("vector", lambda e, i=i, j=j, g=g:
                                                  e.memset(pt[:, g, i * 512:i * 512 + j * 128], 0.0),
                                                  "amk")
                                mdep = ctx.op("vector", lambda e, i=i, j=j, g=g:
                                              e.tensor_tensor(
                                                  pt[:, g, i * 512 + j * 128:i * 512 + (j + 1) * 128],
                                                  pt[:, g, i * 512 + j * 128:i * 512 + (j + 1) * 128],
                                                  sb["c_mask"][:, :], OP.mult), "amk")
                    return mdep

                def emit_AV(gi, ex, mdep):
                    ktg, gn = groups[gi]
                    g = gi % NG
                    items = list(range(gn))
                    if causal:
                        items.sort(key=lambda i: (ktg + i != 0,
                                                  0 <= (ktg + i) - qc * 4 <= 3))
                    av = None
                    ndone = 0
                    for i in items:
                        kt = ktg + i
                        diag = causal and 0 <= kt - qc * 4 <= 3
                        ctx.wait("tensor", mdep if (diag or kt == 0) else ex)
                        ndone += 1
                        av = ctx.op("tensor", lambda e, kt=kt, i=i, ob=ob, h=h, g=g,
                                    first=(kt == 0), last=(kt == kt_hi - 1):
                                    e.matmul(po[0:65, ob, :],
                                             vh[:, kt, h * 65:(h + 1) * 65],
                                             pt[:, g, i * 512:(i + 1) * 512],
                                             start=first, stop=last),
                                    "aav" if ndone == gn else None)
                    av_free[g] = av
                    return av

                pend = []
                for gi in range(len(groups)):
                    sv = emit_S(gi)
                    md = emit_exp(gi, sv)
                    ex_tok = exp_free[gi % NG]
                    if len(pend) >= 2:
                        e0 = pend.pop(0)
                        emit_AV(*e0)
                    pend.append((gi, ex_tok, md))
                for e0 in pend:
                    emit_AV(*e0)
                ctx.wait("vector", av_free[(len(groups) - 1) % NG], ctx.tok("aav"))
                if og_free[ob] is not None:
                    ctx.wait("vector", og_free[ob])
                v1 = ctx.op("vector", lambda e, ob=ob: e.tensor_copy(
                    og[:, ob, :], po[0:64, ob, :]), "aed")
                ctx.wait("vector", v1)
                v2 = ctx.op("vector", lambda e, ob=ob: e.tensor_copy(
                    rl[:, ob, :], po[64:65, ob, :]), "aed")
                po_free[ob] = v2
                ctx.wait("sync", v2)
                ctx.dma("sync", d["lsum"][h, qc * 512:(qc + 1) * 512],
                        rl[0:1, ob, :], "aos%d" % ob)
                og_free[ob] = ctx.dma("sync", d["ofm"][h // 2, qc,
                                                       (h % 2) * 64:(h % 2) * 64 + 64, :],
                                      og[:, ob, :], "aos%d" % ob)
            kv_free[hb] = [ctx.tok("aav"), ctx.tok("asm")]
        ctx.wait("sync", ctx.tok("aos0"), ctx.tok("aos1"))

    def oproj(self, w_name, src, dst, add_bo, st):
        ctx, sb, d, NT = self.ctx, self.sb, self.d, self.NT
        wo = self.sbuf("opwo", [128, 8, DM], BF16, st)
        ctx.dma("gpsimd", wo[:, :, :],
                d[w_name][:, :, :].rearrange("dc p f -> p dc f"), "wks")
        ctx.wait("tensor", ctx.tok("wks"))
        osb = self.sbuf("oposb", [128, 2, 8, 512], BF16, st)
        osn = self.sbuf("oposn", [128, 2, 8, 128], BF16, st)
        osc = self.sbuf("oposc", [128, 2, 8, 512], BF16, st)
        res = self.sbuf("opres", [128, 4, DM], F32, st)
        out = self.sbuf("opout", [128, 4, DM], F32, st)
        lsb = self.sbuf("oplsb", [16, self.T], F32, st)
        pp = self.psum("oppp", [128, 2, 2, 512], F32, st)
        lt = ctx.dma("sync", lsb[:, :], d["lsum"][:, :], "opll")
        ctx.wait("vector", lt)
        vr = ctx.op("vector", lambda e: e.reciprocal(lsb[:, :], lsb[:, :]), "opd")
        ctx.wait("sync", vr)
        st1 = ctx.dma("sync", d["lsum"][:, :], lsb[:, :], "opll")
        ctx.wait("gpsimd", st1)
        ot_toks = {}
        oc_toks = {}
        rt_toks = {}
        vn_toks = {}
        mm_toks = {}
        vv_toks = {}
        so_toks = {}

        def emit_vv(t):
            b, b4 = t % 2, t % 4
            ctx.wait("vector", mm_toks[t], rt_toks[t])
            if t >= 4:
                ctx.wait("vector", so_toks[t - 4])
            ppf = pp[:, b, :, :].rearrange("p a c -> p (a c)")
            vv = ctx.op("vector", lambda e, b4=b4, ppf=ppf: e.tensor_tensor(
                out[:, b4, :], ppf, res[:, b4, :], OP.add), "opd")
            if add_bo:
                ctx.wait("vector", vv)
                vv = ctx.op("vector", lambda e, b4=b4: e.tensor_tensor(
                    out[:, b4, :], out[:, b4, :], sb["c_bo"][:, :], OP.add), "opd")
            vv_toks[t] = vv
            ctx.wait("sync", vv)
            so_toks[t] = ctx.dma("sync", dst[t * 128:(t + 1) * 128, :],
                                 out[:, b4, :], "opo%d" % (t % 2))

        for tt in range(NT):
            b = tt % 2
            b4 = tt % 4
            g = tt // 4
            gb = g % 2
            if tt % 4 == 0:
                if tt >= 8:
                    ctx.wait("sync", vn_toks[tt - 5])
                    ctx.wait("gpsimd", vn_toks[tt - 5])
                ot = None
                for fc in range(8):
                    ot = ctx.dma("sync", osb[:, gb, fc, :], d["ofm"][fc, g, :, :],
                                 "opl%d" % gb)
                ot_toks[g] = ot
                oc = None
                for fc in range(8):
                    for half in range(2):
                        oc = ctx.dma(
                            "gpsimd",
                            osc[:, gb, fc, :][half * 64:(half + 1) * 64, :],
                            bass.AP(d["lsum"].tensor,
                                    (2 * fc + half) * self.T + g * 512,
                                    [[0, 64], [1, 512]]), "opc%d" % gb)
                oc_toks[g] = oc
            if tt >= 4:
                ctx.wait("scalar", vv_toks[tt - 4])
            rt_toks[tt] = ctx.dma("scalar", res[:, b4, :],
                                  src[tt * 128:(tt + 1) * 128, :], "opr")
            ctx.wait("vector", ot_toks[g], oc_toks[g])
            if tt >= 2:
                ctx.wait("vector", mm_toks[tt - 2])
            vn = ctx.op("vector", lambda e, b=b, gb=gb, b4=b4: e.tensor_tensor(
                osn[:, b, :, :], osb[:, gb, :, b4 * 128:(b4 + 1) * 128],
                osc[:, gb, :, b4 * 128:(b4 + 1) * 128], OP.mult), "opn")
            vn_toks[tt] = vn
            ctx.wait("tensor", vn)
            if tt >= 2:
                ctx.wait("tensor", vv_toks[tt - 2])
            mm = None
            for dc in range(8):
                for fh in range(2):
                    mm = ctx.op("tensor", lambda e, dc=dc, fh=fh, b=b:
                                e.matmul(pp[:, b, fh, :],
                                         osn[:, b, dc, :],
                                         wo[:, dc, fh * 512:(fh + 1) * 512],
                                         start=(dc == 0), stop=(dc == 7)),
                                "pmm" if (dc == 7 and fh == 1) else None)
            mm_toks[tt] = mm
            if tt > 0:
                emit_vv(tt - 1)
        emit_vv(NT - 1)
        ctx.wait("sync", ctx.tok("opo0"), ctx.tok("opo1"))

    def moe_router_setup(self, st):
        ctx, sb, d, NT = self.ctx, self.sb, self.d, self.NT
        w8 = self.sbuf("moe_w", [128, NT, 8], F32, st)
        rt_st = self.estack()
        pr = self.psum("moe_pr", [128, 2, 8], F32, rt_st)
        ptpr = self.psum("moe_ptpr", [128, 2, 1024], BF16, rt_st)
        rf = self.sbuf("moe_rf", [128, 4, 8], F32, rt_st)
        self._zb = None
        rwf = self.sbuf("moe_rwf", [128, 8, 8], F32, rt_st)
        rwhi = self.sbuf("moe_rwhi", [128, 8, 8], BF16, rt_st)
        rwlo = self.sbuf("moe_rwlo", [128, 8, 8], BF16, rt_st)
        lot = self.sbuf("moe_lot", [128, 2, DM], BF16, rt_st)
        lofm = self.sbuf("moe_lofm", [128, 2, 8, 128], BF16, rt_st)
        hfm = self.sbuf("moe_hfm", [128, 2, 8, 128], BF16, rt_st)
        lw = ctx.dma("sync", rwf[:, :, :],
                     d["c_router"][:, :, :].rearrange("kc p e -> p kc e"), "rtl")
        ctx.wait("vector", lw)
        vh = ctx.op("vector", lambda e: e.tensor_copy(rwhi[:, :, :], rwf[:, :, :]), "rtd")
        ctx.wait("vector", vh)
        vl = ctx.op("vector", lambda e: e.tensor_tensor(
            rwlo[:, :, :], rwf[:, :, :], rwhi[:, :, :], OP.subtract), "rtd")
        g1 = ctx.op("gpsimd", lambda e: e.memset(w8[:, :, 7:8], 1.0), "gpk")
        zb = self.sbuf("moe_zb", [128, NT, 8], F32, rt_st)
        pp = self.sbuf("moe_pp", [128, NT, 8], F32, rt_st)
        ms = self.sbuf("moe_ms", [128, NT, 8], F32, rt_st)
        sc = self.sbuf("moe_sc", [128, 4, NT], F32, rt_st)
        self._rt = dict(w8=w8, rt_st=rt_st, pr=pr, ptpr=ptpr, rf=rf, rwhi=rwhi,
                        rwlo=rwlo, lot=lot, lofm=lofm, hfm=hfm, wdep=(vl, g1),
                        zb=zb, pp=pp, ms=ms, sc=sc)
        return self.moe_router_tile

    def moe_router_tile(self, tt, f1, g, v6, v7):
        ctx, sb = self.ctx, self.sb
        R = self._rt
        b = tt % 2
        pr, ptpr, rf, w8 = R["pr"], R["ptpr"], R["rf"], R["w8"]
        lot, lofm, hfm = R["lot"][:, b], R["lofm"][:, b], R["hfm"][:, b]
        ctx.wait("sync", v6)
        xst = ctx.dma("sync", self.d["xn_tm"][tt * 128:(tt + 1) * 128, :],
                      g[:, :], "xts")
        ctx.wait("vector", v6, ctx.tok("rmm"))
        vlo = ctx.op("vector", lambda e: e.tensor_tensor(
            lot[:, :], f1[:, :], g[:, :], OP.subtract), "rtd")
        ctx.wait("tensor", vlo, v6, ctx.tok("rtd"))
        pe = None
        for fc in range(8):
            ctx.op("tensor", lambda e, fc=fc: e.transpose(
                ptpr[:, 0, fc * 128:(fc + 1) * 128],
                g[:, fc * 128:(fc + 1) * 128], sb["c_ident"][:, :]), None)
            pe = ctx.op("tensor", lambda e, fc=fc: e.transpose(
                ptpr[:, 1, fc * 128:(fc + 1) * 128], lot[:, fc * 128:(fc + 1) * 128],
                sb["c_ident"][:, :]), "rtp" if fc == 7 else None)
        ctx.wait("vector", pe)
        vev = ctx.op("vector", lambda e: e.tensor_copy(
            hfm[:, :, :], ptpr[:, 0, :].rearrange("p (f t) -> p f t", f=8)), "rtd")
        vev = ctx.op("vector", lambda e: e.tensor_copy(
            lofm[:, :, :], ptpr[:, 1, :].rearrange("p (f t) -> p f t", f=8)), "rtd")
        ctx.wait("tensor", vev, *R["wdep"])
        mm = None
        n = 0
        for kc in range(8):
            hi = hfm[:, kc, :]
            lo = lofm[:, kc, :]
            for (sa, wb_) in ((hi, R["rwhi"][:, kc, :]), (hi, R["rwlo"][:, kc, :]),
                              (lo, R["rwhi"][:, kc, :]), (lo, R["rwlo"][:, kc, :])):
                n += 1
                mm = ctx.op("tensor", lambda e, sa=sa, wb_=wb_, b=b, first=(n == 1),
                            last=(n == 32): e.matmul(pr[:, b, :], sa, wb_,
                                                     start=first, stop=last),
                            "rmm" if n == 32 else None)
        ctx.wait("vector", mm)
        vz = ctx.op("vector", lambda e, b=b, tt=tt: e.scalar_tensor_tensor(
            R["zb"][:, tt, :], pr[:, b, :], 1.0, sb["c_rbias"][:, :],
            OP.mult, OP.add), "rtd")
        return (vz, xst)

    def moe_topk_batch(self):
        ctx, sb, NT = self.ctx, self.sb, self.NT
        R = self._rt
        zb, w8 = R["zb"], R["w8"]
        p = R["pp"]
        ms = R["ms"]
        m1, m2, den, tmp = (R["sc"][:, i, :] for i in range(4))
        ctx.wait("scalar", ctx.tok("rtd"))
        a1 = ctx.op("scalar", lambda e: e.activation(
            p[:, :, :], zb[:, :, :], AF.Exp, scale=-1.0), "rta")
        ctx.wait("vector", a1)
        v1 = ctx.op("vector", lambda e: e.tensor_scalar(
            p[:, :, :], p[:, :, :], 1.0, None, OP.add), "rtd")
        ctx.wait("vector", v1)
        v2 = ctx.op("vector", lambda e: e.reciprocal(p[:, :, :], p[:, :, :]), "rtd")
        ctx.wait("vector", v2)
        v3 = ctx.op("vector", lambda e: e.tensor_reduce(
            m1, p[:, :, 0:7], AX.X, OP.max), "rtd")
        ctx.wait("vector", v3)
        for j in range(7):
            ctx.op("vector", lambda e, j=j: e.tensor_tensor(
                ms[:, :, j], p[:, :, j], m1, OP.is_lt), "rtd")
        ctx.wait("vector", ctx.tok("rtd"))
        for j in range(7):
            ctx.op("vector", lambda e, j=j: e.tensor_tensor(
                ms[:, :, j], p[:, :, j], ms[:, :, j], OP.mult), "rtd")
        ctx.wait("vector", ctx.tok("rtd"))
        v4 = ctx.op("vector", lambda e: e.tensor_reduce(
            m2, ms[:, :, 0:7], AX.X, OP.max), "rtd")
        ctx.wait("vector", v4)
        v5 = ctx.op("vector", lambda e: e.tensor_tensor(den, m1, m2, OP.add), "rtd")
        ctx.wait("vector", v5)
        v6 = ctx.op("vector", lambda e: e.reciprocal(den, den), "rtd")
        ctx.wait("vector", v6)
        for j in range(7):
            ctx.op("vector", lambda e, j=j: e.tensor_tensor(
                ms[:, :, j], p[:, :, j], m2, OP.is_ge), "rtd")
        ctx.wait("vector", ctx.tok("rtd"))
        for j in range(7):
            ctx.op("vector", lambda e, j=j: e.tensor_tensor(
                tmp, p[:, :, j], den, OP.mult), "rtd")
            ctx.wait("vector", ctx.tok("rtd"))
            ctx.op("vector", lambda e, j=j: e.tensor_tensor(
                w8[:, :, j], tmp, ms[:, :, j], OP.mult), "rtd")
            ctx.wait("vector", ctx.tok("rtd"))
        return ctx.tok("rtd")

    def moe_routing_meta(self, st):
        ctx, sb, NT, NCH = self.ctx, self.sb, self.NT, self.NCH
        I32 = mybir.dt.int32
        R = self._rt
        rt_st = R["rt_st"]
        ms, w8 = R["ms"], R["w8"]

        def A(base, dims, off=0):
            return bass.AP(base.tensor, base.offset + off, dims)
        msb, w8b = ms[:, :, :], w8[:, :, :]
        scn_a = self.sbuf("rm_scna", [128, NT, 7], F32, rt_st)
        scn_b = self.sbuf("rm_scnb", [128, NT, 7], F32, rt_st)
        cnt8 = self.sbuf("rm_cnt", [128, 8], F32, rt_st)
        cntb8 = self.sbuf("rm_cntb", [128, 8], BF16, rt_st)
        pre8 = self.sbuf("rm_pre", [128, 8], F32, rt_st)
        tot8 = self.sbuf("rm_tot", [128, 8], F32, rt_st)
        ns8 = self.sbuf("rm_ns", [128, 8], F32, rt_st)
        segi8 = self.sbuf("rm_segi", [128, 8], F32, rt_st)
        sege8 = self.sbuf("rm_sege", [128, 8], F32, rt_st)
        base8 = self.sbuf("rm_base", [128, 8], F32, rt_st)
        t37 = self.sbuf("rm_t37", [128, 7, NCH], F32, rt_st)
        t77 = self.sbuf("rm_t77", [128, NT, 7, 7], F32, rt_st)
        fst = self.sbuf("rm_fst", [128, NT, 7], F32, rt_st)
        Sdt = self.sbuf("rm_sd", [128, NT, 7], F32, rt_st)
        sAf = self.sbuf("rm_saf", [128, NT], F32, rt_st)
        sBf = self.sbuf("rm_sbf", [128, NT], F32, rt_st)
        wAf = self.sbuf("rm_waf", [128, NT], F32, rt_st)
        wBf = self.sbuf("rm_wbf", [128, NT], F32, rt_st)
        teid = self.sbuf("rm_teid", [128, NCH, 7], F32, rt_st)
        eidv = self.sbuf("rm_eid", [128, NCH], F32, rt_st)
        tgu = self.sbuf("rm_tgu", [128, NCH, 16], F32, rt_st)
        pmt = self.psum("rm_pmt", [128, 2, 8], F32, rt_st)
        sAi = self.sbuf("rm_sai", [128, NT], I32, st, side="right")
        sBi = self.sbuf("rm_sbi", [128, NT], I32, st, side="right")
        idx_gu = self.sbuf("rm_igu", [128, NCH, 16], I32, st, side="right")
        idx_dn = self.sbuf("rm_idn", [128, NCH], I32, st, side="right")
        self._sp = dict(sAi=sAi, sBi=sBi, idx_gu=idx_gu, idx_dn=idx_dn)

        def v(fn):
            t = ctx.op("vector", fn, "rtd")
            ctx.wait("vector", t)
            return t
        pm = msb.ap[0]
        sel3 = A(msb, [pm, [8, NT], [1, 7]])
        selT = A(msb, [pm, [1, 7], [8, NT]])
        sel4 = A(msb, [pm, [8, NT], [0, 7], [1, 7]])
        w83 = A(w8b, [w8b.ap[0], [8, NT], [1, 7]])
        lt7_b, le7_b = self.rmeta("lt7", 49), self.rmeta("le7", 49)
        thr_b, chk_b = self.rmeta("thr", NCH), self.rmeta("chunk", NCH)
        ipp_b, pid_b = self.rmeta("ipp", 16), self.rmeta("pid", 1)

        v(lambda e: e.tensor_reduce(cnt8[:, 0:7], selT, AX.X, OP.add))
        v(lambda e: e.tensor_copy(cntb8[:, 0:7], cnt8[:, 0:7]))
        v(lambda e: e.tensor_copy(scn_a[:, :, :], sel3))
        src, dst = scn_a, scn_b
        k = 1
        while k < NT:
            v(lambda e, s=src, d2=dst, k=k: e.tensor_copy(
                d2[:, 0:k, :], s[:, 0:k, :]))
            v(lambda e, s=src, d2=dst, k=k: e.tensor_tensor(
                d2[:, k:NT, :], s[:, k:NT, :], s[:, 0:NT - k, :], OP.add))
            src, dst = dst, src
            k *= 2
        scn_e, tmps = src, dst
        v(lambda e, s=scn_e: e.tensor_tensor(
            s[:, :, :], s[:, :, :], sel3, OP.subtract))
        ctx.wait("tensor", ctx.tok("rtd"), ctx.tok("gpk"))
        ctx.op("tensor", lambda e: e.matmul(
            pmt[:, 0, 0:7], sb["c_mask"][:, :], cntb8[:, 0:7],
            start=True, stop=True), None)
        mm2 = ctx.op("tensor", lambda e: e.matmul(
            pmt[:, 1, 0:7], sb["c_ones128"][:, :], cntb8[:, 0:7],
            start=True, stop=True), "rmm")
        ctx.wait("vector", mm2)
        v(lambda e: e.tensor_tensor(
            pre8[:, 0:7], pmt[:, 0, 0:7], cnt8[:, 0:7], OP.subtract))
        v(lambda e: e.tensor_copy(tot8[:, 0:7], pmt[:, 1, 0:7]))
        v(lambda e: e.tensor_tensor(
            t37[:, :, :], A(tot8[:, :], [tot8[:, :].ap[0], [1, 7], [0, NCH]]),
            A(thr_b, [thr_b.ap[0], [0, 7], [1, NCH]]), OP.is_gt))
        v(lambda e: e.tensor_reduce(ns8[:, 0:7], t37[:, :, :], AX.X, OP.add))
        v(lambda e: e.tensor_tensor(
            t37[:, :, 0:7], A(ns8[:, :], [ns8[:, :].ap[0], [0, 7], [1, 7]]),
            A(le7_b, [le7_b.ap[0], [7, 7], [1, 7]]), OP.mult))
        v(lambda e: e.tensor_reduce(segi8[:, 0:7], t37[:, :, 0:7], AX.X, OP.add))
        v(lambda e: e.tensor_tensor(
            sege8[:, 0:7], segi8[:, 0:7], ns8[:, 0:7], OP.subtract))
        v(lambda e: e.scalar_tensor_tensor(
            base8[:, 0:7], sege8[:, 0:7], 512.0, pre8[:, 0:7], OP.mult, OP.add))
        v(lambda e: e.tensor_tensor(
            Sdt[:, :, :], scn_e[:, :, :],
            A(base8[:, :], [base8[:, :].ap[0], [0, NT], [1, 7]]), OP.add))
        v(lambda e: e.tensor_tensor(
            t77[:, :, :, :], sel4,
            A(lt7_b, [lt7_b.ap[0], [0, NT], [7, 7], [1, 7]]), OP.mult))
        v(lambda e: e.tensor_reduce(fst[:, :, :], t77[:, :, :, :], AX.X, OP.add))
        v(lambda e: e.tensor_scalar(
            fst[:, :, :], fst[:, :, :], 0.0, None, OP.is_equal))
        v(lambda e: e.tensor_tensor(fst[:, :, :], fst[:, :, :], sel3, OP.mult))
        v(lambda e: e.tensor_tensor(
            tmps[:, :, :], Sdt[:, :, :], fst[:, :, :], OP.mult))
        v(lambda e: e.tensor_reduce(sAf[:, :], tmps[:, :, :], AX.X, OP.add))
        v(lambda e: e.tensor_tensor(tmps[:, :, :], Sdt[:, :, :], sel3, OP.mult))
        v(lambda e: e.tensor_reduce(sBf[:, :], tmps[:, :, :], AX.X, OP.add))
        v(lambda e: e.tensor_tensor(sBf[:, :], sBf[:, :], sAf[:, :], OP.subtract))
        v(lambda e: e.tensor_tensor(tmps[:, :, :], w83, fst[:, :, :], OP.mult))
        v(lambda e: e.tensor_reduce(wAf[:, :], tmps[:, :, :], AX.X, OP.add))
        v(lambda e: e.tensor_reduce(wBf[:, :], w83, AX.X, OP.add))
        v(lambda e: e.tensor_tensor(wBf[:, :], wBf[:, :], wAf[:, :], OP.subtract))
        v(lambda e: e.tensor_scalar(
            sAf[:, :], sAf[:, :], float(self.NSLOT - 1), None, OP.min))
        v(lambda e: e.tensor_scalar(
            sBf[:, :], sBf[:, :], float(self.NSLOT - 1), None, OP.min))
        v(lambda e: e.tensor_copy(sAi[:, :], sAf[:, :]))
        v(lambda e: e.tensor_copy(sBi[:, :], sBf[:, :]))
        v(lambda e: e.tensor_tensor(
            teid[:, :, :], A(chk_b, [chk_b.ap[0], [1, NCH], [0, 7]]),
            A(segi8[:, :], [segi8[:, :].ap[0], [0, NCH], [1, 7]]), OP.is_ge))
        v(lambda e: e.tensor_reduce(eidv[:, :], teid[:, :, :], AX.X, OP.add))
        v(lambda e: e.scalar_tensor_tensor(
            tgu[:, :, :], A(eidv[:, :], [eidv[:, :].ap[0], [1, NCH], [0, 16]]),
            2048.0, A(ipp_b, [ipp_b.ap[0], [0, NCH], [1, 16]]), OP.mult, OP.add))
        v(lambda e: e.tensor_copy(idx_gu[:, :, :], tgu[:, :, :]))
        v(lambda e: e.scalar_tensor_tensor(
            eidv[:, :], eidv[:, :], 128.0,
            A(pid_b, [pid_b.ap[0], [0, NCH]]), OP.mult, OP.add))
        v(lambda e: e.tensor_copy(idx_dn[:, :], eidv[:, :]))
        ctx.wait("sync", ctx.tok("rtd"))
        ctx.dma("sync", self.d["ridx"][0, :, :], sAi[:, :], "rms")
        ctx.dma("sync", self.d["ridx"][1, :, :], sBi[:, :], "rms")
        ctx.dma("sync", self.d["rw"][0, :, :], wAf[:, :], "rms")
        ctx.dma("sync", self.d["rw"][1, :, :], wBf[:, :], "rms")
        self._sp["meta_tok"] = ctx.tok("rtd")
        self._sp["store_tok"] = ctx.tok("rms")

    def moe_experts(self, st):
        ctx, sb, d, NT, T = self.ctx, self.sb, self.d, self.NT, self.T
        NSH, NCH = self.NSH, self.NCH
        NK = NSH + NCH
        w8 = self._rt["w8"]
        rt_st = self._rt["rt_st"]
        self.moe_topk_batch()
        self.moe_routing_meta(st)
        if "dbg_w8" in self.dbg:
            dw = self.dram("dbg_w8", [128, NT, 8], F32, kind="ExternalOutput")
            ctx.wait("sync", ctx.tok("rtd"), ctx.tok("gpk"))
            ctx.dma("sync", dw[:, :, :], w8[:, :, :], "dbgs")
            ctx.wait("sync", ctx.tok("dbgs"))
        ctx.barrier()
        self.close_stack(rt_st)
        SP = self._sp
        sAi, sBi = SP["sAi"], SP["sBi"]
        idx_gu, idx_dn = SP["idx_gu"], SP["idx_dn"]
        gu = self.sbuf("moe_gu", [128, 4, 2, 8, 128], BF16, st)
        dnb = self.sbuf("moe_dn", [128, 2, 16, 2, 512], BF16, st)
        hsb = self.sbuf("moe_h", [128, 16, 512], BF16, st)
        stg = self.sbuf("moe_stg", [128, 4, DM], F32, st)
        eo_sb = self.sbuf("moe_eo", [128, 2, 4, DM], BF16, st)
        xtm = self.sbuf("moe_xtm", [128, 2, 4, DM], BF16, st)
        xfm = self.sbuf("moe_xfm", [128, 2, 8, 512], BF16, st)
        pay = self.sbuf("moe_pay", [128, 2, DM], BF16, st)
        sg = self.sbuf("moe_sg", [128, 512], BF16, st)
        ptx = self.psum("moe_ptx", [128, 2, 1024], BF16, st)
        pg = self.psum("moe_pg", [128, 512], F32, st)
        pu = self.psum("moe_pu", [128, 512], F32, st)
        pg2 = self.psum("moe_pg2", [128, 512], F32, st)
        pu2 = self.psum("moe_pu2", [128, 512], F32, st)
        pd = self.psum("moe_pd", [128, 2, 512], F32, st)
        guview = d["gu_bf"][:, :]
        dnview = d["dn_bf"][:, :]
        wc, xz = self._wprep
        ctx.wait("gpsimd", wc)

        def flat2(apx, n):
            return bass.AP(apx.tensor, apx.offset, [apx.ap[0], [1, n]])

        S = dict(gu_free=[None] * 4, h_evac=[None, None], pd_free=[None, None],
                 ptx_free=[None, None], tp_done=[None, None],
                 ev_done=[None, None], gu_last=[None, None], stg_free=None,
                 eo_free=[None, None], pay_free=[None] * 4, dn_tok={},
                 down_done={}, xtl_tok={})

        def dn_slot(k):
            if k < NSH:
                return 0
            return 1 if (k - NSH) % 2 == 0 else 0

        def dn_load(k):
            s = dn_slot(k)
            prev = None
            if k == NSH + 1:
                prev = NSH - 1
            elif k >= NSH + 2:
                prev = k - 2
            if prev is not None:
                ctx.wait("gpsimd", S["down_done"][prev])
            if k < NSH:
                t = ctx.dma("gpsimd", flat2(dnb[:, s, :, :, :], 16384),
                            dnview[7 * 128:8 * 128, :], "mdw%d" % s)
            else:
                ctx.wait("gpsimd", SP["meta_tok"])
                rc = k - NSH
                t = ctx.idma(flat2(dnb[:, s, :, :, :], 16384), None, dnview,
                             IOA(idx_dn[:, rc:rc + 1], 0), "mdw%d" % s)
            S["dn_tok"][k] = t

        def issue_gu(k, ip):
            rb = ip % 4
            if S["gu_free"][rb] is not None:
                ctx.wait("gpsimd", S["gu_free"][rb])
            if k < NSH:
                r0 = (7 * 16 + ip) * 128
                return ctx.dma("gpsimd", flat2(gu[:, rb, :, :, :], 2048),
                               guview[r0:r0 + 128, :], "mgw%d" % rb)
            ctx.wait("gpsimd", SP["meta_tok"])
            rc = k - NSH
            return ctx.idma(flat2(gu[:, rb, :, :, :], 2048), None, guview,
                            IOA(idx_gu[:, rc, ip:ip + 1], 0), "mgw%d" % rb)

        def xtl(k):
            kb = k % 2
            ctx.wait("sync", S["tp_done"][kb])
            if k == NSH:
                ctx.wait("sync", ctx.tok("dsp0"), ctx.tok("dsp1"))
            src = (d["xn_tm"] if k < NSH else d["xnd"])
            off = (k if k < NSH else k - NSH) * 512 * DM
            S["xtl_tok"][k] = ctx.dma(
                "sync", xtm[:, kb, :, :],
                bass.AP(src.tensor, off, [[DM, 128], [128 * DM, 4], [1, DM]]),
                "xtl%d" % kb)

        def tp_group(k):
            kb = k % 2
            last = None
            for j in range(4):
                tb = j % 2
                ctx.wait("tensor", S["xtl_tok"][k], S["ptx_free"][tb])
                pe = None
                for fc in range(8):
                    pe = ctx.op("tensor", lambda e, kb=kb, j=j, tb=tb, fc=fc:
                                e.transpose(
                                    ptx[:, tb, fc * 128:(fc + 1) * 128],
                                    xtm[:, kb, j, fc * 128:(fc + 1) * 128],
                                    sb["c_ident"][:, :]),
                                "mtp" if fc == 7 else None)
                ctx.wait("vector", pe, S["gu_last"][kb])
                base = xfm[:, :, :, :]
                dst = bass.AP(base.tensor, base.offset + kb * 8 * 512 + j * 128,
                              [base.ap[0], [512, 8], [1, 128]])
                last = ctx.op("vector", lambda e, tb=tb, dst=dst: e.tensor_copy(
                    dst, ptx[:, tb, :].rearrange("p (f t) -> p f t", f=8)), "mte")
                S["ptx_free"][tb] = last
            S["tp_done"][kb] = ctx.tok("mtp")
            S["ev_done"][kb] = last

        def gu_mm(k):
            kb = k % 2
            gu_toks = {}
            for ip in range(3):
                gu_toks[ip] = issue_gu(k, ip)
            mm = None
            for ip in range(16):
                rb = ip % 4
                if ip + 3 < 16:
                    gu_toks[ip + 3] = issue_gu(k, ip + 3)
                pga, pua = (pg, pu) if ip % 2 == 0 else (pg2, pu2)
                ctx.wait("tensor", gu_toks[ip], S["ev_done"][kb])
                if S["h_evac"][ip % 2] is not None:
                    ctx.wait("tensor", *S["h_evac"][ip % 2])
                for kc in range(8):
                    mm = ctx.op("tensor", lambda e, kc=kc, rb=rb, kb=kb, pga=pga:
                                e.matmul(pga[:, :], gu[:, rb, 0, kc, :],
                                         xfm[:, kb, kc, :],
                                         start=(kc == 0), stop=(kc == 7)), None)
                    mm = ctx.op("tensor", lambda e, kc=kc, rb=rb, kb=kb, pua=pua:
                                e.matmul(pua[:, :], gu[:, rb, 1, kc, :],
                                         xfm[:, kb, kc, :],
                                         start=(kc == 0), stop=(kc == 7)),
                                "gmm" if kc == 7 else None)
                S["gu_free"][rb] = mm
                ctx.wait("scalar", mm, ctx.tok("mhd"))
                aa = ctx.op("scalar", lambda e, pga=pga: e.activation(
                    sg[:, :], pga[:, :], AF.Silu), "mha")
                ctx.wait("vector", aa, ctx.tok("dmm"))
                hv = ctx.op("vector", lambda e, ip=ip, pua=pua: e.tensor_tensor(
                    hsb[:, ip, :], sg[:, :], pua[:, :], OP.mult), "mhd")
                S["h_evac"][ip % 2] = [aa, hv]
            S["gu_last"][kb] = mm

        def dispatch(k):
            for tj in range(4):
                tt = 4 * k + tj
                pb = tt % 2
                if S["pay_free"][pb] is not None:
                    ctx.wait("sync", *S["pay_free"][pb])
                pl = ctx.dma("sync", pay[:, pb, :],
                             d["xn_tm"][tt * 128:(tt + 1) * 128, :], "pyl%d" % pb)
                ctx.wait("gpsimd", pl, SP["meta_tok"], xz)
                t1 = ctx.idma(d["xnd"][:, :], IOA(sAi[:, tt:tt + 1], 0),
                              pay[:, pb, :], None, "dsp%d" % pb)
                t2 = ctx.idma(d["xnd"][:, :], IOA(sBi[:, tt:tt + 1], 0),
                              pay[:, pb, :], None, "dsp%d" % pb)
                S["pay_free"][pb] = [t2]

        def down(k):
            kb = k % 2
            s = dn_slot(k)
            shared = k < NSH
            dtok = S["dn_tok"][k if (k >= NSH or k == 0) else 0]
            ctx.wait("tensor", ctx.tok("mhd"), dtok)
            mm = None
            for tt4 in range(4):
                for fh in range(2):
                    db = (tt4 * 2 + fh) % 2
                    if S["pd_free"][db] is not None:
                        ctx.wait("tensor", S["pd_free"][db])
                    for i in range(16):
                        mm = ctx.op("tensor",
                                    lambda e, i=i, fh=fh, db=db, s=s, tt4=tt4:
                                    e.matmul(
                                        pd[:, db, :],
                                        hsb[:, i, tt4 * 128:(tt4 + 1) * 128],
                                        dnb[:, s, i, fh, :],
                                        start=(i == 0), stop=(i == 15)),
                                    "dmm" if i == 15 else None)
                    ctx.wait("vector", mm)
                    if shared:
                        if tt4 == 0 and fh == 0 and S["stg_free"] is not None:
                            ctx.wait("vector", S["stg_free"])
                        ev = ctx.op("vector", lambda e, tt4=tt4, fh=fh, db=db:
                                    e.tensor_copy(
                                        stg[:, tt4, fh * 512:(fh + 1) * 512],
                                        pd[:, db, :]), "mdd")
                    else:
                        if tt4 == 0 and fh == 0 and S["eo_free"][kb] is not None:
                            ctx.wait("vector", *S["eo_free"][kb])
                        ev = ctx.op("vector",
                                    lambda e, kb=kb, tt4=tt4, fh=fh, db=db:
                                    e.tensor_copy(
                                        eo_sb[:, kb, tt4, fh * 512:(fh + 1) * 512],
                                        pd[:, db, :]), "mdd")
                    S["pd_free"][db] = ev
                if tt4 == 3:
                    if shared:
                        ctx.wait("sync", ev, ctx.tok("mac"))
                        S["stg_free"] = ctx.dma(
                            "sync",
                            bass.AP(d["acc"].tensor, k * 512 * DM,
                                    [[DM, 128], [128 * DM, 4], [1, DM]]),
                            stg[:, :, :], "mac")
                    else:
                        rc = k - NSH
                        ctx.wait("sync", ev)
                        so = ctx.dma(
                            "sync",
                            bass.AP(d["eod"].tensor, rc * 512 * DM,
                                    [[DM, 128], [128 * DM, 4], [1, DM]]),
                            eo_sb[:, kb, :, :], "meo%d" % kb)
                        S["eo_free"][kb] = [so]
            S["down_done"][k] = ctx.tok("dmm")

        dn_load(0)
        xtl(0)
        for k in range(NK):
            if k == 0:
                tp_group(0)
            gu_mm(k)
            if k < NSH:
                dispatch(k)
            if k + 1 < NK:
                if k + 1 >= NSH:
                    dn_load(k + 1)
                xtl(k + 1)
                tp_group(k + 1)
            down(k)
        ctx.wait("sync", ctx.tok("mac"), ctx.tok("meo0"), ctx.tok("meo1"))

    def combine(self, a, bsrc, csrc, dst, st):
        ctx, NT = self.ctx, self.NT
        ta = self.sbuf("cb_a", [128, 2, DM], F32, st)
        tb = self.sbuf("cb_b", [128, 2, DM], F32, st)
        tc_ = self.sbuf("cb_c", [128, 2, DM], F32, st)
        to = self.sbuf("cb_o", [128, 2, DM], F32, st)
        free = [None, None]
        for tt in range(NT):
            b = tt % 2
            r0, r1 = tt * 128, (tt + 1) * 128
            if free[b] is not None:
                ctx.wait("sync", *free[b])
            l1 = ctx.dma("sync", ta[:, b, :], a[r0:r1, :], "cbl%d" % b)
            l2 = ctx.dma("sync", tb[:, b, :], bsrc[r0:r1, :], "cbl%d" % b)
            l3 = ctx.dma("sync", tc_[:, b, :], csrc[r0:r1, :], "cbl%d" % b)
            ctx.wait("vector", l3)
            v1 = ctx.op("vector", lambda e, b=b: e.tensor_tensor(
                to[:, b, :], tb[:, b, :], tc_[:, b, :], OP.add), "cbd")
            ctx.wait("vector", v1)
            v2 = ctx.op("vector", lambda e, b=b: e.scalar_tensor_tensor(
                to[:, b, :], to[:, b, :], 0.5, ta[:, b, :], OP.mult, OP.add), "cbd")
            ctx.wait("sync", v2)
            so = ctx.dma("sync", dst[r0:r1, :], to[:, b, :], "cbs%d" % b)
            free[b] = [so, v2]
        ctx.wait("sync", ctx.tok("cbs0"), ctx.tok("cbs1"))

    def combine_moe(self, a, bsrc, dst, st):
        ctx, d, NT = self.ctx, self.d, self.NT
        I32 = mybir.dt.int32
        ta = self.sbuf("cm_a", [128, 2, DM], F32, st)
        tb = self.sbuf("cm_b", [128, 2, DM], F32, st)
        tc_ = self.sbuf("cm_c", [128, 2, DM], F32, st)
        to = self.sbuf("cm_o", [128, 2, DM], F32, st)
        gsb = self.sbuf("cm_g", [128, 2, 2, DM], BF16, st)
        zsb = self.sbuf("cm_z", [128, 2, 2, DM], F32, st)
        six = self.sbuf("cm_six", [128, 2, NT], I32, st)
        wx = self.sbuf("cm_wx", [128, 2, NT], F32, st)
        l0 = ctx.dma("sync", six[:, :, :],
                     d["ridx"][:, :, :].rearrange("j p t -> p j t"), "cbr")
        l1 = ctx.dma("sync", wx[:, :, :],
                     d["rw"][:, :, :].rearrange("j p t -> p j t"), "cbr2")
        ctx.wait("gpsimd", l0)
        ctx.wait("scalar", l1)
        load_toks = {}
        gath_toks = {}
        v4s = {}
        zfree = [None, None]
        gfree = [None, None]

        def loads(tt):
            b = tt % 2
            if tt >= 2:
                ctx.wait("sync", ctx.tok("cbs%d" % b), v4s[tt - 2])
            ctx.dma("sync", ta[:, b, :], a[tt * 128:(tt + 1) * 128, :],
                    "cbl%d" % b)
            ctx.dma("sync", tb[:, b, :], bsrc[tt * 128:(tt + 1) * 128, :],
                    "cbl%d" % b)
            load_toks[tt] = ctx.dma(
                "sync", tc_[:, b, :], d["acc"][tt * 128:(tt + 1) * 128, :],
                "cbl%d" % b)

        def gathers(tt):
            b = tt % 2
            if gfree[b] is not None:
                ctx.wait("gpsimd", *gfree[b])
            g1 = ctx.idma(gsb[:, b, 0, :], None, d["eod"][:, :],
                          IOA(six[:, 0, tt:tt + 1], 0), "cbg%d" % (2 * b))
            g2 = ctx.idma(gsb[:, b, 1, :], None, d["eod"][:, :],
                          IOA(six[:, 1, tt:tt + 1], 0), "cbg%d" % (2 * b + 1))
            gath_toks[tt] = (g1, g2)

        loads(0)
        loads(1)
        gathers(0)
        for tt in range(NT):
            b = tt % 2
            g1, g2 = gath_toks[tt]
            ctx.wait("scalar", g1, g2, ctx.tok("cba"), zfree[b])
            z1 = ctx.op("scalar", lambda e, b=b, tt=tt: e.activation(
                zsb[:, b, 0, :], gsb[:, b, 0, :], AF.Copy,
                scale=wx[:, 0, tt:tt + 1]), "cba")
            z2 = ctx.op("scalar", lambda e, b=b, tt=tt: e.activation(
                zsb[:, b, 1, :], gsb[:, b, 1, :], AF.Copy,
                scale=wx[:, 1, tt:tt + 1]), "cba")
            gfree[b] = [z2]
            if tt + 1 < NT:
                gathers(tt + 1)
            ctx.wait("vector", load_toks[tt], z2)
            v1 = ctx.op("vector", lambda e, b=b: e.tensor_tensor(
                to[:, b, :], tb[:, b, :], tc_[:, b, :], OP.add), "cbd")
            ctx.wait("vector", v1)
            v2 = ctx.op("vector", lambda e, b=b: e.tensor_tensor(
                to[:, b, :], to[:, b, :], zsb[:, b, 0, :], OP.add), "cbd")
            ctx.wait("vector", v2)
            v3 = ctx.op("vector", lambda e, b=b: e.tensor_tensor(
                to[:, b, :], to[:, b, :], zsb[:, b, 1, :], OP.add), "cbd")
            ctx.wait("vector", v3)
            v4 = ctx.op("vector", lambda e, b=b: e.scalar_tensor_tensor(
                to[:, b, :], to[:, b, :], 0.5, ta[:, b, :], OP.mult, OP.add), "cbd")
            zfree[b] = v3
            v4s[tt] = v4
            ctx.wait("gpsimd", v4)
            ctx.dma("gpsimd", dst[tt * 128:(tt + 1) * 128, :], to[:, b, :],
                    "cbs%d" % b)
            if tt + 2 < NT:
                loads(tt + 2)
        ctx.wait("sync", ctx.tok("cbs0"), ctx.tok("cbs1"))

    def ffn_block(self, xsrc, dst):
        ctx, d = self.ctx, self.d
        st = self.estack()
        self.sbuf("vfm", [128, 8, self.T], BF16, st)
        self.ln_pass(xsrc, dst_tm=d["u_s"])
        ctx.barrier()
        ctx.flush()
        self.phase()
        if not hasattr(self, "_wprep"):
            self.prep_weights()
        self.ln_pass(d["u_s"], fm_sb=self.sb["vfm"], gamma=self.gamma(0))
        ctx.barrier()
        ctx.flush()
        self.phase()
        st2 = self.estack()
        self.sbuf("cbias", [128, 1], F32, st2, side="right")
        st3 = self.estack()
        bt = self.mla_prep(st3)
        ctx.barrier()
        ctx.flush()
        self.phase()
        self.close_stack(st3)
        self.close_stack(st)
        st4 = self.estack()
        self.attention(True, 1.0 / 8.0, bt, st4)
        ctx.barrier()
        ctx.flush()
        self.phase()
        self.close_stack(st4)
        self.close_stack(st2)
        st5 = self.estack()
        self.oproj("m_o_T", d["u_s"], d["u_s"], False, st5)
        ctx.barrier()
        ctx.flush()
        self.phase()
        self.close_stack(st5)
        st7 = self.estack()
        hook = self.moe_router_setup(st7)
        self.ln_pass(d["u_s"], gamma=self.gamma(1), hook=hook)
        ctx.barrier()
        ctx.flush()
        self.phase()
        self.moe_experts(st7)
        ctx.barrier()
        ctx.flush()
        self.phase()
        self.close_stack(st7)
        st8 = self.estack()
        self.combine_moe(xsrc, d["u_s"], dst, st8)
        ctx.barrier()
        ctx.flush()
        self.phase()
        self.close_stack(st8)

    def attn_block(self, xsrc, dst):
        ctx, d = self.ctx, self.d
        st = self.estack()
        self.sbuf("vfm", [128, 8, self.T], BF16, st)
        self.ln_pass(xsrc, fm_sb=self.sb["vfm"], gamma=self.gamma(2), beta=self.gamma(3))
        ctx.barrier()
        ctx.flush()
        self.phase()
        st2 = self.estack()
        self.sbuf("cbias", [128, 1], F32, st2, side="right")
        st3 = self.estack()
        bt = self.s2_prep(st3)
        ctx.barrier()
        ctx.flush()
        self.phase()
        self.close_stack(st3)
        self.close_stack(st)
        st4 = self.estack()
        self.attention(False, 1.0, bt, st4)
        ctx.barrier()
        ctx.flush()
        self.phase()
        self.close_stack(st4)
        self.close_stack(st2)
        st5 = self.estack()
        self.oproj("a_o_T", xsrc, dst, True, st5)
        ctx.barrier()
        ctx.flush()
        self.phase()
        self.close_stack(st5)

    def build(self):
        d = self.d
        self.declare()
        self.load_consts()
        try:
            self.ffn_block(d["x_in"], d["x_cur"])
            self.attn_block(d["x_cur"], d["x_cur"])
            self.ffn_block(d["x_cur"], d["y_out"])
        except StopBuild:
            self.ctx.flush()
            for s in reversed(self.open_stacks):
                s.close()
            self.open_stacks = []
        self.ctx.flush()
        self.stack.close()
        return self.nc


def _prep_inputs(inp, T):
    f32 = np.float32
    C = np.ascontiguousarray
    out = {}
    out["c_ident"] = np.eye(128, dtype=f32)
    k_idx, q_idx = np.meshgrid(np.arange(128), np.arange(128), indexing="ij")
    out["c_mask"] = (k_idx <= q_idx).astype(f32)
    inv = 1.0 / (10000.0 ** (np.arange(0, RD, 2, dtype=np.float64) / RD))
    tpos = np.arange(T, dtype=np.float64)
    fr = np.outer(tpos, inv)
    emb = np.concatenate([fr, fr], axis=1)
    cos = np.cos(emb).astype(f32)
    sin = np.sin(emb).astype(f32)
    sinS = np.concatenate([-sin[:, :16], sin[:, 16:]], axis=1).astype(f32)
    out["c_cos"] = C(np.tile(cos, (1, NH)))
    out["c_sinS"] = C(np.tile(sinS, (1, NH)))
    gam = np.stack([
        np.asarray(inp["L0_ffn_ln1_w"], f32),
        np.asarray(inp["L0_ffn_ln2_w"], f32),
        np.asarray(inp["L0_attn_ln_gamma"], f32),
        np.asarray(inp["L0_attn_ln_beta"], f32),
        np.tile(np.asarray(inp["L0_attn_q_gamma"], f32), NH),
        np.tile(np.asarray(inp["L0_attn_k_gamma"], f32), NH),
    ])
    out["c_gam"] = C(np.broadcast_to(gam[None], (128, 6, DM)))
    out["c_bo"] = C(np.broadcast_to(np.asarray(inp["L0_attn_b_o"], f32)[None], (128, DM)))
    rw = np.zeros((DM, 8), f32)
    rw[:, :7] = np.asarray(inp["L0_ffn_router_w"], f32).T
    out["c_router"] = C(rw.reshape(8, 128, 8))
    rb = np.zeros((8,), f32)
    rb[:7] = np.asarray(inp["L0_ffn_routing_bias"], f32)
    out["c_rbias"] = C(np.broadcast_to(rb[None], (128, 8)))
    NCH = 2 * (T // 512) + 7
    ei, ej = np.meshgrid(np.arange(7), np.arange(7), indexing="ij")
    rm = np.concatenate([
        np.broadcast_to((ej < ei).astype(f32).reshape(1, 49), (128, 49)),
        np.broadcast_to((ej <= ei).astype(f32).reshape(1, 49), (128, 49)),
        np.broadcast_to(512.0 * np.arange(NCH, dtype=f32)[None], (128, NCH)),
        np.broadcast_to(np.arange(NCH, dtype=f32)[None], (128, NCH)),
        128.0 * np.arange(16, dtype=f32)[None]
        + np.arange(128, dtype=f32)[:, None],
        np.arange(128, dtype=f32)[:, None],
    ], axis=1)
    out["c_rmeta"] = C(rm.astype(f32))
    h2 = np.zeros((2, 128), f32)
    h2[0, 0:64] = 1.0
    h2[1, 64:128] = 1.0
    out["c_half2"] = h2

    def blk_kc(w):
        K, N = w.shape
        return C(w.reshape(K // 128, 128, N))

    kv_d = np.asarray(inp["L0_ffn_kv_d"], f32)
    q_d = np.asarray(inp["L0_ffn_q_d"], f32)
    k_u = np.asarray(inp["L0_ffn_k_u"], f32)
    q_u = np.asarray(inp["L0_ffn_q_u"], f32)
    v_u = np.asarray(inp["L0_ffn_v_u"], f32)
    rq = np.asarray(inp["L0_ffn_rope_q_w"], f32)
    rk = np.asarray(inp["L0_ffn_rope_k_w"], f32)
    ow = np.asarray(inp["L0_ffn_o_w"], f32)
    out["m_kv_T"] = blk_kc(kv_d.T)
    out["m_q_T"] = blk_kc(q_d.T)
    nope_rows = np.concatenate([np.arange(h * 64 + 32, h * 64 + 64) for h in range(NH)])
    out["m_knope_T"] = blk_kc(k_u[nope_rows].T)
    out["m_qnope_T"] = blk_kc(q_u[nope_rows].T)
    out["m_v_T"] = blk_kc(v_u.T)
    out["m_rq_T"] = blk_kc(rq.T)
    out["m_rk_T"] = blk_kc(rk.T)
    out["m_o_T"] = blk_kc(ow.T)
    out["a_q_T"] = blk_kc(np.asarray(inp["L0_attn_w_q"], f32).T)
    out["a_k_T"] = blk_kc(np.asarray(inp["L0_attn_w_k"], f32).T)
    out["a_v_T"] = blk_kc(np.asarray(inp["L0_attn_w_v"], f32).T)
    out["a_o_T"] = blk_kc(np.asarray(inp["L0_attn_w_o"], f32).T)

    sgw = np.asarray(inp["L0_ffn_shared_gate"], f32)[0]
    suw = np.asarray(inp["L0_ffn_shared_up"], f32)[0]
    sdw = np.asarray(inp["L0_ffn_shared_down"], f32)[0]
    rgw = np.asarray(inp["L0_ffn_routed_gate"], f32)
    ruw = np.asarray(inp["L0_ffn_routed_up"], f32)
    rdw = np.asarray(inp["L0_ffn_routed_down"], f32)
    gu = np.zeros((8, 16, 128, 2, 8, 128), f32)
    dn = np.zeros((8, 128, 16, 2, 512), f32)
    for e in range(8):
        gw = rgw[e] if e < 7 else sgw
        uw = ruw[e] if e < 7 else suw
        dw = rdw[e] if e < 7 else sdw
        gT = gw.T.reshape(8, 128, 16, 128)
        uT = uw.T.reshape(8, 128, 16, 128)
        gu[e, :, :, 0] = gT.transpose(2, 1, 0, 3)
        gu[e, :, :, 1] = uT.transpose(2, 1, 0, 3)
        dT = dw.T.reshape(16, 128, 2, 512)
        dn[e] = dT.transpose(1, 0, 2, 3)
    out["moe_gu"] = C(gu)
    out["moe_dn"] = C(dn)
    return out


def kernel(**inputs):
    x = np.asarray(inputs["L0_attn_x"], np.float32)
    B, T, _ = x.shape
    mk = MK(T)
    nc = mk.build()
    weights = _prep_inputs(inputs, T)
    in_maps = []
    core_ids = [0, 2, 4, 6][:B]
    for b in range(B):
        m = dict(weights)
        m["x_in"] = np.ascontiguousarray(x[b])
        in_maps.append(m)
    res = run_bass_kernel_spmd(nc, in_maps, core_ids=core_ids)
    out = np.stack([res.results[b]["y_out"] for b in range(B)])
    return out


# version 28
# speedup vs baseline: 1.0894x; 1.0311x over previous
import math
import numpy as np
import ml_dtypes
import concourse.bass as bass
import concourse.mybir as mybir
from concourse.bass import IndirectOffsetOnAxis as IOA
from concourse.bass_isa import ReduceOp
from concourse.library_config import attnmlp as LIB_ATTNMLP
from concourse.bass_utils import run_bass_kernel_spmd
from contextlib import ExitStack

F32 = mybir.dt.float32
BF16 = mybir.dt.bfloat16
AF = mybir.ActivationFunctionType
OP = mybir.AluOpType
AX = mybir.AxisListType

DM = 1024
NH = 16
HD = 64
RD = 32
LAT = 256
IFF = 2048
LN_EPS = 1e-5

ENGINES = ("sync", "gpsimd", "tensor", "vector", "scalar")


class Ctx:

    def __init__(self, nc, stack):
        self.nc = nc
        self.stack = stack
        self.plan = {e: [] for e in ENGINES}
        self.sems = {}
        self.cnt = {}
        self.last_wait = {}
        self.owner = {}

    def sem(self, name):
        if name not in self.sems:
            self.sems[name] = self.stack.enter_context(self.nc.semaphore(name))
            self.cnt[name] = 0
        return self.sems[name]

    def bump(self, name, n):
        self.sem(name)
        self.cnt[name] += n
        return self.cnt[name]

    def emit(self, eng, fn):
        self.plan[eng].append(fn)

    def dma(self, eng, out, in_, sem_name, accum=False):
        self.owner[sem_name] = eng
        val = self.bump(sem_name, 16)
        s = self.sems[sem_name]
        if accum:
            def fn(e, out=out, in_=in_, s=s):
                e.dma_start(out=out, in_=in_, accum_op=OP.add).then_inc(s, 16)
        else:
            def fn(e, out=out, in_=in_, s=s):
                e.dma_start(out=out, in_=in_).then_inc(s, 16)
        self.emit(eng, fn)
        return (sem_name, val)

    def idma(self, out, out_off, in_, in_off, sem_name):
        self.owner[sem_name] = "gpsimd"
        val = self.bump(sem_name, 16)
        s = self.sems[sem_name]

        def fn(e, out=out, oo=out_off, in_=in_, io=in_off, s=s):
            e.indirect_dma_start(out=out, out_offset=oo, in_=in_,
                                 in_offset=io).then_inc(s, 16)
        self.emit("gpsimd", fn)
        return (sem_name, val)

    def wait(self, eng, *toks):
        for tok in toks:
            if tok is None:
                continue
            name, val = tok
            if val <= 0:
                continue
            if self.last_wait.get((eng, name), -1) >= val:
                continue
            self.last_wait[(eng, name)] = val
            s = self.sem(name)
            self.emit(eng, lambda e, s=s, v=val: e.wait_ge(s, v))

    def op(self, eng, fn, sem_name=None):
        if sem_name is None:
            self.emit(eng, fn)
            return None
        self.owner[sem_name] = eng
        val = self.bump(sem_name, 1)
        s = self.sems[sem_name]

        def wrapped(e, fn=fn, s=s):
            fn(e).then_inc(s, 1)

        self.emit(eng, wrapped)
        return (sem_name, val)

    def tok(self, name):
        return (name, self.cnt.get(name, 0))

    barrier_exempt = ("wcast", "xz")

    def barrier(self):
        for eng in ENGINES:
            for name, owner in list(self.owner.items()):
                if owner == eng and name not in self.barrier_exempt:
                    self.wait(eng, (name, self.cnt[name]))
        vals = {}
        for eng in ENGINES:
            vals[eng] = self.bump("bar_" + eng, 1)
        for eng in ENGINES:
            s = self.sems["bar_" + eng]
            self.emit(eng, lambda e, s=s: e.nop().then_inc(s, 1))
        for eng in ENGINES:
            for other in ENGINES:
                s = self.sems["bar_" + other]
                self.emit(eng, lambda e, s=s, v=vals[other]: e.wait_ge(s, v))

    def flush(self):
        if not any(self.plan[e] for e in ENGINES):
            return
        with self.nc.Block() as block:
            for eng_name in ENGINES:
                items = self.plan[eng_name]
                if not items:
                    continue
                section = getattr(block, eng_name)

                def make(items):
                    def _(engine):
                        for fn in items:
                            fn(engine)
                    return _

                section(make(items))
        self.plan = {e: [] for e in ENGINES}


class StopBuild(Exception):
    pass


class MK:
    def __init__(self, T, dbg=(), upto=None):
        self.upto = upto
        self.phase_n = 0
        self.T = T
        self.NT = T // 128
        self.TC = T // 512
        self.NSH = T // 512
        self.NCH = 2 * (T // 512) + 7
        self.NSLOT = self.NCH * 512
        self.nc = bass.Bass("TRN2", target_bir_lowering=False)
        self.stack = ExitStack()
        self.ctx = Ctx(self.nc, self.stack)
        self.dbg = set(dbg)
        self.d = {}
        self.sb = {}
        self.open_stacks = []

    def phase(self):
        self.phase_n += 1
        if self.upto is not None and self.phase_n > self.upto:
            raise StopBuild()

    def estack(self):
        s = ExitStack()
        self.open_stacks.append(s)
        return s

    def close_stack(self, s):
        self.open_stacks.remove(s)
        s.close()

    def dram(self, name, shape, dtype=F32, kind=None):
        if name in self.dbg and kind is None:
            kind = "ExternalOutput"
        t = (self.nc.dram_tensor(name, shape, dtype, kind=kind) if kind
             else self.nc.dram_tensor(name, shape, dtype))
        self.d[name] = t.ap()
        return self.d[name]

    _uid = 0

    def _uniq(self, name):
        MK._uid += 1
        return "%s_%d" % (name, MK._uid)

    def sbuf(self, name, shape, dtype, stack=None, side=None):
        t = (stack or self.stack).enter_context(
            self.nc.sbuf_tensor(self._uniq(name), shape, dtype, side=side))
        self.sb[name] = t
        return t

    def psum(self, name, shape, dtype, stack=None):
        t = (stack or self.stack).enter_context(
            self.nc.psum_tensor(self._uniq(name), shape, dtype))
        self.sb[name] = t
        return t

    def declare(self):
        d, T = self.dram, self.T
        d("x_in", [T, DM], kind="ExternalInput")
        d("y_out", [T, DM], kind="ExternalOutput")
        d("c_ident", [128, 128], kind="ExternalInput")
        d("c_mask", [128, 128], kind="ExternalInput")
        d("c_cos", [T, 512], kind="ExternalInput")
        d("c_sinS", [T, 512], kind="ExternalInput")
        d("c_gam", [128, 6, DM], kind="ExternalInput")
        d("c_bo", [128, DM], kind="ExternalInput")
        d("c_router", [8, 128, 8], kind="ExternalInput")
        d("c_rbias", [128, 8], kind="ExternalInput")
        d("c_half2", [2, 128], kind="ExternalInput")
        d("m_kv_T", [8, 128, LAT], kind="ExternalInput")
        d("m_q_T", [8, 128, LAT], kind="ExternalInput")
        d("m_knope_T", [2, 128, 512], kind="ExternalInput")
        d("m_qnope_T", [2, 128, 512], kind="ExternalInput")
        d("m_v_T", [2, 128, DM], kind="ExternalInput")
        d("m_rq_T", [2, 128, 512], kind="ExternalInput")
        d("m_rk_T", [8, 128, 512], kind="ExternalInput")
        d("m_o_T", [8, 128, DM], kind="ExternalInput")
        d("a_q_T", [8, 128, DM], kind="ExternalInput")
        d("a_k_T", [8, 128, DM], kind="ExternalInput")
        d("a_v_T", [8, 128, DM], kind="ExternalInput")
        d("a_o_T", [8, 128, DM], kind="ExternalInput")
        d("moe_gu", [8, 16, 128, 2, 8, 128], kind="ExternalInput")
        d("moe_dn", [8, 128, 16, 2, 512], kind="ExternalInput")
        d("c_rmeta", [128, 115 + 2 * self.NCH], kind="ExternalInput")
        d("gu_bf", [8 * 16 * 128, 2048], BF16)
        d("dn_bf", [8 * 128, 16 * 2 * 512], BF16)
        d("xn_tm", [T, DM], BF16)
        d("xnd", [self.NSLOT, DM], BF16)
        d("eod", [self.NSLOT, DM], BF16)
        d("ridx", [2, 128, self.NT], mybir.dt.int32)
        d("rw", [2, 128, self.NT])
        d("u_s", [T, DM])
        d("x_cur", [T, DM])
        d("acc", [T, DM])
        d("kfm", [NH, HD, T], BF16)
        d("qfm", [NH, HD, T], BF16)
        d("vtm", [self.NT, 128, NH * 65], BF16)
        d("ofm", [8, self.TC, 128, 512], BF16)
        d("lsum", [NH, T], F32)

    def load_consts(self):
        ctx, sb, d = self.ctx, self.sb, self.d
        self.sbuf("c_ident", [128, 128], BF16)
        self.sbuf("c_mask", [128, 128], BF16)
        self.sbuf("c_gam", [128, 6, DM], F32)
        self.sbuf("c_bo", [128, DM], F32)
        self.sbuf("c_rbias", [128, 8], F32)
        self.sbuf("c_eps", [128, 1], F32)
        self.sbuf("c_tiny", [128, 1], F32)
        self.sbuf("c_mln8", [128, 1], F32)
        self.sbuf("c_one64", [1, 64], BF16)
        self.sbuf("c_ones1b", [1, 128], BF16)
        self.sbuf("c_half2", [2, 128], BF16)
        self.sbuf("c_rmeta", [128, 115 + 2 * self.NCH], F32)
        self.sbuf("c_ones128", [128, 128], BF16)
        self.sbuf("zbuf", [128, 2, DM], BF16)
        toks = [
            ctx.dma("gpsimd", sb["c_ident"][:, :], d["c_ident"][:, :], "wks"),
            ctx.dma("gpsimd", sb["c_mask"][:, :], d["c_mask"][:, :], "wks"),
            ctx.dma("gpsimd", sb["c_gam"][:, :, :], d["c_gam"][:, :, :], "wks"),
            ctx.dma("gpsimd", sb["c_bo"][:, :], d["c_bo"][:, :], "wks"),
            ctx.dma("gpsimd", sb["c_rbias"][:, :], d["c_rbias"][:, :], "wks"),
            ctx.dma("gpsimd", sb["c_half2"][:, :], d["c_half2"][:, :], "wks"),
            ctx.dma("gpsimd", sb["c_rmeta"][:, :], d["c_rmeta"][:, :], "wks"),
        ]
        g1 = ctx.op("gpsimd", lambda e: (
            e.memset(sb["c_eps"][:, :], float(LN_EPS)),
            e.memset(sb["c_tiny"][:, :], 1e-16),
            e.memset(sb["c_mln8"][:, :], -math.log(8.0)),
            e.memset(sb["c_one64"][:, :], 1.0),
            e.memset(sb["c_ones1b"][:, :], 1.0),
            e.memset(sb["c_ones128"][:, :], 1.0))[-1], "gpk")
        for eng in ENGINES:
            ctx.wait(eng, toks[-1], g1)
            ctx.wait(eng, ("wks", ctx.cnt["wks"]))

    def gamma(self, idx):
        return self.sb["c_gam"][:, idx, :]

    def rmeta(self, name, n):
        off = {"lt7": 0, "le7": 49, "thr": 98, "chunk": 98 + self.NCH,
               "ipp": 98 + 2 * self.NCH, "pid": 114 + 2 * self.NCH}[name]
        return self.sb["c_rmeta"][:, off:off + n]

    def prep_weights(self):
        ctx, d = self.ctx, self.d
        for e_ in [7, 0, 1, 2, 3, 4, 5, 6]:
            for ip in range(16):
                off = (e_ * 16 + ip) * 128 * 2048
                ctx.dma("gpsimd",
                        bass.AP(d["gu_bf"].tensor, off, [[2048, 128], [1, 2048]]),
                        bass.AP(d["moe_gu"].tensor, off, [[2048, 128], [1, 2048]]),
                        "wcast")
            for j in range(4):
                off = e_ * 128 * 16384 + j * 4096
                ctx.dma("gpsimd",
                        bass.AP(d["dn_bf"].tensor, off, [[16384, 128], [1, 4096]]),
                        bass.AP(d["moe_dn"].tensor, off, [[16384, 128], [1, 4096]]),
                        "wcast")
            if e_ == 7:
                self._wc7 = ctx.tok("wcast")
        zb = self.sb["zbuf"]
        gz = ctx.op("gpsimd", lambda e: e.memset(zb[:, :, :], 0.0), "gpz")
        ctx.wait("gpsimd", gz)
        for c in range(2 * self.NCH):
            ctx.dma("gpsimd",
                    bass.AP(d["xnd"].tensor, c * 256 * DM,
                            [[DM, 128], [128 * DM, 2], [1, DM]]),
                    zb[:, :, :], "xz")
        self._wprep = (ctx.tok("wcast"), ctx.tok("xz"))

    def ln_pass(self, src, dst_tm=None, fm_sb=None, gamma=None, beta=None, hook=None):
        ctx, sb, NT = self.ctx, self.sb, self.NT
        G = 4
        st = self.estack()
        xt = self.sbuf("lnxt", [128, 2 * G, DM], F32, st)
        ut = self.sbuf("lnut", [128, 2 * G, DM], F32, st)
        gt = self.sbuf("lngt", [128, 2 * G, DM], BF16, st)
        f1 = self.sbuf("lnf1", [128, 2, DM], F32, st)
        sqj = self.sbuf("lnsqj", [128, DM], BF16, st)
        sm = {n: self.sbuf("ln" + n, [128, 2 * G], F32, st) for n in
              ("s", "q", "m", "t", "v", "l", "r")}
        ptp = self.psum("lnptp", [128, 1024], BF16, st)
        want_fm = fm_sb is not None
        slot_free = [None] * (2 * G)
        hook_last = None
        assert NT % G == 0
        for g0 in range(0, NT, G):
            tts = list(range(g0, g0 + G))
            slots = [tt % (2 * G) for tt in tts]
            for tt, sl in zip(tts, slots):
                if slot_free[sl] is not None:
                    ctx.wait("sync", *slot_free[sl])
                ctx.dma("sync", xt[:, sl, :], src[tt * 128:(tt + 1) * 128, :],
                        "lnl%d" % (sl // G))
            ld = ctx.tok("lnl%d" % (slots[0] // G))
            ctx.wait("vector", ld)
            ctx.wait("scalar", ld)
            for tt, sl in zip(tts, slots):
                ctx.op("vector", lambda e, sl=sl: e.reduce_sum(
                    sm["s"][:, sl:sl + 1], xt[:, sl, :], AX.X), "lnd")
            for tt, sl in zip(tts, slots):
                ctx.wait("scalar", ctx.tok("lna"))
                ctx.op("scalar", lambda e, sl=sl: e.activation(
                    sqj[:, :], xt[:, sl, :], AF.Square,
                    accum_out=sm["q"][:, sl:sl + 1]), "lna")
            ctx.wait("vector", ctx.tok("lnd"))
            for tt, sl in zip(tts, slots):
                ctx.op("vector", lambda e, sl=sl: (
                    e.tensor_scalar(sm["m"][:, sl:sl + 1], sm["s"][:, sl:sl + 1],
                                    1.0 / DM, None, OP.mult),
                    e.tensor_tensor(sm["t"][:, sl:sl + 1], sm["s"][:, sl:sl + 1],
                                    sm["s"][:, sl:sl + 1], OP.mult))[-1], "lnd")
            ctx.wait("vector", ctx.tok("lnd"), ctx.tok("lna"))
            for tt, sl in zip(tts, slots):
                ctx.op("vector", lambda e, sl=sl: e.scalar_tensor_tensor(
                    sm["v"][:, sl:sl + 1], sm["t"][:, sl:sl + 1], -1.0 / DM,
                    sm["q"][:, sl:sl + 1], OP.mult, OP.add), "lnd")
            ctx.wait("scalar", ctx.tok("lnd"))
            for tt, sl in zip(tts, slots):
                ctx.op("scalar", lambda e, sl=sl: e.activation(
                    sm["l"][:, sl:sl + 1], sm["v"][:, sl:sl + 1], AF.Ln,
                    scale=1.0 / DM, bias=sb["c_eps"][:, 0:1]), "lna")
            ctx.wait("scalar", ctx.tok("lna"))
            for tt, sl in zip(tts, slots):
                ctx.op("scalar", lambda e, sl=sl: e.activation(
                    sm["r"][:, sl:sl + 1], sm["l"][:, sl:sl + 1], AF.Exp,
                    scale=-0.5), "lna")
            ctx.wait("vector", ctx.tok("lna"))
            applies = {}
            for tt, sl in zip(tts, slots):
                applies[tt] = ctx.op("vector", lambda e, sl=sl: e.tensor_scalar(
                    ut[:, sl, :], xt[:, sl, :], sm["m"][:, sl:sl + 1],
                    sm["r"][:, sl:sl + 1], OP.subtract, OP.mult), "lnd")
            if dst_tm is not None:
                ctx.wait("sync", applies[tts[-1]])
                for tt, sl in zip(tts, slots):
                    ctx.dma("sync", dst_tm[tt * 128:(tt + 1) * 128, :],
                            ut[:, sl, :], "lns%d" % (sl // G))
                st_tok = ctx.tok("lns%d" % (slots[0] // G))
            else:
                st_tok = None
            if want_fm or hook is not None:
                ctx.wait("vector", ctx.tok("lnd"))
                for tt, sl in zip(tts, slots):
                    b2 = tt % 2
                    fdep = None
                    if gamma is not None and beta is not None:
                        ctx.wait("vector", hook_last)
                        v5 = ctx.op("vector", lambda e, sl=sl, b2=b2: e.tensor_tensor(
                            f1[:, b2, :], ut[:, sl, :], gamma, OP.mult), "lnd")
                        ctx.wait("vector", v5)
                        v6 = ctx.op("vector", lambda e, sl=sl, b2=b2: e.tensor_tensor(
                            gt[:, sl, :], f1[:, b2, :], beta, OP.add), "lnd")
                    elif gamma is not None and hook is not None:
                        ctx.wait("vector", hook_last)
                        v5 = ctx.op("vector", lambda e, sl=sl, b2=b2: e.tensor_tensor(
                            f1[:, b2, :], ut[:, sl, :], gamma, OP.mult), "lnd")
                        ctx.wait("vector", v5)
                        v6 = ctx.op("vector", lambda e, sl=sl, b2=b2: e.tensor_copy(
                            gt[:, sl, :], f1[:, b2, :]), "lnd")
                    elif gamma is not None:
                        v6 = ctx.op("vector", lambda e, sl=sl: e.tensor_tensor(
                            gt[:, sl, :], ut[:, sl, :], gamma, OP.mult), "lnd")
                    else:
                        v6 = ctx.op("vector", lambda e, sl=sl: e.tensor_copy(
                            gt[:, sl, :], ut[:, sl, :]), "lnd")
                    if want_fm:
                        ctx.wait("tensor", v6, ctx.tok("lnt"))
                        pe_v = None
                        for fc in range(8):
                            pe_v = ctx.op("tensor", lambda e, sl=sl, fc=fc: e.transpose(
                                ptp[:, fc * 128:(fc + 1) * 128],
                                gt[:, sl, fc * 128:(fc + 1) * 128],
                                sb["c_ident"][:, :]), "lnp" if fc == 7 else None)
                        ctx.wait("vector", pe_v)
                        v7 = ctx.op("vector", lambda e, tt=tt: e.tensor_copy(
                            fm_sb[:, :, tt * 128:(tt + 1) * 128],
                            ptp[:, :].rearrange("p (f t) -> p f t", f=8)), "lnt")
                    else:
                        v7 = None
                    hk = None
                    hks = ()
                    if hook is not None:
                        hk = hook(tt, f1[:, b2, :], gt[:, sl, :], v6, v7)
                        if isinstance(hk, tuple):
                            hks = hk[1:]
                            hk = hk[0]
                        hook_last = hk
                    slot_free[sl] = [t for t in (applies[tt], v7, hk, st_tok,
                                                 ctx.tok("lna")) + hks
                                     if t is not None]
            else:
                for tt, sl in zip(tts, slots):
                    slot_free[sl] = [t for t in (applies[tt], st_tok,
                                                 ctx.tok("lna")) if t is not None]
        ctx.wait("sync", ctx.tok("lns0"), ctx.tok("lns1"))
        out_toks = [ctx.tok("lnd"), ctx.tok("lna"), ctx.tok("lnt")]
        self.close_stack(st)
        return out_toks

    def tile_norms(self, tile, dep, mstack, tt, sq, s16):
        ctx = self.ctx
        ctx.wait("scalar", dep, ctx.tok("nsd"))
        a = ctx.op("scalar", lambda e: e.activation(sq[:, :], tile, AF.Square), "nsa")
        ctx.wait("vector", a)
        v = ctx.op("vector", lambda e: e.tensor_reduce(
            s16[:, :], sq[:, :].rearrange("p (h d) -> p h d", h=NH), AX.X, OP.add), "nsd")
        ctx.wait("vector", v)
        v2 = ctx.op("vector", lambda e, tt=tt: e.reduce_max(
            mstack[:, tt:tt + 1], s16[:, :], AX.X), "nsd")
        return v2, a

    def bound(self, mq, mk_, scale, deps, ptp, pbnd):
        ctx, sb = self.ctx, self.sb
        ctx.wait("vector", *deps)
        t1 = self.sb["bnd_t1"]
        t1b = self.sb["bnd_t1b"]
        t2 = self.sb["bnd_t2"]
        cb = self.sb["cbias"]
        v1 = ctx.op("vector", lambda e: (
            e.reduce_max(t1[:, 0:1], mq[:, :], AX.X),
            e.reduce_max(t1[:, 1:2], mk_[:, :], AX.X))[-1], "nsd")
        ctx.wait("vector", v1)
        v1b = ctx.op("vector", lambda e: e.tensor_copy(t1b[:, 0:2], t1[:, 0:2]), "nsd")
        ctx.wait("tensor", v1b, ctx.tok("nsd"), ctx.tok("nsa"), ctx.tok("prt"))
        p1 = ctx.op("tensor", lambda e: e.transpose(
            ptp[:, 0:128][0:2, :], t1b[:, 0:2], sb["c_ident"][:, :]), "bnp")
        ctx.wait("vector", p1)
        v2 = ctx.op("vector", lambda e: e.reduce_max(
            t2[:, 0:1][0:2, :], ptp[:, 0:128][0:2, :], AX.X), "nsd")
        ctx.wait("vector", v2)
        v2b = ctx.op("vector", lambda e: e.tensor_copy(t1b[:, 2:3][0:2, :],
                                                       t2[:, 0:1][0:2, :]), "nsd")
        ctx.wait("tensor", v2b)
        p2 = ctx.op("tensor", lambda e: e.transpose(
            ptp[:, 128:130][0:1, :], t1b[:, 2:3][0:2, :], sb["c_ident"][0:2, 0:2]), "bnp")
        ctx.wait("vector", p2)
        v3 = ctx.op("vector", lambda e: e.tensor_copy(t2[:, 1:3][0:1, :],
                                                      ptp[:, 128:130][0:1, :]), "nsd")
        ctx.wait("scalar", v3)
        a1 = ctx.op("scalar", lambda e: e.activation(t2[:, 3:5][0:1, :],
                                                    t2[:, 1:3][0:1, :], AF.Ln), "nsa")
        ctx.wait("vector", a1)
        v4 = ctx.op("vector", lambda e: e.tensor_tensor(
            t2[:, 5:6][0:1, :], t2[:, 3:4][0:1, :], t2[:, 4:5][0:1, :], OP.add), "nsd")
        ctx.wait("scalar", v4)
        a2 = ctx.op("scalar", lambda e: e.activation(t2[:, 6:7][0:1, :],
                                                    t2[:, 5:6][0:1, :], AF.Exp,
                                                    scale=0.5), "nsa")
        ctx.wait("vector", a2)
        v5 = ctx.op("vector", lambda e: e.tensor_scalar(
            t1b[:, 3:4][0:1, :], t2[:, 6:7][0:1, :], -1.02 * scale, None, OP.mult), "nsd")
        ctx.wait("tensor", v5)
        p3 = ctx.op("tensor", lambda e: e.matmul(
            pbnd[:, 0:1], sb["c_ones1b"][0:1, :], t1b[:, 3:4][0:1, :],
            start=True, stop=True), "bnp")
        ctx.wait("vector", p3)
        v6 = ctx.op("vector", lambda e: e.tensor_copy(cb[:, 0:1], pbnd[:, 0:1]), "nsd")
        return v6

    def mla_prep(self, st):
        ctx, sb, d, NT, TC = self.ctx, self.sb, self.d, self.NT, self.TC
        vfm = sb["vfm"]
        wkv = self.sbuf("wkv", [128, 8, LAT], BF16, st)
        wq = self.sbuf("wq", [128, 8, LAT], BF16, st)
        wkn = self.sbuf("wkn", [128, 2, 512], BF16, st)
        wqn = self.sbuf("wqn", [128, 2, 512], BF16, st)
        wv = self.sbuf("wv", [128, 2, DM], BF16, st)
        wrq = self.sbuf("wrq", [128, 2, 512], BF16, st)
        wrk = self.sbuf("wrk", [128, 8, 512], BF16, st)
        wt = []
        wt.append(ctx.dma("gpsimd", wkv[:, :, :],
                          d["m_kv_T"][:, :, :].rearrange("kc p l -> p kc l"), "wks"))
        wt.append(ctx.dma("gpsimd", wq[:, :, :],
                          d["m_q_T"][:, :, :].rearrange("kc p l -> p kc l"), "wks"))
        wt.append(ctx.dma("gpsimd", wkn[:, :, :],
                          d["m_knope_T"][:, :, :].rearrange("kc p l -> p kc l"), "wks"))
        wt.append(ctx.dma("gpsimd", wqn[:, :, :],
                          d["m_qnope_T"][:, :, :].rearrange("kc p l -> p kc l"), "wks"))
        wt.append(ctx.dma("gpsimd", wv[:, :, :],
                          d["m_v_T"][:, :, :].rearrange("kc p l -> p kc l"), "wks"))
        wt.append(ctx.dma("gpsimd", wrq[:, :, :],
                          d["m_rq_T"][:, :, :].rearrange("kc p l -> p kc l"), "wks"))
        wt.append(ctx.dma("gpsimd", wrk[:, :, :],
                          d["m_rk_T"][:, :, :].rearrange("kc p l -> p kc l"), "wks"))
        wtok = ctx.tok("wks")
        ctx.wait("tensor", wtok)

        latk = self.sbuf("latk", [128, 2, self.T], BF16, st)
        latq = self.sbuf("latq", [128, 2, self.T], BF16, st)
        lat_st = self.estack()
        pslat = self.psum("pslat", [128, 4, 512], F32, lat_st)
        ctx.wait("vector", ctx.tok("lnt"))
        ctx.wait("tensor", ctx.tok("lnt"))
        for (lat_sb, wsrc) in ((latk, wkv), (latq, wq)):
            for lc in range(2):
                for tcg in range(0, TC, 4):
                    ng = min(4, TC - tcg)
                    ctx.wait("tensor", ctx.tok("lpe"))
                    pv = None
                    for kc in range(8):
                        for j in range(ng):
                            tc = tcg + j
                            pv = ctx.op("tensor",
                                        lambda e, kc=kc, lc=lc, tc=tc, j=j, wsrc=wsrc:
                                        e.matmul(pslat[:, j, :],
                                                 wsrc[:, kc, lc * 128:(lc + 1) * 128],
                                                 vfm[:, kc, tc * 512:(tc + 1) * 512],
                                                 start=(kc == 0), stop=(kc == 7)),
                                        "lmm" if (kc == 7 and j == ng - 1) else None)
                    ctx.wait("vector", pv)
                    ev = None
                    for j in range(ng):
                        tc = tcg + j
                        ev = ctx.op("vector", lambda e, lat_sb=lat_sb, lc=lc, tc=tc, j=j:
                                    e.tensor_copy(lat_sb[:, lc, tc * 512:(tc + 1) * 512],
                                                  pslat[:, j, :]), "lpe")
        ctx.wait("tensor", ctx.tok("lpe"))
        if "dbg_latk" in self.dbg:
            dl = self.dram("dbg_latk", [128, 2, self.T], BF16, kind="ExternalOutput")
            ctx.wait("sync", ctx.tok("lpe"))
            ctx.dma("sync", dl[:, :, :], latk[:, :, :], "dbgs")
            dl2 = self.dram("dbg_latq", [128, 2, self.T], BF16, kind="ExternalOutput")
            ctx.dma("sync", dl2[:, :, :], latq[:, :, :], "dbgs")
            ctx.wait("sync", ctx.tok("dbgs"))
        ctx.barrier()
        self.close_stack(lat_st)

        pkr = self.psum("pkr", [128, 512], F32, st)
        pkn = self.psum("pkn", [128, 512], F32, st)
        pv_ = self.psum("pv", [128, 2, 512], F32, st)
        pqr = self.psum("pqr", [128, 512], F32, st)
        pqn = self.psum("pqn", [128, 512], F32, st)
        ptp = self.psum("ptp2", [128, 1024], BF16, st)
        cs = self.sbuf("cs", [128, 2, 512], F32, st)
        sn = self.sbuf("sn", [128, 2, 512], F32, st)
        rf1 = self.sbuf("rf1", [128, 512], F32, st)
        rf2 = self.sbuf("rf2", [128, 512], F32, st)
        ktm = self.sbuf("ktm", [128, 2, DM], BF16, st)
        qtm = self.sbuf("qtm", [128, 2, DM], BF16, st)
        vstg = self.sbuf("vstg", [128, 2, NH * 65], BF16, st)
        kstg = self.sbuf("kstg", [128, 8, 512], BF16, st)
        qstg = self.sbuf("qstg", [128, 8, 512], BF16, st)
        sqn = self.sbuf("sqn", [128, DM], BF16, st)
        s16 = self.sbuf("s16", [128, NH], F32, st)
        mq = self.sbuf("mq", [128, NT], F32, st)
        mk = self.sbuf("mk", [128, NT], F32, st)
        self.sbuf("bnd_t1", [128, 8], F32, st)
        self.sbuf("bnd_t1b", [128, 4], BF16, st)
        self.sbuf("bnd_t2", [128, 8], F32, st)

        ktm_free = [None, None]
        qtm_free = [None, None]
        vstg_free = [None, None]
        cs_free = [None, None]
        kq_stage_free = None
        for tt in range(NT):
            b = tt % 2
            r0, r1 = tt * 128, (tt + 1) * 128
            if cs_free[b] is not None:
                ctx.wait("sync", cs_free[b])
            lc1 = ctx.dma("sync", cs[:, b, :], d["c_cos"][r0:r1, :], "csl%d" % b)
            lc2 = ctx.dma("sync", sn[:, b, :], d["c_sinS"][r0:r1, :], "csl%d" % b)
            ctx.wait("tensor", ctx.tok("prd"), ctx.tok("pra"))
            pv_done = []
            for kc in range(8):
                pv_done.append(ctx.op("tensor", lambda e, kc=kc, r0=r0: e.matmul(
                    pkr[:, :], vfm[:, kc, r0:r0 + 128], wrk[:, kc, :],
                    start=(kc == 0), stop=(kc == 7)), "pmm" if kc == 7 else None))
            for lc in range(2):
                ctx.op("tensor", lambda e, lc=lc, r0=r0: e.matmul(
                    pkn[:, :], latk[:, lc, r0:r0 + 128], wkn[:, lc, :],
                    start=(lc == 0), stop=(lc == 1)), None)
                for fh in range(2):
                    ctx.op("tensor", lambda e, lc=lc, fh=fh, r0=r0: e.matmul(
                        pv_[:, fh, :], latk[:, lc, r0:r0 + 128],
                        wv[:, lc, fh * 512:(fh + 1) * 512],
                        start=(lc == 0), stop=(lc == 1)), None)
            for lc in range(2):
                ctx.op("tensor", lambda e, lc=lc, r0=r0: e.matmul(
                    pqr[:, :], latq[:, lc, r0:r0 + 128], wrq[:, lc, :],
                    start=(lc == 0), stop=(lc == 1)), None)
                pv_done.append(ctx.op("tensor", lambda e, lc=lc, r0=r0: e.matmul(
                    pqn[:, :], latq[:, lc, r0:r0 + 128], wqn[:, lc, :],
                    start=(lc == 0), stop=(lc == 1)), "pmm" if lc == 1 else None))
            mm_tok = ctx.tok("pmm")
            ctx.wait("vector", mm_tok, lc2)
            if ktm_free[b] is not None:
                ctx.wait("vector", *ktm_free[b])
            if qtm_free[b] is not None:
                ctx.wait("vector", *qtm_free[b])
            if "dbg_ktm" in self.dbg:
                ctx.wait("vector", ctx.tok("dbgs"))
            if vstg_free[b] is not None:
                ctx.wait("scalar", vstg_free[b])
                ctx.wait("gpsimd", vstg_free[b])
            kt, qt, vs = ktm[:, b], qtm[:, b], vstg[:, b]

            v1 = ctx.op("vector", lambda e, b=b: (
                e.tensor_tensor(rf1[:, :], pkr[:, :], cs[:, b, :], OP.mult),
                e.tensor_tensor(rf2[:, :], bass.AP(
                    pkr[:, :].tensor, pkr[:, :].offset + 16,
                    [list(p) for p in pkr[:, :].ap[:1]] + [[32, 16], [-16, 2], [1, 16]]),
                    sn[:, b, :], OP.mult))[-1], "prd")
            ctx.wait("vector", v1)
            v2 = ctx.op("vector", lambda e, kt=kt: e.tensor_tensor(
                bass.AP(kt.tensor, kt.offset,
                        [list(p) for p in kt.ap[:1]] + [[64, 16], [1, 32]]),
                rf1[:, :], rf2[:, :], OP.add), "prd")
            v3 = ctx.op("vector", lambda e, kt=kt: e.tensor_copy(
                bass.AP(kt.tensor, kt.offset + 32,
                        [list(p) for p in kt.ap[:1]] + [[64, 16], [1, 32]]),
                pkn[:, :]), "prd")
            ctx.wait("vector", v3)
            v4 = ctx.op("vector", lambda e, b=b: (
                e.tensor_tensor(rf1[:, :], pqr[:, :], cs[:, b, :], OP.mult),
                e.tensor_tensor(rf2[:, :], bass.AP(
                    pqr[:, :].tensor, pqr[:, :].offset + 16,
                    [list(p) for p in pqr[:, :].ap[:1]] + [[32, 16], [-16, 2], [1, 16]]),
                    sn[:, b, :], OP.mult))[-1], "prd")
            ctx.wait("vector", v4)
            v5 = ctx.op("vector", lambda e, qt=qt: e.tensor_tensor(
                bass.AP(qt.tensor, qt.offset,
                        [list(p) for p in qt.ap[:1]] + [[64, 16], [1, 32]]),
                rf1[:, :], rf2[:, :], OP.add), "prd")
            v6 = ctx.op("vector", lambda e, qt=qt: e.tensor_copy(
                bass.AP(qt.tensor, qt.offset + 32,
                        [list(p) for p in qt.ap[:1]] + [[64, 16], [1, 32]]),
                pqn[:, :]), "prd")
            cs_free[b] = v4
            ctx.wait("scalar", mm_tok)
            a1 = ctx.op("scalar", lambda e, vs=vs: e.activation(
                bass.AP(vs.tensor, vs.offset,
                        [list(p) for p in vs.ap[:1]] + [[65, 8], [1, 64]]),
                pv_[:, 0, :], AF.Copy), "pra")
            a2 = ctx.op("scalar", lambda e, vs=vs: e.activation(
                bass.AP(vs.tensor, vs.offset + 8 * 65,
                        [list(p) for p in vs.ap[:1]] + [[65, 8], [1, 64]]),
                pv_[:, 1, :], AF.Copy), "pra")
            g1 = ctx.op("gpsimd", lambda e, vs=vs: e.memset(
                bass.AP(vs.tensor, vs.offset + 64,
                        [list(p) for p in vs.ap[:1]] + [[65, 16], [1, 1]]), 1.0), "gpk")
            ctx.wait("sync", a1, a2, g1)
            vstg_free[b] = ctx.dma("sync", d["vtm"][tt, :, :], vs[:, :], "vst%d" % b)
            dbg_tok = None
            if "dbg_ktm" in self.dbg:
                if tt == 0:
                    self.dram("dbg_ktm", [self.T, DM], BF16, kind="ExternalOutput")
                    self.dram("dbg_qtm", [self.T, DM], BF16, kind="ExternalOutput")
                ctx.wait("sync", v3, v6)
                ctx.dma("sync", self.d["dbg_ktm"][tt * 128:(tt + 1) * 128, :], kt[:, :], "dbgs")
                dbg_tok = ctx.dma("sync", self.d["dbg_qtm"][tt * 128:(tt + 1) * 128, :], qt[:, :], "dbgs")
            nk, nka = self.tile_norms(kt[:, :], v3, mk, tt, sqn, s16)
            nq, nqa = self.tile_norms(qt[:, :], v6, mq, tt, sqn, s16)
            ctx.wait("tensor", v3, v6, ctx.tok("prt"))
            pe = None
            for fc in range(8):
                pe = ctx.op("tensor", lambda e, kt=kt, fc=fc: e.transpose(
                    ptp[:, fc * 128:(fc + 1) * 128], kt[:, fc * 128:(fc + 1) * 128],
                    sb["c_ident"][:, :]), "pp2" if fc == 7 else None)
            ctx.wait("vector", pe, nk, nq)
            c4 = tt % 4
            if c4 == 0 and kq_stage_free is not None:
                ctx.wait("vector", kq_stage_free)
            v8 = ctx.op("vector", lambda e, c4=c4: e.tensor_copy(
                kstg[:, :, c4 * 128:(c4 + 1) * 128],
                ptp[:, :].rearrange("p (f t) -> p f t", f=8)), "prt")
            ctx.wait("tensor", v8)
            pe2 = None
            for fc in range(8):
                pe2 = ctx.op("tensor", lambda e, qt=qt, fc=fc: e.transpose(
                    ptp[:, fc * 128:(fc + 1) * 128], qt[:, fc * 128:(fc + 1) * 128],
                    sb["c_ident"][:, :]), "pp2" if fc == 7 else None)
            ctx.wait("vector", pe2)
            v9 = ctx.op("vector", lambda e, c4=c4: e.tensor_copy(
                qstg[:, :, c4 * 128:(c4 + 1) * 128],
                ptp[:, :].rearrange("p (f t) -> p f t", f=8)), "prt")
            ktm_free[b] = [v8, nka]
            qtm_free[b] = [v9, nqa]
            if c4 == 3:
                grp = tt // 4
                ctx.wait("sync", v9)
                last = None
                for fc in range(8):
                    for half in range(2):
                        h = 2 * fc + half
                        last = ctx.dma("sync", d["kfm"][h, :, grp * 512:(grp + 1) * 512],
                                       kstg[:, fc, :][half * 64:(half + 1) * 64, :], "kqs")
                        last = ctx.dma("sync", d["qfm"][h, :, grp * 512:(grp + 1) * 512],
                                       qstg[:, fc, :][half * 64:(half + 1) * 64, :], "kqs")
                kq_stage_free = last
        pbnd = self.psum("pbnd", [128, 1], F32, st)
        bt = self.bound(mq, mk, 1.0 / 8.0, [ctx.tok("nsd")], ptp, pbnd)
        ctx.wait("sync", ctx.tok("kqs"), ctx.tok("vst0"), ctx.tok("vst1"))
        return bt

    def s2_prep(self, st):
        ctx, sb, d, NT = self.ctx, self.sb, self.d, self.NT
        vfm = sb["vfm"]
        waq = self.sbuf("waq", [128, 8, DM], BF16, st)
        wak = self.sbuf("wak", [128, 8, DM], BF16, st)
        wav = self.sbuf("wav", [128, 8, DM], BF16, st)
        wts = []
        for (w, nm) in ((waq, "a_q_T"), (wak, "a_k_T"), (wav, "a_v_T")):
            wts.append(ctx.dma("gpsimd", w[:, :, :],
                               d[nm][:, :, :].rearrange("kc p f -> p kc f"), "wks"))
        ctx.wait("tensor", ctx.tok("wks"))
        ctx.wait("vector", ctx.tok("lnt"))
        ctx.wait("tensor", ctx.tok("lnt"))

        pq = self.psum("s2pq", [128, 2, 512], F32, st)
        pk = self.psum("s2pk", [128, 2, 512], F32, st)
        pv_ = self.psum("s2pv", [128, 2, 512], F32, st)
        ptp = self.psum("s2ptp", [128, 1024], BF16, st)
        ktm = self.sbuf("s2ktm", [128, 2, DM], BF16, st)
        qtm = self.sbuf("s2qtm", [128, 2, DM], BF16, st)
        vstg = self.sbuf("s2vstg", [128, 2, NH * 65], BF16, st)
        kstg = self.sbuf("s2kstg", [128, 8, 512], BF16, st)
        qstg = self.sbuf("s2qstg", [128, 8, 512], BF16, st)
        f1t = self.sbuf("s2f1", [128, 2, DM], F32, st)
        sqnt = self.sbuf("s2sqn", [128, 2, DM], BF16, st)
        s16t = self.sbuf("s2s16", [128, 2, NH], F32, st)
        rrt = self.sbuf("s2rr", [128, 2, 2, NH], F32, st)
        mq = self.sbuf("s2mq", [128, NT], F32, st)
        mk = self.sbuf("s2mk", [128, NT], F32, st)
        self.sbuf("bnd_t1", [128, 8], F32, st)
        self.sbuf("bnd_t1b", [128, 4], BF16, st)
        self.sbuf("bnd_t2", [128, 8], F32, st)

        qkb = self.sbuf("s2qkb", [128, 2, 2, 2, 512], F32, st)
        ktm_free = [None, None]
        qtm_free = [None, None]
        vstg_free = [None, None]
        kq_state = [None]
        ps_free = None
        tile_meta = {}

        def emit_tp(t):
            bb = t % 2
            ktx, qtx = ktm[:, bb], qtm[:, bb]
            chain_done, nk, nq, nka, nqa = tile_meta.pop(t)
            ctx.wait("tensor", chain_done, ctx.tok("prt"))
            for fc in range(8):
                ctx.op("tensor", lambda e, ktx=ktx, fc=fc: e.transpose(
                    ptp[:, fc * 128:(fc + 1) * 128], ktx[:, fc * 128:(fc + 1) * 128],
                    sb["c_ident"][:, :]), "pp2" if fc == 7 else None)
            ctx.wait("vector", ctx.tok("pp2"), nk, nq)
            c4 = t % 4
            if c4 == 0 and kq_state[0] is not None:
                ctx.wait("vector", kq_state[0])
            v8 = ctx.op("vector", lambda e, c4=c4: e.tensor_copy(
                kstg[:, :, c4 * 128:(c4 + 1) * 128],
                ptp[:, :].rearrange("p (f t) -> p f t", f=8)), "prt")
            ctx.wait("tensor", v8)
            for fc in range(8):
                ctx.op("tensor", lambda e, qtx=qtx, fc=fc: e.transpose(
                    ptp[:, fc * 128:(fc + 1) * 128], qtx[:, fc * 128:(fc + 1) * 128],
                    sb["c_ident"][:, :]), "pp2" if fc == 7 else None)
            ctx.wait("vector", ctx.tok("pp2"))
            v9 = ctx.op("vector", lambda e, c4=c4: e.tensor_copy(
                qstg[:, :, c4 * 128:(c4 + 1) * 128],
                ptp[:, :].rearrange("p (f t) -> p f t", f=8)), "prt")
            ktm_free[bb] = [v8, nka]
            qtm_free[bb] = [v9, nqa]
            if c4 == 3:
                grp = t // 4
                ctx.wait("sync", v9)
                last = None
                for fc in range(8):
                    for half in range(2):
                        h = 2 * fc + half
                        last = ctx.dma("sync", d["kfm"][h, :, grp * 512:(grp + 1) * 512],
                                       kstg[:, fc, :][half * 64:(half + 1) * 64, :], "kqs")
                        last = ctx.dma("sync", d["qfm"][h, :, grp * 512:(grp + 1) * 512],
                                       qstg[:, fc, :][half * 64:(half + 1) * 64, :], "kqs")
                kq_state[0] = last

        for tt in range(NT):
            b = tt % 2
            r0 = tt * 128
            if ps_free is not None:
                ctx.wait("tensor", *ps_free)
            mm = None
            for kc in range(8):
                for (ps, w) in ((pq, waq), (pk, wak), (pv_, wav)):
                    for fh in range(2):
                        mm = ctx.op("tensor", lambda e, kc=kc, r0=r0, ps=ps, w=w, fh=fh:
                                    e.matmul(ps[:, fh, :], vfm[:, kc, r0:r0 + 128],
                                             w[:, kc, fh * 512:(fh + 1) * 512],
                                             start=(kc == 0), stop=(kc == 7)),
                                    "pmm" if (kc == 7 and ps is pv_ and fh == 1) else None)
            mm_tok = ctx.tok("pmm")
            kt, qt, vs = ktm[:, b], qtm[:, b], vstg[:, b]
            ctx.wait("vector", mm_tok, ctx.tok("prd"))
            evq = ctx.op("vector", lambda e, b=b: e.tensor_copy(
                qkb[:, b, 0, :, :], pq[:, :, :]), "prd")
            ctx.wait("vector", evq)
            evk = ctx.op("vector", lambda e, b=b: e.tensor_copy(
                qkb[:, b, 1, :, :], pk[:, :, :]), "prd")
            ctx.wait("scalar", mm_tok)
            if vstg_free[b] is not None:
                ctx.wait("scalar", vstg_free[b])
                ctx.wait("gpsimd", vstg_free[b])
            a4 = ctx.op("scalar", lambda e, vs=vs: e.activation(
                bass.AP(vs.tensor, vs.offset,
                        [list(p) for p in vs.ap[:1]] + [[65, 8], [1, 64]]),
                pv_[:, 0, :], AF.Copy), "pra")
            a5 = ctx.op("scalar", lambda e, vs=vs: e.activation(
                bass.AP(vs.tensor, vs.offset + 8 * 65,
                        [list(p) for p in vs.ap[:1]] + [[65, 8], [1, 64]]),
                pv_[:, 1, :], AF.Copy), "pra")
            g1 = ctx.op("gpsimd", lambda e, vs=vs: e.memset(
                bass.AP(vs.tensor, vs.offset + 64,
                        [list(p) for p in vs.ap[:1]] + [[65, 16], [1, 1]]), 1.0), "gpk")
            ctx.wait("sync", a4, a5, g1)
            vstg_free[b] = ctx.dma("sync", d["vtm"][tt, :, :], vs[:, :], "vst%d" % b)
            for (which, dst, gidx, extra) in ((0, qt, 4, True),
                                              (1, kt, 5, False)):
                ctx.wait("scalar", evk)
                a1 = ctx.op("scalar", lambda e, b=b, which=which: e.activation(
                    sqnt[:, b, 0:512], qkb[:, b, which, 0, :], AF.Square), "pra")
                a1b = ctx.op("scalar", lambda e, b=b, which=which: e.activation(
                    sqnt[:, b, 512:1024], qkb[:, b, which, 1, :], AF.Square), "pra")
                ctx.wait("vector", a1b)
                v1 = ctx.op("vector", lambda e, b=b: e.tensor_reduce(
                    s16t[:, b, :], sqnt[:, b, :].rearrange("p (h x) -> p h x", h=NH),
                    AX.X, OP.add), "prd")
                ctx.wait("vector", v1)
                v2 = ctx.op("vector", lambda e, b=b: e.tensor_scalar(
                    s16t[:, b, :], s16t[:, b, :], 1.0 / HD, 1e-16, OP.mult, OP.max), "prd")
                ctx.wait("scalar", v2)
                a2 = ctx.op("scalar", lambda e, b=b: e.activation(
                    f1t[:, b, 0:NH], s16t[:, b, :], AF.Ln), "pra")
                ctx.wait("scalar", a2)
                bias = sb["c_mln8"][:, 0:1] if extra else 0.0
                a3 = ctx.op("scalar", lambda e, b=b, which=which, bias=bias: e.activation(
                    rrt[:, b, which, :], f1t[:, b, 0:NH], AF.Exp, scale=-0.5, bias=bias), "pra")
                ctx.wait("vector", a3)
                if (qtm_free[b] if which == 0 else ktm_free[b]) is not None:
                    ctx.wait("vector", *(qtm_free[b] if which == 0 else ktm_free[b]))
                vz = None
                for fh in range(2):
                    src = qkb[:, b, which, fh, :]
                    rrs = rrt[:, b, which, fh * 8:(fh + 1) * 8]
                    rb = bass.AP(rrs.tensor, rrs.offset,
                                 [rrs.ap[0], [1, 8], [0, HD]])
                    f1s = f1t[:, b, fh * 512:(fh + 1) * 512]
                    sb3 = bass.AP(src.tensor, src.offset,
                                  [src.ap[0], [HD, 8], [1, HD]])
                    db3 = bass.AP(f1s.tensor, f1s.offset,
                                  [f1s.ap[0], [HD, 8], [1, HD]])
                    vz = ctx.op("vector", lambda e, db3=db3, sb3=sb3, rb=rb:
                                e.tensor_tensor(db3, sb3, rb, OP.mult),
                                "prd" if fh == 1 else None)
                ctx.wait("vector", vz)
                vg = ctx.op("vector", lambda e, b=b, dst=dst, gidx=gidx: e.tensor_tensor(
                    dst[:, :], f1t[:, b, :], self.gamma(gidx), OP.mult), "prd")
            nk, nka = self.tile_norms(kt[:, :], ctx.tok("prd"), mk, tt,
                                      sqnt[:, b], s16t[:, b])
            nq, nqa = self.tile_norms(qt[:, :], ctx.tok("prd"), mq, tt,
                                      sqnt[:, b], s16t[:, b])
            tile_meta[tt] = (ctx.tok("prd"), nk, nq, nka, nqa)
            ps_free = [evq, evk, a4, a5]
            if tt > 0:
                emit_tp(tt - 1)
        emit_tp(NT - 1)
        pbnd = self.psum("pbnd", [128, 1], F32, st)
        bt = self.bound(mq, mk, 1.0, [ctx.tok("nsd"), ctx.tok("prt")], ptp, pbnd)
        ctx.wait("sync", ctx.tok("kqs"), ctx.tok("vst0"), ctx.tok("vst1"))
        return bt

    def attention(self, causal, scale, cbias_dep, st):
        ctx, sb, d, NT, TC = self.ctx, self.sb, self.d, self.NT, self.TC
        cb = sb["cbias"]
        ctx.wait("scalar", cbias_dep)
        G = 2
        NG = 3
        kh = self.sbuf("at_kh", [128, 2, self.T], BF16, st)
        qh = self.sbuf("at_qh", [128, 2, self.T], BF16, st)
        vh = self.sbuf("at_vh", [128, NT, NH * 65], BF16, st)
        pt = self.sbuf("at_pt", [128, NG, G * 512], BF16, st)
        rl = self.sbuf("at_rl", [1, 2, 512], F32, st)
        rlb = self.sbuf("at_rlb", [1, 2, 512], BF16, st)
        rbc = self.sbuf("at_rbc", [64, 2, 512], F32, st)
        og = self.sbuf("at_og", [64, 2, 512], BF16, st)
        pss = self.psum("at_ps", [128, NG, G * 512], F32, st)
        po = self.psum("at_po", [128, 2, 512], F32, st)
        lvv = None
        for j in range(0, NT, 4):
            lvv = ctx.dma("sync", vh[:, j:j + 4, :],
                          d["vtm"][j:j + 4, :, :].rearrange("kt p c -> p kt c"),
                          "akv")
        kv_free = [None, None]
        po_free = [None, None]
        og_free = [None, None]
        exp_free = [None] * NG
        av_free = [None] * NG

        kz = ctx.op("scalar", lambda e: (
            e.memzero(kh[:, :, :]), e.memzero(qh[:, :, :]))[-1], "akz")

        def issue_kq(h):
            hb = h % 2
            if kv_free[hb] is not None:
                ctx.wait("sync", *kv_free[hb])
            ctx.wait("sync", kz)
            ctx.dma("sync", kh[0:64, hb, :], d["kfm"][h, :, :], "akl%d" % hb)
            return ctx.dma("sync", qh[0:64, hb, :], d["qfm"][h, :, :], "akl%d" % hb)

        kq_toks = {0: issue_kq(0)}
        for h in range(NH):
            hb = h % 2
            if h + 1 < NH:
                kq_toks[h + 1] = issue_kq(h + 1)
            lq = kq_toks[h]
            ctx.wait("tensor", lq, lvv)
            for qc in range(TC):
                ob = qc % 2
                kt_hi = min(NT, (qc + 1) * 4) if causal else NT
                if po_free[ob] is not None:
                    ctx.wait("tensor", po_free[ob])
                groups = [(g0, min(G, kt_hi - g0)) for g0 in range(0, kt_hi, G)]

                def emit_S(gi):
                    ktg, gn = groups[gi]
                    g = gi % NG
                    if exp_free[g] is not None:
                        ctx.wait("tensor", exp_free[g])
                    sv = None
                    for i in range(gn):
                        kt = ktg + i
                        sv = ctx.op("tensor", lambda e, kt=kt, i=i, qc=qc, hb=hb, g=g:
                                    e.matmul(pss[:, g, i * 512:(i + 1) * 512],
                                             kh[:, hb, kt * 128:(kt + 1) * 128],
                                             qh[:, hb, qc * 512:(qc + 1) * 512],
                                             start=True, stop=True),
                                    "asm" if i == gn - 1 else None)
                    return sv

                def emit_exp(gi, sv):
                    ktg, gn = groups[gi]
                    g = gi % NG
                    ctx.wait("scalar", sv)
                    if av_free[g] is not None:
                        ctx.wait("scalar", av_free[g])
                    ex = ctx.op("scalar", lambda e, gn=gn, g=g: e.activation(
                        pt[:, g, 0:gn * 512], pss[:, g, 0:gn * 512], AF.Exp,
                        bias=cb[:, 0:1], scale=scale), "aex")
                    exp_free[g] = ex
                    mdep = ex
                    if causal:
                        for i in range(gn):
                            kt = ktg + i
                            j = kt - qc * 4
                            if 0 <= j <= 3:
                                ctx.wait("vector", ex)
                                if j > 0:
                                    mdep = ctx.op("vector", lambda e, i=i, j=j, g=g:
                                                  e.memset(pt[:, g, i * 512:i * 512 + j * 128], 0.0),
                                                  "amk")
                                mdep = ctx.op("vector", lambda e, i=i, j=j, g=g:
                                              e.tensor_tensor(
                                                  pt[:, g, i * 512 + j * 128:i * 512 + (j + 1) * 128],
                                                  pt[:, g, i * 512 + j * 128:i * 512 + (j + 1) * 128],
                                                  sb["c_mask"][:, :], OP.mult), "amk")
                    return mdep

                def emit_AV(gi, ex, mdep):
                    ktg, gn = groups[gi]
                    g = gi % NG
                    items = list(range(gn))
                    if causal:
                        items.sort(key=lambda i: (ktg + i != 0,
                                                  0 <= (ktg + i) - qc * 4 <= 3))
                    av = None
                    ndone = 0
                    for i in items:
                        kt = ktg + i
                        diag = causal and 0 <= kt - qc * 4 <= 3
                        ctx.wait("tensor", mdep if (diag or kt == 0) else ex)
                        ndone += 1
                        av = ctx.op("tensor", lambda e, kt=kt, i=i, ob=ob, h=h, g=g,
                                    first=(kt == 0), last=(kt == kt_hi - 1):
                                    e.matmul(po[0:65, ob, :],
                                             vh[:, kt, h * 65:(h + 1) * 65],
                                             pt[:, g, i * 512:(i + 1) * 512],
                                             start=first, stop=last),
                                    "aav" if ndone == gn else None)
                    av_free[g] = av
                    return av

                pend = []
                for gi in range(len(groups)):
                    sv = emit_S(gi)
                    md = emit_exp(gi, sv)
                    ex_tok = exp_free[gi % NG]
                    if len(pend) >= 2:
                        e0 = pend.pop(0)
                        emit_AV(*e0)
                    pend.append((gi, ex_tok, md))
                for e0 in pend:
                    emit_AV(*e0)
                ctx.wait("vector", av_free[(len(groups) - 1) % NG], ctx.tok("aav"))
                if og_free[ob] is not None:
                    ctx.wait("vector", og_free[ob])
                v1 = ctx.op("vector", lambda e, ob=ob: e.tensor_copy(
                    og[:, ob, :], po[0:64, ob, :]), "aed")
                ctx.wait("vector", v1)
                v2 = ctx.op("vector", lambda e, ob=ob: e.tensor_copy(
                    rl[:, ob, :], po[64:65, ob, :]), "aed")
                po_free[ob] = v2
                ctx.wait("sync", v2)
                ctx.dma("sync", d["lsum"][h, qc * 512:(qc + 1) * 512],
                        rl[0:1, ob, :], "aos%d" % ob)
                og_free[ob] = ctx.dma("sync", d["ofm"][h // 2, qc,
                                                       (h % 2) * 64:(h % 2) * 64 + 64, :],
                                      og[:, ob, :], "aos%d" % ob)
            kv_free[hb] = [ctx.tok("aav"), ctx.tok("asm")]
        ctx.wait("sync", ctx.tok("aos0"), ctx.tok("aos1"))

    def oproj(self, w_name, src, dst, add_bo, st):
        ctx, sb, d, NT = self.ctx, self.sb, self.d, self.NT
        wo = self.sbuf("opwo", [128, 8, DM], BF16, st)
        ctx.dma("gpsimd", wo[:, :, :],
                d[w_name][:, :, :].rearrange("dc p f -> p dc f"), "wks")
        ctx.wait("tensor", ctx.tok("wks"))
        osb = self.sbuf("oposb", [128, 2, 8, 512], BF16, st)
        osn = self.sbuf("oposn", [128, 2, 8, 128], BF16, st)
        osc = self.sbuf("oposc", [128, 2, 8, 512], BF16, st)
        res = self.sbuf("opres", [128, 4, DM], F32, st)
        out = self.sbuf("opout", [128, 4, DM], F32, st)
        lsb = self.sbuf("oplsb", [16, self.T], F32, st)
        pp = self.psum("oppp", [128, 2, 2, 512], F32, st)
        lt = ctx.dma("sync", lsb[:, :], d["lsum"][:, :], "opll")
        ctx.wait("vector", lt)
        vr = ctx.op("vector", lambda e: e.reciprocal(lsb[:, :], lsb[:, :]), "opd")
        ctx.wait("sync", vr)
        st1 = ctx.dma("sync", d["lsum"][:, :], lsb[:, :], "opll")
        ctx.wait("gpsimd", st1)
        ot_toks = {}
        oc_toks = {}
        rt_toks = {}
        vn_toks = {}
        mm_toks = {}
        vv_toks = {}
        so_toks = {}

        def emit_vv(t):
            b, b4 = t % 2, t % 4
            ctx.wait("vector", mm_toks[t], rt_toks[t])
            if t >= 4:
                ctx.wait("vector", so_toks[t - 4])
            ppf = pp[:, b, :, :].rearrange("p a c -> p (a c)")
            vv = ctx.op("vector", lambda e, b4=b4, ppf=ppf: e.tensor_tensor(
                out[:, b4, :], ppf, res[:, b4, :], OP.add), "opd")
            if add_bo:
                ctx.wait("vector", vv)
                vv = ctx.op("vector", lambda e, b4=b4: e.tensor_tensor(
                    out[:, b4, :], out[:, b4, :], sb["c_bo"][:, :], OP.add), "opd")
            vv_toks[t] = vv
            ctx.wait("sync", vv)
            so_toks[t] = ctx.dma("sync", dst[t * 128:(t + 1) * 128, :],
                                 out[:, b4, :], "opo%d" % (t % 2))

        for tt in range(NT):
            b = tt % 2
            b4 = tt % 4
            g = tt // 4
            gb = g % 2
            if tt % 4 == 0:
                if tt >= 8:
                    ctx.wait("sync", vn_toks[tt - 5])
                    ctx.wait("gpsimd", vn_toks[tt - 5])
                ot = None
                for fc in range(8):
                    ot = ctx.dma("sync", osb[:, gb, fc, :], d["ofm"][fc, g, :, :],
                                 "opl%d" % gb)
                ot_toks[g] = ot
                oc = None
                for fc in range(8):
                    for half in range(2):
                        oc = ctx.dma(
                            "gpsimd",
                            osc[:, gb, fc, :][half * 64:(half + 1) * 64, :],
                            bass.AP(d["lsum"].tensor,
                                    (2 * fc + half) * self.T + g * 512,
                                    [[0, 64], [1, 512]]), "opc%d" % gb)
                oc_toks[g] = oc
            if tt >= 4:
                ctx.wait("scalar", vv_toks[tt - 4])
            rt_toks[tt] = ctx.dma("scalar", res[:, b4, :],
                                  src[tt * 128:(tt + 1) * 128, :], "opr")
            ctx.wait("vector", ot_toks[g], oc_toks[g])
            if tt >= 2:
                ctx.wait("vector", mm_toks[tt - 2])
            vn = ctx.op("vector", lambda e, b=b, gb=gb, b4=b4: e.tensor_tensor(
                osn[:, b, :, :], osb[:, gb, :, b4 * 128:(b4 + 1) * 128],
                osc[:, gb, :, b4 * 128:(b4 + 1) * 128], OP.mult), "opn")
            vn_toks[tt] = vn
            ctx.wait("tensor", vn)
            if tt >= 2:
                ctx.wait("tensor", vv_toks[tt - 2])
            mm = None
            for dc in range(8):
                for fh in range(2):
                    mm = ctx.op("tensor", lambda e, dc=dc, fh=fh, b=b:
                                e.matmul(pp[:, b, fh, :],
                                         osn[:, b, dc, :],
                                         wo[:, dc, fh * 512:(fh + 1) * 512],
                                         start=(dc == 0), stop=(dc == 7)),
                                "pmm" if (dc == 7 and fh == 1) else None)
            mm_toks[tt] = mm
            if tt > 0:
                emit_vv(tt - 1)
        emit_vv(NT - 1)
        ctx.wait("sync", ctx.tok("opo0"), ctx.tok("opo1"))

    def moe_router_setup(self, st):
        ctx, sb, d, NT = self.ctx, self.sb, self.d, self.NT
        w8 = self.sbuf("moe_w", [128, NT, 8], F32, st)
        rt_st = self.estack()
        pr = self.psum("moe_pr", [128, 2, 8], F32, rt_st)
        ptpr = self.psum("moe_ptpr", [128, 2, 1024], BF16, rt_st)
        rf = self.sbuf("moe_rf", [128, 4, 8], F32, rt_st)
        self._zb = None
        rwf = self.sbuf("moe_rwf", [128, 8, 8], F32, rt_st)
        rwhi = self.sbuf("moe_rwhi", [128, 8, 8], BF16, rt_st)
        rwlo = self.sbuf("moe_rwlo", [128, 8, 8], BF16, rt_st)
        lot = self.sbuf("moe_lot", [128, 2, DM], BF16, rt_st)
        lofm = self.sbuf("moe_lofm", [128, 2, 8, 128], BF16, rt_st)
        hfm = self.sbuf("moe_hfm", [128, 2, 8, 128], BF16, rt_st)
        lw = ctx.dma("sync", rwf[:, :, :],
                     d["c_router"][:, :, :].rearrange("kc p e -> p kc e"), "rtl")
        ctx.wait("vector", lw)
        vh = ctx.op("vector", lambda e: e.tensor_copy(rwhi[:, :, :], rwf[:, :, :]), "rtd")
        ctx.wait("vector", vh)
        vl = ctx.op("vector", lambda e: e.tensor_tensor(
            rwlo[:, :, :], rwf[:, :, :], rwhi[:, :, :], OP.subtract), "rtd")
        g1 = ctx.op("gpsimd", lambda e: e.memset(w8[:, :, 7:8], 1.0), "gpk")
        zb = self.sbuf("moe_zb", [128, NT, 8], F32, rt_st)
        pp = self.sbuf("moe_pp", [128, NT, 8], F32, rt_st)
        ms = self.sbuf("moe_ms", [128, NT, 8], F32, rt_st)
        sc = self.sbuf("moe_sc", [128, 4, NT], F32, rt_st)
        self._rt = dict(w8=w8, rt_st=rt_st, pr=pr, ptpr=ptpr, rf=rf, rwhi=rwhi,
                        rwlo=rwlo, lot=lot, lofm=lofm, hfm=hfm, wdep=(vl, g1),
                        zb=zb, pp=pp, ms=ms, sc=sc)
        return self.moe_router_tile

    def moe_router_tile(self, tt, f1, g, v6, v7):
        ctx, sb = self.ctx, self.sb
        R = self._rt
        b = tt % 2
        pr, ptpr, rf, w8 = R["pr"], R["ptpr"], R["rf"], R["w8"]
        lot, lofm, hfm = R["lot"][:, b], R["lofm"][:, b], R["hfm"][:, b]
        ctx.wait("sync", v6)
        xst = ctx.dma("sync", self.d["xn_tm"][tt * 128:(tt + 1) * 128, :],
                      g[:, :], "xts")
        ctx.wait("vector", v6, ctx.tok("rmm"))
        vlo = ctx.op("vector", lambda e: e.tensor_tensor(
            lot[:, :], f1[:, :], g[:, :], OP.subtract), "rtd")
        ctx.wait("tensor", vlo, v6, ctx.tok("rtd"))
        pe = None
        for fc in range(8):
            ctx.op("tensor", lambda e, fc=fc: e.transpose(
                ptpr[:, 0, fc * 128:(fc + 1) * 128],
                g[:, fc * 128:(fc + 1) * 128], sb["c_ident"][:, :]), None)
            pe = ctx.op("tensor", lambda e, fc=fc: e.transpose(
                ptpr[:, 1, fc * 128:(fc + 1) * 128], lot[:, fc * 128:(fc + 1) * 128],
                sb["c_ident"][:, :]), "rtp" if fc == 7 else None)
        ctx.wait("vector", pe)
        vev = ctx.op("vector", lambda e: e.tensor_copy(
            hfm[:, :, :], ptpr[:, 0, :].rearrange("p (f t) -> p f t", f=8)), "rtd")
        vev = ctx.op("vector", lambda e: e.tensor_copy(
            lofm[:, :, :], ptpr[:, 1, :].rearrange("p (f t) -> p f t", f=8)), "rtd")
        ctx.wait("tensor", vev, *R["wdep"])
        mm = None
        n = 0
        for kc in range(8):
            hi = hfm[:, kc, :]
            lo = lofm[:, kc, :]
            for (sa, wb_) in ((hi, R["rwhi"][:, kc, :]), (hi, R["rwlo"][:, kc, :]),
                              (lo, R["rwhi"][:, kc, :]), (lo, R["rwlo"][:, kc, :])):
                n += 1
                mm = ctx.op("tensor", lambda e, sa=sa, wb_=wb_, b=b, first=(n == 1),
                            last=(n == 32): e.matmul(pr[:, b, :], sa, wb_,
                                                     start=first, stop=last),
                            "rmm" if n == 32 else None)
        ctx.wait("vector", mm)
        vz = ctx.op("vector", lambda e, b=b, tt=tt: e.scalar_tensor_tensor(
            R["zb"][:, tt, :], pr[:, b, :], 1.0, sb["c_rbias"][:, :],
            OP.mult, OP.add), "rtd")
        return (vz, xst)

    def moe_topk_batch(self):
        ctx, sb, NT = self.ctx, self.sb, self.NT
        R = self._rt
        zb, w8 = R["zb"], R["w8"]
        p = R["pp"]
        ms = R["ms"]
        m1, m2, den, tmp = (R["sc"][:, i, :] for i in range(4))
        ctx.wait("scalar", ctx.tok("rtd"))
        a1 = ctx.op("scalar", lambda e: e.activation(
            p[:, :, :], zb[:, :, :], AF.Exp, scale=-1.0), "rta")
        ctx.wait("vector", a1)
        v1 = ctx.op("vector", lambda e: e.tensor_scalar(
            p[:, :, :], p[:, :, :], 1.0, None, OP.add), "rtd")
        ctx.wait("vector", v1)
        v2 = ctx.op("vector", lambda e: e.reciprocal(p[:, :, :], p[:, :, :]), "rtd")
        ctx.wait("vector", v2)
        v3 = ctx.op("vector", lambda e: e.tensor_reduce(
            m1, p[:, :, 0:7], AX.X, OP.max), "rtd")
        ctx.wait("vector", v3)
        for j in range(7):
            ctx.op("vector", lambda e, j=j: e.tensor_tensor(
                ms[:, :, j], p[:, :, j], m1, OP.is_lt), "rtd")
        ctx.wait("vector", ctx.tok("rtd"))
        for j in range(7):
            ctx.op("vector", lambda e, j=j: e.tensor_tensor(
                ms[:, :, j], p[:, :, j], ms[:, :, j], OP.mult), "rtd")
        ctx.wait("vector", ctx.tok("rtd"))
        v4 = ctx.op("vector", lambda e: e.tensor_reduce(
            m2, ms[:, :, 0:7], AX.X, OP.max), "rtd")
        ctx.wait("vector", v4)
        v5 = ctx.op("vector", lambda e: e.tensor_tensor(den, m1, m2, OP.add), "rtd")
        ctx.wait("vector", v5)
        v6 = ctx.op("vector", lambda e: e.reciprocal(den, den), "rtd")
        ctx.wait("vector", v6)
        for j in range(7):
            ctx.op("vector", lambda e, j=j: e.tensor_tensor(
                ms[:, :, j], p[:, :, j], m2, OP.is_ge), "rtd")
        ctx.wait("vector", ctx.tok("rtd"))
        for j in range(7):
            ctx.op("vector", lambda e, j=j: e.tensor_tensor(
                tmp, p[:, :, j], den, OP.mult), "rtd")
            ctx.wait("vector", ctx.tok("rtd"))
            ctx.op("vector", lambda e, j=j: e.tensor_tensor(
                w8[:, :, j], tmp, ms[:, :, j], OP.mult), "rtd")
            ctx.wait("vector", ctx.tok("rtd"))
        return ctx.tok("rtd")

    def moe_routing_meta(self, st):
        ctx, sb, NT, NCH = self.ctx, self.sb, self.NT, self.NCH
        I32 = mybir.dt.int32
        R = self._rt
        rt_st = R["rt_st"]
        ms, w8 = R["ms"], R["w8"]

        def A(base, dims, off=0):
            return bass.AP(base.tensor, base.offset + off, dims)
        msb, w8b = ms[:, :, :], w8[:, :, :]
        scn_a = self.sbuf("rm_scna", [128, NT, 7], F32, rt_st)
        scn_b = self.sbuf("rm_scnb", [128, NT, 7], F32, rt_st)
        cnt8 = self.sbuf("rm_cnt", [128, 8], F32, rt_st)
        cntb8 = self.sbuf("rm_cntb", [128, 8], BF16, rt_st)
        pre8 = self.sbuf("rm_pre", [128, 8], F32, rt_st)
        tot8 = self.sbuf("rm_tot", [128, 8], F32, rt_st)
        ns8 = self.sbuf("rm_ns", [128, 8], F32, rt_st)
        segi8 = self.sbuf("rm_segi", [128, 8], F32, rt_st)
        sege8 = self.sbuf("rm_sege", [128, 8], F32, rt_st)
        base8 = self.sbuf("rm_base", [128, 8], F32, rt_st)
        t37 = self.sbuf("rm_t37", [128, 7, NCH], F32, rt_st)
        t77 = self.sbuf("rm_t77", [128, NT, 7, 7], F32, rt_st)
        fst = self.sbuf("rm_fst", [128, NT, 7], F32, rt_st)
        Sdt = self.sbuf("rm_sd", [128, NT, 7], F32, rt_st)
        sAf = self.sbuf("rm_saf", [128, NT], F32, rt_st)
        sBf = self.sbuf("rm_sbf", [128, NT], F32, rt_st)
        wAf = self.sbuf("rm_waf", [128, NT], F32, rt_st)
        wBf = self.sbuf("rm_wbf", [128, NT], F32, rt_st)
        teid = self.sbuf("rm_teid", [128, NCH, 7], F32, rt_st)
        eidv = self.sbuf("rm_eid", [128, NCH], F32, rt_st)
        tgu = self.sbuf("rm_tgu", [128, NCH, 16], F32, rt_st)
        pmt = self.psum("rm_pmt", [128, 2, 8], F32, rt_st)
        sAi = self.sbuf("rm_sai", [128, NT], I32, st, side="right")
        sBi = self.sbuf("rm_sbi", [128, NT], I32, st, side="right")
        idx_gu = self.sbuf("rm_igu", [128, NCH, 16], I32, st, side="right")
        idx_dn = self.sbuf("rm_idn", [128, NCH], I32, st, side="right")
        self._sp = dict(sAi=sAi, sBi=sBi, idx_gu=idx_gu, idx_dn=idx_dn)

        def v(fn):
            t = ctx.op("vector", fn, "rtd")
            ctx.wait("vector", t)
            return t
        pm = msb.ap[0]
        sel3 = A(msb, [pm, [8, NT], [1, 7]])
        selT = A(msb, [pm, [1, 7], [8, NT]])
        sel4 = A(msb, [pm, [8, NT], [0, 7], [1, 7]])
        w83 = A(w8b, [w8b.ap[0], [8, NT], [1, 7]])
        lt7_b, le7_b = self.rmeta("lt7", 49), self.rmeta("le7", 49)
        thr_b, chk_b = self.rmeta("thr", NCH), self.rmeta("chunk", NCH)
        ipp_b, pid_b = self.rmeta("ipp", 16), self.rmeta("pid", 1)

        v(lambda e: e.tensor_reduce(cnt8[:, 0:7], selT, AX.X, OP.add))
        v(lambda e: e.tensor_copy(cntb8[:, 0:7], cnt8[:, 0:7]))
        v(lambda e: e.tensor_copy(scn_a[:, :, :], sel3))
        src, dst = scn_a, scn_b
        k = 1
        while k < NT:
            v(lambda e, s=src, d2=dst, k=k: e.tensor_copy(
                d2[:, 0:k, :], s[:, 0:k, :]))
            v(lambda e, s=src, d2=dst, k=k: e.tensor_tensor(
                d2[:, k:NT, :], s[:, k:NT, :], s[:, 0:NT - k, :], OP.add))
            src, dst = dst, src
            k *= 2
        scn_e, tmps = src, dst
        v(lambda e, s=scn_e: e.tensor_tensor(
            s[:, :, :], s[:, :, :], sel3, OP.subtract))
        ctx.wait("tensor", ctx.tok("rtd"), ctx.tok("gpk"))
        ctx.op("tensor", lambda e: e.matmul(
            pmt[:, 0, 0:7], sb["c_mask"][:, :], cntb8[:, 0:7],
            start=True, stop=True), None)
        mm2 = ctx.op("tensor", lambda e: e.matmul(
            pmt[:, 1, 0:7], sb["c_ones128"][:, :], cntb8[:, 0:7],
            start=True, stop=True), "rmm")
        ctx.wait("vector", mm2)
        v(lambda e: e.tensor_tensor(
            pre8[:, 0:7], pmt[:, 0, 0:7], cnt8[:, 0:7], OP.subtract))
        v(lambda e: e.tensor_copy(tot8[:, 0:7], pmt[:, 1, 0:7]))
        v(lambda e: e.tensor_tensor(
            t37[:, :, :], A(tot8[:, :], [tot8[:, :].ap[0], [1, 7], [0, NCH]]),
            A(thr_b, [thr_b.ap[0], [0, 7], [1, NCH]]), OP.is_gt))
        v(lambda e: e.tensor_reduce(ns8[:, 0:7], t37[:, :, :], AX.X, OP.add))
        v(lambda e: e.tensor_tensor(
            t37[:, :, 0:7], A(ns8[:, :], [ns8[:, :].ap[0], [0, 7], [1, 7]]),
            A(le7_b, [le7_b.ap[0], [7, 7], [1, 7]]), OP.mult))
        v(lambda e: e.tensor_reduce(segi8[:, 0:7], t37[:, :, 0:7], AX.X, OP.add))
        v(lambda e: e.tensor_tensor(
            sege8[:, 0:7], segi8[:, 0:7], ns8[:, 0:7], OP.subtract))
        v(lambda e: e.scalar_tensor_tensor(
            base8[:, 0:7], sege8[:, 0:7], 512.0, pre8[:, 0:7], OP.mult, OP.add))
        v(lambda e: e.tensor_tensor(
            Sdt[:, :, :], scn_e[:, :, :],
            A(base8[:, :], [base8[:, :].ap[0], [0, NT], [1, 7]]), OP.add))
        v(lambda e: e.tensor_tensor(
            t77[:, :, :, :], sel4,
            A(lt7_b, [lt7_b.ap[0], [0, NT], [7, 7], [1, 7]]), OP.mult))
        v(lambda e: e.tensor_reduce(fst[:, :, :], t77[:, :, :, :], AX.X, OP.add))
        v(lambda e: e.tensor_scalar(
            fst[:, :, :], fst[:, :, :], 0.0, None, OP.is_equal))
        v(lambda e: e.tensor_tensor(fst[:, :, :], fst[:, :, :], sel3, OP.mult))
        v(lambda e: e.tensor_tensor(
            tmps[:, :, :], Sdt[:, :, :], fst[:, :, :], OP.mult))
        v(lambda e: e.tensor_reduce(sAf[:, :], tmps[:, :, :], AX.X, OP.add))
        v(lambda e: e.tensor_tensor(tmps[:, :, :], Sdt[:, :, :], sel3, OP.mult))
        v(lambda e: e.tensor_reduce(sBf[:, :], tmps[:, :, :], AX.X, OP.add))
        v(lambda e: e.tensor_tensor(sBf[:, :], sBf[:, :], sAf[:, :], OP.subtract))
        v(lambda e: e.tensor_tensor(tmps[:, :, :], w83, fst[:, :, :], OP.mult))
        v(lambda e: e.tensor_reduce(wAf[:, :], tmps[:, :, :], AX.X, OP.add))
        v(lambda e: e.tensor_reduce(wBf[:, :], w83, AX.X, OP.add))
        v(lambda e: e.tensor_tensor(wBf[:, :], wBf[:, :], wAf[:, :], OP.subtract))
        v(lambda e: e.tensor_scalar(
            sAf[:, :], sAf[:, :], float(self.NSLOT - 1), None, OP.min))
        v(lambda e: e.tensor_scalar(
            sBf[:, :], sBf[:, :], float(self.NSLOT - 1), None, OP.min))
        v(lambda e: e.tensor_copy(sAi[:, :], sAf[:, :]))
        v(lambda e: e.tensor_copy(sBi[:, :], sBf[:, :]))
        v(lambda e: e.tensor_tensor(
            teid[:, :, :], A(chk_b, [chk_b.ap[0], [1, NCH], [0, 7]]),
            A(segi8[:, :], [segi8[:, :].ap[0], [0, NCH], [1, 7]]), OP.is_ge))
        v(lambda e: e.tensor_reduce(eidv[:, :], teid[:, :, :], AX.X, OP.add))
        v(lambda e: e.scalar_tensor_tensor(
            tgu[:, :, :], A(eidv[:, :], [eidv[:, :].ap[0], [1, NCH], [0, 16]]),
            2048.0, A(ipp_b, [ipp_b.ap[0], [0, NCH], [1, 16]]), OP.mult, OP.add))
        v(lambda e: e.tensor_copy(idx_gu[:, :, :], tgu[:, :, :]))
        v(lambda e: e.scalar_tensor_tensor(
            eidv[:, :], eidv[:, :], 128.0,
            A(pid_b, [pid_b.ap[0], [0, NCH]]), OP.mult, OP.add))
        v(lambda e: e.tensor_copy(idx_dn[:, :], eidv[:, :]))
        ctx.wait("sync", ctx.tok("rtd"))
        ctx.dma("sync", self.d["ridx"][0, :, :], sAi[:, :], "rms")
        ctx.dma("sync", self.d["ridx"][1, :, :], sBi[:, :], "rms")
        ctx.dma("sync", self.d["rw"][0, :, :], wAf[:, :], "rms")
        ctx.dma("sync", self.d["rw"][1, :, :], wBf[:, :], "rms")
        self._sp["meta_tok"] = ctx.tok("rtd")
        self._sp["store_tok"] = ctx.tok("rms")

    def moe_experts(self, st):
        ctx, sb, d, NT, T = self.ctx, self.sb, self.d, self.NT, self.T
        NSH, NCH = self.NSH, self.NCH
        NK = NSH + NCH
        w8 = self._rt["w8"]
        rt_st = self._rt["rt_st"]
        self.moe_topk_batch()
        self.moe_routing_meta(st)
        if "dbg_w8" in self.dbg:
            dw = self.dram("dbg_w8", [128, NT, 8], F32, kind="ExternalOutput")
            ctx.wait("sync", ctx.tok("rtd"), ctx.tok("gpk"))
            ctx.dma("sync", dw[:, :, :], w8[:, :, :], "dbgs")
            ctx.wait("sync", ctx.tok("dbgs"))
        ctx.barrier()
        self.close_stack(rt_st)
        SP = self._sp
        sAi, sBi = SP["sAi"], SP["sBi"]
        idx_gu, idx_dn = SP["idx_gu"], SP["idx_dn"]
        gu = self.sbuf("moe_gu", [128, 4, 2, 8, 128], BF16, st)
        dnb = self.sbuf("moe_dn", [128, 2, 16, 2, 512], BF16, st)
        hsb = self.sbuf("moe_h", [128, 16, 512], BF16, st)
        stg = self.sbuf("moe_stg", [128, 4, DM], F32, st)
        eo_sb = self.sbuf("moe_eo", [128, 2, 4, DM], BF16, st)
        xtm = self.sbuf("moe_xtm", [128, 2, 4, DM], BF16, st)
        xfm = self.sbuf("moe_xfm", [128, 2, 8, 512], BF16, st)
        pay = self.sbuf("moe_pay", [128, 2, DM], BF16, st)
        sg = self.sbuf("moe_sg", [128, 512], BF16, st)
        ptx = self.psum("moe_ptx", [128, 2, 1024], BF16, st)
        pg = self.psum("moe_pg", [128, 512], F32, st)
        pu = self.psum("moe_pu", [128, 512], F32, st)
        pg2 = self.psum("moe_pg2", [128, 512], F32, st)
        pu2 = self.psum("moe_pu2", [128, 512], F32, st)
        pd = self.psum("moe_pd", [128, 2, 512], F32, st)
        guview = d["gu_bf"][:, :]
        dnview = d["dn_bf"][:, :]
        wc, xz = self._wprep
        ctx.wait("gpsimd", self._wc7)
        self._wcR = wc

        def flat2(apx, n):
            return bass.AP(apx.tensor, apx.offset, [apx.ap[0], [1, n]])

        S = dict(gu_free=[None] * 4, h_evac=[None, None], pd_free=[None, None],
                 ptx_free=[None, None], tp_done=[None, None],
                 ev_done=[None, None], gu_last=[None, None], stg_free=None,
                 eo_free=[None, None], pay_free=[None] * 4, dn_tok={},
                 down_done={}, xtl_tok={})

        def dn_slot(k):
            if k < NSH:
                return 0
            return 1 if (k - NSH) % 2 == 0 else 0

        def dn_load(k):
            s = dn_slot(k)
            prev = None
            if k == NSH + 1:
                prev = NSH - 1
            elif k >= NSH + 2:
                prev = k - 2
            if prev is not None:
                ctx.wait("gpsimd", S["down_done"][prev])
            if k < NSH:
                base = flat2(dnb[:, s, :, :, :], 16384)
                for j in range(4):
                    t = ctx.dma("gpsimd",
                                bass.AP(base.tensor, base.offset + j * 4096,
                                        [list(base.ap[0]), [1, 4096]]),
                                bass.AP(dnview.tensor,
                                        dnview.offset + 7 * 128 * 16384 + j * 4096,
                                        [[16384, 128], [1, 4096]]),
                                "mdw%d" % s)
            else:
                ctx.wait("gpsimd", SP["meta_tok"], self._wcR)
                rc = k - NSH
                t = ctx.idma(flat2(dnb[:, s, :, :, :], 16384), None, dnview,
                             IOA(idx_dn[:, rc:rc + 1], 0), "mdw%d" % s)
            S["dn_tok"][k] = t

        def issue_gu(k, ip):
            rb = ip % 4
            if S["gu_free"][rb] is not None:
                ctx.wait("gpsimd", S["gu_free"][rb])
            if k < NSH:
                r0 = (7 * 16 + ip) * 128
                return ctx.dma("gpsimd", flat2(gu[:, rb, :, :, :], 2048),
                               guview[r0:r0 + 128, :], "mgw%d" % rb)
            ctx.wait("gpsimd", SP["meta_tok"], self._wcR)
            rc = k - NSH
            return ctx.idma(flat2(gu[:, rb, :, :, :], 2048), None, guview,
                            IOA(idx_gu[:, rc, ip:ip + 1], 0), "mgw%d" % rb)

        def xtl(k):
            kb = k % 2
            ctx.wait("sync", S["tp_done"][kb])
            if k == NSH:
                ctx.wait("sync", ctx.tok("dsp0"), ctx.tok("dsp1"))
            src = (d["xn_tm"] if k < NSH else d["xnd"])
            off = (k if k < NSH else k - NSH) * 512 * DM
            S["xtl_tok"][k] = ctx.dma(
                "sync", xtm[:, kb, :, :],
                bass.AP(src.tensor, off, [[DM, 128], [128 * DM, 4], [1, DM]]),
                "xtl%d" % kb)

        def tp_group(k):
            kb = k % 2
            last = None
            for j in range(4):
                tb = j % 2
                ctx.wait("tensor", S["xtl_tok"][k], S["ptx_free"][tb])
                pe = None
                for fc in range(8):
                    pe = ctx.op("tensor", lambda e, kb=kb, j=j, tb=tb, fc=fc:
                                e.transpose(
                                    ptx[:, tb, fc * 128:(fc + 1) * 128],
                                    xtm[:, kb, j, fc * 128:(fc + 1) * 128],
                                    sb["c_ident"][:, :]),
                                "mtp" if fc == 7 else None)
                ctx.wait("vector", pe, S["gu_last"][kb])
                base = xfm[:, :, :, :]
                dst = bass.AP(base.tensor, base.offset + kb * 8 * 512 + j * 128,
                              [base.ap[0], [512, 8], [1, 128]])
                last = ctx.op("vector", lambda e, tb=tb, dst=dst: e.tensor_copy(
                    dst, ptx[:, tb, :].rearrange("p (f t) -> p f t", f=8)), "mte")
                S["ptx_free"][tb] = last
            S["tp_done"][kb] = ctx.tok("mtp")
            S["ev_done"][kb] = last

        def gu_mm(k):
            kb = k % 2
            gu_toks = {}
            for ip in range(3):
                gu_toks[ip] = issue_gu(k, ip)
            mm = None
            for ip in range(16):
                rb = ip % 4
                if ip + 3 < 16:
                    gu_toks[ip + 3] = issue_gu(k, ip + 3)
                pga, pua = (pg, pu) if ip % 2 == 0 else (pg2, pu2)
                ctx.wait("tensor", gu_toks[ip], S["ev_done"][kb])
                if S["h_evac"][ip % 2] is not None:
                    ctx.wait("tensor", *S["h_evac"][ip % 2])
                for kc in range(8):
                    mm = ctx.op("tensor", lambda e, kc=kc, rb=rb, kb=kb, pga=pga:
                                e.matmul(pga[:, :], gu[:, rb, 0, kc, :],
                                         xfm[:, kb, kc, :],
                                         start=(kc == 0), stop=(kc == 7)), None)
                    mm = ctx.op("tensor", lambda e, kc=kc, rb=rb, kb=kb, pua=pua:
                                e.matmul(pua[:, :], gu[:, rb, 1, kc, :],
                                         xfm[:, kb, kc, :],
                                         start=(kc == 0), stop=(kc == 7)),
                                "gmm" if kc == 7 else None)
                S["gu_free"][rb] = mm
                ctx.wait("scalar", mm, ctx.tok("mhd"))
                aa = ctx.op("scalar", lambda e, pga=pga: e.activation(
                    sg[:, :], pga[:, :], AF.Silu), "mha")
                ctx.wait("vector", aa, ctx.tok("dmm"))
                hv = ctx.op("vector", lambda e, ip=ip, pua=pua: e.tensor_tensor(
                    hsb[:, ip, :], sg[:, :], pua[:, :], OP.mult), "mhd")
                S["h_evac"][ip % 2] = [aa, hv]
            S["gu_last"][kb] = mm

        def dispatch(k):
            for tj in range(4):
                tt = 4 * k + tj
                pb = tt % 2
                if S["pay_free"][pb] is not None:
                    ctx.wait("sync", *S["pay_free"][pb])
                pl = ctx.dma("sync", pay[:, pb, :],
                             d["xn_tm"][tt * 128:(tt + 1) * 128, :], "pyl%d" % pb)
                ctx.wait("gpsimd", pl, SP["meta_tok"], xz)
                t1 = ctx.idma(d["xnd"][:, :], IOA(sAi[:, tt:tt + 1], 0),
                              pay[:, pb, :], None, "dsp%d" % pb)
                t2 = ctx.idma(d["xnd"][:, :], IOA(sBi[:, tt:tt + 1], 0),
                              pay[:, pb, :], None, "dsp%d" % pb)
                S["pay_free"][pb] = [t2]

        def down(k):
            kb = k % 2
            s = dn_slot(k)
            shared = k < NSH
            dtok = S["dn_tok"][k if (k >= NSH or k == 0) else 0]
            ctx.wait("tensor", ctx.tok("mhd"), dtok)
            mm = None
            for tt4 in range(4):
                for fh in range(2):
                    db = (tt4 * 2 + fh) % 2
                    if S["pd_free"][db] is not None:
                        ctx.wait("tensor", S["pd_free"][db])
                    for i in range(16):
                        mm = ctx.op("tensor",
                                    lambda e, i=i, fh=fh, db=db, s=s, tt4=tt4:
                                    e.matmul(
                                        pd[:, db, :],
                                        hsb[:, i, tt4 * 128:(tt4 + 1) * 128],
                                        dnb[:, s, i, fh, :],
                                        start=(i == 0), stop=(i == 15)),
                                    "dmm" if i == 15 else None)
                    ctx.wait("vector", mm)
                    if shared:
                        if tt4 == 0 and fh == 0 and S["stg_free"] is not None:
                            ctx.wait("vector", S["stg_free"])
                        ev = ctx.op("vector", lambda e, tt4=tt4, fh=fh, db=db:
                                    e.tensor_copy(
                                        stg[:, tt4, fh * 512:(fh + 1) * 512],
                                        pd[:, db, :]), "mdd")
                    else:
                        if tt4 == 0 and fh == 0 and S["eo_free"][kb] is not None:
                            ctx.wait("vector", *S["eo_free"][kb])
                        ev = ctx.op("vector",
                                    lambda e, kb=kb, tt4=tt4, fh=fh, db=db:
                                    e.tensor_copy(
                                        eo_sb[:, kb, tt4, fh * 512:(fh + 1) * 512],
                                        pd[:, db, :]), "mdd")
                    S["pd_free"][db] = ev
                if tt4 == 3:
                    if shared:
                        ctx.wait("sync", ev, ctx.tok("mac"))
                        S["stg_free"] = ctx.dma(
                            "sync",
                            bass.AP(d["acc"].tensor, k * 512 * DM,
                                    [[DM, 128], [128 * DM, 4], [1, DM]]),
                            stg[:, :, :], "mac")
                    else:
                        rc = k - NSH
                        ctx.wait("sync", ev)
                        so = ctx.dma(
                            "sync",
                            bass.AP(d["eod"].tensor, rc * 512 * DM,
                                    [[DM, 128], [128 * DM, 4], [1, DM]]),
                            eo_sb[:, kb, :, :], "meo%d" % kb)
                        S["eo_free"][kb] = [so]
            S["down_done"][k] = ctx.tok("dmm")

        dn_load(0)
        xtl(0)
        for k in range(NK):
            if k == 0:
                tp_group(0)
            gu_mm(k)
            if k < NSH:
                dispatch(k)
            if k + 1 < NK:
                if k + 1 >= NSH:
                    dn_load(k + 1)
                xtl(k + 1)
                tp_group(k + 1)
            down(k)
        ctx.wait("sync", ctx.tok("mac"), ctx.tok("meo0"), ctx.tok("meo1"))

    def combine(self, a, bsrc, csrc, dst, st):
        ctx, NT = self.ctx, self.NT
        ta = self.sbuf("cb_a", [128, 2, DM], F32, st)
        tb = self.sbuf("cb_b", [128, 2, DM], F32, st)
        tc_ = self.sbuf("cb_c", [128, 2, DM], F32, st)
        to = self.sbuf("cb_o", [128, 2, DM], F32, st)
        free = [None, None]
        for tt in range(NT):
            b = tt % 2
            r0, r1 = tt * 128, (tt + 1) * 128
            if free[b] is not None:
                ctx.wait("sync", *free[b])
            l1 = ctx.dma("sync", ta[:, b, :], a[r0:r1, :], "cbl%d" % b)
            l2 = ctx.dma("sync", tb[:, b, :], bsrc[r0:r1, :], "cbl%d" % b)
            l3 = ctx.dma("sync", tc_[:, b, :], csrc[r0:r1, :], "cbl%d" % b)
            ctx.wait("vector", l3)
            v1 = ctx.op("vector", lambda e, b=b: e.tensor_tensor(
                to[:, b, :], tb[:, b, :], tc_[:, b, :], OP.add), "cbd")
            ctx.wait("vector", v1)
            v2 = ctx.op("vector", lambda e, b=b: e.scalar_tensor_tensor(
                to[:, b, :], to[:, b, :], 0.5, ta[:, b, :], OP.mult, OP.add), "cbd")
            ctx.wait("sync", v2)
            so = ctx.dma("sync", dst[r0:r1, :], to[:, b, :], "cbs%d" % b)
            free[b] = [so, v2]
        ctx.wait("sync", ctx.tok("cbs0"), ctx.tok("cbs1"))

    def combine_moe(self, a, bsrc, dst, st):
        ctx, d, NT = self.ctx, self.d, self.NT
        I32 = mybir.dt.int32
        ta = self.sbuf("cm_a", [128, 2, DM], F32, st)
        tb = self.sbuf("cm_b", [128, 2, DM], F32, st)
        tc_ = self.sbuf("cm_c", [128, 2, DM], F32, st)
        to = self.sbuf("cm_o", [128, 2, DM], F32, st)
        gsb = self.sbuf("cm_g", [128, 2, 2, DM], BF16, st)
        zsb = self.sbuf("cm_z", [128, 2, 2, DM], F32, st)
        six = self.sbuf("cm_six", [128, 2, NT], I32, st)
        wx = self.sbuf("cm_wx", [128, 2, NT], F32, st)
        l0 = ctx.dma("sync", six[:, :, :],
                     d["ridx"][:, :, :].rearrange("j p t -> p j t"), "cbr")
        l1 = ctx.dma("sync", wx[:, :, :],
                     d["rw"][:, :, :].rearrange("j p t -> p j t"), "cbr2")
        ctx.wait("gpsimd", l0)
        ctx.wait("scalar", l1)
        load_toks = {}
        gath_toks = {}
        v4s = {}
        zfree = [None, None]
        gfree = [None, None]

        def loads(tt):
            b = tt % 2
            if tt >= 2:
                ctx.wait("sync", ctx.tok("cbs%d" % b), v4s[tt - 2])
            ctx.dma("sync", ta[:, b, :], a[tt * 128:(tt + 1) * 128, :],
                    "cbl%d" % b)
            ctx.dma("sync", tb[:, b, :], bsrc[tt * 128:(tt + 1) * 128, :],
                    "cbl%d" % b)
            load_toks[tt] = ctx.dma(
                "sync", tc_[:, b, :], d["acc"][tt * 128:(tt + 1) * 128, :],
                "cbl%d" % b)

        def gathers(tt):
            b = tt % 2
            if gfree[b] is not None:
                ctx.wait("gpsimd", *gfree[b])
            g1 = ctx.idma(gsb[:, b, 0, :], None, d["eod"][:, :],
                          IOA(six[:, 0, tt:tt + 1], 0), "cbg%d" % (2 * b))
            g2 = ctx.idma(gsb[:, b, 1, :], None, d["eod"][:, :],
                          IOA(six[:, 1, tt:tt + 1], 0), "cbg%d" % (2 * b + 1))
            gath_toks[tt] = (g1, g2)

        loads(0)
        loads(1)
        gathers(0)
        for tt in range(NT):
            b = tt % 2
            g1, g2 = gath_toks[tt]
            ctx.wait("scalar", g1, g2, ctx.tok("cba"), zfree[b])
            z1 = ctx.op("scalar", lambda e, b=b, tt=tt: e.activation(
                zsb[:, b, 0, :], gsb[:, b, 0, :], AF.Copy,
                scale=wx[:, 0, tt:tt + 1]), "cba")
            z2 = ctx.op("scalar", lambda e, b=b, tt=tt: e.activation(
                zsb[:, b, 1, :], gsb[:, b, 1, :], AF.Copy,
                scale=wx[:, 1, tt:tt + 1]), "cba")
            gfree[b] = [z2]
            if tt + 1 < NT:
                gathers(tt + 1)
            ctx.wait("vector", load_toks[tt], z2)
            v1 = ctx.op("vector", lambda e, b=b: e.tensor_tensor(
                to[:, b, :], tb[:, b, :], tc_[:, b, :], OP.add), "cbd")
            ctx.wait("vector", v1)
            v2 = ctx.op("vector", lambda e, b=b: e.tensor_tensor(
                to[:, b, :], to[:, b, :], zsb[:, b, 0, :], OP.add), "cbd")
            ctx.wait("vector", v2)
            v3 = ctx.op("vector", lambda e, b=b: e.tensor_tensor(
                to[:, b, :], to[:, b, :], zsb[:, b, 1, :], OP.add), "cbd")
            ctx.wait("vector", v3)
            v4 = ctx.op("vector", lambda e, b=b: e.scalar_tensor_tensor(
                to[:, b, :], to[:, b, :], 0.5, ta[:, b, :], OP.mult, OP.add), "cbd")
            zfree[b] = v3
            v4s[tt] = v4
            ctx.wait("gpsimd", v4)
            ctx.dma("gpsimd", dst[tt * 128:(tt + 1) * 128, :], to[:, b, :],
                    "cbs%d" % b)
            if tt + 2 < NT:
                loads(tt + 2)
        ctx.wait("sync", ctx.tok("cbs0"), ctx.tok("cbs1"))

    def ffn_block(self, xsrc, dst):
        ctx, d = self.ctx, self.d
        st = self.estack()
        self.sbuf("vfm", [128, 8, self.T], BF16, st)
        self.ln_pass(xsrc, dst_tm=d["u_s"])
        ctx.barrier()
        ctx.flush()
        self.phase()
        self.ln_pass(d["u_s"], fm_sb=self.sb["vfm"], gamma=self.gamma(0))
        ctx.barrier()
        ctx.flush()
        self.phase()
        st2 = self.estack()
        self.sbuf("cbias", [128, 1], F32, st2, side="right")
        st3 = self.estack()
        bt = self.mla_prep(st3)
        ctx.barrier()
        ctx.flush()
        self.phase()
        self.close_stack(st3)
        self.close_stack(st)
        if not hasattr(self, "_wprep"):
            self.prep_weights()
        st4 = self.estack()
        self.attention(True, 1.0 / 8.0, bt, st4)
        ctx.barrier()
        ctx.flush()
        self.phase()
        self.close_stack(st4)
        self.close_stack(st2)
        st5 = self.estack()
        self.oproj("m_o_T", d["u_s"], d["u_s"], False, st5)
        ctx.barrier()
        ctx.flush()
        self.phase()
        self.close_stack(st5)
        st7 = self.estack()
        hook = self.moe_router_setup(st7)
        self.ln_pass(d["u_s"], gamma=self.gamma(1), hook=hook)
        ctx.barrier()
        ctx.flush()
        self.phase()
        self.moe_experts(st7)
        ctx.barrier()
        ctx.flush()
        self.phase()
        self.close_stack(st7)
        st8 = self.estack()
        self.combine_moe(xsrc, d["u_s"], dst, st8)
        ctx.barrier()
        ctx.flush()
        self.phase()
        self.close_stack(st8)

    def attn_block(self, xsrc, dst):
        ctx, d = self.ctx, self.d
        st = self.estack()
        self.sbuf("vfm", [128, 8, self.T], BF16, st)
        self.ln_pass(xsrc, fm_sb=self.sb["vfm"], gamma=self.gamma(2), beta=self.gamma(3))
        ctx.barrier()
        ctx.flush()
        self.phase()
        st2 = self.estack()
        self.sbuf("cbias", [128, 1], F32, st2, side="right")
        st3 = self.estack()
        bt = self.s2_prep(st3)
        ctx.barrier()
        ctx.flush()
        self.phase()
        self.close_stack(st3)
        self.close_stack(st)
        st4 = self.estack()
        self.attention(False, 1.0, bt, st4)
        ctx.barrier()
        ctx.flush()
        self.phase()
        self.close_stack(st4)
        self.close_stack(st2)
        st5 = self.estack()
        self.oproj("a_o_T", xsrc, dst, True, st5)
        ctx.barrier()
        ctx.flush()
        self.phase()
        self.close_stack(st5)

    def build(self):
        d = self.d
        self.declare()
        self.load_consts()
        try:
            self.ffn_block(d["x_in"], d["x_cur"])
            self.attn_block(d["x_cur"], d["x_cur"])
            self.ffn_block(d["x_cur"], d["y_out"])
        except StopBuild:
            self.ctx.flush()
            for s in reversed(self.open_stacks):
                s.close()
            self.open_stacks = []
        self.ctx.flush()
        self.stack.close()
        return self.nc


def _prep_inputs(inp, T):
    f32 = np.float32
    C = np.ascontiguousarray
    out = {}
    out["c_ident"] = np.eye(128, dtype=f32)
    k_idx, q_idx = np.meshgrid(np.arange(128), np.arange(128), indexing="ij")
    out["c_mask"] = (k_idx <= q_idx).astype(f32)
    inv = 1.0 / (10000.0 ** (np.arange(0, RD, 2, dtype=np.float64) / RD))
    tpos = np.arange(T, dtype=np.float64)
    fr = np.outer(tpos, inv)
    emb = np.concatenate([fr, fr], axis=1)
    cos = np.cos(emb).astype(f32)
    sin = np.sin(emb).astype(f32)
    sinS = np.concatenate([-sin[:, :16], sin[:, 16:]], axis=1).astype(f32)
    out["c_cos"] = C(np.tile(cos, (1, NH)))
    out["c_sinS"] = C(np.tile(sinS, (1, NH)))
    gam = np.stack([
        np.asarray(inp["L0_ffn_ln1_w"], f32),
        np.asarray(inp["L0_ffn_ln2_w"], f32),
        np.asarray(inp["L0_attn_ln_gamma"], f32),
        np.asarray(inp["L0_attn_ln_beta"], f32),
        np.tile(np.asarray(inp["L0_attn_q_gamma"], f32), NH),
        np.tile(np.asarray(inp["L0_attn_k_gamma"], f32), NH),
    ])
    out["c_gam"] = C(np.broadcast_to(gam[None], (128, 6, DM)))
    out["c_bo"] = C(np.broadcast_to(np.asarray(inp["L0_attn_b_o"], f32)[None], (128, DM)))
    rw = np.zeros((DM, 8), f32)
    rw[:, :7] = np.asarray(inp["L0_ffn_router_w"], f32).T
    out["c_router"] = C(rw.reshape(8, 128, 8))
    rb = np.zeros((8,), f32)
    rb[:7] = np.asarray(inp["L0_ffn_routing_bias"], f32)
    out["c_rbias"] = C(np.broadcast_to(rb[None], (128, 8)))
    NCH = 2 * (T // 512) + 7
    ei, ej = np.meshgrid(np.arange(7), np.arange(7), indexing="ij")
    rm = np.concatenate([
        np.broadcast_to((ej < ei).astype(f32).reshape(1, 49), (128, 49)),
        np.broadcast_to((ej <= ei).astype(f32).reshape(1, 49), (128, 49)),
        np.broadcast_to(512.0 * np.arange(NCH, dtype=f32)[None], (128, NCH)),
        np.broadcast_to(np.arange(NCH, dtype=f32)[None], (128, NCH)),
        128.0 * np.arange(16, dtype=f32)[None]
        + np.arange(128, dtype=f32)[:, None],
        np.arange(128, dtype=f32)[:, None],
    ], axis=1)
    out["c_rmeta"] = C(rm.astype(f32))
    h2 = np.zeros((2, 128), f32)
    h2[0, 0:64] = 1.0
    h2[1, 64:128] = 1.0
    out["c_half2"] = h2

    def blk_kc(w):
        K, N = w.shape
        return C(w.reshape(K // 128, 128, N))

    kv_d = np.asarray(inp["L0_ffn_kv_d"], f32)
    q_d = np.asarray(inp["L0_ffn_q_d"], f32)
    k_u = np.asarray(inp["L0_ffn_k_u"], f32)
    q_u = np.asarray(inp["L0_ffn_q_u"], f32)
    v_u = np.asarray(inp["L0_ffn_v_u"], f32)
    rq = np.asarray(inp["L0_ffn_rope_q_w"], f32)
    rk = np.asarray(inp["L0_ffn_rope_k_w"], f32)
    ow = np.asarray(inp["L0_ffn_o_w"], f32)
    out["m_kv_T"] = blk_kc(kv_d.T)
    out["m_q_T"] = blk_kc(q_d.T)
    nope_rows = np.concatenate([np.arange(h * 64 + 32, h * 64 + 64) for h in range(NH)])
    out["m_knope_T"] = blk_kc(k_u[nope_rows].T)
    out["m_qnope_T"] = blk_kc(q_u[nope_rows].T)
    out["m_v_T"] = blk_kc(v_u.T)
    out["m_rq_T"] = blk_kc(rq.T)
    out["m_rk_T"] = blk_kc(rk.T)
    out["m_o_T"] = blk_kc(ow.T)
    out["a_q_T"] = blk_kc(np.asarray(inp["L0_attn_w_q"], f32).T)
    out["a_k_T"] = blk_kc(np.asarray(inp["L0_attn_w_k"], f32).T)
    out["a_v_T"] = blk_kc(np.asarray(inp["L0_attn_w_v"], f32).T)
    out["a_o_T"] = blk_kc(np.asarray(inp["L0_attn_w_o"], f32).T)

    sgw = np.asarray(inp["L0_ffn_shared_gate"], f32)[0]
    suw = np.asarray(inp["L0_ffn_shared_up"], f32)[0]
    sdw = np.asarray(inp["L0_ffn_shared_down"], f32)[0]
    rgw = np.asarray(inp["L0_ffn_routed_gate"], f32)
    ruw = np.asarray(inp["L0_ffn_routed_up"], f32)
    rdw = np.asarray(inp["L0_ffn_routed_down"], f32)
    gu = np.zeros((8, 16, 128, 2, 8, 128), f32)
    dn = np.zeros((8, 128, 16, 2, 512), f32)
    for e in range(8):
        gw = rgw[e] if e < 7 else sgw
        uw = ruw[e] if e < 7 else suw
        dw = rdw[e] if e < 7 else sdw
        gT = gw.T.reshape(8, 128, 16, 128)
        uT = uw.T.reshape(8, 128, 16, 128)
        gu[e, :, :, 0] = gT.transpose(2, 1, 0, 3)
        gu[e, :, :, 1] = uT.transpose(2, 1, 0, 3)
        dT = dw.T.reshape(16, 128, 2, 512)
        dn[e] = dT.transpose(1, 0, 2, 3)
    out["moe_gu"] = C(gu)
    out["moe_dn"] = C(dn)
    return out


def kernel(**inputs):
    x = np.asarray(inputs["L0_attn_x"], np.float32)
    B, T, _ = x.shape
    mk = MK(T)
    nc = mk.build()
    weights = _prep_inputs(inputs, T)
    in_maps = []
    core_ids = [0, 2, 4, 6][:B]
    for b in range(B):
        m = dict(weights)
        m["x_in"] = np.ascontiguousarray(x[b])
        in_maps.append(m)
    res = run_bass_kernel_spmd(nc, in_maps, core_ids=core_ids)
    out = np.stack([res.results[b]["y_out"] for b in range(B)])
    return out
